# version 07
# baseline (speedup 1.0000x reference)
import numpy as np
from contextlib import ExitStack

import concourse.bass as bass
import concourse.mybir as mybir
from concourse.bass_utils import run_bass_kernel_spmd

F32 = mybir.dt.float32
BF16 = mybir.dt.bfloat16
U32 = mybir.dt.uint32
AF = mybir.ActivationFunctionType
ALU = mybir.AluOpType
AX = mybir.AxisListType

T = 4096
D = 1024
NT = T // 128
ND = D // 128
LN_EPS = 1e-5


class Sem:

    def __init__(self, handle):
        self.h = handle
        self.n = 0


def inc(instr, sem, k=1):
    sem.n += k
    return instr.then_inc(sem.h, k)


def mk_sems(nc, ctx, names, pfx=""):
    return [Sem(ctx.enter_context(nc.semaphore(pfx + n))) for n in names]


def emit_ln_tile(block, src_ap, dst_ap, stats, scrap, vs, ss,
                 add_ap=None, width=D):
    w = width

    @block.vector
    def _(v):
        v.tensor_scalar(out=scrap[:, 0:w], in0=src_ap, scalar1=0.0, op0=ALU.add,
                        scalar2=0.0, op1=ALU.add, accum_out=stats[:, 0:1])
        inc(v.scalar_tensor_tensor(out=scrap[:, w:2 * w], in0=src_ap, scalar=1.0,
                                   op0=ALU.mult, op1=ALU.mult, in1=src_ap,
                                   accum_out=stats[:, 1:2]), vs)
        v.wait_ge(vs.h, vs.n)
        v.tensor_scalar(out=stats[:, 2:3], in0=stats[:, 0:1], scalar1=1.0 / w,
                        op0=ALU.mult, scalar2=None)
        inc(v.tensor_scalar(out=stats[:, 4:5], in0=stats[:, 1:2], scalar1=1.0 / w,
                            op0=ALU.mult, scalar2=None), vs)
        v.wait_ge(vs.h, vs.n)
        inc(v.tensor_tensor(out=stats[:, 5:6], in0=stats[:, 2:3], in1=stats[:, 2:3],
                            op=ALU.mult), vs)
        v.wait_ge(vs.h, vs.n)
        inc(v.tensor_scalar(out=stats[:, 4:5], in0=stats[:, 4:5], scalar1=stats[:, 5:6],
                            op0=ALU.subtract, scalar2=LN_EPS, op1=ALU.add), vs)

    @block.scalar
    def _(s):
        s.wait_ge(vs.h, vs.n)
        inc(s.activation(out=stats[:, 5:6], in_=stats[:, 4:5], func=AF.Ln), ss)
        s.wait_ge(ss.h, ss.n)
        inc(s.activation(out=stats[:, 3:4], in_=stats[:, 5:6], func=AF.Exp, scale=-0.5), ss)

    @block.vector
    def _(v):
        v.wait_ge(ss.h, ss.n)
        if add_ap is None:
            op = v.tensor_scalar(out=dst_ap, in0=src_ap, scalar1=stats[:, 2:3],
                                 op0=ALU.subtract, scalar2=stats[:, 3:4], op1=ALU.mult)
        else:
            v.tensor_scalar(out=scrap[:, 0:w], in0=src_ap, scalar1=stats[:, 2:3],
                            op0=ALU.subtract, scalar2=stats[:, 3:4], op1=ALU.mult)
            op = v.tensor_tensor(out=dst_ap, in0=scrap[:, 0:w], in1=add_ap, op=ALU.add)
        inc(op, vs)


def emit_ln_stats(block, src_ap, st8, scrap, vs, ss, width=D):
    w = width

    @block.vector
    def _(v):
        v.tensor_scalar(out=scrap[:, 0:w], in0=src_ap, scalar1=0.0, op0=ALU.add,
                        scalar2=0.0, op1=ALU.add, accum_out=st8[:, 0:1])
        inc(v.scalar_tensor_tensor(out=scrap[:, 0:w], in0=src_ap, scalar=1.0,
                                   op0=ALU.mult, op1=ALU.mult, in1=src_ap,
                                   accum_out=st8[:, 1:2]), vs)
        v.wait_ge(vs.h, vs.n)
        v.tensor_scalar(out=st8[:, 2:3], in0=st8[:, 0:1], scalar1=1.0 / w,
                        op0=ALU.mult, scalar2=None)
        inc(v.tensor_scalar(out=st8[:, 4:5], in0=st8[:, 1:2], scalar1=1.0 / w,
                            op0=ALU.mult, scalar2=None), vs)
        v.wait_ge(vs.h, vs.n)
        inc(v.tensor_tensor(out=st8[:, 5:6], in0=st8[:, 2:3], in1=st8[:, 2:3],
                            op=ALU.mult), vs)
        v.wait_ge(vs.h, vs.n)
        inc(v.tensor_scalar(out=st8[:, 4:5], in0=st8[:, 4:5], scalar1=st8[:, 5:6],
                            op0=ALU.subtract, scalar2=LN_EPS, op1=ALU.add), vs)

    @block.scalar
    def _(s):
        s.wait_ge(vs.h, vs.n)
        inc(s.activation(out=st8[:, 5:6], in_=st8[:, 4:5], func=AF.Ln), ss)
        s.wait_ge(ss.h, ss.n)
        inc(s.activation(out=st8[:, 3:4], in_=st8[:, 5:6], func=AF.Exp, scale=-0.5), ss)
    return ss.n


def emit_ln_apply(block, src_ap, dst_ap, st8, scrap, vs, ss, rstd_ready,
                  add_ap=None, width=D):
    w = width

    @block.vector
    def _(v):
        v.wait_ge(ss.h, rstd_ready)
        if add_ap is None:
            op = v.tensor_scalar(out=dst_ap, in0=src_ap, scalar1=st8[:, 2:3],
                                 op0=ALU.subtract, scalar2=st8[:, 3:4], op1=ALU.mult)
        else:
            v.tensor_scalar(out=scrap[:, 0:w], in0=src_ap, scalar1=st8[:, 2:3],
                            op0=ALU.subtract, scalar2=st8[:, 3:4], op1=ALU.mult)
            op = v.tensor_tensor(out=dst_ap, in0=scrap[:, 0:w], in1=add_ap, op=ALU.add)
        inc(op, vs)


def emit_stage_a(nc, ctx, block, x_in, x_out, pfx="a_", sems=None):
    E = ctx.enter_context
    ld, vs, ss, ts, gs, trs, st = sems or mk_sems(
        nc, ctx, ["ld", "vs", "ss", "ts", "gs", "trs", "st"], pfx)

    aseg_dram = nc.dram_tensor(pfx + "segbf", [512, D], BF16, kind="Internal").ap()

    xs = E(nc.sbuf_tensor(pfx + "xs", [128, 4 * D], F32))
    aseg = E(nc.sbuf_tensor(pfx + "aseg", [128, 4 * D], BF16))
    asegT = E(nc.sbuf_tensor(pfx + "asegT", [128, 8 * 512], BF16))
    pT = E(nc.sbuf_tensor(pfx + "pT", [128, 4 * 512], BF16))
    stats = E(nc.sbuf_tensor(pfx + "stats", [128, 16], F32))
    scrap = E(nc.sbuf_tensor(pfx + "scrap", [128, 3 * D], F32))
    ones = E(nc.sbuf_tensor(pfx + "ones", [128, 2], BF16))
    id1 = E(nc.sbuf_tensor(pfx + "id1", [1, 2], F32))
    rrow = E(nc.sbuf_tensor(pfx + "rrow", [1, 512], F32))
    attn = E(nc.sbuf_tensor(pfx + "attn", [128, 4 * D], F32))
    x1seg = E(nc.sbuf_tensor(pfx + "x1seg", [128, 4 * D], F32))

    psS = E(nc.psum_tensor(pfx + "psS", [128, 2 * 512], F32))
    psO = E(nc.psum_tensor(pfx + "psO", [128, 2 * 1024], F32))
    psden = E(nc.psum_tensor(pfx + "psden", [128, 512], F32))
    psdc = E(nc.psum_tensor(pfx + "psdc", [128, 8], F32))

    SEGS = [(g, r) for g in range(2) for r in range(4)]

    def seg_src(ap, g, r):
        base = g * 2048 + r
        return ap[base:base + 2045:4, :].rearrange("(t p) d -> p t d", p=128)

    x3 = xs[:].rearrange("p (t d) -> p t d", d=D)
    o3 = x1seg[:].rearrange("p (t d) -> p t d", d=D)

    @block.gpsimd
    def _(g_):
        g_.memset(ones[:], 1.0)
        g_.memset(id1[:], 0.0)
        inc(g_.memset(id1[0:1, 0:1], 1.0), gs)

    for i, (g, r) in enumerate(SEGS):
        @block.sync
        def _(sync, g=g, r=r):
            if i > 0:
                sync.wait_ge(st.h, st.n)
            inc(sync.dma_start(out=x3, in_=seg_src(x_in, g, r)), ld, 16)

        @block.vector
        def _(v):
            v.wait_ge(ld.h, ld.n)

        @block.scalar
        def _(s):
            s.wait_ge(ld.h, ld.n)

        for t in range(4):
            emit_ln_tile(block, xs[:, t * D:(t + 1) * D], aseg[:, t * D:(t + 1) * D],
                         stats, scrap, vs, ss)

        @block.gpsimd
        def _(g_):
            g_.wait_ge(vs.h, vs.n)
            inc(g_.dma_start(out=aseg_dram.rearrange("(t p) d -> p t d", p=128),
                             in_=aseg[:].rearrange("p (t d) -> p t d", d=D)), gs, 16)

        @block.scalar
        def _(s):
            s.wait_ge(gs.h, gs.n)
            for dc in range(8):
                inc(s.dma_start(out=asegT[:, dc * 512:(dc + 1) * 512],
                                in_=aseg_dram[:, dc * 128:(dc + 1) * 128],
                                transpose=True), trs, 16)

        for kc in range(4):
            sb = psS[:, (kc % 2) * 512:(kc % 2) * 512 + 512]

            @block.tensor
            def _(t_, kc=kc, sb=sb):
                t_.wait_ge(trs.h, trs.n)
                if kc >= 2:
                    t_.wait_ge(ss.h, ss.n)
                for dc in range(8):
                    mm = t_.matmul(sb, asegT[:, dc * 512 + kc * 128: dc * 512 + kc * 128 + 128],
                                   asegT[:, dc * 512:(dc + 1) * 512],
                                   start=(dc == 0), stop=(dc == 7))
                inc(mm, ts)

            @block.scalar
            def _(s, kc=kc, sb=sb):
                s.wait_ge(ts.h, ts.n)
                inc(s.activation(out=pT[:, kc * 512:(kc + 1) * 512], in_=sb,
                                 func=AF.Exp, scale=1.0 / 32.0), ss)

        @block.tensor
        def _(t_):
            t_.wait_ge(ss.h, ss.n)
            for kc in range(4):
                mm = t_.matmul(psden[0:1, :], ones[:, 0:1], pT[:, kc * 512:(kc + 1) * 512],
                               start=(kc == 0), stop=(kc == 3))
            inc(mm, ts)

        @block.vector
        def _(v):
            v.wait_ge(ts.h, ts.n)
            inc(v.reciprocal(rrow[0:1, :], psden[0:1, :]), vs)

        @block.tensor
        def _(t_):
            t_.wait_ge(vs.h, vs.n)
            for qc in range(4):
                mm = t_.transpose(psdc[:, qc:qc + 1], rrow[0:1, qc * 128:(qc + 1) * 128],
                                  id1[0:1, 0:1])
            inc(mm, ts)

        for qc in range(4):
            ob = psO[:, (qc % 2) * 1024:(qc % 2) * 1024 + 1024]

            @block.tensor
            def _(t_, qc=qc, ob=ob):
                if qc >= 2:
                    t_.wait_ge(vs.h, vs.n)
                for kc in range(4):
                    for h in range(2):
                        mm = t_.matmul(ob[:, h * 512:(h + 1) * 512],
                                       pT[:, kc * 512 + qc * 128: kc * 512 + qc * 128 + 128],
                                       aseg[:, kc * D + h * 512: kc * D + h * 512 + 512],
                                       start=(kc == 0), stop=(kc == 3))
                inc(mm, ts)

            @block.vector
            def _(v, qc=qc, ob=ob):
                v.wait_ge(ts.h, ts.n)
                inc(v.tensor_scalar(out=attn[:, qc * D:(qc + 1) * D], in0=ob,
                                    scalar1=psdc[:, qc:qc + 1], op0=ALU.mult,
                                    scalar2=0.25, op1=ALU.mult), vs)

        @block.scalar
        def _(s):
            s.wait_ge(vs.h, vs.n)

        for t in range(4):
            emit_ln_tile(block, attn[:, t * D:(t + 1) * D], x1seg[:, t * D:(t + 1) * D],
                         stats, scrap, vs, ss, add_ap=xs[:, t * D:(t + 1) * D])

        @block.sync
        def _(sync, g=g, r=r):
            sync.wait_ge(vs.h, vs.n)
            inc(sync.dma_start(out=seg_src(x_out, g, r), in_=o3), st, 16)

    @block.sync
    def _(sync):
        sync.wait_ge(st.h, st.n)

    return st


MAGIC = 12582912.0


def emit_stage_b(nc, ctx, block, x_in, w_in, h_in, x_out, pfx="b_", debug=False, sems=None):
    E = ctx.enter_context
    ld, vs, ss, ts, gs, trs, st = sems or mk_sems(
        nc, ctx, ["ld", "vs", "ss", "ts", "gs", "trs", "st"], pfx)

    kind = "ExternalOutput" if debug else "Internal"
    fdram = nc.dram_tensor(pfx + "fdram", [T, D], BF16, kind=kind).ap()
    kdram = nc.dram_tensor(pfx + "kdram", [T, D], BF16, kind=kind).ap()
    wqdram = nc.dram_tensor(pfx + "wqdram", [D, D], BF16, kind=kind).ap()
    if debug:
        dbg_ffn = nc.dram_tensor(pfx + "dbg_ffn", [T, D], F32, kind="ExternalOutput").ap()
        dbg_scl = nc.dram_tensor(pfx + "dbg_scl", [128, NT + 8], F32, kind="ExternalOutput").ap()
        dbg_xabs = nc.dram_tensor(pfx + "dbg_xabs", [128, D], F32, kind="ExternalOutput").ap()
        dbg_xh = nc.dram_tensor(pfx + "dbg_xh", [128, D], F32, kind="ExternalOutput").ap()

    finT = E(nc.sbuf_tensor(pfx + "finT", [128, 8 * T], BF16))
    hsb = E(nc.sbuf_tensor(pfx + "hsb", [128, 8 * D], BF16))
    wsb = E(nc.sbuf_tensor(pfx + "wsb", [128, 8 * D], F32))
    wqv = E(nc.sbuf_tensor(pfx + "wqv", [128, 8 * D], BF16))
    wqvT = E(nc.sbuf_tensor(pfx + "wqvT", [128, 8 * D], BF16))
    kqT = finT
    scales = E(nc.sbuf_tensor(pfx + "scales", [128, NT + 8], F32))
    stats = E(nc.sbuf_tensor(pfx + "stats", [128, 16], F32))
    scrap = E(nc.sbuf_tensor(pfx + "scrap", [128, 3 * D], F32))
    wstat = E(nc.sbuf_tensor(pfx + "wstat", [128, 16], F32))
    wtmp = E(nc.sbuf_tensor(pfx + "wtmp", [128, 2 * D], F32))

    psA = E(nc.psum_tensor(pfx + "psA", [128, 2 * D], F32))
    psB = E(nc.psum_tensor(pfx + "psB", [128, 2 * D], F32))

    @block.sync
    def _(sync):
        inc(sync.dma_start(out=hsb[:].rearrange("p (c d) -> p c d", d=D),
                           in_=h_in.rearrange("(c p) d -> p c d", p=128)), ld, 16)
        inc(sync.dma_start(out=wsb[:].rearrange("p (c d) -> p c d", d=D),
                           in_=w_in.rearrange("(c p) d -> p c d", p=128)), ld, 16)

    @block.scalar
    def _(s):
        s.wait_ge(ld.h, ld.n)
        for c in range(8):
            inc(s.activation(out=wtmp[:, 0:D], in_=wsb[:, c * D:(c + 1) * D],
                             func=AF.Abs, accum_out=wstat[:, c:c + 1]), ss)

    @block.vector
    def _(v):
        v.wait_ge(ss.h, ss.n)
        for c in range(8):
            inc(v.tensor_scalar(out=wstat[:, c:c + 1], in0=wstat[:, c:c + 1],
                                scalar1=1.0 / D, op0=ALU.mult, scalar2=1e-5, op1=ALU.max), vs)
        v.wait_ge(vs.h, vs.n)
        for c in range(8):
            inc(v.reciprocal(wstat[:, 8 + c:9 + c], wstat[:, c:c + 1]), vs)
        v.wait_ge(vs.h, vs.n)
        for c in range(8):
            w_ = wsb[:, c * D:(c + 1) * D]
            v.tensor_scalar(out=wtmp[:, 0:D], in0=w_, scalar1=wstat[:, 8 + c:9 + c],
                            op0=ALU.mult, scalar2=0.5, op1=ALU.is_gt)
            v.tensor_scalar(out=wtmp[:, D:2 * D], in0=w_, scalar1=wstat[:, 8 + c:9 + c],
                            op0=ALU.mult, scalar2=-0.5, op1=ALU.is_lt)
            v.tensor_tensor(out=wtmp[:, 0:D], in0=wtmp[:, 0:D], in1=wtmp[:, D:2 * D],
                            op=ALU.subtract)
            inc(v.tensor_scalar(out=wqv[:, c * D:(c + 1) * D], in0=wtmp[:, 0:D],
                                scalar1=wstat[:, c:c + 1], op0=ALU.mult, scalar2=None), vs)

    @block.gpsimd
    def _(g_):
        g_.wait_ge(vs.h, vs.n)
        inc(g_.dma_start(out=wqdram.rearrange("(c p) d -> p c d", p=128),
                         in_=wqv[:].rearrange("p (c d) -> p c d", d=D)), gs, 16)

    @block.scalar
    def _(s):
        s.wait_ge(gs.h, gs.n)
        for c in range(8):
            inc(s.dma_start(out=wqvT[:, c * D:(c + 1) * D],
                            in_=wqdram[:, c * 128:(c + 1) * 128], transpose=True), trs, 16)

    xtp = E(nc.sbuf_tensor(pfx + "xtp", [128, 2 * D], F32))
    fbp = E(nc.sbuf_tensor(pfx + "fbp", [128, 2 * D], BF16))
    ld_at, sp_at, rstd_at, ap_at = {}, {}, {}, {}

    def b1_load(t):
        @block.sync
        def _(sync, t=t):
            if t - 2 in ap_at:
                sync.wait_ge(vs.h, ap_at[t - 2])
            inc(sync.dma_start(out=xtp[:, (t % 2) * D:(t % 2 + 1) * D],
                               in_=x_in[t * 128:(t + 1) * 128, :]), ld, 16)
        ld_at[t] = ld.n

    def b1_stats(t):
        @block.vector
        def _(v, t=t):
            v.wait_ge(ld.h, ld_at[t])
        rstd_at[t] = emit_ln_stats(block, xtp[:, (t % 2) * D:(t % 2 + 1) * D],
                                   stats[:, (t % 2) * 8:(t % 2) * 8 + 8], scrap, vs, ss)

    def b1_apply(t):
        if t - 2 in sp_at:
            @block.vector
            def _(v, t=t):
                v.wait_ge(gs.h, sp_at[t - 2])
        emit_ln_apply(block, xtp[:, (t % 2) * D:(t % 2 + 1) * D],
                      fbp[:, (t % 2) * D:(t % 2 + 1) * D],
                      stats[:, (t % 2) * 8:(t % 2) * 8 + 8], scrap, vs, ss, rstd_at[t])
        ap_at[t] = vs.n

        @block.gpsimd
        def _(g_, t=t):
            g_.wait_ge(vs.h, ap_at[t])
            inc(g_.dma_start(out=fdram[t * 128:(t + 1) * 128, :],
                             in_=fbp[:, (t % 2) * D:(t % 2 + 1) * D]), gs, 16)
        sp_at[t] = gs.n

    b1_load(0)
    b1_stats(0)
    b1_load(1)
    for t in range(NT):
        if t + 1 < NT:
            b1_stats(t + 1)
        b1_apply(t)
        if t + 2 < NT:
            b1_load(t + 2)

    @block.scalar
    def _(s):
        s.wait_ge(gs.h, gs.n)
        for c in range(8):
            inc(s.dma_start(out=finT[:, c * T:(c + 1) * T],
                            in_=fdram[:, c * 128:(c + 1) * 128], transpose=True), trs, 16)

    mm3_at, abs3_at, red3_at, q3_at, sp3_at = {}, {}, {}, {}, {}
    xabs2 = xtp
    kq2 = fbp
    for t in range(NT):
        pa = psA[:, (t % 2) * D:(t % 2) * D + D]

        @block.tensor
        def _(t_, t=t, pa=pa):
            t_.wait_ge(trs.h, trs.n)
            if t - 2 in q3_at:
                t_.wait_ge(ss.h, abs3_at[t - 2])
                t_.wait_ge(vs.h, q3_at[t - 2])
            mm = None
            for half in range(2):
                for c in range(8):
                    mm = t_.matmul(pa[:, half * 512:(half + 1) * 512],
                                   finT[:, c * T + t * 128: c * T + t * 128 + 128],
                                   hsb[:, c * D + half * 512: c * D + half * 512 + 512],
                                   start=(c == 0), stop=(c == 7))
            inc(mm, ts)
        mm3_at[t] = ts.n

        @block.scalar
        def _(s, t=t, pa=pa):
            s.wait_ge(ts.h, mm3_at[t])
            if t - 2 in red3_at:
                s.wait_ge(vs.h, red3_at[t - 2])
            inc(s.activation(out=xabs2[:, (t % 2) * D:(t % 2 + 1) * D], in_=pa,
                             func=AF.Abs, accum_out=None), ss)
        abs3_at[t] = ss.n

        @block.vector
        def _(v, t=t, pa=pa):
            st8 = stats[:, (t % 2) * 8:(t % 2) * 8 + 8]
            v.wait_ge(ss.h, abs3_at[t])
            inc(v.reduce_max(st8[:, 0:1], xabs2[:, (t % 2) * D:(t % 2 + 1) * D],
                             axis=AX.X), vs)
            red3_at[t] = vs.n
            v.wait_ge(vs.h, vs.n)
            inc(v.tensor_scalar(out=scales[:, t:t + 1], in0=st8[:, 0:1],
                                scalar1=1e-5, op0=ALU.max, scalar2=1.0 / 7.0, op1=ALU.mult), vs)
            v.wait_ge(vs.h, vs.n)
            inc(v.reciprocal(st8[:, 1:2], scales[:, t:t + 1]), vs)
            v.wait_ge(vs.h, vs.n)
            if t - 2 in sp3_at:
                v.wait_ge(gs.h, sp3_at[t - 2])
            v.tensor_scalar(out=scrap[:, 0:D], in0=pa, scalar1=st8[:, 1:2],
                            op0=ALU.mult, scalar2=MAGIC, op1=ALU.add)
            inc(v.tensor_scalar(out=kq2[:, (t % 2) * D:(t % 2 + 1) * D], in0=scrap[:, 0:D],
                                scalar1=MAGIC, op0=ALU.subtract, scalar2=None), vs)
        q3_at[t] = vs.n

        @block.gpsimd
        def _(g_, t=t):
            g_.wait_ge(vs.h, q3_at[t])
            inc(g_.dma_start(out=kdram[t * 128:(t + 1) * 128, :],
                             in_=kq2[:, (t % 2) * D:(t % 2 + 1) * D]), gs, 16)
        sp3_at[t] = gs.n


    @block.scalar
    def _(s):
        s.wait_ge(gs.h, gs.n)
        for c in range(8):
            inc(s.dma_start(out=kqT[:, c * T:(c + 1) * T],
                            in_=kdram[:, c * 128:(c + 1) * 128], transpose=True), trs, 16)

    mm5_at, ev5_at, rs5_at, ap5_at, st5_at, ld5_at = {}, {}, {}, {}, {}, {}
    ffp = xtp
    x1p = E(nc.sbuf_tensor(pfx + "x1p", [128, 2 * D], F32))
    xo5 = E(nc.sbuf_tensor(pfx + "xo5", [128, 2 * D], F32))

    def b5_load(t):
        @block.sync
        def _(sync, t=t):
            if t - 2 in ap5_at:
                sync.wait_ge(vs.h, ap5_at[t - 2])
            inc(sync.dma_start(out=x1p[:, (t % 2) * D:(t % 2 + 1) * D],
                               in_=x_in[t * 128:(t + 1) * 128, :]), ld, 16)
        ld5_at[t] = ld.n

    def b5_mm(t):
        pb = psB[:, (t % 2) * D:(t % 2) * D + D]

        @block.tensor
        def _(t_, t=t, pb=pb):
            t_.wait_ge(trs.h, trs.n)
            if t - 2 in ev5_at:
                t_.wait_ge(vs.h, ev5_at[t - 2])
            mm = None
            for half in range(2):
                for c in range(8):
                    mm = t_.matmul(pb[:, half * 512:(half + 1) * 512],
                                   kqT[:, c * T + t * 128: c * T + t * 128 + 128],
                                   wqvT[:, c * D + half * 512: c * D + half * 512 + 512],
                                   start=(c == 0), stop=(c == 7))
            inc(mm, ts)
        mm5_at[t] = ts.n

    def b5_evac_stats(t):
        pb = psB[:, (t % 2) * D:(t % 2) * D + D]

        @block.vector
        def _(v, t=t, pb=pb):
            v.wait_ge(ts.h, mm5_at[t])
            inc(v.tensor_scalar(out=ffp[:, (t % 2) * D:(t % 2 + 1) * D], in0=pb,
                                scalar1=scales[:, t:t + 1], op0=ALU.mult, scalar2=None), vs)
        ev5_at[t] = vs.n
        rs5_at[t] = emit_ln_stats(block, ffp[:, (t % 2) * D:(t % 2 + 1) * D],
                                  stats[:, (t % 2) * 8:(t % 2) * 8 + 8], scrap, vs, ss)

    def b5_apply(t):
        if t - 2 in st5_at:
            @block.vector
            def _(v, t=t):
                v.wait_ge(st.h, st5_at[t - 2])

        @block.vector
        def _(v, t=t):
            v.wait_ge(ld.h, ld5_at[t])
        emit_ln_apply(block, ffp[:, (t % 2) * D:(t % 2 + 1) * D],
                      xo5[:, (t % 2) * D:(t % 2 + 1) * D],
                      stats[:, (t % 2) * 8:(t % 2) * 8 + 8], scrap, vs, ss, rs5_at[t],
                      add_ap=x1p[:, (t % 2) * D:(t % 2 + 1) * D])
        ap5_at[t] = vs.n

        @block.sync
        def _(sync, t=t):
            sync.wait_ge(vs.h, ap5_at[t])
            inc(sync.dma_start(out=x_out[t * 128:(t + 1) * 128, :],
                               in_=xo5[:, (t % 2) * D:(t % 2 + 1) * D]), st, 16)
        st5_at[t] = st.n

    b5_load(0)
    b5_mm(0)
    b5_evac_stats(0)
    b5_load(1)
    b5_mm(1)
    for t in range(NT):
        if t + 1 < NT:
            b5_evac_stats(t + 1)
        b5_apply(t)
        if t + 2 < NT:
            b5_load(t + 2)
            b5_mm(t + 2)



    if debug:
        @block.gpsimd
        def _(g_):
            g_.wait_ge(vs.h, vs.n)
            inc(g_.dma_start(out=dbg_scl[:, :], in_=scales[:]), gs, 16)

    @block.sync
    def _(sync):
        sync.wait_ge(st.h, st.n)
        sync.wait_ge(gs.h, gs.n)

    return st


def hadamard_bf16():
    import ml_dtypes
    h = np.array([[1.0]], dtype=np.float64)
    while h.shape[0] < D:
        h = np.block([[h, h], [h, -h]])
    return (h / 32.0).astype(np.float32).astype(ml_dtypes.bfloat16)


def build_prog_b(debug=False):
    nc = bass.Bass("TRN2", target_bir_lowering=False)
    x_in = nc.dram_tensor("x_in", [T, D], F32, kind="ExternalInput").ap()
    w_in = nc.dram_tensor("w_in", [D, D], F32, kind="ExternalInput").ap()
    h_in = nc.dram_tensor("h_in", [D, D], BF16, kind="ExternalInput").ap()
    x_out = nc.dram_tensor("x_out", [T, D], F32, kind="ExternalOutput").ap()
    with ExitStack() as ctx:
        block = ctx.enter_context(nc.Block())
        emit_stage_b(nc, ctx, block, x_in, w_in, h_in, x_out, debug=debug)
    return nc


def qhead_of(tau, half):
    return (tau + 4 * half) if tau < 4 else (tau + 4 + 4 * half)


def emit_stage_d(nc, ctx, block, x_in, wts, x_out, pfx="d_", debug=False, glevel=5,
                 sems=None):
    E = ctx.enter_context
    ld, vs, ss, ts, gs, trs, st = sems or mk_sems(
        nc, ctx, ["ld", "vs", "ss", "ts", "gs", "trs", "st"], pfx)
    HD = 64
    kind = "ExternalOutput" if debug else "Internal"

    din_dram = nc.dram_tensor(pfx + "din", [T, D], F32, kind=kind).ap()
    rms1_dram = nc.dram_tensor(pfx + "rms1", [T, D], BF16, kind=kind).ap()
    wqbf = nc.dram_tensor(pfx + "wqbf", [D, D], BF16, kind="Internal").ap()
    wkbf = nc.dram_tensor(pfx + "wkbf", [256, D], BF16, kind="Internal").ap()
    wvbf = nc.dram_tensor(pfx + "wvbf", [256, D], BF16, kind="Internal").ap()
    wobf = nc.dram_tensor(pfx + "wobf", [D, D], BF16, kind="Internal").ap()
    expbf = nc.dram_tensor(pfx + "expbf", [8 * D, D], BF16, kind="Internal").ap()
    gqa_dram = nc.dram_tensor(pfx + "gqa", [D, T], BF16, kind=kind).ap()
    h_dram = nc.dram_tensor(pfx + "h", [T, D], F32, kind=kind).ap()
    rms2_dram = nc.dram_tensor(pfx + "rms2", [T, D], BF16, kind=kind).ap()

    stats = E(nc.sbuf_tensor(pfx + "stats", [128, 16], F32))
    scrap = E(nc.sbuf_tensor(pfx + "scrap", [128, D], F32))
    xdp = E(nc.sbuf_tensor(pfx + "xdp", [128, 2 * D], F32))
    dinp = E(nc.sbuf_tensor(pfx + "dinp", [128, 2 * D], F32))
    rmsp = E(nc.sbuf_tensor(pfx + "rmsp", [128, 2 * D], BF16))
    anw_sb = E(nc.sbuf_tensor(pfx + "anw", [128, 8], F32))
    fnw_sb = E(nc.sbuf_tensor(pfx + "fnw", [128, 8], F32))
    ones = E(nc.sbuf_tensor(pfx + "ones", [128, 128], F32))
    masks = E(nc.sbuf_tensor(pfx + "masks", [128, 4 * 512], BF16))
    aph = ExitStack()
    qT = aph.enter_context(nc.sbuf_tensor(pfx + "qT", [128, 8 * T], BF16))
    kT = aph.enter_context(nc.sbuf_tensor(pfx + "kT", [128, 2 * T], BF16))
    v_tm = aph.enter_context(nc.sbuf_tensor(pfx + "vtm", [128, NT * 260], BF16))

    nrm_row = aph.enter_context(nc.sbuf_tensor(pfx + "nrmrow", [1, 2 * D], F32))
    psW = aph.enter_context(nc.psum_tensor(pfx + "psW", [128, 16], F32))

    @block.sync
    def _(sync):
        inc(sync.dma_start(out=nrm_row[0:1, 0:D], in_=wts["anw"][None, :]), ld, 16)
        inc(sync.dma_start(out=nrm_row[0:1, D:2 * D], in_=wts["fnw"][None, :]), ld, 16)
        inc(sync.dma_start(out=masks[:].rearrange("p (j n) -> p j n", n=512),
                           in_=wts["masks"].rearrange("(j p) n -> p j n", p=128)), ld, 16)

    @block.tensor
    def _(t_):
        t_.wait_ge(ld.h, ld.n)
        t_.wait_ge(gs.h, 1)
        for c in range(8):
            t_.transpose(psW[:, c:c + 1], nrm_row[0:1, c * 128:(c + 1) * 128], ones[0:1, 0:1])
        for c in range(8):
            mm = t_.transpose(psW[:, 8 + c:9 + c], nrm_row[0:1, D + c * 128:D + (c + 1) * 128],
                              ones[0:1, 0:1])
        inc(mm, ts)

    @block.vector
    def _(v):
        v.wait_ge(ts.h, ts.n)
        inc(v.tensor_copy(anw_sb[:], psW[:, 0:8]), vs)
        inc(v.tensor_copy(fnw_sb[:], psW[:, 8:16]), vs)

    @block.gpsimd
    def _(g_):
        inc(g_.memset(ones[:], 1.0), gs)
        inc(g_.dma_start(out=wqbf[:, :], in_=wts["wq"][:, :]), gs, 16)
        inc(g_.dma_start(out=wkbf[:, :], in_=wts["wk"][:, :]), gs, 16)
        inc(g_.dma_start(out=wvbf[:, :], in_=wts["wv"][:, :]), gs, 16)
        inc(g_.dma_start(out=wobf[:, :], in_=wts["wo"][:, :]), gs, 16)
        inc(g_.dma_start(out=expbf.rearrange("(e r) d -> e r d", e=8),
                         in_=wts["expert_w"][:, :, :]), gs, 16)

    d1_ld, d1_rs, d1_ap, d1_sp = {}, {}, {}, {}

    def d1_load(t):
        @block.sync
        def _(sync, t=t):
            if t - 2 in d1_ap:
                sync.wait_ge(vs.h, d1_ap[t - 2])
            inc(sync.dma_start(out=xdp[:, (t % 2) * D:(t % 2 + 1) * D],
                               in_=x_in[t * 128:(t + 1) * 128, :]), ld, 16)
        d1_ld[t] = ld.n

    def d1_stats(t):
        @block.vector
        def _(v, t=t):
            v.wait_ge(ld.h, d1_ld[t])
        st8 = stats[:, (t % 2) * 8:(t % 2) * 8 + 8]
        emit_ln_stats(block, xdp[:, (t % 2) * D:(t % 2 + 1) * D], st8, scrap, vs, ss)

        @block.vector
        def _(v, t=t):
            v.wait_ge(ss.h, ss.n)
            inc(v.tensor_tensor(out=st8[:, 6:7], in0=st8[:, 3:4], in1=st8[:, 3:4],
                                op=ALU.mult), vs)
            v.wait_ge(vs.h, vs.n)
            inc(v.tensor_scalar(out=st8[:, 6:7], in0=st8[:, 6:7], scalar1=-LN_EPS,
                                op0=ALU.mult, scalar2=1.0 + 1e-6, op1=ALU.add), vs)

        @block.scalar
        def _(s, t=t):
            s.wait_ge(vs.h, vs.n)
            inc(s.activation(out=st8[:, 7:8], in_=st8[:, 6:7], func=AF.Ln), ss)
            s.wait_ge(ss.h, ss.n)
            inc(s.activation(out=st8[:, 6:7], in_=st8[:, 7:8], func=AF.Exp, scale=-0.5), ss)
        d1_rs[t] = ss.n

    def d1_apply(t):
        st8 = stats[:, (t % 2) * 8:(t % 2) * 8 + 8]
        if t - 2 in d1_sp:
            @block.vector
            def _(v, t=t):
                v.wait_ge(gs.h, d1_sp[t - 2])

        @block.vector
        def _(v, t=t):
            v.wait_ge(ss.h, d1_rs[t])
            v.tensor_scalar(out=dinp[:, (t % 2) * D:(t % 2 + 1) * D],
                            in0=xdp[:, (t % 2) * D:(t % 2 + 1) * D],
                            scalar1=st8[:, 2:3], op0=ALU.subtract,
                            scalar2=st8[:, 3:4], op1=ALU.mult)
            inc(v.tensor_scalar(out=rmsp[:, (t % 2) * D:(t % 2 + 1) * D],
                                in0=dinp[:, (t % 2) * D:(t % 2 + 1) * D],
                                scalar1=st8[:, 6:7], op0=ALU.mult, scalar2=None), vs)
        d1_ap[t] = vs.n

        @block.gpsimd
        def _(g_, t=t):
            g_.wait_ge(vs.h, d1_ap[t])
            inc(g_.dma_start(out=din_dram[t * 128:(t + 1) * 128, :],
                             in_=dinp[:, (t % 2) * D:(t % 2 + 1) * D]), gs, 16)
            inc(g_.dma_start(out=rms1_dram[t * 128:(t + 1) * 128, :],
                             in_=rmsp[:, (t % 2) * D:(t % 2 + 1) * D]), gs, 16)
        d1_sp[t] = gs.n

    d1_load(0)
    d1_stats(0)
    d1_load(1)
    for t in range(NT):
        if t + 1 < NT:
            d1_stats(t + 1)
        d1_apply(t)
        if t + 2 < NT:
            d1_load(t + 2)


    TH = T // 2
    with ExitStack() as ph:
        rms1T = ph.enter_context(nc.sbuf_tensor(pfx + "rms1T", [128, 8 * TH], BF16))
        wqT = ph.enter_context(nc.sbuf_tensor(pfx + "wqT", [128, 8 * D], BF16))
        wkvT = ph.enter_context(nc.sbuf_tensor(pfx + "wkvT", [128, 8 * 512], BF16))
        psq = ph.enter_context(nc.psum_tensor(pfx + "psq", [128, 4 * 512], F32))
        psv = ph.enter_context(nc.psum_tensor(pfx + "psv", [128, 2 * 256], F32))

        @block.scalar
        def _(s):
            s.wait_ge(gs.h, gs.n)
            for c in range(8):
                inc(s.dma_start(out=wqT[:, c * D:(c + 1) * D],
                                in_=wqbf[:, c * 128:(c + 1) * 128], transpose=True), trs, 16)
                inc(s.dma_start(out=wkvT[:, c * 512:c * 512 + 256],
                                in_=wkbf[:, c * 128:(c + 1) * 128], transpose=True), trs, 16)
                inc(s.dma_start(out=wkvT[:, c * 512 + 256:(c + 1) * 512],
                                in_=wvbf[:, c * 128:(c + 1) * 128], transpose=True), trs, 16)

        @block.vector
        def _(v):
            v.wait_ge(trs.h, trs.n)
            for c in range(8):
                inc(v.tensor_scalar(out=wqT[:, c * D:(c + 1) * D], in0=wqT[:, c * D:(c + 1) * D],
                                    scalar1=anw_sb[:, c:c + 1], op0=ALU.mult, scalar2=None), vs)
                inc(v.tensor_scalar(out=wkvT[:, c * 512:(c + 1) * 512],
                                    in0=wkvT[:, c * 512:(c + 1) * 512],
                                    scalar1=anw_sb[:, c:c + 1], op0=ALU.mult, scalar2=None), vs)

        nbank = 0
        for hf in range(2):
            tok0 = hf * TH

            @block.scalar
            def _(s, hf=hf, tok0=tok0):
                if hf > 0:
                    s.wait_ge(ts.h, ts.n)
                for c in range(8):
                    inc(s.dma_start(out=rms1T[:, c * TH:(c + 1) * TH],
                                    in_=rms1_dram[tok0:tok0 + TH, c * 128:(c + 1) * 128],
                                    transpose=True), trs, 16)

            for tau in range(8):
                hA, hB = qhead_of(tau, 0), qhead_of(tau, 1)
                for n in range(4):
                    pq = psq[:, (nbank % 4) * 512:(nbank % 4) * 512 + 512]
                    nbank += 1

                    @block.tensor
                    def _(t_, tau=tau, hA=hA, hB=hB, n=n, pq=pq):
                        t_.wait_ge(vs.h, vs.n)
                        t_.wait_ge(trs.h, trs.n)
                        mm = None
                        for c in range(8):
                            rhs = rms1T[:, c * TH + n * 512: c * TH + (n + 1) * 512]
                            t_.matmul(pq[0:64, :], wqT[:, c * D + 64 * hA: c * D + 64 * hA + 64],
                                      rhs, start=(c == 0), stop=(c == 7))
                            mm = t_.matmul(pq[64:128, :], wqT[:, c * D + 64 * hB: c * D + 64 * hB + 64],
                                           rhs, start=(c == 0), stop=(c == 7))
                        inc(mm, ts)

                    @block.vector
                    def _(v, tau=tau, n=n, pq=pq, tok0=tok0):
                        v.wait_ge(ts.h, ts.n)
                        inc(v.tensor_copy(
                            qT[:, tau * T + tok0 + n * 512: tau * T + tok0 + (n + 1) * 512], pq), vs)

            for tau in range(2):
                for n in range(4):
                    pq = psq[:, (nbank % 4) * 512:(nbank % 4) * 512 + 512]
                    nbank += 1

                    @block.tensor
                    def _(t_, tau=tau, n=n, pq=pq):
                        t_.wait_ge(vs.h, vs.n)
                        mm = None
                        for c in range(8):
                            mm = t_.matmul(pq, wkvT[:, c * 512 + tau * 128: c * 512 + tau * 128 + 128],
                                           rms1T[:, c * TH + n * 512: c * TH + (n + 1) * 512],
                                           start=(c == 0), stop=(c == 7))
                        inc(mm, ts)

                    @block.vector
                    def _(v, tau=tau, n=n, pq=pq, tok0=tok0):
                        v.wait_ge(ts.h, ts.n)
                        inc(v.tensor_copy(
                            kT[:, tau * T + tok0 + n * 512: tau * T + tok0 + (n + 1) * 512], pq), vs)

            for tt in range(16):
                pv = psv[:, (tt % 2) * 256:(tt % 2) * 256 + 256]
                gt = hf * 16 + tt

                @block.tensor
                def _(t_, tt=tt, pv=pv):
                    t_.wait_ge(vs.h, vs.n)
                    mm = None
                    for c in range(8):
                        mm = t_.matmul(pv, rms1T[:, c * TH + tt * 128: c * TH + tt * 128 + 128],
                                       wkvT[:, c * 512 + 256:(c + 1) * 512],
                                       start=(c == 0), stop=(c == 7))
                    inc(mm, ts)

                @block.vector
                def _(v, gt=gt, pv=pv):
                    v.wait_ge(ts.h, ts.n)
                    dst = v_tm[:, gt * 260: gt * 260 + 260].rearrange("p (g e) -> p g e", e=65)[:, :, 0:64]
                    inc(v.tensor_copy(dst, pv[:].rearrange("p (g e) -> p g e", e=64)), vs)

        @block.gpsimd
        def _(g_):
            g_.wait_ge(vs.h, vs.n)
            ocol = v_tm[:].rearrange("p (t g e) -> p t g e", g=4, e=65)[:, :, :, 64:65]
            inc(g_.memset(ocol, 1.0), gs)

        if debug:
            dbg_qT = nc.dram_tensor(pfx + "dbg_qT", [128, 8 * T], BF16, kind="ExternalOutput").ap()
            dbg_kT = nc.dram_tensor(pfx + "dbg_kT", [128, 2 * T], BF16, kind="ExternalOutput").ap()
            dbg_v = nc.dram_tensor(pfx + "dbg_v", [128, NT * 260], BF16, kind="ExternalOutput").ap()

            @block.gpsimd
            def _(g_):
                g_.wait_ge(gs.h, gs.n)
                inc(g_.dma_start(out=dbg_qT[:, :], in_=qT[:]), gs, 16)
                inc(g_.dma_start(out=dbg_kT[:, :], in_=kT[:]), gs, 16)
                inc(g_.dma_start(out=dbg_v[:, :], in_=v_tm[:]), gs, 16)

        @block.tensor
        def _(t_):
            t_.wait_ge(gs.h, gs.n)

    if debug == "d3":
        @block.sync
        def _(sync):
            sync.wait_ge(gs.h, gs.n)
            sync.wait_ge(vs.h, vs.n)
        aph.close()
        return st

    if True:
        ph = ph_attn = ExitStack()
        attnT = ph.enter_context(nc.sbuf_tensor(pfx + "attnT", [128, 8 * T], BF16))
        ph_flash = ExitStack()
        pTbuf = ph_flash.enter_context(nc.sbuf_tensor(pfx + "pTbuf", [128, 4 * 512], BF16))
        rrow = ph_flash.enter_context(nc.sbuf_tensor(pfx + "rrow", [1, 512], F32))
        rbc = ph_flash.enter_context(nc.sbuf_tensor(pfx + "rbc", [128, 512], F32))
        psS = ph_flash.enter_context(nc.psum_tensor(pfx + "psS", [128, 4 * 512], F32))
        psO = ph_flash.enter_context(nc.psum_tensor(pfx + "psO", [128, 2 * 512], F32))
        psR = ph_flash.enter_context(nc.psum_tensor(pfx + "psR", [128, 512], F32))
        for tau in range(8):
            ktile = 0 if tau < 4 else 1
            for qb in range(8):
                nkb = 4 * (qb + 1)
                n = nkb
                tsb, ssb, vsb = ts.n, ss.n, vs.n

                def idxS(x, k):
                    return (2 * k if k < 2 else 4 * k - 2) + x

                def idxP(x, k):
                    if k <= n - 3:
                        return 4 + 4 * k + x
                    return (4 * n - 4 + x) if k == n - 2 else (4 * n - 2 + x)

                def mcnt(x, k):
                    kk = min(k, n - 1) - (n - 4)
                    return max(0, 2 * kk + 1 + x)

                def sbank(x, k):
                    return (k % 2) * 2 + x

                def emit_S(t_, x, k):
                    if k >= 2:
                        t_.wait_ge(ss.h, ssb + (k - 2) + 1)
                    base = 64 * x
                    mm = t_.matmul(psS[:, sbank(x, k) * 512:(sbank(x, k) + 1) * 512],
                                   kT[base:base + 64,
                                      ktile * T + k * 128: ktile * T + k * 128 + 128],
                                   qT[base:base + 64,
                                      tau * T + qb * 512: tau * T + (qb + 1) * 512],
                                   start=True, stop=True)
                    inc(mm, ts)

                def emit_P(t_, x, k):
                    t_.wait_ge(ss.h, ssb + k + 1)
                    if k >= n - 4:
                        t_.wait_ge(vs.h, vsb + mcnt(x, k))
                    g = qhead_of(tau, x) // 4
                    mm = t_.matmul(psO[x * 512:x * 512 + 65, 0:512]
                                   if False else psO[0:65, x * 512:(x + 1) * 512],
                                   v_tm[:, k * 260 + g * 65: k * 260 + g * 65 + 65],
                                   pTbuf[:, sbank(x, k) * 512:(sbank(x, k) + 1) * 512],
                                   start=(k == 0), stop=(k == n - 1))
                    inc(mm, ts)

                @block.tensor
                def _(t_):
                    t_.wait_ge(vs.h, vsb)
                    for k in (0, 1):
                        emit_S(t_, 0, k)
                        emit_S(t_, 1, k)
                    for k in range(n - 2):
                        emit_P(t_, 0, k)
                        emit_P(t_, 1, k)
                        emit_S(t_, 0, k + 2)
                        emit_S(t_, 1, k + 2)
                    for k in (n - 2, n - 1):
                        emit_P(t_, 0, k)
                        emit_P(t_, 1, k)

                @block.scalar
                def _(s):
                    for k in range(n):
                        s.wait_ge(ts.h, tsb + idxS(1, k) + 1)
                        if k >= 2:
                            s.wait_ge(ts.h, tsb + idxP(1, k - 2) + 1)
                        w0 = (k % 2) * 1024
                        inc(s.activation(out=pTbuf[:, w0:w0 + 1024],
                                         in_=psS[:, w0:w0 + 1024],
                                         func=AF.Exp, scale=0.125), ss)

                @block.vector
                def _(v):
                    for k in range(max(0, n - 4), n):
                        for x in (0, 1):
                            v.wait_ge(ss.h, ssb + k + 1)
                            inc(v.tensor_tensor(
                                out=pTbuf[:, sbank(x, k) * 512:(sbank(x, k) + 1) * 512],
                                in0=pTbuf[:, sbank(x, k) * 512:(sbank(x, k) + 1) * 512],
                                in1=masks[:, (k - 4 * qb) * 512:(k - 4 * qb + 1) * 512],
                                op=ALU.mult), vs)

                assert ts.n == tsb + 4 * n and ss.n == ssb + n and vs.n == vsb + 8

                for half in range(2):
                    hh = qhead_of(tau, half)
                    pO = psO[:, half * 512:(half + 1) * 512]

                    @block.vector
                    def _(v, pO=pO):
                        v.wait_ge(ts.h, ts.n)
                        inc(v.reciprocal(rrow[0:1, :], pO[64:65, :]), vs)

                    @block.tensor
                    def _(t_):
                        t_.wait_ge(vs.h, vs.n)
                        inc(t_.matmul(psR[0:64, :], ones[0:1, 0:64], rrow[0:1, :],
                                      start=True, stop=True), ts)

                    @block.vector
                    def _(v, pO=pO, hh=hh, qb=qb):
                        v.wait_ge(ts.h, ts.n)
                        inc(v.tensor_copy(rbc[0:64, :], psR[0:64, :]), vs)
                        v.wait_ge(vs.h, vs.n)
                        dst = attnT[64 * (hh % 2):64 * (hh % 2) + 64,
                                    (hh // 2) * T + qb * 512:(hh // 2) * T + (qb + 1) * 512]
                        inc(v.tensor_tensor(out=dst, in0=pO[0:64, :], in1=rbc[0:64, :],
                                            op=ALU.mult), vs)

        ph_flash.close()
        woT = ph.enter_context(nc.sbuf_tensor(pfx + "woT", [128, 4 * D], BF16))
        gout = ph.enter_context(nc.sbuf_tensor(pfx + "gout", [128, 2 * 512], BF16))
        psG = ph.enter_context(nc.psum_tensor(pfx + "psG", [128, 2 * 512], F32))

        d5_mm, d5_ev, d5_sp = {}, {}, {}
        for mhalf in range(2):
            @block.scalar
            def _(s, mhalf=mhalf):
                if mhalf == 0:
                    s.wait_ge(vs.h, vs.n)
                else:
                    s.wait_ge(ts.h, ts.n)
                for c in range(8):
                    inc(s.dma_start(out=woT[:, c * 512:(c + 1) * 512],
                                    in_=wobf[mhalf * 512:(mhalf + 1) * 512,
                                             c * 128:(c + 1) * 128], transpose=True), trs, 16)

            for mo in range(mhalf * 4, mhalf * 4 + 4):
                for n in range(8):
                    i = mo * 8 + n
                    par = i % 2
                    pg = psG[:, par * 512:(par + 1) * 512]

                    @block.tensor
                    def _(t_, mo=mo, n=n, pg=pg, i=i,
                          first=(mo == mhalf * 4 and n == 0)):
                        if first:
                            t_.wait_ge(trs.h, trs.n)
                        if i - 2 in d5_ev:
                            t_.wait_ge(vs.h, d5_ev[i - 2])
                        else:
                            t_.wait_ge(vs.h, vs.n)
                        mm = None
                        for c in range(8):
                            mm = t_.matmul(pg, woT[:, c * 512 + (mo % 4) * 128: c * 512 + (mo % 4) * 128 + 128],
                                           attnT[:, c * T + n * 512:c * T + (n + 1) * 512],
                                           start=(c == 0), stop=(c == 7))
                        inc(mm, ts)
                    d5_mm[i] = ts.n

                    @block.vector
                    def _(v, pg=pg, par=par, i=i):
                        v.wait_ge(ts.h, d5_mm[i])
                        if i - 2 in d5_sp:
                            v.wait_ge(gs.h, d5_sp[i - 2])
                        inc(v.tensor_copy(gout[:, par * 512:(par + 1) * 512], pg), vs)
                    d5_ev[i] = vs.n

                    @block.gpsimd
                    def _(g_, mo=mo, n=n, par=par, i=i):
                        g_.wait_ge(vs.h, d5_ev[i])
                        inc(g_.dma_start(out=gqa_dram[mo * 128:(mo + 1) * 128, n * 512:(n + 1) * 512],
                                         in_=gout[:, par * 512:(par + 1) * 512]), gs, 16)
                    d5_sp[i] = gs.n

    ph_attn.close()
    aph.close()

    if debug == "d5":
        @block.sync
        def _(sync):
            sync.wait_ge(gs.h, gs.n)
            sync.wait_ge(vs.h, vs.n)
        return st

    ph6 = ExitStack()
    comb = ph6.enter_context(nc.sbuf_tensor(pfx + "comb", [128, NT * 8], F32))
    glog = ph6.enter_context(nc.sbuf_tensor(pfx + "glog", [128, 64], F32))
    idT = ph6.enter_context(nc.sbuf_tensor(pfx + "idT", [128, 2 * 128], F32))
    gateT = ph6.enter_context(nc.sbuf_tensor(pfx + "gateT", [128, 8 * 8], F32))
    gsb = ph6.enter_context(nc.sbuf_tensor(pfx + "gsb", [128, D], F32))
    hT_st = ph6.enter_context(nc.sbuf_tensor(pfx + "hTst", [128, 2 * D], F32))
    ph6p = ExitStack()
    psHT = ph6p.enter_context(nc.psum_tensor(pfx + "psHT", [128, 2 * 1024], F32))
    psg = ph6p.enter_context(nc.psum_tensor(pfx + "psg", [128, 1024], F32))

    if debug:
        dbg_lg_d = nc.dram_tensor(pfx + "dbg_lg", [128, NT * 8], F32, kind="ExternalOutput").ap()
        dbg_lg = ph6.enter_context(nc.sbuf_tensor(pfx + "dbglg", [128, NT * 8], F32))

    @block.sync
    def _(sync):
        sync.wait_ge(vs.h, vs.n)
        sync.wait_ge(ts.h, ts.n)

    @block.gpsimd
    def _(g_):
        g_.wait_ge(vs.h, vs.n)
        g_.wait_ge(ts.h, ts.n)

    if glevel < 1:
        @block.gpsimd
        def _(g_):
            g_.memset(comb[:], 0.25)

    if glevel >= 1:
        @block.sync
        def _(sync):
            inc(sync.dma_start(out=gsb[0:8, :], in_=wts["gate_w"][:, :]), ld, 16)

    if glevel >= 1:
        @block.gpsimd
        def _(g_):
            inc(g_.iota(out=idT[:, 0:128], base=0, pattern=[[1, 128]], channel_multiplier=0,
                        allow_small_or_imprecise_dtypes=True), gs)
            inc(g_.iota(out=idT[:, 128:256], base=0, pattern=[[0, 128]], channel_multiplier=1,
                        allow_small_or_imprecise_dtypes=True), gs)

        @block.vector
        def _(v):
            v.wait_ge(gs.h, gs.n)
            inc(v.tensor_tensor(out=idT[:, 0:128], in0=idT[:, 0:128], in1=idT[:, 128:256],
                                op=ALU.is_equal), vs)

        for c in range(8):
            @block.tensor
            def _(t_, c=c):
                t_.wait_ge(vs.h, vs.n)
                t_.wait_ge(ld.h, ld.n)
                inc(t_.transpose(psg[:, 8:16], gsb[0:8, c * 128:(c + 1) * 128],
                                 idT[0:8, 0:8]), ts)

            @block.vector
            def _(v, c=c):
                v.wait_ge(ts.h, ts.n)
                inc(v.tensor_scalar(out=gateT[:, c * 8:(c + 1) * 8], in0=psg[:, 8:16],
                                    scalar1=fnw_sb[:, c:c + 1], op0=ALU.mult, scalar2=None), vs)

        if debug:
            dbg_gT0 = nc.dram_tensor(pfx + "dbg_gT0", [128, 64], F32, kind="ExternalOutput").ap()
            dbg_gsb = nc.dram_tensor(pfx + "dbg_gsb", [8, D], F32, kind="ExternalOutput").ap()
            dbg_idT = nc.dram_tensor(pfx + "dbg_idT", [128, 128], F32, kind="ExternalOutput").ap()
            dbg_fnw = nc.dram_tensor(pfx + "dbg_fnw", [128, 8], F32, kind="ExternalOutput").ap()

            @block.gpsimd
            def _(g_):
                g_.wait_ge(vs.h, vs.n)
                inc(g_.dma_start(out=dbg_gT0[:, :], in_=gateT[:]), gs, 16)
                inc(g_.dma_start(out=dbg_gsb[:, :], in_=gsb[0:8, :]), gs, 16)
                inc(g_.dma_start(out=dbg_idT[:, :], in_=idT[:, 0:128]), gs, 16)
                inc(g_.dma_start(out=dbg_fnw[:, :], in_=fnw_sb[:]), gs, 16)

    d6_ld, d6_add, d6_rs, d6_rms, d6_sp = {}, {}, {}, {}, {}
    d6_tr, d6_hT, d6_mm, d6_lg, d6_d, d6_sg = {}, {}, {}, {}, {}, {}

    def d6_load(t):
        p = t % 2

        @block.sync
        def _(sync, t=t, p=p):
            if t - 2 in d6_add:
                sync.wait_ge(vs.h, d6_add[t - 2])
                sync.wait_ge(gs.h, d6_sp[t - 2])
            else:
                sync.wait_ge(gs.h, gs.n)
            inc(sync.dma_start(out=xdp[:, p * D:(p + 1) * D],
                               in_=din_dram[t * 128:(t + 1) * 128, :]), ld, 16)
            inc(sync.dma_start(out=rmsp[:, p * D:(p + 1) * D],
                               in_=gqa_dram[:, t * 128:(t + 1) * 128],
                               transpose=True), ld, 16)
        d6_ld[t] = ld.n

    def d6_stats(t):
        p = t % 2
        b = p * 8

        @block.vector
        def _(v, t=t, p=p):
            v.wait_ge(ld.h, d6_ld[t])
            inc(v.tensor_tensor(out=dinp[:, p * D:(p + 1) * D],
                                in0=xdp[:, p * D:(p + 1) * D],
                                in1=rmsp[:, p * D:(p + 1) * D], op=ALU.add), vs)
        d6_add[t] = vs.n

        @block.vector
        def _(v, p=p, b=b):
            inc(v.scalar_tensor_tensor(out=scrap[:, 0:D],
                                       in0=dinp[:, p * D:(p + 1) * D], scalar=1.0,
                                       op0=ALU.mult, op1=ALU.mult,
                                       in1=dinp[:, p * D:(p + 1) * D],
                                       accum_out=stats[:, b:b + 1]), vs)
            v.wait_ge(vs.h, vs.n)
            inc(v.tensor_scalar(out=stats[:, b + 1:b + 2], in0=stats[:, b:b + 1],
                                scalar1=1.0 / D, op0=ALU.mult, scalar2=1e-6,
                                op1=ALU.add), vs)

        @block.scalar
        def _(s, b=b):
            s.wait_ge(vs.h, vs.n)
            inc(s.activation(out=stats[:, b + 3:b + 4], in_=stats[:, b + 1:b + 2],
                             func=AF.Ln), ss)
            s.wait_ge(ss.h, ss.n)
            inc(s.activation(out=stats[:, b + 2:b + 3], in_=stats[:, b + 3:b + 4],
                             func=AF.Exp, scale=-0.5), ss)
        d6_rs[t] = ss.n

    def d6_rms_spill(t):
        p = t % 2
        b = p * 8

        @block.vector
        def _(v, t=t, p=p, b=b):
            v.wait_ge(ss.h, d6_rs[t])
            inc(v.tensor_scalar(out=rmsp[:, p * D:(p + 1) * D],
                                in0=dinp[:, p * D:(p + 1) * D],
                                scalar1=stats[:, b + 2:b + 3], op0=ALU.mult,
                                scalar2=None), vs)
        d6_rms[t] = vs.n

        @block.gpsimd
        def _(g_, t=t, p=p):
            g_.wait_ge(vs.h, d6_rms[t])
            inc(g_.dma_start(out=h_dram[t * 128:(t + 1) * 128, :],
                             in_=dinp[:, p * D:(p + 1) * D]), gs, 16)
            inc(g_.dma_start(out=rms2_dram[t * 128:(t + 1) * 128, :],
                             in_=rmsp[:, p * D:(p + 1) * D]), gs, 16)
        d6_sp[t] = gs.n

    def d6_gate_pe(t):
        p = t % 2

        @block.tensor
        def _(t_, t=t, p=p):
            t_.wait_ge(vs.h, d6_add[t])
            if t - 2 in d6_hT:
                t_.wait_ge(vs.h, d6_hT[t - 2])
            mm = None
            for c in range(8):
                mm = t_.transpose(psHT[:, p * 1024 + c * 128:p * 1024 + (c + 1) * 128],
                                  dinp[:, p * D + c * 128:p * D + (c + 1) * 128],
                                  idT[:, 0:128])
            inc(mm, ts)
        d6_tr[t] = ts.n

        @block.vector
        def _(v, t=t, p=p):
            v.wait_ge(ts.h, d6_tr[t])
            inc(v.tensor_copy(hT_st[:, p * D:(p + 1) * D],
                              psHT[:, p * 1024:(p + 1) * 1024]), vs)
        d6_hT[t] = vs.n

        @block.tensor
        def _(t_, t=t, p=p):
            t_.wait_ge(vs.h, d6_hT[t])
            if t - 2 in d6_lg:
                t_.wait_ge(vs.h, d6_lg[t - 2])
            mm = None
            for c in range(8):
                mm = t_.matmul(psg[:, p * 512:p * 512 + 8],
                               hT_st[:, p * D + c * 128:p * D + (c + 1) * 128],
                               gateT[:, c * 8:(c + 1) * 8], start=(c == 0), stop=(c == 7))
            inc(mm, ts)
        d6_mm[t] = ts.n

    def d6_gate_dve(t):
        p = t % 2
        b = p * 8
        gl = glog[:, p * 32:(p + 1) * 32]

        @block.vector
        def _(v, t=t, p=p, b=b, gl=gl):
            v.wait_ge(ts.h, d6_mm[t])
            inc(v.tensor_scalar(out=gl[:, 0:8], in0=psg[:, p * 512:p * 512 + 8],
                                scalar1=stats[:, b + 2:b + 3], op0=ALU.mult,
                                scalar2=None), vs)
        d6_lg[t] = vs.n

        @block.vector
        def _(v, b=b, gl=gl):
            v.wait_ge(vs.h, vs.n)
            inc(v.reduce_max(gl[:, 8:9], gl[:, 0:8], axis=AX.X), vs)
            v.wait_ge(vs.h, vs.n)
            inc(v.tensor_scalar(out=gl[:, 16:24], in0=gl[:, 0:8], scalar1=gl[:, 8:9],
                                op0=ALU.is_equal, scalar2=None), vs)
            v.wait_ge(vs.h, vs.n)
            inc(v.scalar_tensor_tensor(out=gl[:, 24:32], in0=gl[:, 16:24], scalar=-1e30,
                                       op0=ALU.mult, op1=ALU.add, in1=gl[:, 0:8]), vs)
            v.wait_ge(vs.h, vs.n)
            inc(v.reduce_max(gl[:, 9:10], gl[:, 24:32], axis=AX.X), vs)
            v.wait_ge(vs.h, vs.n)
            inc(v.tensor_tensor(out=stats[:, b + 4:b + 5], in0=gl[:, 8:9],
                                in1=gl[:, 9:10], op=ALU.subtract), vs)
        d6_d[t] = vs.n

        @block.scalar
        def _(s, t=t, b=b):
            s.wait_ge(vs.h, d6_d[t])
            inc(s.activation(out=stats[:, b + 5:b + 6], in_=stats[:, b + 4:b + 5],
                             func=AF.Exp, scale=-1.0), ss)
        d6_sg[t] = ss.n

        @block.vector
        def _(v, t=t, b=b, gl=gl):
            v.wait_ge(ss.h, d6_sg[t])
            inc(v.tensor_scalar(out=stats[:, b + 7:b + 8], in0=stats[:, b + 5:b + 6],
                                scalar1=1.0, op0=ALU.add, scalar2=None), vs)
            v.wait_ge(vs.h, vs.n)
            inc(v.reciprocal(stats[:, b + 4:b + 5], stats[:, b + 7:b + 8]), vs)
            v.wait_ge(vs.h, vs.n)
            inc(v.tensor_copy(stats[:, b + 5:b + 6], stats[:, b + 4:b + 5]), vs)
            v.wait_ge(vs.h, vs.n)
            inc(v.tensor_scalar(out=stats[:, b + 6:b + 7], in0=stats[:, b + 5:b + 6],
                                scalar1=-1.0, op0=ALU.mult, scalar2=1.0, op1=ALU.add), vs)
            inc(v.tensor_scalar(out=gl[:, 24:32], in0=gl[:, 0:8], scalar1=gl[:, 9:10],
                                op0=ALU.is_equal, scalar2=None), vs)
            v.wait_ge(vs.h, vs.n)
            inc(v.tensor_scalar(out=gl[:, 16:24], in0=gl[:, 16:24],
                                scalar1=stats[:, b + 5:b + 6], op0=ALU.mult,
                                scalar2=None), vs)
            inc(v.tensor_scalar(out=gl[:, 24:32], in0=gl[:, 24:32],
                                scalar1=stats[:, b + 6:b + 7], op0=ALU.mult,
                                scalar2=None), vs)
            v.wait_ge(vs.h, vs.n)
            inc(v.tensor_tensor(out=comb[:, t * 8:(t + 1) * 8], in0=gl[:, 16:24],
                                in1=gl[:, 24:32], op=ALU.add), vs)

    d6_load(0)
    d6_stats(0)
    d6_load(1)
    for t in range(NT):
        if t + 1 < NT:
            d6_stats(t + 1)
        d6_rms_spill(t)
        d6_gate_pe(t)
        d6_gate_dve(t)
        if t + 2 < NT:
            d6_load(t + 2)

    if debug:
        dbg_gT = nc.dram_tensor(pfx + "dbg_gT", [128, 64], F32, kind="ExternalOutput").ap()
        dbg_hT = nc.dram_tensor(pfx + "dbg_hT", [128, D], F32, kind="ExternalOutput").ap()

        @block.gpsimd
        def _(g_):
            g_.wait_ge(vs.h, vs.n)
            inc(g_.dma_start(out=dbg_lg_d[:, :], in_=dbg_lg[:]), gs, 16)
            inc(g_.dma_start(out=dbg_gT[:, :], in_=gateT[:]), gs, 16)
            inc(g_.dma_start(out=dbg_hT[:, :], in_=hT_st[:]), gs, 16)

    if debug == "d6":
        @block.sync
        def _(sync):
            sync.wait_ge(gs.h, gs.n)
            sync.wait_ge(vs.h, vs.n)
        ph6.close()
        return st

    ph6p.close()

    with ExitStack() as ph:
        rms2T = ph.enter_context(nc.sbuf_tensor(pfx + "rms2T", [128, 8 * T], BF16))
        expT = ph.enter_context(nc.sbuf_tensor(pfx + "expT", [128, 2 * 8 * D], BF16))
        moe_out = ph.enter_context(nc.sbuf_tensor(pfx + "moeout", [128, 16 * D], F32))
        psE = ph.enter_context(nc.psum_tensor(pfx + "psE", [128, 4 * 512], F32))

        @block.scalar
        def _(s):
            s.wait_ge(gs.h, gs.n)
            for c in range(8):
                inc(s.dma_start(out=rms2T[:, c * T:(c + 1) * T],
                                in_=rms2_dram[:, c * 128:(c + 1) * 128], transpose=True), trs, 16)

        if debug:
            dbg_comb = nc.dram_tensor(pfx + "dbg_comb", [128, NT * 8], F32, kind="ExternalOutput").ap()

            @block.gpsimd
            def _(g_):
                g_.wait_ge(vs.h, vs.n)
                inc(g_.dma_start(out=dbg_comb[:, :], in_=comb[:]), gs, 16)

        scale_done = {}
        evac_hist = []
        mm_hist = []
        d7_ld, d7_rs, d7_ap, d7_st = {}, {}, {}, {}

        def d7_load(t):
            p = t % 2

            @block.sync
            def _(sync, t=t, p=p):
                if t - 2 in d7_st:
                    sync.wait_ge(st.h, d7_st[t - 2])
                else:
                    sync.wait_ge(vs.h, vs.n)
                inc(sync.dma_start(out=xdp[:, p * D:(p + 1) * D],
                                   in_=h_dram[t * 128:(t + 1) * 128, :]), ld, 16)
                inc(sync.dma_start(out=dinp[:, p * D:(p + 1) * D],
                                   in_=x_in[t * 128:(t + 1) * 128, :]), ld, 16)
            d7_ld[t] = ld.n

        def d7_stats(t, ti):
            p = t % 2

            @block.vector
            def _(v, t=t, ti=ti, p=p):
                v.wait_ge(ld.h, d7_ld[t])
                inc(v.tensor_tensor(out=moe_out[:, ti * D:(ti + 1) * D],
                                    in0=moe_out[:, ti * D:(ti + 1) * D],
                                    in1=xdp[:, p * D:(p + 1) * D], op=ALU.add), vs)
            d7_rs[t] = emit_ln_stats(block, moe_out[:, ti * D:(ti + 1) * D],
                                     stats[:, p * 8:p * 8 + 8], scrap, vs, ss)

        def d7_apply(t, ti):
            p = t % 2
            emit_ln_apply(block, moe_out[:, ti * D:(ti + 1) * D],
                          xdp[:, p * D:(p + 1) * D], stats[:, p * 8:p * 8 + 8],
                          scrap, vs, ss, d7_rs[t], add_ap=dinp[:, p * D:(p + 1) * D])
            d7_ap[t] = vs.n

            @block.sync
            def _(sync, t=t, p=p):
                sync.wait_ge(vs.h, d7_ap[t])
                inc(sync.dma_start(out=x_out[t * 128:(t + 1) * 128, :],
                                   in_=xdp[:, p * D:(p + 1) * D]), st, 16)
            d7_st[t] = st.n


        def load_expert(e, rnd):
            buf = e % 2

            @block.scalar
            def _(s, e=e, buf=buf):
                s.wait_ge(ts.h, ts.n)
                for c in range(8):
                    inc(s.dma_start(out=expT[:, buf * 8 * D + c * D:buf * 8 * D + (c + 1) * D],
                                    in_=expbf[e * D:(e + 1) * D, c * 128:(c + 1) * 128],
                                    transpose=True), trs, 16)

            @block.vector
            def _(v, e=e, buf=buf):
                v.wait_ge(trs.h, trs.n)
                for c in range(8):
                    inc(v.tensor_scalar(out=expT[:, buf * 8 * D + c * D:buf * 8 * D + (c + 1) * D],
                                        in0=expT[:, buf * 8 * D + c * D:buf * 8 * D + (c + 1) * D],
                                        scalar1=fnw_sb[:, c:c + 1], op0=ALU.mult, scalar2=None), vs)
            scale_done[(e, rnd)] = vs.n

        ebank = 0
        for grp in range(2):
            tiles = range(grp * 16, grp * 16 + 16)
            for e in range(8):
                if grp == 0 and e < 2:
                    load_expert(e, 0)
                for ti, t in enumerate(tiles):
                    pe_ = psE[:, (ebank % 2) * 1024:(ebank % 2) * 1024 + 1024]
                    ebank += 1

                    @block.tensor
                    def _(t_, t=t, e=e, grp=grp, buf=e % 2, pe_=pe_):
                        key = (e, grp) if (e, grp) in scale_done else (e, 0)
                        t_.wait_ge(vs.h, scale_done[key])
                        if len(evac_hist) >= 2:
                            t_.wait_ge(vs.h, evac_hist[-2])
                        mm = None
                        for half in range(2):
                            for c in range(8):
                                mm = t_.matmul(pe_[:, half * 512:(half + 1) * 512],
                                               rms2T[:, c * T + t * 128: c * T + t * 128 + 128],
                                               expT[:, buf * 8 * D + c * D + half * 512:
                                                    buf * 8 * D + c * D + half * 512 + 512],
                                               start=(c == 0), stop=(c == 7))
                        inc(mm, ts)
                        mm_hist.append(ts.n)

                    @block.vector
                    def _(v, t=t, e=e, ti=ti, pe_=pe_):
                        v.wait_ge(ts.h, mm_hist[-1])
                        cc = comb[:, t * 8 + e: t * 8 + e + 1]
                        dst = moe_out[:, ti * D:(ti + 1) * D]
                        if e == 0:
                            op = v.tensor_scalar(out=dst, in0=pe_, scalar1=cc,
                                                 op0=ALU.mult, scalar2=None)
                        else:
                            op = v.scalar_tensor_tensor(out=dst, in0=pe_, scalar=cc,
                                                        op0=ALU.mult, op1=ALU.add, in1=dst)
                        inc(op, vs)
                        evac_hist.append(vs.n)

                if e + 2 < 8:
                    load_expert(e + 2, grp)
                elif grp == 0 and e >= 6:
                    load_expert(e - 6, 1)

            d7_load(tiles[0])
            d7_stats(tiles[0], 0)
            d7_load(tiles[1])
            for ti, t in enumerate(tiles):
                if ti + 1 < 16:
                    d7_stats(t + 1, ti + 1)
                d7_apply(t, ti)
                if ti + 2 < 16:
                    d7_load(t + 2)

    ph6.close()

    @block.sync
    def _(sync):
        sync.wait_ge(st.h, st.n)

    return st


def causal_masks_bf16():
    import ml_dtypes
    m = np.zeros((4, 128, 512), np.float32)
    for j in range(4):
        for kk in range(128):
            q0 = j * 128 + kk
            m[j, kk, q0:] = 1.0
    return m.reshape(4 * 128, 512).astype(ml_dtypes.bfloat16)


def build_prog_d(debug=False, glevel=5):
    nc = bass.Bass("TRN2", target_bir_lowering=False)
    x_in = nc.dram_tensor("x_in", [T, D], F32, kind="ExternalInput").ap()
    wts = {
        "wq": nc.dram_tensor("wq", [D, D], F32, kind="ExternalInput").ap(),
        "wk": nc.dram_tensor("wk", [256, D], F32, kind="ExternalInput").ap(),
        "wv": nc.dram_tensor("wv", [256, D], F32, kind="ExternalInput").ap(),
        "wo": nc.dram_tensor("wo", [D, D], F32, kind="ExternalInput").ap(),
        "gate_w": nc.dram_tensor("gate_w", [8, D], F32, kind="ExternalInput").ap(),
        "expert_w": nc.dram_tensor("expert_w", [8, D, D], F32, kind="ExternalInput").ap(),
        "anw": nc.dram_tensor("anw", [D], F32, kind="ExternalInput").ap(),
        "fnw": nc.dram_tensor("fnw", [D], F32, kind="ExternalInput").ap(),
        "masks": nc.dram_tensor("masks", [4 * 128, 512], BF16, kind="ExternalInput").ap(),
    }
    x_out = nc.dram_tensor("x_out", [T, D], F32, kind="ExternalOutput").ap()
    with ExitStack() as ctx:
        block = ctx.enter_context(nc.Block())
        emit_stage_d(nc, ctx, block, x_in, wts, x_out, debug=debug, glevel=glevel)
    return nc


def emit_stage_c(nc, ctx, block, x_in, wts, x_out, pfx="c_", debug=False, sems=None):
    E = ctx.enter_context
    ld, vs, ss, ts, gs, trs, st = sems or mk_sems(
        nc, ctx, ["ld", "vs", "ss", "ts", "gs", "trs", "st"], pfx)
    kind = "ExternalOutput" if debug else "Internal"

    cin_dram = nc.dram_tensor(pfx + "cin", [T, D], BF16, kind=kind).ap()
    qkvbf = nc.dram_tensor(pfx + "qkvbf", [3 * D, D], BF16, kind="Internal").ap()
    wobf = nc.dram_tensor(pfx + "wobf", [D, D], BF16, kind="Internal").ap()
    attnfull = nc.dram_tensor(pfx + "attnfull", [T, D], BF16, kind=kind).ap()
    if debug:
        dbg_scores = nc.dram_tensor(pfx + "dbg_scores", [128, 48], F32, kind="ExternalOutput").ap()
        dbg_idx = nc.dram_tensor(pfx + "dbg_idx", [2, 256], F32, kind="ExternalOutput").ap()
        dbg_ysel = nc.dram_tensor(pfx + "dbg_ysel", [256, D], BF16, kind="ExternalOutput").ap()

    stats = E(nc.sbuf_tensor(pfx + "stats", [128, 16], F32))
    scrap = E(nc.sbuf_tensor(pfx + "scrap", [128, 3 * D], F32))
    ones = E(nc.sbuf_tensor(pfx + "ones", [128, 128], F32))
    onesb = E(nc.sbuf_tensor(pfx + "onesb", [128, 2], BF16))
    onesrowb = E(nc.sbuf_tensor(pfx + "onesrowb", [1, 256], BF16))
    idT = E(nc.sbuf_tensor(pfx + "idT", [128, 2 * 128], F32))
    ltri = E(nc.sbuf_tensor(pfx + "ltri", [128, 128], F32))
    iotaS = E(nc.sbuf_tensor(pfx + "iotaS", [128, 256], F32))
    tokid = E(nc.sbuf_tensor(pfx + "tokid", [128, 32], F32))
    emat = E(nc.sbuf_tensor(pfx + "emat", [128, 64], F32))
    scores_tm = E(nc.sbuf_tensor(pfx + "scorestm", [128, 48], F32))
    gate_tm = E(nc.sbuf_tensor(pfx + "gatetm", [128, 48], F32))
    bis = E(nc.sbuf_tensor(pfx + "bis", [1, 16], F32))
    rowbuf = E(nc.sbuf_tensor(pfx + "rowbuf", [1, D], F32))
    vrow = E(nc.sbuf_tensor(pfx + "vrow", [1, 4 * D], F32))
    idx_sb = E(nc.sbuf_tensor(pfx + "idx", [128, 4], U32))
    gsel_col = E(nc.sbuf_tensor(pfx + "gselcol", [128, 2], F32))
    gbc = E(nc.sbuf_tensor(pfx + "gbc", [128, 256], F32))
    bout_col = E(nc.sbuf_tensor(pfx + "boutcol", [128, 8], F32))
    yuni_bc = E(nc.sbuf_tensor(pfx + "yunibc", [128, D], BF16))

    psA = E(nc.psum_tensor(pfx + "psA", [128, 2 * 512], F32))
    psB = E(nc.psum_tensor(pfx + "psB", [128, 512], F32))
    psC = E(nc.psum_tensor(pfx + "psC", [128, 512], F32))

    @block.gpsimd
    def _(g_):
        inc(g_.memset(ones[:], 1.0), gs)
        inc(g_.memset(onesb[:], 1.0), gs)
        inc(g_.memset(onesrowb[:], 1.0), gs)
        inc(g_.iota(out=idT[:, 0:128], base=0, pattern=[[1, 128]], channel_multiplier=0,
                    allow_small_or_imprecise_dtypes=True), gs)
        inc(g_.iota(out=idT[:, 128:256], base=0, pattern=[[0, 128]], channel_multiplier=1,
                    allow_small_or_imprecise_dtypes=True), gs)
        inc(g_.iota(out=iotaS[:], base=1, pattern=[[1, 256]], channel_multiplier=0,
                    allow_small_or_imprecise_dtypes=True), gs)
        inc(g_.iota(out=tokid[:], base=0, pattern=[[128, 32]], channel_multiplier=1,
                    allow_small_or_imprecise_dtypes=True), gs)
        inc(g_.iota(out=emat[:], base=0, pattern=[[-1, 64]], channel_multiplier=1,
                    allow_small_or_imprecise_dtypes=True), gs)
        inc(g_.dma_start(out=qkvbf[:, :], in_=wts["w_qkv"][:, :]), gs, 16)
        inc(g_.dma_start(out=wobf[:, :], in_=wts["w_out"][:, :]), gs, 16)

    @block.vector
    def _(v):
        v.wait_ge(gs.h, gs.n)
        inc(v.tensor_tensor(out=ltri[:], in0=idT[:, 128:256], in1=idT[:, 0:128], op=ALU.is_le), vs)
        inc(v.tensor_scalar(out=scrap[:, 0:64], in0=emat[:], scalar1=0.0, op0=ALU.is_equal,
                            scalar2=None), vs)
        inc(v.tensor_scalar(out=scrap[:, 64:128], in0=emat[:], scalar1=64.0, op0=ALU.is_equal,
                            scalar2=None), vs)
        v.wait_ge(vs.h, vs.n)
        inc(v.tensor_tensor(out=emat[:], in0=scrap[:, 0:64], in1=scrap[:, 64:128], op=ALU.max), vs)
        inc(v.tensor_tensor(out=idT[:, 0:128], in0=idT[:, 128:256], in1=idT[:, 0:128],
                            op=ALU.is_equal), vs)
        v.wait_ge(vs.h, vs.n)
        inc(v.tensor_scalar(out=emat[:], in0=emat[:], scalar1=1.0 / 16.0, op0=ALU.mult,
                            scalar2=None), vs)

    @block.sync
    def _(sync):
        inc(sync.dma_start(out=vrow[0:1, 0:D], in_=wts["b_out"][None, :]), ld, 16)

    @block.tensor
    def _(t_):
        t_.wait_ge(ld.h, ld.n)
        t_.wait_ge(gs.h, gs.n)
        mm = None
        for c in range(8):
            mm = t_.transpose(psB[:, c:c + 1], vrow[0:1, c * 128:(c + 1) * 128], ones[0:1, 0:1])
        inc(mm, ts)

    @block.vector
    def _(v):
        v.wait_ge(ts.h, ts.n)
        inc(v.tensor_copy(bout_col[:], psB[:, 0:8]), vs)

    xtp = E(nc.sbuf_tensor(pfx + "xtp", [128, 2 * D], F32))
    fbp = E(nc.sbuf_tensor(pfx + "fbp", [128, 2 * D], BF16))
    ld_at, sp_at, rstd_at, ap_at, xs_at = {}, {}, {}, {}, {}

    def c1_load(t):
        @block.sync
        def _(sync, t=t):
            if t - 2 in ap_at:
                sync.wait_ge(vs.h, ap_at[t - 2])
            inc(sync.dma_start(out=xtp[:, (t % 2) * D:(t % 2 + 1) * D],
                               in_=x_in[t * 128:(t + 1) * 128, :]), ld, 16)
        ld_at[t] = ld.n

    def c1_stats(t):
        @block.vector
        def _(v, t=t):
            v.wait_ge(ld.h, ld_at[t])
        rstd_at[t] = emit_ln_stats(block, xtp[:, (t % 2) * D:(t % 2 + 1) * D],
                                   stats[:, (t % 2) * 8:(t % 2) * 8 + 8], scrap, vs, ss)

    def c1_apply(t):
        if t - 2 in sp_at:
            @block.vector
            def _(v, t=t):
                v.wait_ge(gs.h, sp_at[t - 2])
                v.wait_ge(ts.h, xs_at[t - 2])
        emit_ln_apply(block, xtp[:, (t % 2) * D:(t % 2 + 1) * D],
                      fbp[:, (t % 2) * D:(t % 2 + 1) * D],
                      stats[:, (t % 2) * 8:(t % 2) * 8 + 8], scrap, vs, ss, rstd_at[t])
        ap_at[t] = vs.n

        @block.tensor
        def _(t_, t=t):
            t_.wait_ge(vs.h, ap_at[t])
            t_.matmul(psA[0:1, 0:512], onesb[:, 0:1], fbp[:, (t % 2) * D:(t % 2) * D + 512],
                      start=(t == 0), stop=(t == NT - 1))
            mm = t_.matmul(psA[0:1, 512:1024], onesb[:, 0:1],
                           fbp[:, (t % 2) * D + 512:(t % 2 + 1) * D],
                           start=(t == 0), stop=(t == NT - 1))
            inc(mm, ts)
        xs_at[t] = ts.n

        @block.gpsimd
        def _(g_, t=t):
            g_.wait_ge(vs.h, ap_at[t])
            inc(g_.dma_start(out=cin_dram[t * 128:(t + 1) * 128, :],
                             in_=fbp[:, (t % 2) * D:(t % 2 + 1) * D]), gs, 16)
        sp_at[t] = gs.n

    c1_load(0)
    c1_stats(0)
    c1_load(1)
    for t in range(NT):
        if t + 1 < NT:
            c1_stats(t + 1)
        c1_apply(t)
        if t + 2 < NT:
            c1_load(t + 2)

    @block.vector
    def _(v):
        v.wait_ge(ts.h, ts.n)
        inc(v.tensor_copy(vrow[0:1, 0:D], psA[0:1, 0:1024]), vs)

    if debug == "c1":
        @block.sync
        def _(sync):
            sync.wait_ge(gs.h, gs.n)
            sync.wait_ge(vs.h, vs.n)
        return st

    with ExitStack() as ph:
        WqkT = ph.enter_context(nc.sbuf_tensor(pfx + "WqkT", [128, 8 * 2048], BF16))
        WvT = ph.enter_context(nc.sbuf_tensor(pfx + "WvT", [128, 8 * D], BF16))
        WoT = ph.enter_context(nc.sbuf_tensor(pfx + "WoT", [128, 8 * D], BF16))
        wqmb = ph.enter_context(nc.sbuf_tensor(pfx + "wqmb", [128, 8 * 64], BF16))
        qmT = ph.enter_context(nc.sbuf_tensor(pfx + "qmT", [128, T], F32))
        h1T = ph.enter_context(nc.sbuf_tensor(pfx + "h1T", [128, T], F32))
        w1T = ph.enter_context(nc.sbuf_tensor(pfx + "w1T", [128, 32], F32))
        ph_mrow = ExitStack()
        mrow = ph_mrow.enter_context(nc.sbuf_tensor(pfx + "mrow", [1, 2 * T], F32))
        ph_cin = ExitStack()
        cinT = ph_cin.enter_context(nc.sbuf_tensor(pfx + "cinT", [128, 8 * (T // 4)], BF16))
        wrows = ph_cin.enter_context(nc.sbuf_tensor(pfx + "wrows", [128, D], F32))

        @block.sync
        def _(sync):
            sync.wait_ge(vs.h, vs.n)
            sync.wait_ge(ts.h, ts.n)

        @block.gpsimd
        def _(g_):
            g_.wait_ge(vs.h, vs.n)
            g_.wait_ge(ts.h, ts.n)

        for c in range(8):
            @block.sync
            def _(sync, c=c):
                if c > 0:
                    sync.wait_ge(ts.h, ts.n)
                inc(sync.dma_start(out=wrows[:], in_=wts["w_qkv"][c * 128:(c + 1) * 128, :]), ld, 16)

            @block.tensor
            def _(t_, c=c):
                t_.wait_ge(ld.h, ld.n)
                t_.wait_ge(vs.h, vs.n)
                t_.matmul(psA[0:64, 0:512], emat[:], wrows[:, 0:512],
                          start=(c == 0), stop=(c == 7))
                mm = t_.matmul(psA[0:64, 512:1024], emat[:], wrows[:, 512:1024],
                               start=(c == 0), stop=(c == 7))
                inc(mm, ts)

        @block.vector
        def _(v):
            v.wait_ge(ts.h, ts.n)
            inc(v.tensor_copy(scrap[0:64, 0:D], psA[0:64, 0:1024]), vs)

        for c in range(8):
            @block.tensor
            def _(t_, c=c):
                t_.wait_ge(vs.h, vs.n)
                inc(t_.transpose(psB[:, 0:64], scrap[0:64, c * 128:(c + 1) * 128],
                                 idT[0:64, 0:64]), ts)

            @block.vector
            def _(v, c=c):
                v.wait_ge(ts.h, ts.n)
                inc(v.tensor_copy(wqmb[:, c * 64:(c + 1) * 64], psB[:, 0:64]), vs)

        @block.scalar
        def _(s):
            s.wait_ge(gs.h, gs.n)
            for c in range(8):
                inc(s.dma_start(out=WqkT[:, c * 2048:c * 2048 + 1024],
                                in_=qkvbf[0:1024, c * 128:(c + 1) * 128], transpose=True), trs, 16)
                inc(s.dma_start(out=WqkT[:, c * 2048 + 1024:(c + 1) * 2048],
                                in_=qkvbf[1024:2048, c * 128:(c + 1) * 128], transpose=True), trs, 16)
                inc(s.dma_start(out=WvT[:, c * D:(c + 1) * D],
                                in_=qkvbf[2048:3072, c * 128:(c + 1) * 128], transpose=True), trs, 16)
                inc(s.dma_start(out=WoT[:, c * D:(c + 1) * D],
                                in_=wobf[0:1024, c * 128:(c + 1) * 128], transpose=True), trs, 16)

        TH2 = T // 4
        for half in range(4):
            @block.scalar
            def _(s, half=half):
                if half > 0:
                    s.wait_ge(ts.h, ts.n)
                for c in range(8):
                    inc(s.dma_start(out=cinT[:, c * TH2:(c + 1) * TH2],
                                    in_=cin_dram[half * TH2:(half + 1) * TH2,
                                                 c * 128:(c + 1) * 128], transpose=True), trs, 16)

            for n in range(2):
                @block.tensor
                def _(t_, half=half, n=n):
                    t_.wait_ge(trs.h, trs.n)
                    t_.wait_ge(vs.h, vs.n)
                    mm = None
                    for c in range(8):
                        mm = t_.matmul(psA[0:64, 0:512], wqmb[:, c * 64:(c + 1) * 64],
                                       cinT[:, c * TH2 + n * 512: c * TH2 + (n + 1) * 512],
                                       start=(c == 0), stop=(c == 7))
                    inc(mm, ts)

                @block.vector
                def _(v, half=half, n=n):
                    v.wait_ge(ts.h, ts.n)
                    inc(v.tensor_copy(
                        qmT[0:64, half * TH2 + n * 512: half * TH2 + (n + 1) * 512],
                        psA[0:64, 0:512]), vs)

        ph_cin.close()

        if debug:
            dbg_qm = nc.dram_tensor(pfx + "dbg_qm", [64, T], F32, kind="ExternalOutput").ap()
            dbg_wqm = nc.dram_tensor(pfx + "dbg_wqm", [128, 8 * 64], BF16, kind="ExternalOutput").ap()

            @block.gpsimd
            def _(g_):
                g_.wait_ge(vs.h, vs.n)
                inc(g_.dma_start(out=dbg_wqm[:, :], in_=wqmb[:]), gs, 16)

            @block.gpsimd
            def _(g_):
                g_.wait_ge(vs.h, vs.n)
                inc(g_.dma_start(out=dbg_qm[:, :], in_=qmT[0:64, :]), gs, 16)

        if debug == "c2":
            @block.sync
            def _(sync):
                sync.wait_ge(gs.h, gs.n)
                sync.wait_ge(vs.h, vs.n)
                sync.wait_ge(ts.h, ts.n)
            ph_mrow.close()
            return st

        @block.sync
        def _(sync):
            sync.wait_ge(vs.h, vs.n)
            sync.wait_ge(ts.h, ts.n)

        @block.gpsimd
        def _(g_):
            g_.wait_ge(vs.h, vs.n)
            g_.wait_ge(ts.h, ts.n)

        @block.scalar
        def _(s):
            s.wait_ge(vs.h, vs.n)
            inc(s.activation(out=h1T[0:64, :], in_=qmT[0:64, :], func=AF.Square), ss)

        for n in range(8):
            @block.tensor
            def _(t_, n=n):
                t_.wait_ge(ss.h, ss.n)
                t_.wait_ge(vs.h, vs.n)
                t_.matmul(psA[0:1, 0:512], ones[0:64, 0:1],
                          qmT[0:64, n * 512:(n + 1) * 512], start=True, stop=True)
                mm = t_.matmul(psA[0:1, 512:1024], ones[0:64, 0:1],
                               h1T[0:64, n * 512:(n + 1) * 512], start=True, stop=True)
                inc(mm, ts)

            @block.vector
            def _(v, n=n):
                v.wait_ge(ts.h, ts.n)
                inc(v.tensor_scalar(out=mrow[0:1, n * 512:(n + 1) * 512], in0=psA[0:1, 0:512],
                                    scalar1=1.0 / 64, op0=ALU.mult, scalar2=None), vs)
                inc(v.tensor_scalar(out=mrow[0:1, T + n * 512:T + (n + 1) * 512],
                                    in0=psA[0:1, 512:1024],
                                    scalar1=1.0 / 64, op0=ALU.mult, scalar2=None), vs)


        if debug == "c2s":
            @block.sync
            def _(sync):
                sync.wait_ge(gs.h, gs.n)
                sync.wait_ge(vs.h, vs.n)
                sync.wait_ge(ts.h, ts.n)
            ph_mrow.close()
            return st

        @block.vector
        def _(v):
            v.wait_ge(vs.h, vs.n)
            v.tensor_tensor(out=h1T[0:1, 0:T], in0=mrow[0:1, 0:T], in1=mrow[0:1, 0:T],
                            op=ALU.mult)
            inc(v.tensor_tensor(out=mrow[0:1, T:2 * T], in0=mrow[0:1, T:2 * T],
                                in1=h1T[0:1, 0:T], op=ALU.subtract), vs)
            v.wait_ge(vs.h, vs.n)
            inc(v.tensor_scalar(out=mrow[0:1, T:2 * T], in0=mrow[0:1, T:2 * T],
                                scalar1=LN_EPS, op0=ALU.add, scalar2=None), vs)

        @block.scalar
        def _(s):
            s.wait_ge(vs.h, vs.n)
            inc(s.activation(out=h1T[0:1, 0:T], in_=mrow[0:1, T:2 * T], func=AF.Ln), ss)
            s.wait_ge(ss.h, ss.n)
            inc(s.activation(out=mrow[0:1, T:2 * T], in_=h1T[0:1, 0:T], func=AF.Exp,
                             scale=-0.5), ss)


        if debug == "c2m":
            @block.sync
            def _(sync):
                sync.wait_ge(gs.h, gs.n)
                sync.wait_ge(vs.h, vs.n)
                sync.wait_ge(ts.h, ts.n)
            ph_mrow.close()
            return st

        @block.sync
        def _(sync):
            inc(sync.dma_start(out=rowbuf[0:1, 0:64], in_=wts["ln_g"][None, :]), ld, 16)
            inc(sync.dma_start(out=rowbuf[0:1, 64:128], in_=wts["ln_b"][None, :]), ld, 16)
            inc(sync.dma_start(out=rowbuf[0:1, 128:144], in_=wts["b1"][None, :]), ld, 16)
            inc(sync.dma_start(out=rowbuf[0:1, 144:160], in_=wts["w2"][0:1, :]), ld, 16)
            inc(sync.dma_start(out=rowbuf[0:1, 160:161], in_=wts["b2"][None, :]), ld, 16)
            inc(sync.dma_start(out=gate_tm[0:64, 0:16], in_=wts["w1"].rearrange("a b -> b a"),
                               allow_slow_non_contiguous=True), ld, 16)

        @block.tensor
        def _(t_):
            t_.wait_ge(ld.h, ld.n)
            t_.wait_ge(vs.h, vs.n)
            t_.transpose(psB[0:64, 0:1], rowbuf[0:1, 0:64], ones[0:1, 0:1])
            t_.transpose(psB[0:64, 1:2], rowbuf[0:1, 64:128], ones[0:1, 0:1])
            t_.transpose(psB[0:16, 2:3], rowbuf[0:1, 128:144], ones[0:1, 0:1])
            t_.transpose(psB[0:16, 3:4], rowbuf[0:1, 144:160], ones[0:1, 0:1])
            inc(t_.transpose(psB[0:16, 4:5], rowbuf[0:1, 144:160], ones[0:1, 0:1]), ts)

        @block.vector
        def _(v):
            v.wait_ge(ts.h, ts.n)
            inc(v.tensor_copy(w1T[:, 0:8], psB[:, 0:8]), vs)
            v.wait_ge(ld.h, ld.n)
            inc(v.tensor_copy(w1T[0:64, 8:24], gate_tm[0:64, 0:16]), vs)

        for n in range(8):
            @block.tensor
            def _(t_, n=n):
                t_.wait_ge(vs.h, vs.n)
                t_.matmul(psA[0:64, 0:512], ones[0:1, 0:64], mrow[0:1, n * 512:(n + 1) * 512],
                          start=True, stop=True)
                inc(t_.matmul(psA[0:64, 512:1024], ones[0:1, 0:64],
                              mrow[0:1, T + n * 512:T + (n + 1) * 512], start=True, stop=True), ts)

            @block.vector
            def _(v, n=n):
                v.wait_ge(ts.h, ts.n)
                sl = slice(n * 512, (n + 1) * 512)
                v.tensor_tensor(out=scrap[0:64, 0:512], in0=qmT[0:64, sl], in1=psA[0:64, 0:512],
                                op=ALU.subtract)
                inc(v.engine_nop() if False else v.tensor_tensor(
                    out=scrap[0:64, 512:1024], in0=scrap[0:64, 0:512], in1=psA[0:64, 512:1024],
                    op=ALU.mult), vs)
                v.wait_ge(vs.h, vs.n)
                inc(v.tensor_scalar(out=scrap[0:64, 0:512], in0=scrap[0:64, 512:1024],
                                    scalar1=w1T[0:64, 0:1], op0=ALU.mult,
                                    scalar2=w1T[0:64, 1:2], op1=ALU.add), vs)
                v.wait_ge(vs.h, vs.n)
                inc(v.tensor_copy(qmT[0:64, sl], scrap[0:64, 0:512]), vs)


        if debug:
            dbg_hn = nc.dram_tensor(pfx + "dbg_hn", [64, T], F32, kind="ExternalOutput").ap()

            @block.gpsimd
            def _(g_):
                g_.wait_ge(vs.h, vs.n)
                inc(g_.dma_start(out=dbg_hn[:, :], in_=qmT[0:64, :]), gs, 16)

        if debug == "c2h":
            @block.sync
            def _(sync):
                sync.wait_ge(gs.h, gs.n)
                sync.wait_ge(vs.h, vs.n)
                sync.wait_ge(ts.h, ts.n)
            ph_mrow.close()
            return st

        ph_mrow.close()

        for n in range(8):
            @block.tensor
            def _(t_, n=n):
                t_.wait_ge(vs.h, vs.n)
                if n > 0:
                    t_.wait_ge(ss.h, ss.n)
                inc(t_.matmul(psB[0:16, 0:512], w1T[0:64, 8:24],
                              qmT[0:64, n * 512:(n + 1) * 512], start=True, stop=True), ts)

            @block.scalar
            def _(s, n=n):
                s.wait_ge(ts.h, ts.n)
                inc(s.activation(out=h1T[0:16, n * 512:(n + 1) * 512], in_=psB[0:16, 0:512],
                                 func=AF.Gelu, bias=w1T[0:16, 2:3], scale=1.0), ss)

        if debug:
            dbg_h1 = nc.dram_tensor(pfx + "dbg_h1", [16, T], F32, kind="ExternalOutput").ap()
            dbg_w1T = nc.dram_tensor(pfx + "dbg_w1T", [128, 32], F32, kind="ExternalOutput").ap()

            @block.gpsimd
            def _(g_):
                g_.wait_ge(ss.h, ss.n)
                inc(g_.dma_start(out=dbg_w1T[:, :], in_=w1T[:]), gs, 16)

            @block.gpsimd
            def _(g_):
                g_.wait_ge(ss.h, ss.n)
                inc(g_.dma_start(out=dbg_h1[:, :], in_=h1T[0:16, :]), gs, 16)

        @block.tensor
        def _(t_):
            t_.wait_ge(ss.h, ss.n)
            mm = None
            for t in range(NT):
                mm = t_.matmul(psC[:, t:t + 1], h1T[0:16, t * 128:(t + 1) * 128],
                               w1T[0:16, 3:4], start=True, stop=True)
            mm2 = t_.matmul(psC[:, 40:41], ones[0:1, 0:128], rowbuf[0:1, 160:161],
                            start=True, stop=True)
            inc(mm2, ts)

        @block.vector
        def _(v):
            v.wait_ge(ts.h, ts.n)
            inc(v.tensor_scalar(out=scores_tm[:, 0:32], in0=psC[:, 0:32],
                                scalar1=psC[:, 40:41], op0=ALU.add, scalar2=None), vs)

        if debug:
            @block.gpsimd
            def _(g_):
                g_.wait_ge(vs.h, vs.n)
                inc(g_.dma_start(out=dbg_scores[:, :], in_=scores_tm[:]), gs, 16)

        if debug == "c2b":
            @block.sync
            def _(sync):
                sync.wait_ge(gs.h, gs.n)
                sync.wait_ge(vs.h, vs.n)
                sync.wait_ge(ts.h, ts.n)
            return st

        @block.sync
        def _(sync):
            sync.wait_ge(vs.h, vs.n)
            sync.wait_ge(ts.h, ts.n)

        @block.gpsimd
        def _(g_):
            g_.wait_ge(vs.h, vs.n)
            g_.wait_ge(ts.h, ts.n)

        @block.vector
        def _(v):
            v.wait_ge(vs.h, vs.n)
            inc(v.reduce_max(scores_tm[:, 32:33], scores_tm[:, 0:32], axis=AX.X), vs)
            inc(v.tensor_scalar(out=scrap[:, 0:32], in0=scores_tm[:, 0:32], scalar1=-1.0,
                                op0=ALU.mult, scalar2=None), vs)
            v.wait_ge(vs.h, vs.n)
            inc(v.reduce_max(scores_tm[:, 33:34], scrap[:, 0:32], axis=AX.X), vs)

        @block.tensor
        def _(t_):
            t_.wait_ge(vs.h, vs.n)
            t_.transpose(psB[0:1, 0:128], scores_tm[:, 32:33], idT[:, 0:128])
            inc(t_.transpose(psB[0:1, 128:256], scores_tm[:, 33:34], idT[:, 0:128]), ts)

        @block.vector
        def _(v):
            v.wait_ge(ts.h, ts.n)
            inc(v.reduce_max(bis[0:1, 1:2], psB[0:1, 0:128], axis=AX.X), vs)
            inc(v.reduce_max(bis[0:1, 3:4], psB[0:1, 128:256], axis=AX.X), vs)
            v.wait_ge(vs.h, vs.n)
            inc(v.tensor_scalar(out=bis[0:1, 1:2], in0=bis[0:1, 1:2], scalar1=1.0,
                                op0=ALU.add, scalar2=None), vs)
            inc(v.tensor_scalar(out=bis[0:1, 0:1], in0=bis[0:1, 3:4], scalar1=-1.0,
                                op0=ALU.mult, scalar2=-1.0, op1=ALU.add), vs)

        for it in range(45):
            @block.vector
            def _(v):
                v.wait_ge(vs.h, vs.n)
                if it > 0:
                    v.wait_ge(ts.h, ts.n)
                inc(v.tensor_tensor(out=bis[0:1, 2:3], in0=bis[0:1, 0:1], in1=bis[0:1, 1:2],
                                    op=ALU.add), vs)
                v.wait_ge(vs.h, vs.n)
                inc(v.tensor_scalar(out=bis[0:1, 2:3], in0=bis[0:1, 2:3], scalar1=0.5,
                                    op0=ALU.mult, scalar2=None), vs)

            @block.tensor
            def _(t_):
                t_.wait_ge(vs.h, vs.n)
                inc(t_.matmul(psB[:, 300:301], ones[0:1, 0:128], bis[0:1, 2:3],
                              start=True, stop=True), ts)

            @block.vector
            def _(v):
                v.wait_ge(ts.h, ts.n)
                inc(v.tensor_scalar(out=scrap[:, 0:32], in0=scores_tm[:, 0:32],
                                    scalar1=psB[:, 300:301], op0=ALU.is_ge,
                                    scalar2=0.0, op1=ALU.add, accum_out=scores_tm[:, 34:35]), vs)

            @block.tensor
            def _(t_):
                t_.wait_ge(vs.h, vs.n)
                inc(t_.matmul(psB[0:1, 310:311], scores_tm[:, 34:35], ones[:, 0:1],
                              start=True, stop=True), ts)

            @block.vector
            def _(v):
                v.wait_ge(ts.h, ts.n)
                inc(v.tensor_scalar(out=bis[0:1, 4:5], in0=psB[0:1, 310:311], scalar1=255.5,
                                    op0=ALU.is_ge, scalar2=None), vs)
                v.wait_ge(vs.h, vs.n)
                inc(v.tensor_tensor(out=bis[0:1, 5:6], in0=bis[0:1, 2:3], in1=bis[0:1, 0:1],
                                    op=ALU.subtract), vs)
                inc(v.tensor_tensor(out=bis[0:1, 6:7], in0=bis[0:1, 1:2], in1=bis[0:1, 2:3],
                                    op=ALU.subtract), vs)
                v.wait_ge(vs.h, vs.n)
                inc(v.tensor_tensor(out=bis[0:1, 5:6], in0=bis[0:1, 5:6], in1=bis[0:1, 4:5],
                                    op=ALU.mult), vs)
                inc(v.tensor_tensor(out=bis[0:1, 6:7], in0=bis[0:1, 6:7], in1=bis[0:1, 4:5],
                                    op=ALU.mult), vs)
                v.wait_ge(vs.h, vs.n)
                inc(v.tensor_tensor(out=bis[0:1, 0:1], in0=bis[0:1, 0:1], in1=bis[0:1, 5:6],
                                    op=ALU.add), vs)
                inc(v.tensor_tensor(out=bis[0:1, 1:2], in0=bis[0:1, 2:3], in1=bis[0:1, 6:7],
                                    op=ALU.add), vs)

        if debug:
            dbg_bis = nc.dram_tensor(pfx + "dbg_bis", [1, 16], F32, kind="ExternalOutput").ap()

            @block.gpsimd
            def _(g_):
                g_.wait_ge(vs.h, vs.n)
                inc(g_.dma_start(out=dbg_bis[:, :], in_=bis[:]), gs, 16)

        @block.tensor
        def _(t_):
            t_.wait_ge(vs.h, vs.n)
            inc(t_.matmul(psB[:, 300:301], ones[0:1, 0:128], bis[0:1, 0:1],
                          start=True, stop=True), ts)

        @block.scalar
        def _(s):
            s.wait_ge(vs.h, vs.n)
            inc(s.activation(out=gate_tm[:, 0:32], in_=scores_tm[:, 0:32], func=AF.Exp,
                             scale=-1.0), ss)

        @block.vector
        def _(v):
            v.wait_ge(ts.h, ts.n)
            v.wait_ge(ss.h, ss.n)
            inc(v.tensor_scalar(out=gate_tm[:, 0:32], in0=gate_tm[:, 0:32],
                                scalar1=1.0, op0=ALU.add, scalar2=None), vs)
            v.wait_ge(vs.h, vs.n)
            inc(v.reciprocal(gate_tm[:, 0:32], gate_tm[:, 0:32]), vs)
            v.wait_ge(vs.h, vs.n)
            inc(v.tensor_scalar(out=scores_tm[:, 0:32], in0=scores_tm[:, 0:32],
                                scalar1=psB[:, 300:301], op0=ALU.is_ge, scalar2=None), vs)
            v.wait_ge(vs.h, vs.n)
            inc(v.tensor_tensor(out=gate_tm[:, 0:32], in0=gate_tm[:, 0:32],
                                in1=scores_tm[:, 0:32], op=ALU.mult), vs)

        @block.tensor
        def _(t_):
            t_.wait_ge(vs.h, vs.n)
            t_.matmul(psA[:, 0:32], ltri[:], scores_tm[:, 0:32], start=True, stop=False)
            inc(t_.matmul(psC[0:1, 0:32], ones[:, 0:1], scores_tm[:, 0:32],
                          start=True, stop=True), ts)

        @block.vector
        def _(v):
            v.wait_ge(ts.h, ts.n)
            inc(v.tensor_copy(rowbuf[0:1, 232:264], psC[0:1, 0:32]), vs)
            v.wait_ge(vs.h, vs.n)
            inc(v.tensor_tensor_scan(out=rowbuf[0:1, 200:232], data0=rowbuf[0:1, 232:264],
                                     data1=rowbuf[0:1, 232:264], initial=0.0,
                                     op0=ALU.add, op1=ALU.bypass), vs)

        @block.tensor
        def _(t_):
            t_.wait_ge(vs.h, vs.n)
            inc(t_.matmul(psA[:, 1:32], ones[0:1, 0:128], rowbuf[0:1, 200:231],
                          start=False, stop=True), ts)

        @block.vector
        def _(v):
            v.wait_ge(ts.h, ts.n)
            inc(v.tensor_tensor(out=gate_tm[:, 32:64] if False else scrap[:, 0:32],
                                in0=psA[:, 0:32], in1=scores_tm[:, 0:32], op=ALU.mult), vs)

        for j in range(NT):
            @block.vector
            def _(v, j=j):
                v.wait_ge(vs.h, vs.n)
                if j > 0:
                    v.wait_ge(ts.h, ts.n)
                inc(v.tensor_scalar(out=scrap[:, 64:320], in0=iotaS[:],
                                    scalar1=scrap[:, j:j + 1], op0=ALU.is_equal, scalar2=None), vs)
                v.wait_ge(vs.h, vs.n)
                v.tensor_copy(scrap[:, 320:321], tokid[:, j:j + 1])
                inc(v.tensor_copy(scrap[:, 321:322], gate_tm[:, j:j + 1]), vs)

            @block.tensor
            def _(t_, j=j):
                t_.wait_ge(vs.h, vs.n)
                t_.matmul(psC[0:1, 0:256], scrap[:, 320:321], scrap[:, 64:320],
                          start=(j == 0), stop=(j == NT - 1))
                inc(t_.matmul(psC[0:1, 256:512], scrap[:, 321:322], scrap[:, 64:320],
                              start=(j == 0), stop=(j == NT - 1)), ts)

        @block.vector
        def _(v):
            v.wait_ge(ts.h, ts.n)
            inc(v.tensor_copy(rowbuf[0:1, 0:256], psC[0:1, 0:256]), vs)
            inc(v.tensor_copy(rowbuf[0:1, 256:512], psC[0:1, 256:512]), vs)

        @block.tensor
        def _(t_):
            t_.wait_ge(vs.h, vs.n)
            t_.transpose(psB[:, 0:1], rowbuf[0:1, 0:128], ones[0:1, 0:1])
            t_.transpose(psB[:, 1:2], rowbuf[0:1, 128:256], ones[0:1, 0:1])
            t_.transpose(psB[:, 2:3], rowbuf[0:1, 256:384], ones[0:1, 0:1])
            t_.transpose(psB[:, 3:4], rowbuf[0:1, 384:512], ones[0:1, 0:1])
            inc(t_.matmul(psA[:, 0:256], ones[0:1, 0:128], rowbuf[0:1, 256:512],
                          start=True, stop=True), ts)

        @block.vector
        def _(v):
            v.wait_ge(ts.h, ts.n)
            inc(v.tensor_copy(idx_sb[:, 0:2], psB[:, 0:2]), vs)
            inc(v.tensor_copy(gsel_col[:, 0:2], psB[:, 2:4]), vs)
            inc(v.tensor_copy(gbc[:], psA[:, 0:256]), vs)

        if debug:
            @block.gpsimd
            def _(g_):
                g_.wait_ge(vs.h, vs.n)
                inc(g_.dma_start(out=dbg_idx.rearrange("b s -> (b s)")[None, :],
                                 in_=rowbuf[0:1, 0:512]), gs, 16)

        if debug == "c3":
            @block.sync
            def _(sync):
                sync.wait_ge(gs.h, gs.n)
                sync.wait_ge(vs.h, vs.n)
                sync.wait_ge(ts.h, ts.n)
            return st

        @block.sync
        def _(sync):
            sync.wait_ge(vs.h, vs.n)
            sync.wait_ge(ts.h, ts.n)

        @block.gpsimd
        def _(g_):
            g_.wait_ge(vs.h, vs.n)
            g_.wait_ge(ts.h, ts.n)

        Xsel = ph.enter_context(nc.sbuf_tensor(pfx + "Xsel", [128, 2 * D], BF16))
        XselT = ph.enter_context(nc.sbuf_tensor(pfx + "XselT", [128, 8 * 256], BF16))
        qkselT = ph.enter_context(nc.sbuf_tensor(pfx + "qkselT", [128, 8 * 512], BF16))
        Vsel = ph.enter_context(nc.sbuf_tensor(pfx + "Vsel", [128, 2 * D], BF16))
        outT = ph.enter_context(nc.sbuf_tensor(pfx + "outT", [128, 8 * 256], BF16))
        pTt = ph.enter_context(nc.sbuf_tensor(pfx + "pTt", [128, 512], BF16))
        rdrow = ph.enter_context(nc.sbuf_tensor(pfx + "rdrow", [1, 512], F32))
        rdbc = ph.enter_context(nc.sbuf_tensor(pfx + "rdbc", [128, 256], F32))
        idTb = ph.enter_context(nc.sbuf_tensor(pfx + "idTb", [128, 128], BF16))
        vrowb = ph.enter_context(nc.sbuf_tensor(pfx + "vrowb", [1, D], BF16))
        xcolb = ph.enter_context(nc.sbuf_tensor(pfx + "xcolb", [128, 16], BF16))
        psTb = ph.enter_context(nc.psum_tensor(pfx + "psTb", [128, 256], BF16))

        from concourse.bass import IndirectOffsetOnAxis as IOA

        @block.vector
        def _(v):
            v.wait_ge(vs.h, vs.n)
            inc(v.tensor_copy(idTb[:], idT[:, 0:128]), vs)

        @block.gpsimd
        def _(g_):
            g_.wait_ge(vs.h, vs.n)
            for sb_ in range(2):
                inc(g_.indirect_dma_start(
                    out=Xsel[:, sb_ * D:(sb_ + 1) * D], in_=cin_dram[:, :],
                    out_offset=None,
                    in_offset=IOA(ap=idx_sb[:, sb_:sb_ + 1], axis=0)), gs, 16)

        for sb_ in range(2):
            for c in range(8):
                @block.tensor
                def _(t_, sb_=sb_, c=c):
                    t_.wait_ge(gs.h, gs.n)
                    t_.wait_ge(vs.h, vs.n)
                    inc(t_.transpose(psTb[:, 0:128],
                                     Xsel[:, sb_ * D + c * 128: sb_ * D + (c + 1) * 128],
                                     idTb[:]), ts)

                @block.vector
                def _(v, sb_=sb_, c=c):
                    v.wait_ge(ts.h, ts.n)
                    inc(v.tensor_copy(
                        XselT[:, c * 256 + sb_ * 128: c * 256 + sb_ * 128 + 128],
                        psTb[:, 0:128]), vs)

        for part in range(2):
            for m in range(8):
                @block.tensor
                def _(t_, part=part, m=m):
                    t_.wait_ge(vs.h, vs.n)
                    t_.wait_ge(trs.h, trs.n)
                    mm = None
                    for c in range(8):
                        mm = t_.matmul(psA[:, 0:256],
                                       WqkT[:, c * 2048 + part * 1024 + m * 128:
                                            c * 2048 + part * 1024 + m * 128 + 128],
                                       XselT[:, c * 256:(c + 1) * 256],
                                       start=(c == 0), stop=(c == 7))
                    inc(mm, ts)

                @block.vector
                def _(v, part=part, m=m):
                    v.wait_ge(ts.h, ts.n)
                    inc(v.tensor_copy(qkselT[:, m * 512 + part * 256: m * 512 + part * 256 + 256],
                                      psA[:, 0:256]), vs)

        for sb_ in range(2):
            @block.tensor
            def _(t_, sb_=sb_):
                t_.wait_ge(vs.h, vs.n)
                mm = None
                for c in range(8):
                    for hf in range(2):
                        mm = t_.matmul(psA[:, hf * 512:(hf + 1) * 512],
                                       XselT[:, c * 256 + sb_ * 128: c * 256 + sb_ * 128 + 128],
                                       WvT[:, c * D + hf * 512: c * D + hf * 512 + 512],
                                       start=(c == 0), stop=(c == 7))
                inc(mm, ts)

            @block.vector
            def _(v, sb_=sb_):
                v.wait_ge(ts.h, ts.n)
                inc(v.tensor_copy(Vsel[:, sb_ * D:(sb_ + 1) * D], psA[:, 0:1024]), vs)

        @block.tensor
        def _(t_):
            t_.wait_ge(vs.h, vs.n)
            mm = None
            for c in range(8):
                mm = t_.transpose(psB[:, c:c + 1], vrow[0:1, c * 128:(c + 1) * 128],
                                  ones[0:1, 0:1])
            inc(mm, ts)

        @block.vector
        def _(v):
            v.wait_ge(ts.h, ts.n)
            inc(v.tensor_copy(xcolb[:, 0:8], psB[:, 0:8]), vs)

        @block.tensor
        def _(t_):
            t_.wait_ge(vs.h, vs.n)
            mm = None
            for c in range(8):
                for hf in range(2):
                    mm = t_.matmul(psA[0:1, hf * 512:(hf + 1) * 512], xcolb[:, c:c + 1],
                                   WvT[:, c * D + hf * 512: c * D + hf * 512 + 512],
                                   start=(c == 0), stop=(c == 7))
            for sb_ in range(2):
                mm = t_.matmul(psC[0:1, 0:512], onesb[:, 0:1], Vsel[:, sb_ * D: sb_ * D + 512],
                               start=(sb_ == 0), stop=(sb_ == 1))
                mm = t_.matmul(psC[0:1, 512:1024]
                               if False else psB[0:1, 0:512], onesb[:, 0:1],
                               Vsel[:, sb_ * D + 512: sb_ * D + 1024],
                               start=(sb_ == 0), stop=(sb_ == 1))
            inc(mm, ts)

        @block.vector
        def _(v):
            v.wait_ge(ts.h, ts.n)
            inc(v.tensor_copy(vrow[0:1, D:2 * D], psA[0:1, 0:1024]), vs)
            v.wait_ge(vs.h, vs.n)
            v.tensor_tensor(out=vrow[0:1, 2 * D:2 * D + 512], in0=vrow[0:1, D:D + 512],
                            in1=psC[0:1, 0:512], op=ALU.subtract)
            inc(v.tensor_tensor(out=vrow[0:1, 2 * D + 512:3 * D], in0=vrow[0:1, D + 512:2 * D],
                                in1=psB[0:1, 0:512], op=ALU.subtract), vs)
            v.wait_ge(vs.h, vs.n)
            inc(v.tensor_copy(vrowb[0:1, :], vrow[0:1, 2 * D:3 * D]), vs)

        @block.sync
        def _(sync):
            sync.wait_ge(vs.h, vs.n)
            sync.wait_ge(ts.h, ts.n)

        @block.gpsimd
        def _(g_):
            g_.wait_ge(vs.h, vs.n)
            g_.wait_ge(ts.h, ts.n)

        for h in range(16):
            jb = h // 2
            base = 64 * (h % 2)

            @block.tensor
            def _(t_, h=h, jb=jb, base=base):
                t_.wait_ge(vs.h, vs.n)
                mm = None
                for kc in range(2):
                    mm = t_.matmul(psA[:, kc * 256:(kc + 1) * 256],
                                   qkselT[base:base + 64,
                                          jb * 512 + 256 + kc * 128: jb * 512 + 256 + kc * 128 + 128],
                                   qkselT[base:base + 64, jb * 512: jb * 512 + 256],
                                   start=True, stop=True)
                inc(mm, ts)

            @block.vector
            def _(v, h=h):
                v.wait_ge(ts.h, ts.n)
                for kc in range(2):
                    inc(v.tensor_scalar(out=scrap[:, kc * 256:(kc + 1) * 256],
                                        in0=psA[:, kc * 256:(kc + 1) * 256],
                                        scalar1=gsel_col[:, kc:kc + 1], op0=ALU.mult,
                                        scalar2=0.125, op1=ALU.mult), vs)
                v.wait_ge(vs.h, vs.n)
                for kc in range(2):
                    inc(v.tensor_tensor(out=scrap[:, kc * 256:(kc + 1) * 256],
                                        in0=scrap[:, kc * 256:(kc + 1) * 256],
                                        in1=gbc[:, 0:256], op=ALU.mult), vs)

            @block.scalar
            def _(s, h=h):
                s.wait_ge(vs.h, vs.n)
                inc(s.activation(out=pTt[:], in_=scrap[:, 0:512], func=AF.Exp), ss)

            @block.tensor
            def _(t_, h=h):
                t_.wait_ge(ss.h, ss.n)
                t_.wait_ge(vs.h, vs.n)
                for kc in range(2):
                    t_.matmul(psC[0:64, 0:256], Vsel[:, kc * D + h * 64: kc * D + h * 64 + 64],
                              pTt[:, kc * 256:(kc + 1) * 256],
                              start=(kc == 0), stop=False)
                    t_.matmul(psC[64:65, 0:256], onesb[:, 0:1], pTt[:, kc * 256:(kc + 1) * 256],
                              start=False, stop=False)
                mm = t_.matmul(psC[0:64, 0:256], vrowb[0:1, h * 64: h * 64 + 64],
                               onesrowb[0:1, 0:256], start=False, stop=True)
                inc(mm, ts)

            @block.vector
            def _(v, h=h):
                v.wait_ge(ts.h, ts.n)
                inc(v.tensor_scalar(out=rdrow[0:1, 0:256], in0=psC[64:65, 0:256],
                                    scalar1=3840.0, op0=ALU.add, scalar2=None), vs)
                v.wait_ge(vs.h, vs.n)
                inc(v.reciprocal(rdrow[0:1, 256:512], rdrow[0:1, 0:256]), vs)

            @block.tensor
            def _(t_, h=h):
                t_.wait_ge(vs.h, vs.n)
                inc(t_.matmul(psB[0:64, 0:256], ones[0:1, 0:64], rdrow[0:1, 256:512],
                              start=True, stop=True), ts)

            @block.vector
            def _(v, h=h, jb=jb, base=base):
                v.wait_ge(ts.h, ts.n)
                inc(v.tensor_copy(rdbc[0:64, :], psB[0:64, 0:256]), vs)
                v.wait_ge(vs.h, vs.n)
                inc(v.tensor_tensor(out=outT[base:base + 64, jb * 256:(jb + 1) * 256],
                                    in0=psC[0:64, 0:256], in1=rdbc[0:64, :], op=ALU.mult), vs)

        if debug == "c5":
            @block.sync
            def _(sync):
                sync.wait_ge(gs.h, gs.n)
                sync.wait_ge(vs.h, vs.n)
                sync.wait_ge(ts.h, ts.n)
            return st

        @block.sync
        def _(sync):
            sync.wait_ge(vs.h, vs.n)
            sync.wait_ge(ts.h, ts.n)

        @block.gpsimd
        def _(g_):
            g_.wait_ge(vs.h, vs.n)
            g_.wait_ge(ts.h, ts.n)

        for m in range(8):
            @block.tensor
            def _(t_, m=m):
                t_.wait_ge(vs.h, vs.n)
                mm = None
                for c in range(8):
                    mm = t_.matmul(psA[:, 0:256], WoT[:, c * D + m * 128: c * D + m * 128 + 128],
                                   outT[:, c * 256:(c + 1) * 256],
                                   start=(c == 0), stop=(c == 7))
                inc(mm, ts)

            @block.vector
            def _(v, m=m):
                v.wait_ge(ts.h, ts.n)
                inc(v.tensor_scalar(out=Xsel[:, m * 256: m * 256 + 256], in0=psA[:, 0:256],
                                    scalar1=bout_col[:, m:m + 1], op0=ALU.add, scalar2=None), vs)

        for m in range(8):
            for sb_ in range(2):
                @block.tensor
                def _(t_, m=m, sb_=sb_):
                    t_.wait_ge(vs.h, vs.n)
                    inc(t_.transpose(psTb[:, 128:256],
                                     Xsel[:, m * 256 + sb_ * 128: m * 256 + sb_ * 128 + 128],
                                     idTb[:]), ts)

                @block.vector
                def _(v, m=m, sb_=sb_):
                    v.wait_ge(ts.h, ts.n)
                    inc(v.tensor_copy(Vsel[:, sb_ * D + m * 128: sb_ * D + (m + 1) * 128],
                                      psTb[:, 128:256]), vs)

        @block.vector
        def _(v):
            v.wait_ge(vs.h, vs.n)
            inc(v.tensor_scalar(out=vrow[0:1, 2 * D:3 * D], in0=vrow[0:1, D:2 * D],
                                scalar1=1.0 / T, op0=ALU.mult, scalar2=None), vs)

        @block.tensor
        def _(t_):
            t_.wait_ge(vs.h, vs.n)
            mm = None
            for c in range(8):
                mm = t_.transpose(psB[:, c:c + 1], vrow[0:1, 2 * D + c * 128: 2 * D + (c + 1) * 128],
                                  ones[0:1, 0:1])
            inc(mm, ts)

        @block.vector
        def _(v):
            v.wait_ge(ts.h, ts.n)
            inc(v.tensor_copy(xcolb[:, 8:16], psB[:, 0:8]), vs)

        @block.sync
        def _(sync):
            sync.wait_ge(vs.h, vs.n)
            inc(sync.dma_start(out=rowbuf[0:1, 0:D], in_=wts["b_out"][None, :]), ld, 16)

        @block.tensor
        def _(t_):
            t_.wait_ge(vs.h, vs.n)
            mm = None
            for c in range(8):
                for hf in range(2):
                    mm = t_.matmul(psA[0:1, hf * 512:(hf + 1) * 512], xcolb[:, 8 + c:9 + c],
                                   WoT[:, c * D + hf * 512: c * D + hf * 512 + 512],
                                   start=(c == 0), stop=(c == 7))
            inc(mm, ts)

        @block.vector
        def _(v):
            v.wait_ge(ts.h, ts.n)
            v.wait_ge(ld.h, ld.n)
            inc(v.tensor_tensor(out=vrow[0:1, 3 * D:4 * D], in0=psA[0:1, 0:1024],
                                in1=rowbuf[0:1, 0:D], op=ALU.add), vs)

        @block.tensor
        def _(t_):
            t_.wait_ge(vs.h, vs.n)
            t_.matmul(psA[:, 0:512], ones[0:1, 0:128], vrow[0:1, 3 * D:3 * D + 512],
                      start=True, stop=True)
            inc(t_.matmul(psA[:, 512:1024], ones[0:1, 0:128], vrow[0:1, 3 * D + 512:4 * D],
                          start=True, stop=True), ts)

        @block.vector
        def _(v):
            v.wait_ge(ts.h, ts.n)
            inc(v.tensor_copy(yuni_bc[:], psA[:, 0:1024]), vs)

        @block.gpsimd
        def _(g_):
            g_.wait_ge(vs.h, vs.n)
            for t in range(NT):
                inc(g_.dma_start(out=attnfull[t * 128:(t + 1) * 128, :], in_=yuni_bc[:]), gs, 16)

        @block.gpsimd
        def _(g_):
            g_.wait_ge(gs.h, gs.n)
            for sb_ in range(2):
                inc(g_.indirect_dma_start(
                    out=attnfull[:, :], in_=Vsel[:, sb_ * D:(sb_ + 1) * D],
                    out_offset=IOA(ap=idx_sb[:, sb_:sb_ + 1], axis=0),
                    in_offset=None), gs, 16)

        if debug:
            @block.gpsimd
            def _(g_):
                g_.wait_ge(gs.h, gs.n)
                inc(g_.dma_start(out=dbg_ysel.rearrange("(b p) d -> p b d", p=128),
                                 in_=Vsel[:].rearrange("p (b d) -> p b d", d=D)), gs, 16)

    @block.sync
    def _(sync):
        sync.wait_ge(vs.h, vs.n)
        sync.wait_ge(ts.h, ts.n)

    @block.gpsimd
    def _(g_):
        g_.wait_ge(vs.h, vs.n)
        g_.wait_ge(ts.h, ts.n)

    c7_ld, c7_ap, c7_st, c7_rs = {}, {}, {}, {}
    gs_fill = gs.n

    def c7_load(t):
        @block.sync
        def _(sync, t=t):
            if t == 0:
                sync.wait_ge(gs.h, gs_fill)
            if t - 2 in c7_ap:
                sync.wait_ge(vs.h, c7_ap[t - 2])
            inc(sync.dma_start(out=xtp[:, (t % 2) * D:(t % 2 + 1) * D],
                               in_=x_in[t * 128:(t + 1) * 128, :]), ld, 16)
            inc(sync.dma_start(out=fbp[:, (t % 2) * D:(t % 2 + 1) * D],
                               in_=attnfull[t * 128:(t + 1) * 128, :]), ld, 16)
        c7_ld[t] = ld.n

    def c7_stats(t):
        @block.vector
        def _(v, t=t):
            v.wait_ge(ld.h, c7_ld[t])
        c7_rs[t] = emit_ln_stats(block, fbp[:, (t % 2) * D:(t % 2 + 1) * D],
                                 stats[:, (t % 2) * 8:(t % 2) * 8 + 8], scrap, vs, ss)

    def c7_apply(t):
        if t - 2 in c7_st:
            @block.vector
            def _(v, t=t):
                v.wait_ge(st.h, c7_st[t - 2])
        emit_ln_apply(block, fbp[:, (t % 2) * D:(t % 2 + 1) * D],
                      scrap[:, (2 + (t % 2)) * D:(3 + (t % 2)) * D]
                      if False else xop[:, (t % 2) * D:(t % 2 + 1) * D],
                      stats[:, (t % 2) * 8:(t % 2) * 8 + 8], scrap, vs, ss, c7_rs[t],
                      add_ap=xtp[:, (t % 2) * D:(t % 2 + 1) * D])
        c7_ap[t] = vs.n

        @block.sync
        def _(sync, t=t):
            sync.wait_ge(vs.h, c7_ap[t])
            inc(sync.dma_start(out=x_out[t * 128:(t + 1) * 128, :],
                               in_=xop[:, (t % 2) * D:(t % 2 + 1) * D]), st, 16)
        c7_st[t] = st.n

    xop = E(nc.sbuf_tensor(pfx + "xop", [128, 2 * D], F32))
    c7_load(0)
    c7_stats(0)
    c7_load(1)
    for t in range(NT):
        if t + 1 < NT:
            c7_stats(t + 1)
        c7_apply(t)
        if t + 2 < NT:
            c7_load(t + 2)

    @block.sync
    def _(sync):
        sync.wait_ge(st.h, st.n)

    return st



def build_prog_c(debug=False):
    nc = bass.Bass("TRN2", target_bir_lowering=False)
    x_in = nc.dram_tensor("x_in", [T, D], F32, kind="ExternalInput").ap()
    wts = {
        "w_qkv": nc.dram_tensor("w_qkv", [3 * D, D], F32, kind="ExternalInput").ap(),
        "w_out": nc.dram_tensor("w_out", [D, D], F32, kind="ExternalInput").ap(),
        "b_out": nc.dram_tensor("b_out", [D], F32, kind="ExternalInput").ap(),
        "ln_g": nc.dram_tensor("ln_g", [64], F32, kind="ExternalInput").ap(),
        "ln_b": nc.dram_tensor("ln_b", [64], F32, kind="ExternalInput").ap(),
        "w1": nc.dram_tensor("w1", [16, 64], F32, kind="ExternalInput").ap(),
        "b1": nc.dram_tensor("b1", [16], F32, kind="ExternalInput").ap(),
        "w2": nc.dram_tensor("w2", [1, 16], F32, kind="ExternalInput").ap(),
        "b2": nc.dram_tensor("b2", [1], F32, kind="ExternalInput").ap(),
    }
    x_out = nc.dram_tensor("x_out", [T, D], F32, kind="ExternalOutput").ap()
    with ExitStack() as ctx:
        block = ctx.enter_context(nc.Block())
        emit_stage_c(nc, ctx, block, x_in, wts, x_out, debug=debug)
    return nc


def build_prog_a():
    nc = bass.Bass("TRN2", target_bir_lowering=False)
    x_in = nc.dram_tensor("x_in", [T, D], F32, kind="ExternalInput").ap()
    x_out = nc.dram_tensor("x_out", [T, D], F32, kind="ExternalOutput").ap()
    with ExitStack() as ctx:
        block = ctx.enter_context(nc.Block())
        emit_stage_a(nc, ctx, block, x_in, x_out)
    return nc


def emit_stage_barrier(block, sems):
    for section in [block.sync, block.gpsimd, block.vector, block.scalar, block.tensor]:
        @section
        def _(e):
            for s in sems:
                e.wait_ge(s.h, s.n)


def build_prog_fused():
    nc = bass.Bass("TRN2", target_bir_lowering=False)
    x_in = nc.dram_tensor("x_in", [T, D], F32, kind="ExternalInput").ap()
    w0 = nc.dram_tensor("w_in", [D, D], F32, kind="ExternalInput").ap()
    h_in = nc.dram_tensor("h_in", [D, D], BF16, kind="ExternalInput").ap()
    wc = {
        "w_qkv": nc.dram_tensor("w_qkv", [3 * D, D], F32, kind="ExternalInput").ap(),
        "w_out": nc.dram_tensor("w_out", [D, D], F32, kind="ExternalInput").ap(),
        "b_out": nc.dram_tensor("b_out", [D], F32, kind="ExternalInput").ap(),
        "ln_g": nc.dram_tensor("ln_g", [64], F32, kind="ExternalInput").ap(),
        "ln_b": nc.dram_tensor("ln_b", [64], F32, kind="ExternalInput").ap(),
        "w1": nc.dram_tensor("w1", [16, 64], F32, kind="ExternalInput").ap(),
        "b1": nc.dram_tensor("b1", [16], F32, kind="ExternalInput").ap(),
        "w2": nc.dram_tensor("w2", [1, 16], F32, kind="ExternalInput").ap(),
        "b2": nc.dram_tensor("b2", [1], F32, kind="ExternalInput").ap(),
    }
    wd = {
        "wq": nc.dram_tensor("wq", [D, D], F32, kind="ExternalInput").ap(),
        "wk": nc.dram_tensor("wk", [256, D], F32, kind="ExternalInput").ap(),
        "wv": nc.dram_tensor("wv", [256, D], F32, kind="ExternalInput").ap(),
        "wo": nc.dram_tensor("wo", [D, D], F32, kind="ExternalInput").ap(),
        "gate_w": nc.dram_tensor("gate_w", [8, D], F32, kind="ExternalInput").ap(),
        "expert_w": nc.dram_tensor("expert_w", [8, D, D], F32, kind="ExternalInput").ap(),
        "anw": nc.dram_tensor("anw", [D], F32, kind="ExternalInput").ap(),
        "fnw": nc.dram_tensor("fnw", [D], F32, kind="ExternalInput").ap(),
        "masks": nc.dram_tensor("masks", [4 * 128, 512], BF16, kind="ExternalInput").ap(),
    }
    x_out = nc.dram_tensor("x_out", [T, D], F32, kind="ExternalOutput").ap()
    x1d = nc.dram_tensor("x1d", [T, D], F32, kind="Internal").ap()
    x2d = nc.dram_tensor("x2d", [T, D], F32, kind="Internal").ap()
    x3d = nc.dram_tensor("x3d", [T, D], F32, kind="Internal").ap()

    with ExitStack() as ctx:
        block = ctx.enter_context(nc.Block())
        sems = mk_sems(nc, ctx, ["ld", "vs", "ss", "ts", "gs", "trs", "st"], "f_")
        with ExitStack() as sctx:
            emit_stage_a(nc, sctx, block, x_in, x1d, sems=sems)
        emit_stage_barrier(block, sems)
        with ExitStack() as sctx:
            emit_stage_b(nc, sctx, block, x1d, w0, h_in, x2d, sems=sems)
        emit_stage_barrier(block, sems)
        with ExitStack() as sctx:
            emit_stage_c(nc, sctx, block, x2d, wc, x3d, sems=sems)
        emit_stage_barrier(block, sems)
        with ExitStack() as sctx:
            emit_stage_d(nc, sctx, block, x3d, wd, x_out, sems=sems)
    return nc


def kernel(L0_attn_x, L0_ffn_weight, L1_attn_w_qkv, L1_attn_w_out, L1_attn_b_out,
           L1_attn_ln_g, L1_attn_ln_b, L1_attn_w1, L1_attn_b1, L1_attn_w2,
           L1_attn_b2, L1_ffn_attn_norm_w, L1_ffn_ffn_norm_w, L1_ffn_wq,
           L1_ffn_wk, L1_ffn_wv, L1_ffn_wo, L1_ffn_gate_w, L1_ffn_expert_w):
    f32 = lambda a: np.ascontiguousarray(np.asarray(a, dtype=np.float32))
    x = f32(L0_attn_x)
    nc = build_prog_fused()
    shared = {
        "w_in": f32(L0_ffn_weight), "h_in": hadamard_bf16(),
        "w_qkv": f32(L1_attn_w_qkv), "w_out": f32(L1_attn_w_out),
        "b_out": f32(L1_attn_b_out), "ln_g": f32(L1_attn_ln_g),
        "ln_b": f32(L1_attn_ln_b), "w1": f32(L1_attn_w1), "b1": f32(L1_attn_b1),
        "w2": f32(L1_attn_w2), "b2": f32(L1_attn_b2),
        "wq": f32(L1_ffn_wq), "wk": f32(L1_ffn_wk), "wv": f32(L1_ffn_wv),
        "wo": f32(L1_ffn_wo), "gate_w": f32(L1_ffn_gate_w),
        "expert_w": f32(L1_ffn_expert_w), "anw": f32(L1_ffn_attn_norm_w),
        "fnw": f32(L1_ffn_ffn_norm_w), "masks": causal_masks_bf16(),
    }
    res = run_bass_kernel_spmd(nc, [dict(shared, x_in=x[b]) for b in range(8)],
                               core_ids=list(range(8)))
    return np.stack([np.asarray(res.results[b]["x_out"]) for b in range(8)], axis=0)


# version 08
# speedup vs baseline: 1.1967x; 1.1967x over previous
import numpy as np
from contextlib import ExitStack

import concourse.bass as bass
import concourse.mybir as mybir
from concourse.bass_utils import run_bass_kernel_spmd

F32 = mybir.dt.float32
BF16 = mybir.dt.bfloat16
U32 = mybir.dt.uint32
AF = mybir.ActivationFunctionType
ALU = mybir.AluOpType
AX = mybir.AxisListType

T = 4096
D = 1024
NT = T // 128
ND = D // 128
LN_EPS = 1e-5


class Sem:

    def __init__(self, handle):
        self.h = handle
        self.n = 0


def inc(instr, sem, k=1):
    sem.n += k
    return instr.then_inc(sem.h, k)


def mk_sems(nc, ctx, names, pfx=""):
    return [Sem(ctx.enter_context(nc.semaphore(pfx + n))) for n in names]


def emit_ln_tile(block, src_ap, dst_ap, stats, scrap, vs, ss,
                 add_ap=None, width=D):
    w = width

    @block.vector
    def _(v):
        v.tensor_scalar(out=scrap[:, 0:w], in0=src_ap, scalar1=0.0, op0=ALU.add,
                        scalar2=0.0, op1=ALU.add, accum_out=stats[:, 0:1])
        inc(v.scalar_tensor_tensor(out=scrap[:, w:2 * w], in0=src_ap, scalar=1.0,
                                   op0=ALU.mult, op1=ALU.mult, in1=src_ap,
                                   accum_out=stats[:, 1:2]), vs)
        v.wait_ge(vs.h, vs.n)
        v.tensor_scalar(out=stats[:, 2:3], in0=stats[:, 0:1], scalar1=1.0 / w,
                        op0=ALU.mult, scalar2=None)
        inc(v.tensor_scalar(out=stats[:, 4:5], in0=stats[:, 1:2], scalar1=1.0 / w,
                            op0=ALU.mult, scalar2=None), vs)
        v.wait_ge(vs.h, vs.n)
        inc(v.tensor_tensor(out=stats[:, 5:6], in0=stats[:, 2:3], in1=stats[:, 2:3],
                            op=ALU.mult), vs)
        v.wait_ge(vs.h, vs.n)
        inc(v.tensor_scalar(out=stats[:, 4:5], in0=stats[:, 4:5], scalar1=stats[:, 5:6],
                            op0=ALU.subtract, scalar2=LN_EPS, op1=ALU.add), vs)

    @block.scalar
    def _(s):
        s.wait_ge(vs.h, vs.n)
        inc(s.activation(out=stats[:, 5:6], in_=stats[:, 4:5], func=AF.Ln), ss)
        s.wait_ge(ss.h, ss.n)
        inc(s.activation(out=stats[:, 3:4], in_=stats[:, 5:6], func=AF.Exp, scale=-0.5), ss)

    @block.vector
    def _(v):
        v.wait_ge(ss.h, ss.n)
        if add_ap is None:
            op = v.tensor_scalar(out=dst_ap, in0=src_ap, scalar1=stats[:, 2:3],
                                 op0=ALU.subtract, scalar2=stats[:, 3:4], op1=ALU.mult)
        else:
            v.tensor_scalar(out=scrap[:, 0:w], in0=src_ap, scalar1=stats[:, 2:3],
                            op0=ALU.subtract, scalar2=stats[:, 3:4], op1=ALU.mult)
            op = v.tensor_tensor(out=dst_ap, in0=scrap[:, 0:w], in1=add_ap, op=ALU.add)
        inc(op, vs)


def emit_ln_stats(block, src_ap, st8, scrap, vs, ss, width=D):
    w = width

    @block.vector
    def _(v):
        v.tensor_scalar(out=scrap[:, 0:w], in0=src_ap, scalar1=0.0, op0=ALU.add,
                        scalar2=0.0, op1=ALU.add, accum_out=st8[:, 0:1])
        inc(v.scalar_tensor_tensor(out=scrap[:, 0:w], in0=src_ap, scalar=1.0,
                                   op0=ALU.mult, op1=ALU.mult, in1=src_ap,
                                   accum_out=st8[:, 1:2]), vs)
        v.wait_ge(vs.h, vs.n)
        v.tensor_scalar(out=st8[:, 2:3], in0=st8[:, 0:1], scalar1=1.0 / w,
                        op0=ALU.mult, scalar2=None)
        inc(v.tensor_scalar(out=st8[:, 4:5], in0=st8[:, 1:2], scalar1=1.0 / w,
                            op0=ALU.mult, scalar2=None), vs)
        v.wait_ge(vs.h, vs.n)
        inc(v.tensor_tensor(out=st8[:, 5:6], in0=st8[:, 2:3], in1=st8[:, 2:3],
                            op=ALU.mult), vs)
        v.wait_ge(vs.h, vs.n)
        inc(v.tensor_scalar(out=st8[:, 4:5], in0=st8[:, 4:5], scalar1=st8[:, 5:6],
                            op0=ALU.subtract, scalar2=LN_EPS, op1=ALU.add), vs)

    @block.scalar
    def _(s):
        s.wait_ge(vs.h, vs.n)
        inc(s.activation(out=st8[:, 5:6], in_=st8[:, 4:5], func=AF.Ln), ss)
        s.wait_ge(ss.h, ss.n)
        inc(s.activation(out=st8[:, 3:4], in_=st8[:, 5:6], func=AF.Exp, scale=-0.5), ss)
    return ss.n


def emit_ln_apply(block, src_ap, dst_ap, st8, scrap, vs, ss, rstd_ready,
                  add_ap=None, width=D):
    w = width

    @block.vector
    def _(v):
        v.wait_ge(ss.h, rstd_ready)
        if add_ap is None:
            op = v.tensor_scalar(out=dst_ap, in0=src_ap, scalar1=st8[:, 2:3],
                                 op0=ALU.subtract, scalar2=st8[:, 3:4], op1=ALU.mult)
        else:
            v.tensor_scalar(out=scrap[:, 0:w], in0=src_ap, scalar1=st8[:, 2:3],
                            op0=ALU.subtract, scalar2=st8[:, 3:4], op1=ALU.mult)
            op = v.tensor_tensor(out=dst_ap, in0=scrap[:, 0:w], in1=add_ap, op=ALU.add)
        inc(op, vs)


def emit_stage_a(nc, ctx, block, x_in, x_out, pfx="a_", sems=None):
    E = ctx.enter_context
    ld, vs, ss, ts, gs, trs, st = sems or mk_sems(
        nc, ctx, ["ld", "vs", "ss", "ts", "gs", "trs", "st"], pfx)

    aseg_dram = nc.dram_tensor(pfx + "segbf", [512, D], BF16, kind="Internal").ap()

    xs = E(nc.sbuf_tensor(pfx + "xs", [128, 4 * D], F32))
    aseg = E(nc.sbuf_tensor(pfx + "aseg", [128, 4 * D], BF16))
    asegT = E(nc.sbuf_tensor(pfx + "asegT", [128, 8 * 512], BF16))
    pT = E(nc.sbuf_tensor(pfx + "pT", [128, 4 * 512], BF16))
    stats = E(nc.sbuf_tensor(pfx + "stats", [128, 16], F32))
    scrap = E(nc.sbuf_tensor(pfx + "scrap", [128, 3 * D], F32))
    ones = E(nc.sbuf_tensor(pfx + "ones", [128, 2], BF16))
    id1 = E(nc.sbuf_tensor(pfx + "id1", [1, 2], F32))
    rrow = E(nc.sbuf_tensor(pfx + "rrow", [1, 512], F32))
    attn = E(nc.sbuf_tensor(pfx + "attn", [128, 4 * D], F32))
    x1seg = E(nc.sbuf_tensor(pfx + "x1seg", [128, 4 * D], F32))

    psS = E(nc.psum_tensor(pfx + "psS", [128, 2 * 512], F32))
    psO = E(nc.psum_tensor(pfx + "psO", [128, 2 * 1024], F32))
    psden = E(nc.psum_tensor(pfx + "psden", [128, 512], F32))
    psdc = E(nc.psum_tensor(pfx + "psdc", [128, 8], F32))

    SEGS = [(g, r) for g in range(2) for r in range(4)]

    def seg_src(ap, g, r):
        base = g * 2048 + r
        return ap[base:base + 2045:4, :].rearrange("(t p) d -> p t d", p=128)

    x3 = xs[:].rearrange("p (t d) -> p t d", d=D)
    o3 = x1seg[:].rearrange("p (t d) -> p t d", d=D)

    @block.gpsimd
    def _(g_):
        g_.memset(ones[:], 1.0)
        g_.memset(id1[:], 0.0)
        inc(g_.memset(id1[0:1, 0:1], 1.0), gs)

    for i, (g, r) in enumerate(SEGS):
        @block.sync
        def _(sync, g=g, r=r):
            if i > 0:
                sync.wait_ge(st.h, st.n)
            inc(sync.dma_start(out=x3, in_=seg_src(x_in, g, r)), ld, 16)

        @block.vector
        def _(v):
            v.wait_ge(ld.h, ld.n)

        @block.scalar
        def _(s):
            s.wait_ge(ld.h, ld.n)

        for t in range(4):
            emit_ln_tile(block, xs[:, t * D:(t + 1) * D], aseg[:, t * D:(t + 1) * D],
                         stats, scrap, vs, ss)

        @block.gpsimd
        def _(g_):
            g_.wait_ge(vs.h, vs.n)
            inc(g_.dma_start(out=aseg_dram.rearrange("(t p) d -> p t d", p=128),
                             in_=aseg[:].rearrange("p (t d) -> p t d", d=D)), gs, 16)

        @block.scalar
        def _(s):
            s.wait_ge(gs.h, gs.n)
            for dc in range(8):
                inc(s.dma_start(out=asegT[:, dc * 512:(dc + 1) * 512],
                                in_=aseg_dram[:, dc * 128:(dc + 1) * 128],
                                transpose=True), trs, 16)

        for kc in range(4):
            sb = psS[:, (kc % 2) * 512:(kc % 2) * 512 + 512]

            @block.tensor
            def _(t_, kc=kc, sb=sb):
                t_.wait_ge(trs.h, trs.n)
                if kc >= 2:
                    t_.wait_ge(ss.h, ss.n)
                for dc in range(8):
                    mm = t_.matmul(sb, asegT[:, dc * 512 + kc * 128: dc * 512 + kc * 128 + 128],
                                   asegT[:, dc * 512:(dc + 1) * 512],
                                   start=(dc == 0), stop=(dc == 7))
                inc(mm, ts)

            @block.scalar
            def _(s, kc=kc, sb=sb):
                s.wait_ge(ts.h, ts.n)
                inc(s.activation(out=pT[:, kc * 512:(kc + 1) * 512], in_=sb,
                                 func=AF.Exp, scale=1.0 / 32.0), ss)

        @block.tensor
        def _(t_):
            t_.wait_ge(ss.h, ss.n)
            for kc in range(4):
                mm = t_.matmul(psden[0:1, :], ones[:, 0:1], pT[:, kc * 512:(kc + 1) * 512],
                               start=(kc == 0), stop=(kc == 3))
            inc(mm, ts)

        @block.vector
        def _(v):
            v.wait_ge(ts.h, ts.n)
            inc(v.reciprocal(rrow[0:1, :], psden[0:1, :]), vs)

        @block.tensor
        def _(t_):
            t_.wait_ge(vs.h, vs.n)
            for qc in range(4):
                mm = t_.transpose(psdc[:, qc:qc + 1], rrow[0:1, qc * 128:(qc + 1) * 128],
                                  id1[0:1, 0:1])
            inc(mm, ts)

        for qc in range(4):
            ob = psO[:, (qc % 2) * 1024:(qc % 2) * 1024 + 1024]

            @block.tensor
            def _(t_, qc=qc, ob=ob):
                if qc >= 2:
                    t_.wait_ge(vs.h, vs.n)
                for kc in range(4):
                    for h in range(2):
                        mm = t_.matmul(ob[:, h * 512:(h + 1) * 512],
                                       pT[:, kc * 512 + qc * 128: kc * 512 + qc * 128 + 128],
                                       aseg[:, kc * D + h * 512: kc * D + h * 512 + 512],
                                       start=(kc == 0), stop=(kc == 3))
                inc(mm, ts)

            @block.vector
            def _(v, qc=qc, ob=ob):
                v.wait_ge(ts.h, ts.n)
                inc(v.tensor_scalar(out=attn[:, qc * D:(qc + 1) * D], in0=ob,
                                    scalar1=psdc[:, qc:qc + 1], op0=ALU.mult,
                                    scalar2=0.25, op1=ALU.mult), vs)

        @block.scalar
        def _(s):
            s.wait_ge(vs.h, vs.n)

        for t in range(4):
            emit_ln_tile(block, attn[:, t * D:(t + 1) * D], x1seg[:, t * D:(t + 1) * D],
                         stats, scrap, vs, ss, add_ap=xs[:, t * D:(t + 1) * D])

        @block.sync
        def _(sync, g=g, r=r):
            sync.wait_ge(vs.h, vs.n)
            inc(sync.dma_start(out=seg_src(x_out, g, r), in_=o3), st, 16)

    @block.sync
    def _(sync):
        sync.wait_ge(st.h, st.n)

    return st


MAGIC = 12582912.0


def emit_stage_b(nc, ctx, block, x_in, w_in, h_in, x_out, pfx="b_", debug=False, sems=None):
    E = ctx.enter_context
    ld, vs, ss, ts, gs, trs, st = sems or mk_sems(
        nc, ctx, ["ld", "vs", "ss", "ts", "gs", "trs", "st"], pfx)

    kind = "ExternalOutput" if debug else "Internal"
    fdram = nc.dram_tensor(pfx + "fdram", [T, D], BF16, kind=kind).ap()
    kdram = nc.dram_tensor(pfx + "kdram", [T, D], BF16, kind=kind).ap()
    wqdram = nc.dram_tensor(pfx + "wqdram", [D, D], BF16, kind=kind).ap()
    if debug:
        dbg_ffn = nc.dram_tensor(pfx + "dbg_ffn", [T, D], F32, kind="ExternalOutput").ap()
        dbg_scl = nc.dram_tensor(pfx + "dbg_scl", [128, NT + 8], F32, kind="ExternalOutput").ap()
        dbg_xabs = nc.dram_tensor(pfx + "dbg_xabs", [128, D], F32, kind="ExternalOutput").ap()
        dbg_xh = nc.dram_tensor(pfx + "dbg_xh", [128, D], F32, kind="ExternalOutput").ap()

    finT = E(nc.sbuf_tensor(pfx + "finT", [128, 8 * T], BF16))
    hsb = E(nc.sbuf_tensor(pfx + "hsb", [128, 8 * D], BF16))
    wsb = E(nc.sbuf_tensor(pfx + "wsb", [128, 8 * D], F32))
    wqv = E(nc.sbuf_tensor(pfx + "wqv", [128, 8 * D], BF16))
    wqvT = E(nc.sbuf_tensor(pfx + "wqvT", [128, 8 * D], BF16))
    kqT = finT
    scales = E(nc.sbuf_tensor(pfx + "scales", [128, NT + 8], F32))
    stats = E(nc.sbuf_tensor(pfx + "stats", [128, 16], F32))
    scrap = E(nc.sbuf_tensor(pfx + "scrap", [128, 3 * D], F32))
    wstat = E(nc.sbuf_tensor(pfx + "wstat", [128, 16], F32))
    wtmp = E(nc.sbuf_tensor(pfx + "wtmp", [128, 2 * D], F32))

    psA = E(nc.psum_tensor(pfx + "psA", [128, 2 * D], F32))
    psB = E(nc.psum_tensor(pfx + "psB", [128, 2 * D], F32))

    @block.sync
    def _(sync):
        inc(sync.dma_start(out=hsb[:].rearrange("p (c d) -> p c d", d=D),
                           in_=h_in.rearrange("(c p) d -> p c d", p=128)), ld, 16)
        inc(sync.dma_start(out=wsb[:].rearrange("p (c d) -> p c d", d=D),
                           in_=w_in.rearrange("(c p) d -> p c d", p=128)), ld, 16)

    @block.scalar
    def _(s):
        s.wait_ge(ld.h, ld.n)
        for c in range(8):
            inc(s.activation(out=wtmp[:, 0:D], in_=wsb[:, c * D:(c + 1) * D],
                             func=AF.Abs, accum_out=wstat[:, c:c + 1]), ss)

    @block.vector
    def _(v):
        v.wait_ge(ss.h, ss.n)
        for c in range(8):
            inc(v.tensor_scalar(out=wstat[:, c:c + 1], in0=wstat[:, c:c + 1],
                                scalar1=1.0 / D, op0=ALU.mult, scalar2=1e-5, op1=ALU.max), vs)
        v.wait_ge(vs.h, vs.n)
        for c in range(8):
            inc(v.reciprocal(wstat[:, 8 + c:9 + c], wstat[:, c:c + 1]), vs)
        v.wait_ge(vs.h, vs.n)
        for c in range(8):
            w_ = wsb[:, c * D:(c + 1) * D]
            v.tensor_scalar(out=wtmp[:, 0:D], in0=w_, scalar1=wstat[:, 8 + c:9 + c],
                            op0=ALU.mult, scalar2=0.5, op1=ALU.is_gt)
            v.tensor_scalar(out=wtmp[:, D:2 * D], in0=w_, scalar1=wstat[:, 8 + c:9 + c],
                            op0=ALU.mult, scalar2=-0.5, op1=ALU.is_lt)
            v.tensor_tensor(out=wtmp[:, 0:D], in0=wtmp[:, 0:D], in1=wtmp[:, D:2 * D],
                            op=ALU.subtract)
            inc(v.tensor_scalar(out=wqv[:, c * D:(c + 1) * D], in0=wtmp[:, 0:D],
                                scalar1=wstat[:, c:c + 1], op0=ALU.mult, scalar2=None), vs)

    @block.gpsimd
    def _(g_):
        g_.wait_ge(vs.h, vs.n)
        inc(g_.dma_start(out=wqdram.rearrange("(c p) d -> p c d", p=128),
                         in_=wqv[:].rearrange("p (c d) -> p c d", d=D)), gs, 16)

    @block.scalar
    def _(s):
        s.wait_ge(gs.h, gs.n)
        for c in range(8):
            inc(s.dma_start(out=wqvT[:, c * D:(c + 1) * D],
                            in_=wqdram[:, c * 128:(c + 1) * 128], transpose=True), trs, 16)

    xtp = E(nc.sbuf_tensor(pfx + "xtp", [128, 2 * D], F32))
    fbp = E(nc.sbuf_tensor(pfx + "fbp", [128, 2 * D], BF16))
    ld_at, sp_at, rstd_at, ap_at = {}, {}, {}, {}

    def b1_load(t):
        @block.sync
        def _(sync, t=t):
            if t - 2 in ap_at:
                sync.wait_ge(vs.h, ap_at[t - 2])
            inc(sync.dma_start(out=xtp[:, (t % 2) * D:(t % 2 + 1) * D],
                               in_=x_in[t * 128:(t + 1) * 128, :]), ld, 16)
        ld_at[t] = ld.n

    def b1_stats(t):
        @block.vector
        def _(v, t=t):
            v.wait_ge(ld.h, ld_at[t])
        rstd_at[t] = emit_ln_stats(block, xtp[:, (t % 2) * D:(t % 2 + 1) * D],
                                   stats[:, (t % 2) * 8:(t % 2) * 8 + 8], scrap, vs, ss)

    def b1_apply(t):
        if t - 2 in sp_at:
            @block.vector
            def _(v, t=t):
                v.wait_ge(gs.h, sp_at[t - 2])
        emit_ln_apply(block, xtp[:, (t % 2) * D:(t % 2 + 1) * D],
                      fbp[:, (t % 2) * D:(t % 2 + 1) * D],
                      stats[:, (t % 2) * 8:(t % 2) * 8 + 8], scrap, vs, ss, rstd_at[t])
        ap_at[t] = vs.n

        @block.gpsimd
        def _(g_, t=t):
            g_.wait_ge(vs.h, ap_at[t])
            inc(g_.dma_start(out=fdram[t * 128:(t + 1) * 128, :],
                             in_=fbp[:, (t % 2) * D:(t % 2 + 1) * D]), gs, 16)
        sp_at[t] = gs.n

    b1_load(0)
    b1_stats(0)
    b1_load(1)
    for t in range(NT):
        if t + 1 < NT:
            b1_stats(t + 1)
        b1_apply(t)
        if t + 2 < NT:
            b1_load(t + 2)

    @block.scalar
    def _(s):
        s.wait_ge(gs.h, gs.n)
        for c in range(8):
            inc(s.dma_start(out=finT[:, c * T:(c + 1) * T],
                            in_=fdram[:, c * 128:(c + 1) * 128], transpose=True), trs, 16)

    mm3_at, abs3_at, red3_at, q3_at, sp3_at = {}, {}, {}, {}, {}
    xabs2 = xtp
    kq2 = fbp
    for t in range(NT):
        pa = psA[:, (t % 2) * D:(t % 2) * D + D]

        @block.tensor
        def _(t_, t=t, pa=pa):
            t_.wait_ge(trs.h, trs.n)
            if t - 2 in q3_at:
                t_.wait_ge(ss.h, abs3_at[t - 2])
                t_.wait_ge(vs.h, q3_at[t - 2])
            mm = None
            for half in range(2):
                for c in range(8):
                    mm = t_.matmul(pa[:, half * 512:(half + 1) * 512],
                                   finT[:, c * T + t * 128: c * T + t * 128 + 128],
                                   hsb[:, c * D + half * 512: c * D + half * 512 + 512],
                                   start=(c == 0), stop=(c == 7))
            inc(mm, ts)
        mm3_at[t] = ts.n

        @block.scalar
        def _(s, t=t, pa=pa):
            s.wait_ge(ts.h, mm3_at[t])
            if t - 2 in red3_at:
                s.wait_ge(vs.h, red3_at[t - 2])
            inc(s.activation(out=xabs2[:, (t % 2) * D:(t % 2 + 1) * D], in_=pa,
                             func=AF.Abs, accum_out=None), ss)
        abs3_at[t] = ss.n

        @block.vector
        def _(v, t=t, pa=pa):
            st8 = stats[:, (t % 2) * 8:(t % 2) * 8 + 8]
            v.wait_ge(ss.h, abs3_at[t])
            inc(v.reduce_max(st8[:, 0:1], xabs2[:, (t % 2) * D:(t % 2 + 1) * D],
                             axis=AX.X), vs)
            red3_at[t] = vs.n
            v.wait_ge(vs.h, vs.n)
            inc(v.tensor_scalar(out=scales[:, t:t + 1], in0=st8[:, 0:1],
                                scalar1=1e-5, op0=ALU.max, scalar2=1.0 / 7.0, op1=ALU.mult), vs)
            v.wait_ge(vs.h, vs.n)
            inc(v.reciprocal(st8[:, 1:2], scales[:, t:t + 1]), vs)
            v.wait_ge(vs.h, vs.n)
            if t - 2 in sp3_at:
                v.wait_ge(gs.h, sp3_at[t - 2])
            v.tensor_scalar(out=scrap[:, 0:D], in0=pa, scalar1=st8[:, 1:2],
                            op0=ALU.mult, scalar2=MAGIC, op1=ALU.add)
            inc(v.tensor_scalar(out=kq2[:, (t % 2) * D:(t % 2 + 1) * D], in0=scrap[:, 0:D],
                                scalar1=MAGIC, op0=ALU.subtract, scalar2=None), vs)
        q3_at[t] = vs.n

        @block.gpsimd
        def _(g_, t=t):
            g_.wait_ge(vs.h, q3_at[t])
            inc(g_.dma_start(out=kdram[t * 128:(t + 1) * 128, :],
                             in_=kq2[:, (t % 2) * D:(t % 2 + 1) * D]), gs, 16)
        sp3_at[t] = gs.n


    @block.scalar
    def _(s):
        s.wait_ge(gs.h, gs.n)
        for c in range(8):
            inc(s.dma_start(out=kqT[:, c * T:(c + 1) * T],
                            in_=kdram[:, c * 128:(c + 1) * 128], transpose=True), trs, 16)

    mm5_at, ev5_at, rs5_at, ap5_at, st5_at, ld5_at = {}, {}, {}, {}, {}, {}
    ffp = xtp
    x1p = E(nc.sbuf_tensor(pfx + "x1p", [128, 2 * D], F32))
    xo5 = E(nc.sbuf_tensor(pfx + "xo5", [128, 2 * D], F32))

    def b5_load(t):
        @block.sync
        def _(sync, t=t):
            if t - 2 in ap5_at:
                sync.wait_ge(vs.h, ap5_at[t - 2])
            inc(sync.dma_start(out=x1p[:, (t % 2) * D:(t % 2 + 1) * D],
                               in_=x_in[t * 128:(t + 1) * 128, :]), ld, 16)
        ld5_at[t] = ld.n

    def b5_mm(t):
        pb = psB[:, (t % 2) * D:(t % 2) * D + D]

        @block.tensor
        def _(t_, t=t, pb=pb):
            t_.wait_ge(trs.h, trs.n)
            if t - 2 in ev5_at:
                t_.wait_ge(vs.h, ev5_at[t - 2])
            mm = None
            for half in range(2):
                for c in range(8):
                    mm = t_.matmul(pb[:, half * 512:(half + 1) * 512],
                                   kqT[:, c * T + t * 128: c * T + t * 128 + 128],
                                   wqvT[:, c * D + half * 512: c * D + half * 512 + 512],
                                   start=(c == 0), stop=(c == 7))
            inc(mm, ts)
        mm5_at[t] = ts.n

    def b5_evac_stats(t):
        pb = psB[:, (t % 2) * D:(t % 2) * D + D]

        @block.vector
        def _(v, t=t, pb=pb):
            v.wait_ge(ts.h, mm5_at[t])
            inc(v.tensor_scalar(out=ffp[:, (t % 2) * D:(t % 2 + 1) * D], in0=pb,
                                scalar1=scales[:, t:t + 1], op0=ALU.mult, scalar2=None), vs)
        ev5_at[t] = vs.n
        rs5_at[t] = emit_ln_stats(block, ffp[:, (t % 2) * D:(t % 2 + 1) * D],
                                  stats[:, (t % 2) * 8:(t % 2) * 8 + 8], scrap, vs, ss)

    def b5_apply(t):
        if t - 2 in st5_at:
            @block.vector
            def _(v, t=t):
                v.wait_ge(st.h, st5_at[t - 2])

        @block.vector
        def _(v, t=t):
            v.wait_ge(ld.h, ld5_at[t])
        emit_ln_apply(block, ffp[:, (t % 2) * D:(t % 2 + 1) * D],
                      xo5[:, (t % 2) * D:(t % 2 + 1) * D],
                      stats[:, (t % 2) * 8:(t % 2) * 8 + 8], scrap, vs, ss, rs5_at[t],
                      add_ap=x1p[:, (t % 2) * D:(t % 2 + 1) * D])
        ap5_at[t] = vs.n

        @block.sync
        def _(sync, t=t):
            sync.wait_ge(vs.h, ap5_at[t])
            inc(sync.dma_start(out=x_out[t * 128:(t + 1) * 128, :],
                               in_=xo5[:, (t % 2) * D:(t % 2 + 1) * D]), st, 16)
        st5_at[t] = st.n

    b5_load(0)
    b5_mm(0)
    b5_evac_stats(0)
    b5_load(1)
    b5_mm(1)
    for t in range(NT):
        if t + 1 < NT:
            b5_evac_stats(t + 1)
        b5_apply(t)
        if t + 2 < NT:
            b5_load(t + 2)
            b5_mm(t + 2)



    if debug:
        @block.gpsimd
        def _(g_):
            g_.wait_ge(vs.h, vs.n)
            inc(g_.dma_start(out=dbg_scl[:, :], in_=scales[:]), gs, 16)

    @block.sync
    def _(sync):
        sync.wait_ge(st.h, st.n)
        sync.wait_ge(gs.h, gs.n)

    return st


def hadamard_bf16():
    import ml_dtypes
    h = np.array([[1.0]], dtype=np.float64)
    while h.shape[0] < D:
        h = np.block([[h, h], [h, -h]])
    return (h / 32.0).astype(np.float32).astype(ml_dtypes.bfloat16)


def build_prog_b(debug=False):
    nc = bass.Bass("TRN2", target_bir_lowering=False)
    x_in = nc.dram_tensor("x_in", [T, D], F32, kind="ExternalInput").ap()
    w_in = nc.dram_tensor("w_in", [D, D], F32, kind="ExternalInput").ap()
    h_in = nc.dram_tensor("h_in", [D, D], BF16, kind="ExternalInput").ap()
    x_out = nc.dram_tensor("x_out", [T, D], F32, kind="ExternalOutput").ap()
    with ExitStack() as ctx:
        block = ctx.enter_context(nc.Block())
        emit_stage_b(nc, ctx, block, x_in, w_in, h_in, x_out, debug=debug)
    return nc


def qhead_of(tau, half):
    return (tau + 4 * half) if tau < 4 else (tau + 4 + 4 * half)


def emit_stage_d(nc, ctx, block, x_in, wts, x_out, pfx="d_", debug=False, glevel=5,
                 sems=None):
    E = ctx.enter_context
    ld, vs, ss, ts, gs, trs, st = sems or mk_sems(
        nc, ctx, ["ld", "vs", "ss", "ts", "gs", "trs", "st"], pfx)
    HD = 64
    kind = "ExternalOutput" if debug else "Internal"

    din_dram = nc.dram_tensor(pfx + "din", [T, D], F32, kind=kind).ap()
    rms1_dram = nc.dram_tensor(pfx + "rms1", [T, D], BF16, kind=kind).ap()
    wqbf = nc.dram_tensor(pfx + "wqbf", [D, D], BF16, kind="Internal").ap()
    wkbf = nc.dram_tensor(pfx + "wkbf", [256, D], BF16, kind="Internal").ap()
    wvbf = nc.dram_tensor(pfx + "wvbf", [256, D], BF16, kind="Internal").ap()
    wobf = nc.dram_tensor(pfx + "wobf", [D, D], BF16, kind="Internal").ap()
    expbf = nc.dram_tensor(pfx + "expbf", [8 * D, D], BF16, kind="Internal").ap()
    gqa_dram = nc.dram_tensor(pfx + "gqa", [D, T], BF16, kind=kind).ap()
    h_dram = nc.dram_tensor(pfx + "h", [T, D], F32, kind=kind).ap()
    rms2_dram = nc.dram_tensor(pfx + "rms2", [T, D], BF16, kind=kind).ap()

    stats = E(nc.sbuf_tensor(pfx + "stats", [128, 16], F32))
    scrap = E(nc.sbuf_tensor(pfx + "scrap", [128, D], F32))
    xdp = E(nc.sbuf_tensor(pfx + "xdp", [128, 2 * D], F32))
    dinp = E(nc.sbuf_tensor(pfx + "dinp", [128, 2 * D], F32))
    rmsp = E(nc.sbuf_tensor(pfx + "rmsp", [128, 2 * D], BF16))
    anw_sb = E(nc.sbuf_tensor(pfx + "anw", [128, 8], F32))
    fnw_sb = E(nc.sbuf_tensor(pfx + "fnw", [128, 8], F32))
    ones = E(nc.sbuf_tensor(pfx + "ones", [128, 128], F32))
    masks = E(nc.sbuf_tensor(pfx + "masks", [128, 4 * 512], BF16))
    aph = ExitStack()
    qT = aph.enter_context(nc.sbuf_tensor(pfx + "qT", [128, 8 * T], BF16))
    kT = aph.enter_context(nc.sbuf_tensor(pfx + "kT", [128, 2 * T], BF16))
    v_tm = aph.enter_context(nc.sbuf_tensor(pfx + "vtm", [128, NT * 260], BF16))

    nrm_row = aph.enter_context(nc.sbuf_tensor(pfx + "nrmrow", [1, 2 * D], F32))
    psW = aph.enter_context(nc.psum_tensor(pfx + "psW", [128, 16], F32))

    @block.sync
    def _(sync):
        inc(sync.dma_start(out=nrm_row[0:1, 0:D], in_=wts["anw"][None, :]), ld, 16)
        inc(sync.dma_start(out=nrm_row[0:1, D:2 * D], in_=wts["fnw"][None, :]), ld, 16)
        inc(sync.dma_start(out=masks[:].rearrange("p (j n) -> p j n", n=512),
                           in_=wts["masks"].rearrange("(j p) n -> p j n", p=128)), ld, 16)

    @block.tensor
    def _(t_):
        t_.wait_ge(ld.h, ld.n)
        t_.wait_ge(gs.h, 1)
        for c in range(8):
            t_.transpose(psW[:, c:c + 1], nrm_row[0:1, c * 128:(c + 1) * 128], ones[0:1, 0:1])
        for c in range(8):
            mm = t_.transpose(psW[:, 8 + c:9 + c], nrm_row[0:1, D + c * 128:D + (c + 1) * 128],
                              ones[0:1, 0:1])
        inc(mm, ts)

    @block.vector
    def _(v):
        v.wait_ge(ts.h, ts.n)
        inc(v.tensor_copy(anw_sb[:], psW[:, 0:8]), vs)
        inc(v.tensor_copy(fnw_sb[:], psW[:, 8:16]), vs)

    @block.gpsimd
    def _(g_):
        inc(g_.memset(ones[:], 1.0), gs)
        inc(g_.dma_start(out=wqbf[:, :], in_=wts["wq"][:, :]), gs, 16)
        inc(g_.dma_start(out=wkbf[:, :], in_=wts["wk"][:, :]), gs, 16)
        inc(g_.dma_start(out=wvbf[:, :], in_=wts["wv"][:, :]), gs, 16)
        inc(g_.dma_start(out=wobf[:, :], in_=wts["wo"][:, :]), gs, 16)
        inc(g_.dma_start(out=expbf.rearrange("(e r) d -> e r d", e=8),
                         in_=wts["expert_w"][:, :, :]), gs, 16)

    d1_ld, d1_rs, d1_ap, d1_sp = {}, {}, {}, {}

    def d1_load(t):
        @block.sync
        def _(sync, t=t):
            if t - 2 in d1_ap:
                sync.wait_ge(vs.h, d1_ap[t - 2])
            inc(sync.dma_start(out=xdp[:, (t % 2) * D:(t % 2 + 1) * D],
                               in_=x_in[t * 128:(t + 1) * 128, :]), ld, 16)
        d1_ld[t] = ld.n

    def d1_stats(t):
        @block.vector
        def _(v, t=t):
            v.wait_ge(ld.h, d1_ld[t])
        st8 = stats[:, (t % 2) * 8:(t % 2) * 8 + 8]
        emit_ln_stats(block, xdp[:, (t % 2) * D:(t % 2 + 1) * D], st8, scrap, vs, ss)

        @block.vector
        def _(v, t=t):
            v.wait_ge(ss.h, ss.n)
            inc(v.tensor_tensor(out=st8[:, 6:7], in0=st8[:, 3:4], in1=st8[:, 3:4],
                                op=ALU.mult), vs)
            v.wait_ge(vs.h, vs.n)
            inc(v.tensor_scalar(out=st8[:, 6:7], in0=st8[:, 6:7], scalar1=-LN_EPS,
                                op0=ALU.mult, scalar2=1.0 + 1e-6, op1=ALU.add), vs)

        @block.scalar
        def _(s, t=t):
            s.wait_ge(vs.h, vs.n)
            inc(s.activation(out=st8[:, 7:8], in_=st8[:, 6:7], func=AF.Ln), ss)
            s.wait_ge(ss.h, ss.n)
            inc(s.activation(out=st8[:, 6:7], in_=st8[:, 7:8], func=AF.Exp, scale=-0.5), ss)
        d1_rs[t] = ss.n

    def d1_apply(t):
        st8 = stats[:, (t % 2) * 8:(t % 2) * 8 + 8]
        if t - 2 in d1_sp:
            @block.vector
            def _(v, t=t):
                v.wait_ge(gs.h, d1_sp[t - 2])

        @block.vector
        def _(v, t=t):
            v.wait_ge(ss.h, d1_rs[t])
            v.tensor_scalar(out=dinp[:, (t % 2) * D:(t % 2 + 1) * D],
                            in0=xdp[:, (t % 2) * D:(t % 2 + 1) * D],
                            scalar1=st8[:, 2:3], op0=ALU.subtract,
                            scalar2=st8[:, 3:4], op1=ALU.mult)
            inc(v.tensor_scalar(out=rmsp[:, (t % 2) * D:(t % 2 + 1) * D],
                                in0=dinp[:, (t % 2) * D:(t % 2 + 1) * D],
                                scalar1=st8[:, 6:7], op0=ALU.mult, scalar2=None), vs)
        d1_ap[t] = vs.n

        @block.gpsimd
        def _(g_, t=t):
            g_.wait_ge(vs.h, d1_ap[t])
            inc(g_.dma_start(out=din_dram[t * 128:(t + 1) * 128, :],
                             in_=dinp[:, (t % 2) * D:(t % 2 + 1) * D]), gs, 16)
            inc(g_.dma_start(out=rms1_dram[t * 128:(t + 1) * 128, :],
                             in_=rmsp[:, (t % 2) * D:(t % 2 + 1) * D]), gs, 16)
        d1_sp[t] = gs.n

    d1_load(0)
    d1_stats(0)
    d1_load(1)
    for t in range(NT):
        if t + 1 < NT:
            d1_stats(t + 1)
        d1_apply(t)
        if t + 2 < NT:
            d1_load(t + 2)


    TH = T // 2
    with ExitStack() as ph:
        rms1T = ph.enter_context(nc.sbuf_tensor(pfx + "rms1T", [128, 8 * TH], BF16))
        wqT = ph.enter_context(nc.sbuf_tensor(pfx + "wqT", [128, 8 * D], BF16))
        wkvT = ph.enter_context(nc.sbuf_tensor(pfx + "wkvT", [128, 8 * 512], BF16))
        psq = ph.enter_context(nc.psum_tensor(pfx + "psq", [128, 4 * 512], F32))
        psv = ph.enter_context(nc.psum_tensor(pfx + "psv", [128, 2 * 256], F32))

        @block.scalar
        def _(s):
            s.wait_ge(gs.h, gs.n)
            for c in range(8):
                inc(s.dma_start(out=wqT[:, c * D:(c + 1) * D],
                                in_=wqbf[:, c * 128:(c + 1) * 128], transpose=True), trs, 16)
                inc(s.dma_start(out=wkvT[:, c * 512:c * 512 + 256],
                                in_=wkbf[:, c * 128:(c + 1) * 128], transpose=True), trs, 16)
                inc(s.dma_start(out=wkvT[:, c * 512 + 256:(c + 1) * 512],
                                in_=wvbf[:, c * 128:(c + 1) * 128], transpose=True), trs, 16)

        @block.vector
        def _(v):
            v.wait_ge(trs.h, trs.n)
            for c in range(8):
                inc(v.tensor_scalar(out=wqT[:, c * D:(c + 1) * D], in0=wqT[:, c * D:(c + 1) * D],
                                    scalar1=anw_sb[:, c:c + 1], op0=ALU.mult, scalar2=None), vs)
                inc(v.tensor_scalar(out=wkvT[:, c * 512:(c + 1) * 512],
                                    in0=wkvT[:, c * 512:(c + 1) * 512],
                                    scalar1=anw_sb[:, c:c + 1], op0=ALU.mult, scalar2=None), vs)

        nbank = 0
        for hf in range(2):
            tok0 = hf * TH

            @block.scalar
            def _(s, hf=hf, tok0=tok0):
                if hf > 0:
                    s.wait_ge(ts.h, ts.n)
                for c in range(8):
                    inc(s.dma_start(out=rms1T[:, c * TH:(c + 1) * TH],
                                    in_=rms1_dram[tok0:tok0 + TH, c * 128:(c + 1) * 128],
                                    transpose=True), trs, 16)

            for tau in range(8):
                hA, hB = qhead_of(tau, 0), qhead_of(tau, 1)
                for n in range(4):
                    pq = psq[:, (nbank % 4) * 512:(nbank % 4) * 512 + 512]
                    nbank += 1

                    @block.tensor
                    def _(t_, tau=tau, hA=hA, hB=hB, n=n, pq=pq):
                        t_.wait_ge(vs.h, vs.n)
                        t_.wait_ge(trs.h, trs.n)
                        mm = None
                        for c in range(8):
                            rhs = rms1T[:, c * TH + n * 512: c * TH + (n + 1) * 512]
                            t_.matmul(pq[0:64, :], wqT[:, c * D + 64 * hA: c * D + 64 * hA + 64],
                                      rhs, start=(c == 0), stop=(c == 7))
                            mm = t_.matmul(pq[64:128, :], wqT[:, c * D + 64 * hB: c * D + 64 * hB + 64],
                                           rhs, start=(c == 0), stop=(c == 7))
                        inc(mm, ts)

                    @block.vector
                    def _(v, tau=tau, n=n, pq=pq, tok0=tok0):
                        v.wait_ge(ts.h, ts.n)
                        inc(v.tensor_copy(
                            qT[:, tau * T + tok0 + n * 512: tau * T + tok0 + (n + 1) * 512], pq), vs)

            for tau in range(2):
                for n in range(4):
                    pq = psq[:, (nbank % 4) * 512:(nbank % 4) * 512 + 512]
                    nbank += 1

                    @block.tensor
                    def _(t_, tau=tau, n=n, pq=pq):
                        t_.wait_ge(vs.h, vs.n)
                        mm = None
                        for c in range(8):
                            mm = t_.matmul(pq, wkvT[:, c * 512 + tau * 128: c * 512 + tau * 128 + 128],
                                           rms1T[:, c * TH + n * 512: c * TH + (n + 1) * 512],
                                           start=(c == 0), stop=(c == 7))
                        inc(mm, ts)

                    @block.vector
                    def _(v, tau=tau, n=n, pq=pq, tok0=tok0):
                        v.wait_ge(ts.h, ts.n)
                        inc(v.tensor_copy(
                            kT[:, tau * T + tok0 + n * 512: tau * T + tok0 + (n + 1) * 512], pq), vs)

            for tt in range(16):
                pv = psv[:, (tt % 2) * 256:(tt % 2) * 256 + 256]
                gt = hf * 16 + tt

                @block.tensor
                def _(t_, tt=tt, pv=pv):
                    t_.wait_ge(vs.h, vs.n)
                    mm = None
                    for c in range(8):
                        mm = t_.matmul(pv, rms1T[:, c * TH + tt * 128: c * TH + tt * 128 + 128],
                                       wkvT[:, c * 512 + 256:(c + 1) * 512],
                                       start=(c == 0), stop=(c == 7))
                    inc(mm, ts)

                @block.vector
                def _(v, gt=gt, pv=pv):
                    v.wait_ge(ts.h, ts.n)
                    dst = v_tm[:, gt * 260: gt * 260 + 260].rearrange("p (g e) -> p g e", e=65)[:, :, 0:64]
                    inc(v.tensor_copy(dst, pv[:].rearrange("p (g e) -> p g e", e=64)), vs)

        @block.gpsimd
        def _(g_):
            g_.wait_ge(vs.h, vs.n)
            ocol = v_tm[:].rearrange("p (t g e) -> p t g e", g=4, e=65)[:, :, :, 64:65]
            inc(g_.memset(ocol, 1.0), gs)

        if debug:
            dbg_qT = nc.dram_tensor(pfx + "dbg_qT", [128, 8 * T], BF16, kind="ExternalOutput").ap()
            dbg_kT = nc.dram_tensor(pfx + "dbg_kT", [128, 2 * T], BF16, kind="ExternalOutput").ap()
            dbg_v = nc.dram_tensor(pfx + "dbg_v", [128, NT * 260], BF16, kind="ExternalOutput").ap()

            @block.gpsimd
            def _(g_):
                g_.wait_ge(gs.h, gs.n)
                inc(g_.dma_start(out=dbg_qT[:, :], in_=qT[:]), gs, 16)
                inc(g_.dma_start(out=dbg_kT[:, :], in_=kT[:]), gs, 16)
                inc(g_.dma_start(out=dbg_v[:, :], in_=v_tm[:]), gs, 16)

        @block.tensor
        def _(t_):
            t_.wait_ge(gs.h, gs.n)

    if debug == "d3":
        @block.sync
        def _(sync):
            sync.wait_ge(gs.h, gs.n)
            sync.wait_ge(vs.h, vs.n)
        aph.close()
        return st

    if True:
        ph = ph_attn = ExitStack()
        attnT = ph.enter_context(nc.sbuf_tensor(pfx + "attnT", [128, 8 * T], BF16))
        ph_flash = ExitStack()
        pTbuf = ph_flash.enter_context(nc.sbuf_tensor(pfx + "pTbuf", [128, 4 * 512], BF16))
        rrow = ph_flash.enter_context(nc.sbuf_tensor(pfx + "rrow", [1, 512], F32))
        rbc = ph_flash.enter_context(nc.sbuf_tensor(pfx + "rbc", [128, 512], F32))
        psS = ph_flash.enter_context(nc.psum_tensor(pfx + "psS", [128, 4 * 512], F32))
        psO = ph_flash.enter_context(nc.psum_tensor(pfx + "psO", [128, 2 * 512], F32))
        psR = ph_flash.enter_context(nc.psum_tensor(pfx + "psR", [128, 512], F32))
        for tau in range(8):
            ktile = 0 if tau < 4 else 1
            for qb in range(8):
                nkb = 4 * (qb + 1)
                n = nkb
                tsb, ssb, vsb = ts.n, ss.n, vs.n

                def idxS(x, k):
                    return (2 * k if k < 2 else 4 * k - 2) + x

                def idxP(x, k):
                    if k <= n - 3:
                        return 4 + 4 * k + x
                    return (4 * n - 4 + x) if k == n - 2 else (4 * n - 2 + x)

                def mcnt(x, k):
                    kk = min(k, n - 1) - (n - 4)
                    return max(0, 2 * kk + 1 + x)

                def sbank(x, k):
                    return (k % 2) * 2 + x

                def emit_S(t_, x, k):
                    if k >= 2:
                        t_.wait_ge(ss.h, ssb + (k - 2) + 1)
                    base = 64 * x
                    mm = t_.matmul(psS[:, sbank(x, k) * 512:(sbank(x, k) + 1) * 512],
                                   kT[base:base + 64,
                                      ktile * T + k * 128: ktile * T + k * 128 + 128],
                                   qT[base:base + 64,
                                      tau * T + qb * 512: tau * T + (qb + 1) * 512],
                                   start=True, stop=True)
                    inc(mm, ts)

                def emit_P(t_, x, k):
                    t_.wait_ge(ss.h, ssb + k + 1)
                    if k >= n - 4:
                        t_.wait_ge(vs.h, vsb + mcnt(x, k))
                    g = qhead_of(tau, x) // 4
                    mm = t_.matmul(psO[x * 512:x * 512 + 65, 0:512]
                                   if False else psO[0:65, x * 512:(x + 1) * 512],
                                   v_tm[:, k * 260 + g * 65: k * 260 + g * 65 + 65],
                                   pTbuf[:, sbank(x, k) * 512:(sbank(x, k) + 1) * 512],
                                   start=(k == 0), stop=(k == n - 1))
                    inc(mm, ts)

                @block.tensor
                def _(t_):
                    t_.wait_ge(vs.h, vsb)
                    for k in (0, 1):
                        emit_S(t_, 0, k)
                        emit_S(t_, 1, k)
                    for k in range(n - 2):
                        emit_P(t_, 0, k)
                        emit_P(t_, 1, k)
                        emit_S(t_, 0, k + 2)
                        emit_S(t_, 1, k + 2)
                    for k in (n - 2, n - 1):
                        emit_P(t_, 0, k)
                        emit_P(t_, 1, k)

                @block.scalar
                def _(s):
                    for k in range(n):
                        s.wait_ge(ts.h, tsb + idxS(1, k) + 1)
                        if k >= 2:
                            s.wait_ge(ts.h, tsb + idxP(1, k - 2) + 1)
                        w0 = (k % 2) * 1024
                        inc(s.activation(out=pTbuf[:, w0:w0 + 1024],
                                         in_=psS[:, w0:w0 + 1024],
                                         func=AF.Exp, scale=0.125), ss)

                @block.vector
                def _(v):
                    for k in range(max(0, n - 4), n):
                        for x in (0, 1):
                            v.wait_ge(ss.h, ssb + k + 1)
                            inc(v.tensor_tensor(
                                out=pTbuf[:, sbank(x, k) * 512:(sbank(x, k) + 1) * 512],
                                in0=pTbuf[:, sbank(x, k) * 512:(sbank(x, k) + 1) * 512],
                                in1=masks[:, (k - 4 * qb) * 512:(k - 4 * qb + 1) * 512],
                                op=ALU.mult), vs)

                assert ts.n == tsb + 4 * n and ss.n == ssb + n and vs.n == vsb + 8

                for half in range(2):
                    hh = qhead_of(tau, half)
                    pO = psO[:, half * 512:(half + 1) * 512]

                    @block.vector
                    def _(v, pO=pO):
                        v.wait_ge(ts.h, ts.n)
                        inc(v.reciprocal(rrow[0:1, :], pO[64:65, :]), vs)

                    @block.tensor
                    def _(t_):
                        t_.wait_ge(vs.h, vs.n)
                        inc(t_.matmul(psR[0:64, :], ones[0:1, 0:64], rrow[0:1, :],
                                      start=True, stop=True), ts)

                    @block.vector
                    def _(v, pO=pO, hh=hh, qb=qb):
                        v.wait_ge(ts.h, ts.n)
                        inc(v.tensor_copy(rbc[0:64, :], psR[0:64, :]), vs)
                        v.wait_ge(vs.h, vs.n)
                        dst = attnT[64 * (hh % 2):64 * (hh % 2) + 64,
                                    (hh // 2) * T + qb * 512:(hh // 2) * T + (qb + 1) * 512]
                        inc(v.tensor_tensor(out=dst, in0=pO[0:64, :], in1=rbc[0:64, :],
                                            op=ALU.mult), vs)

        ph_flash.close()
        woT = ph.enter_context(nc.sbuf_tensor(pfx + "woT", [128, 4 * D], BF16))
        gout = ph.enter_context(nc.sbuf_tensor(pfx + "gout", [128, 2 * 512], BF16))
        psG = ph.enter_context(nc.psum_tensor(pfx + "psG", [128, 2 * 512], F32))

        d5_mm, d5_ev, d5_sp = {}, {}, {}
        for mhalf in range(2):
            @block.scalar
            def _(s, mhalf=mhalf):
                if mhalf == 0:
                    s.wait_ge(vs.h, vs.n)
                else:
                    s.wait_ge(ts.h, ts.n)
                for c in range(8):
                    inc(s.dma_start(out=woT[:, c * 512:(c + 1) * 512],
                                    in_=wobf[mhalf * 512:(mhalf + 1) * 512,
                                             c * 128:(c + 1) * 128], transpose=True), trs, 16)

            for mo in range(mhalf * 4, mhalf * 4 + 4):
                for n in range(8):
                    i = mo * 8 + n
                    par = i % 2
                    pg = psG[:, par * 512:(par + 1) * 512]

                    @block.tensor
                    def _(t_, mo=mo, n=n, pg=pg, i=i,
                          first=(mo == mhalf * 4 and n == 0)):
                        if first:
                            t_.wait_ge(trs.h, trs.n)
                        if i - 2 in d5_ev:
                            t_.wait_ge(vs.h, d5_ev[i - 2])
                        else:
                            t_.wait_ge(vs.h, vs.n)
                        mm = None
                        for c in range(8):
                            mm = t_.matmul(pg, woT[:, c * 512 + (mo % 4) * 128: c * 512 + (mo % 4) * 128 + 128],
                                           attnT[:, c * T + n * 512:c * T + (n + 1) * 512],
                                           start=(c == 0), stop=(c == 7))
                        inc(mm, ts)
                    d5_mm[i] = ts.n

                    @block.vector
                    def _(v, pg=pg, par=par, i=i):
                        v.wait_ge(ts.h, d5_mm[i])
                        if i - 2 in d5_sp:
                            v.wait_ge(gs.h, d5_sp[i - 2])
                        inc(v.tensor_copy(gout[:, par * 512:(par + 1) * 512], pg), vs)
                    d5_ev[i] = vs.n

                    @block.gpsimd
                    def _(g_, mo=mo, n=n, par=par, i=i):
                        g_.wait_ge(vs.h, d5_ev[i])
                        inc(g_.dma_start(out=gqa_dram[mo * 128:(mo + 1) * 128, n * 512:(n + 1) * 512],
                                         in_=gout[:, par * 512:(par + 1) * 512]), gs, 16)
                    d5_sp[i] = gs.n

    ph_attn.close()
    aph.close()

    if debug == "d5":
        @block.sync
        def _(sync):
            sync.wait_ge(gs.h, gs.n)
            sync.wait_ge(vs.h, vs.n)
        return st

    ph6 = ExitStack()
    comb = ph6.enter_context(nc.sbuf_tensor(pfx + "comb", [128, NT * 8], F32))
    glog = ph6.enter_context(nc.sbuf_tensor(pfx + "glog", [128, 64], F32))
    idT = ph6.enter_context(nc.sbuf_tensor(pfx + "idT", [128, 2 * 128], F32))
    gateT = ph6.enter_context(nc.sbuf_tensor(pfx + "gateT", [128, 8 * 8], F32))
    gsb = ph6.enter_context(nc.sbuf_tensor(pfx + "gsb", [128, D], F32))
    hT_st = ph6.enter_context(nc.sbuf_tensor(pfx + "hTst", [128, 2 * D], F32))
    ph6p = ExitStack()
    psHT = ph6p.enter_context(nc.psum_tensor(pfx + "psHT", [128, 2 * 1024], F32))
    psg = ph6p.enter_context(nc.psum_tensor(pfx + "psg", [128, 1024], F32))

    if debug:
        dbg_lg_d = nc.dram_tensor(pfx + "dbg_lg", [128, NT * 8], F32, kind="ExternalOutput").ap()
        dbg_lg = ph6.enter_context(nc.sbuf_tensor(pfx + "dbglg", [128, NT * 8], F32))

    @block.sync
    def _(sync):
        sync.wait_ge(vs.h, vs.n)
        sync.wait_ge(ts.h, ts.n)

    @block.gpsimd
    def _(g_):
        g_.wait_ge(vs.h, vs.n)
        g_.wait_ge(ts.h, ts.n)

    if glevel < 1:
        @block.gpsimd
        def _(g_):
            g_.memset(comb[:], 0.25)

    if glevel >= 1:
        @block.sync
        def _(sync):
            inc(sync.dma_start(out=gsb[0:8, :], in_=wts["gate_w"][:, :]), ld, 16)

    if glevel >= 1:
        @block.gpsimd
        def _(g_):
            inc(g_.iota(out=idT[:, 0:128], base=0, pattern=[[1, 128]], channel_multiplier=0,
                        allow_small_or_imprecise_dtypes=True), gs)
            inc(g_.iota(out=idT[:, 128:256], base=0, pattern=[[0, 128]], channel_multiplier=1,
                        allow_small_or_imprecise_dtypes=True), gs)

        @block.vector
        def _(v):
            v.wait_ge(gs.h, gs.n)
            inc(v.tensor_tensor(out=idT[:, 0:128], in0=idT[:, 0:128], in1=idT[:, 128:256],
                                op=ALU.is_equal), vs)

        for c in range(8):
            @block.tensor
            def _(t_, c=c):
                t_.wait_ge(vs.h, vs.n)
                t_.wait_ge(ld.h, ld.n)
                inc(t_.transpose(psg[:, 8:16], gsb[0:8, c * 128:(c + 1) * 128],
                                 idT[0:8, 0:8]), ts)

            @block.vector
            def _(v, c=c):
                v.wait_ge(ts.h, ts.n)
                inc(v.tensor_scalar(out=gateT[:, c * 8:(c + 1) * 8], in0=psg[:, 8:16],
                                    scalar1=fnw_sb[:, c:c + 1], op0=ALU.mult, scalar2=None), vs)

        if debug:
            dbg_gT0 = nc.dram_tensor(pfx + "dbg_gT0", [128, 64], F32, kind="ExternalOutput").ap()
            dbg_gsb = nc.dram_tensor(pfx + "dbg_gsb", [8, D], F32, kind="ExternalOutput").ap()
            dbg_idT = nc.dram_tensor(pfx + "dbg_idT", [128, 128], F32, kind="ExternalOutput").ap()
            dbg_fnw = nc.dram_tensor(pfx + "dbg_fnw", [128, 8], F32, kind="ExternalOutput").ap()

            @block.gpsimd
            def _(g_):
                g_.wait_ge(vs.h, vs.n)
                inc(g_.dma_start(out=dbg_gT0[:, :], in_=gateT[:]), gs, 16)
                inc(g_.dma_start(out=dbg_gsb[:, :], in_=gsb[0:8, :]), gs, 16)
                inc(g_.dma_start(out=dbg_idT[:, :], in_=idT[:, 0:128]), gs, 16)
                inc(g_.dma_start(out=dbg_fnw[:, :], in_=fnw_sb[:]), gs, 16)

    d6_ld, d6_add, d6_rs, d6_rms, d6_sp = {}, {}, {}, {}, {}
    d6_tr, d6_hT, d6_mm, d6_lg, d6_d, d6_sg = {}, {}, {}, {}, {}, {}

    def d6_load(t):
        p = t % 2

        @block.sync
        def _(sync, t=t, p=p):
            if t - 2 in d6_add:
                sync.wait_ge(vs.h, d6_add[t - 2])
                sync.wait_ge(gs.h, d6_sp[t - 2])
            else:
                sync.wait_ge(gs.h, gs.n)
            inc(sync.dma_start(out=xdp[:, p * D:(p + 1) * D],
                               in_=din_dram[t * 128:(t + 1) * 128, :]), ld, 16)
            inc(sync.dma_start(out=rmsp[:, p * D:(p + 1) * D],
                               in_=gqa_dram[:, t * 128:(t + 1) * 128],
                               transpose=True), ld, 16)
        d6_ld[t] = ld.n

    def d6_stats(t):
        p = t % 2
        b = p * 8

        @block.vector
        def _(v, t=t, p=p):
            v.wait_ge(ld.h, d6_ld[t])
            inc(v.tensor_tensor(out=dinp[:, p * D:(p + 1) * D],
                                in0=xdp[:, p * D:(p + 1) * D],
                                in1=rmsp[:, p * D:(p + 1) * D], op=ALU.add), vs)
        d6_add[t] = vs.n

        @block.vector
        def _(v, p=p, b=b):
            inc(v.scalar_tensor_tensor(out=scrap[:, 0:D],
                                       in0=dinp[:, p * D:(p + 1) * D], scalar=1.0,
                                       op0=ALU.mult, op1=ALU.mult,
                                       in1=dinp[:, p * D:(p + 1) * D],
                                       accum_out=stats[:, b:b + 1]), vs)
            v.wait_ge(vs.h, vs.n)
            inc(v.tensor_scalar(out=stats[:, b + 1:b + 2], in0=stats[:, b:b + 1],
                                scalar1=1.0 / D, op0=ALU.mult, scalar2=1e-6,
                                op1=ALU.add), vs)

        @block.scalar
        def _(s, b=b):
            s.wait_ge(vs.h, vs.n)
            inc(s.activation(out=stats[:, b + 3:b + 4], in_=stats[:, b + 1:b + 2],
                             func=AF.Ln), ss)
            s.wait_ge(ss.h, ss.n)
            inc(s.activation(out=stats[:, b + 2:b + 3], in_=stats[:, b + 3:b + 4],
                             func=AF.Exp, scale=-0.5), ss)
        d6_rs[t] = ss.n

    def d6_rms_spill(t):
        p = t % 2
        b = p * 8

        @block.vector
        def _(v, t=t, p=p, b=b):
            v.wait_ge(ss.h, d6_rs[t])
            inc(v.tensor_scalar(out=rmsp[:, p * D:(p + 1) * D],
                                in0=dinp[:, p * D:(p + 1) * D],
                                scalar1=stats[:, b + 2:b + 3], op0=ALU.mult,
                                scalar2=None), vs)
        d6_rms[t] = vs.n

        @block.gpsimd
        def _(g_, t=t, p=p):
            g_.wait_ge(vs.h, d6_rms[t])
            inc(g_.dma_start(out=h_dram[t * 128:(t + 1) * 128, :],
                             in_=dinp[:, p * D:(p + 1) * D]), gs, 16)
            inc(g_.dma_start(out=rms2_dram[t * 128:(t + 1) * 128, :],
                             in_=rmsp[:, p * D:(p + 1) * D]), gs, 16)
        d6_sp[t] = gs.n

    def d6_gate_pe(t):
        p = t % 2

        @block.tensor
        def _(t_, t=t, p=p):
            t_.wait_ge(vs.h, d6_add[t])
            if t - 2 in d6_hT:
                t_.wait_ge(vs.h, d6_hT[t - 2])
            mm = None
            for c in range(8):
                mm = t_.transpose(psHT[:, p * 1024 + c * 128:p * 1024 + (c + 1) * 128],
                                  dinp[:, p * D + c * 128:p * D + (c + 1) * 128],
                                  idT[:, 0:128])
            inc(mm, ts)
        d6_tr[t] = ts.n

        @block.vector
        def _(v, t=t, p=p):
            v.wait_ge(ts.h, d6_tr[t])
            inc(v.tensor_copy(hT_st[:, p * D:(p + 1) * D],
                              psHT[:, p * 1024:(p + 1) * 1024]), vs)
        d6_hT[t] = vs.n

        @block.tensor
        def _(t_, t=t, p=p):
            t_.wait_ge(vs.h, d6_hT[t])
            if t - 2 in d6_lg:
                t_.wait_ge(vs.h, d6_lg[t - 2])
            mm = None
            for c in range(8):
                mm = t_.matmul(psg[:, p * 512:p * 512 + 8],
                               hT_st[:, p * D + c * 128:p * D + (c + 1) * 128],
                               gateT[:, c * 8:(c + 1) * 8], start=(c == 0), stop=(c == 7))
            inc(mm, ts)
        d6_mm[t] = ts.n

    def d6_gate_dve(t):
        p = t % 2
        b = p * 8
        gl = glog[:, p * 32:(p + 1) * 32]

        @block.vector
        def _(v, t=t, p=p, b=b, gl=gl):
            v.wait_ge(ts.h, d6_mm[t])
            inc(v.tensor_scalar(out=gl[:, 0:8], in0=psg[:, p * 512:p * 512 + 8],
                                scalar1=stats[:, b + 2:b + 3], op0=ALU.mult,
                                scalar2=None), vs)
        d6_lg[t] = vs.n

        @block.vector
        def _(v, b=b, gl=gl):
            v.wait_ge(vs.h, vs.n)
            inc(v.reduce_max(gl[:, 8:9], gl[:, 0:8], axis=AX.X), vs)
            v.wait_ge(vs.h, vs.n)
            inc(v.tensor_scalar(out=gl[:, 16:24], in0=gl[:, 0:8], scalar1=gl[:, 8:9],
                                op0=ALU.is_equal, scalar2=None), vs)
            v.wait_ge(vs.h, vs.n)
            inc(v.scalar_tensor_tensor(out=gl[:, 24:32], in0=gl[:, 16:24], scalar=-1e30,
                                       op0=ALU.mult, op1=ALU.add, in1=gl[:, 0:8]), vs)
            v.wait_ge(vs.h, vs.n)
            inc(v.reduce_max(gl[:, 9:10], gl[:, 24:32], axis=AX.X), vs)
            v.wait_ge(vs.h, vs.n)
            inc(v.tensor_tensor(out=stats[:, b + 4:b + 5], in0=gl[:, 8:9],
                                in1=gl[:, 9:10], op=ALU.subtract), vs)
        d6_d[t] = vs.n

        @block.scalar
        def _(s, t=t, b=b):
            s.wait_ge(vs.h, d6_d[t])
            inc(s.activation(out=stats[:, b + 5:b + 6], in_=stats[:, b + 4:b + 5],
                             func=AF.Exp, scale=-1.0), ss)
        d6_sg[t] = ss.n

        @block.vector
        def _(v, t=t, b=b, gl=gl):
            v.wait_ge(ss.h, d6_sg[t])
            inc(v.tensor_scalar(out=stats[:, b + 7:b + 8], in0=stats[:, b + 5:b + 6],
                                scalar1=1.0, op0=ALU.add, scalar2=None), vs)
            v.wait_ge(vs.h, vs.n)
            inc(v.reciprocal(stats[:, b + 4:b + 5], stats[:, b + 7:b + 8]), vs)
            v.wait_ge(vs.h, vs.n)
            inc(v.tensor_copy(stats[:, b + 5:b + 6], stats[:, b + 4:b + 5]), vs)
            v.wait_ge(vs.h, vs.n)
            inc(v.tensor_scalar(out=stats[:, b + 6:b + 7], in0=stats[:, b + 5:b + 6],
                                scalar1=-1.0, op0=ALU.mult, scalar2=1.0, op1=ALU.add), vs)
            inc(v.tensor_scalar(out=gl[:, 24:32], in0=gl[:, 0:8], scalar1=gl[:, 9:10],
                                op0=ALU.is_equal, scalar2=None), vs)
            v.wait_ge(vs.h, vs.n)
            inc(v.tensor_scalar(out=gl[:, 16:24], in0=gl[:, 16:24],
                                scalar1=stats[:, b + 5:b + 6], op0=ALU.mult,
                                scalar2=None), vs)
            inc(v.tensor_scalar(out=gl[:, 24:32], in0=gl[:, 24:32],
                                scalar1=stats[:, b + 6:b + 7], op0=ALU.mult,
                                scalar2=None), vs)
            v.wait_ge(vs.h, vs.n)
            inc(v.tensor_tensor(out=comb[:, t * 8:(t + 1) * 8], in0=gl[:, 16:24],
                                in1=gl[:, 24:32], op=ALU.add), vs)

    d6_load(0)
    d6_stats(0)
    d6_load(1)
    for t in range(NT):
        if t + 1 < NT:
            d6_stats(t + 1)
        d6_rms_spill(t)
        d6_gate_pe(t)
        d6_gate_dve(t)
        if t + 2 < NT:
            d6_load(t + 2)

    if debug:
        dbg_gT = nc.dram_tensor(pfx + "dbg_gT", [128, 64], F32, kind="ExternalOutput").ap()
        dbg_hT = nc.dram_tensor(pfx + "dbg_hT", [128, D], F32, kind="ExternalOutput").ap()

        @block.gpsimd
        def _(g_):
            g_.wait_ge(vs.h, vs.n)
            inc(g_.dma_start(out=dbg_lg_d[:, :], in_=dbg_lg[:]), gs, 16)
            inc(g_.dma_start(out=dbg_gT[:, :], in_=gateT[:]), gs, 16)
            inc(g_.dma_start(out=dbg_hT[:, :], in_=hT_st[:]), gs, 16)

    if debug == "d6":
        @block.sync
        def _(sync):
            sync.wait_ge(gs.h, gs.n)
            sync.wait_ge(vs.h, vs.n)
        ph6.close()
        return st

    ph6p.close()

    with ExitStack() as ph:
        rms2T = ph.enter_context(nc.sbuf_tensor(pfx + "rms2T", [128, 8 * T], BF16))
        expT = ph.enter_context(nc.sbuf_tensor(pfx + "expT", [128, 2 * 8 * D], BF16))
        moe_out = ph.enter_context(nc.sbuf_tensor(pfx + "moeout", [128, 16 * D], F32))
        psE = ph.enter_context(nc.psum_tensor(pfx + "psE", [128, 4 * 512], F32))

        @block.scalar
        def _(s):
            s.wait_ge(gs.h, gs.n)
            for c in range(8):
                inc(s.dma_start(out=rms2T[:, c * T:(c + 1) * T],
                                in_=rms2_dram[:, c * 128:(c + 1) * 128], transpose=True), trs, 16)

        if debug:
            dbg_comb = nc.dram_tensor(pfx + "dbg_comb", [128, NT * 8], F32, kind="ExternalOutput").ap()

            @block.gpsimd
            def _(g_):
                g_.wait_ge(vs.h, vs.n)
                inc(g_.dma_start(out=dbg_comb[:, :], in_=comb[:]), gs, 16)

        scale_done = {}
        evac_hist = []
        mm_hist = []
        d7_ld, d7_rs, d7_ap, d7_st = {}, {}, {}, {}

        def d7_load(t):
            p = t % 2

            @block.sync
            def _(sync, t=t, p=p):
                if t - 2 in d7_st:
                    sync.wait_ge(st.h, d7_st[t - 2])
                else:
                    sync.wait_ge(vs.h, vs.n)
                inc(sync.dma_start(out=xdp[:, p * D:(p + 1) * D],
                                   in_=h_dram[t * 128:(t + 1) * 128, :]), ld, 16)
                inc(sync.dma_start(out=dinp[:, p * D:(p + 1) * D],
                                   in_=x_in[t * 128:(t + 1) * 128, :]), ld, 16)
            d7_ld[t] = ld.n

        def d7_stats(t, ti):
            p = t % 2

            @block.vector
            def _(v, t=t, ti=ti, p=p):
                v.wait_ge(ld.h, d7_ld[t])
                inc(v.tensor_tensor(out=moe_out[:, ti * D:(ti + 1) * D],
                                    in0=moe_out[:, ti * D:(ti + 1) * D],
                                    in1=xdp[:, p * D:(p + 1) * D], op=ALU.add), vs)
            d7_rs[t] = emit_ln_stats(block, moe_out[:, ti * D:(ti + 1) * D],
                                     stats[:, p * 8:p * 8 + 8], scrap, vs, ss)

        def d7_apply(t, ti):
            p = t % 2
            emit_ln_apply(block, moe_out[:, ti * D:(ti + 1) * D],
                          xdp[:, p * D:(p + 1) * D], stats[:, p * 8:p * 8 + 8],
                          scrap, vs, ss, d7_rs[t], add_ap=dinp[:, p * D:(p + 1) * D])
            d7_ap[t] = vs.n

            @block.sync
            def _(sync, t=t, p=p):
                sync.wait_ge(vs.h, d7_ap[t])
                inc(sync.dma_start(out=x_out[t * 128:(t + 1) * 128, :],
                                   in_=xdp[:, p * D:(p + 1) * D]), st, 16)
            d7_st[t] = st.n


        def load_expert(e, rnd):
            buf = e % 2

            @block.scalar
            def _(s, e=e, buf=buf):
                s.wait_ge(ts.h, ts.n)
                for c in range(8):
                    inc(s.dma_start(out=expT[:, buf * 8 * D + c * D:buf * 8 * D + (c + 1) * D],
                                    in_=expbf[e * D:(e + 1) * D, c * 128:(c + 1) * 128],
                                    transpose=True), trs, 16)

            @block.vector
            def _(v, e=e, buf=buf):
                v.wait_ge(trs.h, trs.n)
                for c in range(8):
                    inc(v.tensor_scalar(out=expT[:, buf * 8 * D + c * D:buf * 8 * D + (c + 1) * D],
                                        in0=expT[:, buf * 8 * D + c * D:buf * 8 * D + (c + 1) * D],
                                        scalar1=fnw_sb[:, c:c + 1], op0=ALU.mult, scalar2=None), vs)
            scale_done[(e, rnd)] = vs.n

        ebank = 0
        for grp in range(2):
            tiles = range(grp * 16, grp * 16 + 16)
            for e in range(8):
                if grp == 0 and e < 2:
                    load_expert(e, 0)
                for ti, t in enumerate(tiles):
                    pe_ = psE[:, (ebank % 2) * 1024:(ebank % 2) * 1024 + 1024]
                    ebank += 1

                    @block.tensor
                    def _(t_, t=t, e=e, grp=grp, buf=e % 2, pe_=pe_):
                        key = (e, grp) if (e, grp) in scale_done else (e, 0)
                        t_.wait_ge(vs.h, scale_done[key])
                        if len(evac_hist) >= 2:
                            t_.wait_ge(vs.h, evac_hist[-2])
                        mm = None
                        for half in range(2):
                            for c in range(8):
                                mm = t_.matmul(pe_[:, half * 512:(half + 1) * 512],
                                               rms2T[:, c * T + t * 128: c * T + t * 128 + 128],
                                               expT[:, buf * 8 * D + c * D + half * 512:
                                                    buf * 8 * D + c * D + half * 512 + 512],
                                               start=(c == 0), stop=(c == 7))
                        inc(mm, ts)
                        mm_hist.append(ts.n)

                    @block.vector
                    def _(v, t=t, e=e, ti=ti, pe_=pe_):
                        v.wait_ge(ts.h, mm_hist[-1])
                        cc = comb[:, t * 8 + e: t * 8 + e + 1]
                        dst = moe_out[:, ti * D:(ti + 1) * D]
                        if e == 0:
                            op = v.tensor_scalar(out=dst, in0=pe_, scalar1=cc,
                                                 op0=ALU.mult, scalar2=None)
                        else:
                            op = v.scalar_tensor_tensor(out=dst, in0=pe_, scalar=cc,
                                                        op0=ALU.mult, op1=ALU.add, in1=dst)
                        inc(op, vs)
                        evac_hist.append(vs.n)

                if e + 2 < 8:
                    load_expert(e + 2, grp)
                elif grp == 0 and e >= 6:
                    load_expert(e - 6, 1)

            d7_load(tiles[0])
            d7_stats(tiles[0], 0)
            d7_load(tiles[1])
            for ti, t in enumerate(tiles):
                if ti + 1 < 16:
                    d7_stats(t + 1, ti + 1)
                d7_apply(t, ti)
                if ti + 2 < 16:
                    d7_load(t + 2)

    ph6.close()

    @block.sync
    def _(sync):
        sync.wait_ge(st.h, st.n)

    return st


def causal_masks_bf16():
    import ml_dtypes
    m = np.zeros((4, 128, 512), np.float32)
    for j in range(4):
        for kk in range(128):
            q0 = j * 128 + kk
            m[j, kk, q0:] = 1.0
    return m.reshape(4 * 128, 512).astype(ml_dtypes.bfloat16)


def build_prog_d(debug=False, glevel=5):
    nc = bass.Bass("TRN2", target_bir_lowering=False)
    x_in = nc.dram_tensor("x_in", [T, D], F32, kind="ExternalInput").ap()
    wts = {
        "wq": nc.dram_tensor("wq", [D, D], F32, kind="ExternalInput").ap(),
        "wk": nc.dram_tensor("wk", [256, D], F32, kind="ExternalInput").ap(),
        "wv": nc.dram_tensor("wv", [256, D], F32, kind="ExternalInput").ap(),
        "wo": nc.dram_tensor("wo", [D, D], F32, kind="ExternalInput").ap(),
        "gate_w": nc.dram_tensor("gate_w", [8, D], F32, kind="ExternalInput").ap(),
        "expert_w": nc.dram_tensor("expert_w", [8, D, D], F32, kind="ExternalInput").ap(),
        "anw": nc.dram_tensor("anw", [D], F32, kind="ExternalInput").ap(),
        "fnw": nc.dram_tensor("fnw", [D], F32, kind="ExternalInput").ap(),
        "masks": nc.dram_tensor("masks", [4 * 128, 512], BF16, kind="ExternalInput").ap(),
    }
    x_out = nc.dram_tensor("x_out", [T, D], F32, kind="ExternalOutput").ap()
    with ExitStack() as ctx:
        block = ctx.enter_context(nc.Block())
        emit_stage_d(nc, ctx, block, x_in, wts, x_out, debug=debug, glevel=glevel)
    return nc


def emit_stage_c(nc, ctx, block, x_in, wts, x_out, pfx="c_", debug=False, sems=None):
    E = ctx.enter_context
    ld, vs, ss, ts, gs, trs, st = sems or mk_sems(
        nc, ctx, ["ld", "vs", "ss", "ts", "gs", "trs", "st"], pfx)
    kind = "ExternalOutput" if debug else "Internal"

    cin_dram = nc.dram_tensor(pfx + "cin", [T, D], BF16, kind=kind).ap()
    qkvbf = nc.dram_tensor(pfx + "qkvbf", [3 * D, D], BF16, kind="Internal").ap()
    wobf = nc.dram_tensor(pfx + "wobf", [D, D], BF16, kind="Internal").ap()
    attnfull = nc.dram_tensor(pfx + "attnfull", [T, D], BF16, kind=kind).ap()
    if debug:
        dbg_scores = nc.dram_tensor(pfx + "dbg_scores", [128, 48], F32, kind="ExternalOutput").ap()
        dbg_idx = nc.dram_tensor(pfx + "dbg_idx", [2, 256], F32, kind="ExternalOutput").ap()
        dbg_ysel = nc.dram_tensor(pfx + "dbg_ysel", [256, D], BF16, kind="ExternalOutput").ap()

    stats = E(nc.sbuf_tensor(pfx + "stats", [128, 16], F32))
    scrap = E(nc.sbuf_tensor(pfx + "scrap", [128, 2 * D], F32))
    ones = E(nc.sbuf_tensor(pfx + "ones", [128, 128], F32))
    onesb = E(nc.sbuf_tensor(pfx + "onesb", [128, 2], BF16))
    onesrowb = E(nc.sbuf_tensor(pfx + "onesrowb", [1, 256], BF16))
    idT = E(nc.sbuf_tensor(pfx + "idT", [128, 2 * 128], F32))
    ltri = E(nc.sbuf_tensor(pfx + "ltri", [128, 128], F32))
    iotaS = E(nc.sbuf_tensor(pfx + "iotaS", [128, 256], F32))
    tokid = E(nc.sbuf_tensor(pfx + "tokid", [128, 32], F32))
    emat = E(nc.sbuf_tensor(pfx + "emat", [128, 64], F32))
    scores_tm = E(nc.sbuf_tensor(pfx + "scorestm", [128, 48], F32))
    gate_tm = E(nc.sbuf_tensor(pfx + "gatetm", [128, 48], F32))
    bis = E(nc.sbuf_tensor(pfx + "bis", [1, 16], F32))
    rowbuf = E(nc.sbuf_tensor(pfx + "rowbuf", [1, D], F32))
    vrow = E(nc.sbuf_tensor(pfx + "vrow", [1, 4 * D], F32))
    idx_sb = E(nc.sbuf_tensor(pfx + "idx", [128, 4], U32))
    gsel_col = E(nc.sbuf_tensor(pfx + "gselcol", [128, 2], F32))
    gbc = E(nc.sbuf_tensor(pfx + "gbc", [128, 256], F32))
    bout_col = E(nc.sbuf_tensor(pfx + "boutcol", [128, 8], F32))
    yuni_bc = E(nc.sbuf_tensor(pfx + "yunibc", [128, D], BF16))

    psA = E(nc.psum_tensor(pfx + "psA", [128, 2 * 512], F32))
    psB = E(nc.psum_tensor(pfx + "psB", [128, 512], F32))
    psC = E(nc.psum_tensor(pfx + "psC", [128, 512], F32))

    @block.gpsimd
    def _(g_):
        inc(g_.memset(ones[:], 1.0), gs)
        inc(g_.memset(onesb[:], 1.0), gs)
        inc(g_.memset(onesrowb[:], 1.0), gs)
        inc(g_.iota(out=idT[:, 0:128], base=0, pattern=[[1, 128]], channel_multiplier=0,
                    allow_small_or_imprecise_dtypes=True), gs)
        inc(g_.iota(out=idT[:, 128:256], base=0, pattern=[[0, 128]], channel_multiplier=1,
                    allow_small_or_imprecise_dtypes=True), gs)
        inc(g_.iota(out=iotaS[:], base=1, pattern=[[1, 256]], channel_multiplier=0,
                    allow_small_or_imprecise_dtypes=True), gs)
        inc(g_.iota(out=tokid[:], base=0, pattern=[[128, 32]], channel_multiplier=1,
                    allow_small_or_imprecise_dtypes=True), gs)
        inc(g_.iota(out=emat[:], base=0, pattern=[[-1, 64]], channel_multiplier=1,
                    allow_small_or_imprecise_dtypes=True), gs)
        inc(g_.dma_start(out=qkvbf[:, :], in_=wts["w_qkv"][:, :]), gs, 16)
        inc(g_.dma_start(out=wobf[:, :], in_=wts["w_out"][:, :]), gs, 16)

    @block.vector
    def _(v):
        v.wait_ge(gs.h, gs.n)
        inc(v.tensor_tensor(out=ltri[:], in0=idT[:, 128:256], in1=idT[:, 0:128], op=ALU.is_le), vs)
        inc(v.tensor_scalar(out=scrap[:, 0:64], in0=emat[:], scalar1=0.0, op0=ALU.is_equal,
                            scalar2=None), vs)
        inc(v.tensor_scalar(out=scrap[:, 64:128], in0=emat[:], scalar1=64.0, op0=ALU.is_equal,
                            scalar2=None), vs)
        v.wait_ge(vs.h, vs.n)
        inc(v.tensor_tensor(out=emat[:], in0=scrap[:, 0:64], in1=scrap[:, 64:128], op=ALU.max), vs)
        inc(v.tensor_tensor(out=idT[:, 0:128], in0=idT[:, 128:256], in1=idT[:, 0:128],
                            op=ALU.is_equal), vs)
        v.wait_ge(vs.h, vs.n)
        inc(v.tensor_scalar(out=emat[:], in0=emat[:], scalar1=1.0 / 16.0, op0=ALU.mult,
                            scalar2=None), vs)

    @block.sync
    def _(sync):
        inc(sync.dma_start(out=vrow[0:1, 0:D], in_=wts["b_out"][None, :]), ld, 16)

    @block.tensor
    def _(t_):
        t_.wait_ge(ld.h, ld.n)
        t_.wait_ge(gs.h, gs.n)
        mm = None
        for c in range(8):
            mm = t_.transpose(psB[:, c:c + 1], vrow[0:1, c * 128:(c + 1) * 128], ones[0:1, 0:1])
        inc(mm, ts)

    @block.vector
    def _(v):
        v.wait_ge(ts.h, ts.n)
        inc(v.tensor_copy(bout_col[:], psB[:, 0:8]), vs)

    xtp = E(nc.sbuf_tensor(pfx + "xtp", [128, 2 * D], F32))
    fbp = E(nc.sbuf_tensor(pfx + "fbp", [128, 2 * D], BF16))
    xop = E(nc.sbuf_tensor(pfx + "xop", [128, 2 * D], F32))
    xcorr_dram = nc.dram_tensor(pfx + "xcorrd", [256, D], F32, kind="Internal").ap()
    ld_at, sp_at, rstd_at, ap_at, xs_at = {}, {}, {}, {}, {}

    def c1_load(t):
        @block.sync
        def _(sync, t=t):
            if t - 2 in ap_at:
                sync.wait_ge(vs.h, ap_at[t - 2])
            inc(sync.dma_start(out=xtp[:, (t % 2) * D:(t % 2 + 1) * D],
                               in_=x_in[t * 128:(t + 1) * 128, :]), ld, 16)
        ld_at[t] = ld.n

    def c1_stats(t):
        @block.vector
        def _(v, t=t):
            v.wait_ge(ld.h, ld_at[t])
        rstd_at[t] = emit_ln_stats(block, xtp[:, (t % 2) * D:(t % 2 + 1) * D],
                                   stats[:, (t % 2) * 8:(t % 2) * 8 + 8], scrap, vs, ss)

    def c1_apply(t):
        if t - 2 in sp_at:
            @block.vector
            def _(v, t=t):
                v.wait_ge(gs.h, sp_at[t - 2])
                v.wait_ge(ts.h, xs_at[t - 2])
        emit_ln_apply(block, xtp[:, (t % 2) * D:(t % 2 + 1) * D],
                      fbp[:, (t % 2) * D:(t % 2 + 1) * D],
                      stats[:, (t % 2) * 8:(t % 2) * 8 + 8], scrap, vs, ss, rstd_at[t])
        ap_at[t] = vs.n

        @block.tensor
        def _(t_, t=t):
            t_.wait_ge(vs.h, ap_at[t])
            t_.matmul(psA[0:1, 0:512], onesb[:, 0:1], fbp[:, (t % 2) * D:(t % 2) * D + 512],
                      start=(t == 0), stop=(t == NT - 1))
            mm = t_.matmul(psA[0:1, 512:1024], onesb[:, 0:1],
                           fbp[:, (t % 2) * D + 512:(t % 2 + 1) * D],
                           start=(t == 0), stop=(t == NT - 1))
            inc(mm, ts)
        xs_at[t] = ts.n

        @block.gpsimd
        def _(g_, t=t):
            g_.wait_ge(vs.h, ap_at[t])
            inc(g_.dma_start(out=cin_dram[t * 128:(t + 1) * 128, :],
                             in_=fbp[:, (t % 2) * D:(t % 2 + 1) * D]), gs, 16)
        sp_at[t] = gs.n

    c1_load(0)
    c1_stats(0)
    c1_load(1)
    for t in range(NT):
        if t + 1 < NT:
            c1_stats(t + 1)
        c1_apply(t)
        if t + 2 < NT:
            c1_load(t + 2)

    @block.vector
    def _(v):
        v.wait_ge(ts.h, ts.n)
        inc(v.tensor_copy(vrow[0:1, 0:D], psA[0:1, 0:1024]), vs)

    if debug == "c1":
        @block.sync
        def _(sync):
            sync.wait_ge(gs.h, gs.n)
            sync.wait_ge(vs.h, vs.n)
        return st

    with ExitStack() as ph:
        WqkT = ph.enter_context(nc.sbuf_tensor(pfx + "WqkT", [128, 8 * 2048], BF16))
        WvT = ph.enter_context(nc.sbuf_tensor(pfx + "WvT", [128, 8 * D], BF16))
        WoT = ph.enter_context(nc.sbuf_tensor(pfx + "WoT", [128, 8 * D], BF16))
        wqmb = ph.enter_context(nc.sbuf_tensor(pfx + "wqmb", [128, 8 * 64], BF16))
        qmT = ph.enter_context(nc.sbuf_tensor(pfx + "qmT", [128, T], F32))
        h1T = ph.enter_context(nc.sbuf_tensor(pfx + "h1T", [128, T], F32))
        w1T = ph.enter_context(nc.sbuf_tensor(pfx + "w1T", [128, 32], F32))
        ph_mrow = ExitStack()
        mrow = ph_mrow.enter_context(nc.sbuf_tensor(pfx + "mrow", [1, 2 * T], F32))
        ph_cin = ExitStack()
        cinT = ph_cin.enter_context(nc.sbuf_tensor(pfx + "cinT", [128, 8 * (T // 4)], BF16))
        wrows = ph_cin.enter_context(nc.sbuf_tensor(pfx + "wrows", [128, D], F32))

        @block.sync
        def _(sync):
            sync.wait_ge(vs.h, vs.n)
            sync.wait_ge(ts.h, ts.n)

        @block.gpsimd
        def _(g_):
            g_.wait_ge(vs.h, vs.n)
            g_.wait_ge(ts.h, ts.n)

        for c in range(8):
            @block.sync
            def _(sync, c=c):
                if c > 0:
                    sync.wait_ge(ts.h, ts.n)
                inc(sync.dma_start(out=wrows[:], in_=wts["w_qkv"][c * 128:(c + 1) * 128, :]), ld, 16)

            @block.tensor
            def _(t_, c=c):
                t_.wait_ge(ld.h, ld.n)
                t_.wait_ge(vs.h, vs.n)
                t_.matmul(psA[0:64, 0:512], emat[:], wrows[:, 0:512],
                          start=(c == 0), stop=(c == 7))
                mm = t_.matmul(psA[0:64, 512:1024], emat[:], wrows[:, 512:1024],
                               start=(c == 0), stop=(c == 7))
                inc(mm, ts)

        @block.vector
        def _(v):
            v.wait_ge(ts.h, ts.n)
            inc(v.tensor_copy(scrap[0:64, 0:D], psA[0:64, 0:1024]), vs)

        for c in range(8):
            @block.tensor
            def _(t_, c=c):
                t_.wait_ge(vs.h, vs.n)
                inc(t_.transpose(psB[:, 0:64], scrap[0:64, c * 128:(c + 1) * 128],
                                 idT[0:64, 0:64]), ts)

            @block.vector
            def _(v, c=c):
                v.wait_ge(ts.h, ts.n)
                inc(v.tensor_copy(wqmb[:, c * 64:(c + 1) * 64], psB[:, 0:64]), vs)

        @block.scalar
        def _(s):
            s.wait_ge(gs.h, gs.n)
            for c in range(8):
                inc(s.dma_start(out=WqkT[:, c * 2048:c * 2048 + 1024],
                                in_=qkvbf[0:1024, c * 128:(c + 1) * 128], transpose=True), trs, 16)
                inc(s.dma_start(out=WqkT[:, c * 2048 + 1024:(c + 1) * 2048],
                                in_=qkvbf[1024:2048, c * 128:(c + 1) * 128], transpose=True), trs, 16)
                inc(s.dma_start(out=WvT[:, c * D:(c + 1) * D],
                                in_=qkvbf[2048:3072, c * 128:(c + 1) * 128], transpose=True), trs, 16)
                inc(s.dma_start(out=WoT[:, c * D:(c + 1) * D],
                                in_=wobf[0:1024, c * 128:(c + 1) * 128], transpose=True), trs, 16)

        TH2 = T // 4
        for half in range(4):
            @block.scalar
            def _(s, half=half):
                if half > 0:
                    s.wait_ge(ts.h, ts.n)
                for c in range(8):
                    inc(s.dma_start(out=cinT[:, c * TH2:(c + 1) * TH2],
                                    in_=cin_dram[half * TH2:(half + 1) * TH2,
                                                 c * 128:(c + 1) * 128], transpose=True), trs, 16)

            for n in range(2):
                @block.tensor
                def _(t_, half=half, n=n):
                    t_.wait_ge(trs.h, trs.n)
                    t_.wait_ge(vs.h, vs.n)
                    mm = None
                    for c in range(8):
                        mm = t_.matmul(psA[0:64, 0:512], wqmb[:, c * 64:(c + 1) * 64],
                                       cinT[:, c * TH2 + n * 512: c * TH2 + (n + 1) * 512],
                                       start=(c == 0), stop=(c == 7))
                    inc(mm, ts)

                @block.vector
                def _(v, half=half, n=n):
                    v.wait_ge(ts.h, ts.n)
                    inc(v.tensor_copy(
                        qmT[0:64, half * TH2 + n * 512: half * TH2 + (n + 1) * 512],
                        psA[0:64, 0:512]), vs)

        ph_cin.close()

        if debug:
            dbg_qm = nc.dram_tensor(pfx + "dbg_qm", [64, T], F32, kind="ExternalOutput").ap()
            dbg_wqm = nc.dram_tensor(pfx + "dbg_wqm", [128, 8 * 64], BF16, kind="ExternalOutput").ap()

            @block.gpsimd
            def _(g_):
                g_.wait_ge(vs.h, vs.n)
                inc(g_.dma_start(out=dbg_wqm[:, :], in_=wqmb[:]), gs, 16)

            @block.gpsimd
            def _(g_):
                g_.wait_ge(vs.h, vs.n)
                inc(g_.dma_start(out=dbg_qm[:, :], in_=qmT[0:64, :]), gs, 16)

        if debug == "c2":
            @block.sync
            def _(sync):
                sync.wait_ge(gs.h, gs.n)
                sync.wait_ge(vs.h, vs.n)
                sync.wait_ge(ts.h, ts.n)
            ph_mrow.close()
            return st

        @block.sync
        def _(sync):
            sync.wait_ge(vs.h, vs.n)
            sync.wait_ge(ts.h, ts.n)

        @block.gpsimd
        def _(g_):
            g_.wait_ge(vs.h, vs.n)
            g_.wait_ge(ts.h, ts.n)

        @block.scalar
        def _(s):
            s.wait_ge(vs.h, vs.n)
            inc(s.activation(out=h1T[0:64, :], in_=qmT[0:64, :], func=AF.Square), ss)

        for n in range(8):
            @block.tensor
            def _(t_, n=n):
                t_.wait_ge(ss.h, ss.n)
                t_.wait_ge(vs.h, vs.n)
                t_.matmul(psA[0:1, 0:512], ones[0:64, 0:1],
                          qmT[0:64, n * 512:(n + 1) * 512], start=True, stop=True)
                mm = t_.matmul(psA[0:1, 512:1024], ones[0:64, 0:1],
                               h1T[0:64, n * 512:(n + 1) * 512], start=True, stop=True)
                inc(mm, ts)

            @block.vector
            def _(v, n=n):
                v.wait_ge(ts.h, ts.n)
                inc(v.tensor_scalar(out=mrow[0:1, n * 512:(n + 1) * 512], in0=psA[0:1, 0:512],
                                    scalar1=1.0 / 64, op0=ALU.mult, scalar2=None), vs)
                inc(v.tensor_scalar(out=mrow[0:1, T + n * 512:T + (n + 1) * 512],
                                    in0=psA[0:1, 512:1024],
                                    scalar1=1.0 / 64, op0=ALU.mult, scalar2=None), vs)


        if debug == "c2s":
            @block.sync
            def _(sync):
                sync.wait_ge(gs.h, gs.n)
                sync.wait_ge(vs.h, vs.n)
                sync.wait_ge(ts.h, ts.n)
            ph_mrow.close()
            return st

        @block.vector
        def _(v):
            v.wait_ge(vs.h, vs.n)
            v.tensor_tensor(out=h1T[0:1, 0:T], in0=mrow[0:1, 0:T], in1=mrow[0:1, 0:T],
                            op=ALU.mult)
            inc(v.tensor_tensor(out=mrow[0:1, T:2 * T], in0=mrow[0:1, T:2 * T],
                                in1=h1T[0:1, 0:T], op=ALU.subtract), vs)
            v.wait_ge(vs.h, vs.n)
            inc(v.tensor_scalar(out=mrow[0:1, T:2 * T], in0=mrow[0:1, T:2 * T],
                                scalar1=LN_EPS, op0=ALU.add, scalar2=None), vs)

        @block.scalar
        def _(s):
            s.wait_ge(vs.h, vs.n)
            inc(s.activation(out=h1T[0:1, 0:T], in_=mrow[0:1, T:2 * T], func=AF.Ln), ss)
            s.wait_ge(ss.h, ss.n)
            inc(s.activation(out=mrow[0:1, T:2 * T], in_=h1T[0:1, 0:T], func=AF.Exp,
                             scale=-0.5), ss)


        if debug == "c2m":
            @block.sync
            def _(sync):
                sync.wait_ge(gs.h, gs.n)
                sync.wait_ge(vs.h, vs.n)
                sync.wait_ge(ts.h, ts.n)
            ph_mrow.close()
            return st

        @block.sync
        def _(sync):
            inc(sync.dma_start(out=rowbuf[0:1, 0:64], in_=wts["ln_g"][None, :]), ld, 16)
            inc(sync.dma_start(out=rowbuf[0:1, 64:128], in_=wts["ln_b"][None, :]), ld, 16)
            inc(sync.dma_start(out=rowbuf[0:1, 128:144], in_=wts["b1"][None, :]), ld, 16)
            inc(sync.dma_start(out=rowbuf[0:1, 144:160], in_=wts["w2"][0:1, :]), ld, 16)
            inc(sync.dma_start(out=rowbuf[0:1, 160:161], in_=wts["b2"][None, :]), ld, 16)
            inc(sync.dma_start(out=gate_tm[0:64, 0:16], in_=wts["w1"].rearrange("a b -> b a"),
                               allow_slow_non_contiguous=True), ld, 16)

        @block.tensor
        def _(t_):
            t_.wait_ge(ld.h, ld.n)
            t_.wait_ge(vs.h, vs.n)
            t_.transpose(psB[0:64, 0:1], rowbuf[0:1, 0:64], ones[0:1, 0:1])
            t_.transpose(psB[0:64, 1:2], rowbuf[0:1, 64:128], ones[0:1, 0:1])
            t_.transpose(psB[0:16, 2:3], rowbuf[0:1, 128:144], ones[0:1, 0:1])
            t_.transpose(psB[0:16, 3:4], rowbuf[0:1, 144:160], ones[0:1, 0:1])
            inc(t_.transpose(psB[0:16, 4:5], rowbuf[0:1, 144:160], ones[0:1, 0:1]), ts)

        @block.vector
        def _(v):
            v.wait_ge(ts.h, ts.n)
            inc(v.tensor_copy(w1T[:, 0:8], psB[:, 0:8]), vs)
            v.wait_ge(ld.h, ld.n)
            inc(v.tensor_copy(w1T[0:64, 8:24], gate_tm[0:64, 0:16]), vs)

        for n in range(8):
            @block.tensor
            def _(t_, n=n):
                t_.wait_ge(vs.h, vs.n)
                t_.matmul(psA[0:64, 0:512], ones[0:1, 0:64], mrow[0:1, n * 512:(n + 1) * 512],
                          start=True, stop=True)
                inc(t_.matmul(psA[0:64, 512:1024], ones[0:1, 0:64],
                              mrow[0:1, T + n * 512:T + (n + 1) * 512], start=True, stop=True), ts)

            @block.vector
            def _(v, n=n):
                v.wait_ge(ts.h, ts.n)
                sl = slice(n * 512, (n + 1) * 512)
                v.tensor_tensor(out=scrap[0:64, 0:512], in0=qmT[0:64, sl], in1=psA[0:64, 0:512],
                                op=ALU.subtract)
                inc(v.engine_nop() if False else v.tensor_tensor(
                    out=scrap[0:64, 512:1024], in0=scrap[0:64, 0:512], in1=psA[0:64, 512:1024],
                    op=ALU.mult), vs)
                v.wait_ge(vs.h, vs.n)
                inc(v.tensor_scalar(out=scrap[0:64, 0:512], in0=scrap[0:64, 512:1024],
                                    scalar1=w1T[0:64, 0:1], op0=ALU.mult,
                                    scalar2=w1T[0:64, 1:2], op1=ALU.add), vs)
                v.wait_ge(vs.h, vs.n)
                inc(v.tensor_copy(qmT[0:64, sl], scrap[0:64, 0:512]), vs)


        if debug:
            dbg_hn = nc.dram_tensor(pfx + "dbg_hn", [64, T], F32, kind="ExternalOutput").ap()

            @block.gpsimd
            def _(g_):
                g_.wait_ge(vs.h, vs.n)
                inc(g_.dma_start(out=dbg_hn[:, :], in_=qmT[0:64, :]), gs, 16)

        if debug == "c2h":
            @block.sync
            def _(sync):
                sync.wait_ge(gs.h, gs.n)
                sync.wait_ge(vs.h, vs.n)
                sync.wait_ge(ts.h, ts.n)
            ph_mrow.close()
            return st

        ph_mrow.close()

        for n in range(8):
            @block.tensor
            def _(t_, n=n):
                t_.wait_ge(vs.h, vs.n)
                if n > 0:
                    t_.wait_ge(ss.h, ss.n)
                inc(t_.matmul(psB[0:16, 0:512], w1T[0:64, 8:24],
                              qmT[0:64, n * 512:(n + 1) * 512], start=True, stop=True), ts)

            @block.scalar
            def _(s, n=n):
                s.wait_ge(ts.h, ts.n)
                inc(s.activation(out=h1T[0:16, n * 512:(n + 1) * 512], in_=psB[0:16, 0:512],
                                 func=AF.Gelu, bias=w1T[0:16, 2:3], scale=1.0), ss)

        if debug:
            dbg_h1 = nc.dram_tensor(pfx + "dbg_h1", [16, T], F32, kind="ExternalOutput").ap()
            dbg_w1T = nc.dram_tensor(pfx + "dbg_w1T", [128, 32], F32, kind="ExternalOutput").ap()

            @block.gpsimd
            def _(g_):
                g_.wait_ge(ss.h, ss.n)
                inc(g_.dma_start(out=dbg_w1T[:, :], in_=w1T[:]), gs, 16)

            @block.gpsimd
            def _(g_):
                g_.wait_ge(ss.h, ss.n)
                inc(g_.dma_start(out=dbg_h1[:, :], in_=h1T[0:16, :]), gs, 16)

        @block.tensor
        def _(t_):
            t_.wait_ge(ss.h, ss.n)
            mm = None
            for t in range(NT):
                mm = t_.matmul(psC[:, t:t + 1], h1T[0:16, t * 128:(t + 1) * 128],
                               w1T[0:16, 3:4], start=True, stop=True)
            mm2 = t_.matmul(psC[:, 40:41], ones[0:1, 0:128], rowbuf[0:1, 160:161],
                            start=True, stop=True)
            inc(mm2, ts)

        @block.vector
        def _(v):
            v.wait_ge(ts.h, ts.n)
            inc(v.tensor_scalar(out=scores_tm[:, 0:32], in0=psC[:, 0:32],
                                scalar1=psC[:, 40:41], op0=ALU.add, scalar2=None), vs)

        if debug:
            @block.gpsimd
            def _(g_):
                g_.wait_ge(vs.h, vs.n)
                inc(g_.dma_start(out=dbg_scores[:, :], in_=scores_tm[:]), gs, 16)

        if debug == "c2b":
            @block.sync
            def _(sync):
                sync.wait_ge(gs.h, gs.n)
                sync.wait_ge(vs.h, vs.n)
                sync.wait_ge(ts.h, ts.n)
            return st

        @block.sync
        def _(sync):
            sync.wait_ge(vs.h, vs.n)
            sync.wait_ge(ts.h, ts.n)

        @block.gpsimd
        def _(g_):
            g_.wait_ge(vs.h, vs.n)
            g_.wait_ge(ts.h, ts.n)

        @block.vector
        def _(v):
            v.wait_ge(vs.h, vs.n)
            inc(v.reduce_max(scores_tm[:, 32:33], scores_tm[:, 0:32], axis=AX.X), vs)
            inc(v.tensor_scalar(out=scrap[:, 0:32], in0=scores_tm[:, 0:32], scalar1=-1.0,
                                op0=ALU.mult, scalar2=None), vs)
            v.wait_ge(vs.h, vs.n)
            inc(v.reduce_max(scores_tm[:, 33:34], scrap[:, 0:32], axis=AX.X), vs)

        @block.tensor
        def _(t_):
            t_.wait_ge(vs.h, vs.n)
            t_.transpose(psB[0:1, 0:128], scores_tm[:, 32:33], idT[:, 0:128])
            inc(t_.transpose(psB[0:1, 128:256], scores_tm[:, 33:34], idT[:, 0:128]), ts)

        @block.vector
        def _(v):
            v.wait_ge(ts.h, ts.n)
            inc(v.reduce_max(bis[0:1, 1:2], psB[0:1, 0:128], axis=AX.X), vs)
            inc(v.reduce_max(bis[0:1, 3:4], psB[0:1, 128:256], axis=AX.X), vs)
            v.wait_ge(vs.h, vs.n)
            inc(v.tensor_scalar(out=bis[0:1, 1:2], in0=bis[0:1, 1:2], scalar1=1.0,
                                op0=ALU.add, scalar2=None), vs)
            inc(v.tensor_scalar(out=bis[0:1, 0:1], in0=bis[0:1, 3:4], scalar1=-1.0,
                                op0=ALU.mult, scalar2=-1.0, op1=ALU.add), vs)

        for it in range(45):
            @block.vector
            def _(v):
                v.wait_ge(vs.h, vs.n)
                if it > 0:
                    v.wait_ge(ts.h, ts.n)
                inc(v.tensor_tensor(out=bis[0:1, 2:3], in0=bis[0:1, 0:1], in1=bis[0:1, 1:2],
                                    op=ALU.add), vs)
                v.wait_ge(vs.h, vs.n)
                inc(v.tensor_scalar(out=bis[0:1, 2:3], in0=bis[0:1, 2:3], scalar1=0.5,
                                    op0=ALU.mult, scalar2=None), vs)

            @block.tensor
            def _(t_):
                t_.wait_ge(vs.h, vs.n)
                inc(t_.matmul(psB[:, 300:301], ones[0:1, 0:128], bis[0:1, 2:3],
                              start=True, stop=True), ts)

            @block.vector
            def _(v):
                v.wait_ge(ts.h, ts.n)
                inc(v.tensor_scalar(out=scrap[:, 0:32], in0=scores_tm[:, 0:32],
                                    scalar1=psB[:, 300:301], op0=ALU.is_ge,
                                    scalar2=0.0, op1=ALU.add, accum_out=scores_tm[:, 34:35]), vs)

            @block.tensor
            def _(t_):
                t_.wait_ge(vs.h, vs.n)
                inc(t_.matmul(psB[0:1, 310:311], scores_tm[:, 34:35], ones[:, 0:1],
                              start=True, stop=True), ts)

            @block.vector
            def _(v):
                v.wait_ge(ts.h, ts.n)
                inc(v.tensor_scalar(out=bis[0:1, 4:5], in0=psB[0:1, 310:311], scalar1=255.5,
                                    op0=ALU.is_ge, scalar2=None), vs)
                v.wait_ge(vs.h, vs.n)
                inc(v.tensor_tensor(out=bis[0:1, 5:6], in0=bis[0:1, 2:3], in1=bis[0:1, 0:1],
                                    op=ALU.subtract), vs)
                inc(v.tensor_tensor(out=bis[0:1, 6:7], in0=bis[0:1, 1:2], in1=bis[0:1, 2:3],
                                    op=ALU.subtract), vs)
                v.wait_ge(vs.h, vs.n)
                inc(v.tensor_tensor(out=bis[0:1, 5:6], in0=bis[0:1, 5:6], in1=bis[0:1, 4:5],
                                    op=ALU.mult), vs)
                inc(v.tensor_tensor(out=bis[0:1, 6:7], in0=bis[0:1, 6:7], in1=bis[0:1, 4:5],
                                    op=ALU.mult), vs)
                v.wait_ge(vs.h, vs.n)
                inc(v.tensor_tensor(out=bis[0:1, 0:1], in0=bis[0:1, 0:1], in1=bis[0:1, 5:6],
                                    op=ALU.add), vs)
                inc(v.tensor_tensor(out=bis[0:1, 1:2], in0=bis[0:1, 2:3], in1=bis[0:1, 6:7],
                                    op=ALU.add), vs)

        if debug:
            dbg_bis = nc.dram_tensor(pfx + "dbg_bis", [1, 16], F32, kind="ExternalOutput").ap()

            @block.gpsimd
            def _(g_):
                g_.wait_ge(vs.h, vs.n)
                inc(g_.dma_start(out=dbg_bis[:, :], in_=bis[:]), gs, 16)

        @block.tensor
        def _(t_):
            t_.wait_ge(vs.h, vs.n)
            inc(t_.matmul(psB[:, 300:301], ones[0:1, 0:128], bis[0:1, 0:1],
                          start=True, stop=True), ts)

        @block.scalar
        def _(s):
            s.wait_ge(vs.h, vs.n)
            inc(s.activation(out=gate_tm[:, 0:32], in_=scores_tm[:, 0:32], func=AF.Exp,
                             scale=-1.0), ss)

        @block.vector
        def _(v):
            v.wait_ge(ts.h, ts.n)
            v.wait_ge(ss.h, ss.n)
            inc(v.tensor_scalar(out=gate_tm[:, 0:32], in0=gate_tm[:, 0:32],
                                scalar1=1.0, op0=ALU.add, scalar2=None), vs)
            v.wait_ge(vs.h, vs.n)
            inc(v.reciprocal(gate_tm[:, 0:32], gate_tm[:, 0:32]), vs)
            v.wait_ge(vs.h, vs.n)
            inc(v.tensor_scalar(out=scores_tm[:, 0:32], in0=scores_tm[:, 0:32],
                                scalar1=psB[:, 300:301], op0=ALU.is_ge, scalar2=None), vs)
            v.wait_ge(vs.h, vs.n)
            inc(v.tensor_tensor(out=gate_tm[:, 0:32], in0=gate_tm[:, 0:32],
                                in1=scores_tm[:, 0:32], op=ALU.mult), vs)

        @block.tensor
        def _(t_):
            t_.wait_ge(vs.h, vs.n)
            t_.matmul(psA[:, 0:32], ltri[:], scores_tm[:, 0:32], start=True, stop=False)
            inc(t_.matmul(psC[0:1, 0:32], ones[:, 0:1], scores_tm[:, 0:32],
                          start=True, stop=True), ts)

        @block.vector
        def _(v):
            v.wait_ge(ts.h, ts.n)
            inc(v.tensor_copy(rowbuf[0:1, 232:264], psC[0:1, 0:32]), vs)
            v.wait_ge(vs.h, vs.n)
            inc(v.tensor_tensor_scan(out=rowbuf[0:1, 200:232], data0=rowbuf[0:1, 232:264],
                                     data1=rowbuf[0:1, 232:264], initial=0.0,
                                     op0=ALU.add, op1=ALU.bypass), vs)

        @block.tensor
        def _(t_):
            t_.wait_ge(vs.h, vs.n)
            inc(t_.matmul(psA[:, 1:32], ones[0:1, 0:128], rowbuf[0:1, 200:231],
                          start=False, stop=True), ts)

        @block.vector
        def _(v):
            v.wait_ge(ts.h, ts.n)
            inc(v.tensor_tensor(out=gate_tm[:, 32:64] if False else scrap[:, 0:32],
                                in0=psA[:, 0:32], in1=scores_tm[:, 0:32], op=ALU.mult), vs)

        for j in range(NT):
            @block.vector
            def _(v, j=j):
                v.wait_ge(vs.h, vs.n)
                if j > 0:
                    v.wait_ge(ts.h, ts.n)
                inc(v.tensor_scalar(out=scrap[:, 64:320], in0=iotaS[:],
                                    scalar1=scrap[:, j:j + 1], op0=ALU.is_equal, scalar2=None), vs)
                v.wait_ge(vs.h, vs.n)
                v.tensor_copy(scrap[:, 320:321], tokid[:, j:j + 1])
                inc(v.tensor_copy(scrap[:, 321:322], gate_tm[:, j:j + 1]), vs)

            @block.tensor
            def _(t_, j=j):
                t_.wait_ge(vs.h, vs.n)
                t_.matmul(psC[0:1, 0:256], scrap[:, 320:321], scrap[:, 64:320],
                          start=(j == 0), stop=(j == NT - 1))
                inc(t_.matmul(psC[0:1, 256:512], scrap[:, 321:322], scrap[:, 64:320],
                              start=(j == 0), stop=(j == NT - 1)), ts)

        @block.vector
        def _(v):
            v.wait_ge(ts.h, ts.n)
            inc(v.tensor_copy(rowbuf[0:1, 0:256], psC[0:1, 0:256]), vs)
            inc(v.tensor_copy(rowbuf[0:1, 256:512], psC[0:1, 256:512]), vs)

        @block.tensor
        def _(t_):
            t_.wait_ge(vs.h, vs.n)
            t_.transpose(psB[:, 0:1], rowbuf[0:1, 0:128], ones[0:1, 0:1])
            t_.transpose(psB[:, 1:2], rowbuf[0:1, 128:256], ones[0:1, 0:1])
            t_.transpose(psB[:, 2:3], rowbuf[0:1, 256:384], ones[0:1, 0:1])
            t_.transpose(psB[:, 3:4], rowbuf[0:1, 384:512], ones[0:1, 0:1])
            inc(t_.matmul(psA[:, 0:256], ones[0:1, 0:128], rowbuf[0:1, 256:512],
                          start=True, stop=True), ts)

        @block.vector
        def _(v):
            v.wait_ge(ts.h, ts.n)
            inc(v.tensor_copy(idx_sb[:, 0:2], psB[:, 0:2]), vs)
            inc(v.tensor_copy(gsel_col[:, 0:2], psB[:, 2:4]), vs)
            inc(v.tensor_copy(gbc[:], psA[:, 0:256]), vs)

        if debug:
            @block.gpsimd
            def _(g_):
                g_.wait_ge(vs.h, vs.n)
                inc(g_.dma_start(out=dbg_idx.rearrange("b s -> (b s)")[None, :],
                                 in_=rowbuf[0:1, 0:512]), gs, 16)

        if debug == "c3":
            @block.sync
            def _(sync):
                sync.wait_ge(gs.h, gs.n)
                sync.wait_ge(vs.h, vs.n)
                sync.wait_ge(ts.h, ts.n)
            return st

        @block.sync
        def _(sync):
            sync.wait_ge(vs.h, vs.n)
            sync.wait_ge(ts.h, ts.n)

        @block.gpsimd
        def _(g_):
            g_.wait_ge(vs.h, vs.n)
            g_.wait_ge(ts.h, ts.n)

        Xsel = ph.enter_context(nc.sbuf_tensor(pfx + "Xsel", [128, 2 * D], BF16))
        XselT = ph.enter_context(nc.sbuf_tensor(pfx + "XselT", [128, 8 * 256], BF16))
        qkselT = ph.enter_context(nc.sbuf_tensor(pfx + "qkselT", [128, 8 * 512], BF16))
        Vsel = ph.enter_context(nc.sbuf_tensor(pfx + "Vsel", [128, 2 * D], BF16))
        outT = ph.enter_context(nc.sbuf_tensor(pfx + "outT", [128, 8 * 256], BF16))
        pTt = ph.enter_context(nc.sbuf_tensor(pfx + "pTt", [128, 512], BF16))
        rdrow = ph.enter_context(nc.sbuf_tensor(pfx + "rdrow", [1, 512], F32))
        rdbc = ph.enter_context(nc.sbuf_tensor(pfx + "rdbc", [128, 256], F32))
        idTb = ph.enter_context(nc.sbuf_tensor(pfx + "idTb", [128, 128], BF16))
        vrowb = ph.enter_context(nc.sbuf_tensor(pfx + "vrowb", [1, D], BF16))
        xcolb = ph.enter_context(nc.sbuf_tensor(pfx + "xcolb", [128, 16], BF16))
        psTb = ph.enter_context(nc.psum_tensor(pfx + "psTb", [128, 256], BF16))

        from concourse.bass import IndirectOffsetOnAxis as IOA

        @block.vector
        def _(v):
            v.wait_ge(vs.h, vs.n)
            inc(v.tensor_copy(idTb[:], idT[:, 0:128]), vs)

        @block.gpsimd
        def _(g_):
            g_.wait_ge(vs.h, vs.n)
            for sb_ in range(2):
                inc(g_.indirect_dma_start(
                    out=Xsel[:, sb_ * D:(sb_ + 1) * D], in_=cin_dram[:, :],
                    out_offset=None,
                    in_offset=IOA(ap=idx_sb[:, sb_:sb_ + 1], axis=0)), gs, 16)

        for sb_ in range(2):
            for c in range(8):
                @block.tensor
                def _(t_, sb_=sb_, c=c):
                    t_.wait_ge(gs.h, gs.n)
                    t_.wait_ge(vs.h, vs.n)
                    inc(t_.transpose(psTb[:, 0:128],
                                     Xsel[:, sb_ * D + c * 128: sb_ * D + (c + 1) * 128],
                                     idTb[:]), ts)

                @block.vector
                def _(v, sb_=sb_, c=c):
                    v.wait_ge(ts.h, ts.n)
                    inc(v.tensor_copy(
                        XselT[:, c * 256 + sb_ * 128: c * 256 + sb_ * 128 + 128],
                        psTb[:, 0:128]), vs)

        for part in range(2):
            for m in range(8):
                @block.tensor
                def _(t_, part=part, m=m):
                    t_.wait_ge(vs.h, vs.n)
                    t_.wait_ge(trs.h, trs.n)
                    mm = None
                    for c in range(8):
                        mm = t_.matmul(psA[:, 0:256],
                                       WqkT[:, c * 2048 + part * 1024 + m * 128:
                                            c * 2048 + part * 1024 + m * 128 + 128],
                                       XselT[:, c * 256:(c + 1) * 256],
                                       start=(c == 0), stop=(c == 7))
                    inc(mm, ts)

                @block.vector
                def _(v, part=part, m=m):
                    v.wait_ge(ts.h, ts.n)
                    inc(v.tensor_copy(qkselT[:, m * 512 + part * 256: m * 512 + part * 256 + 256],
                                      psA[:, 0:256]), vs)

        for sb_ in range(2):
            @block.tensor
            def _(t_, sb_=sb_):
                t_.wait_ge(vs.h, vs.n)
                mm = None
                for c in range(8):
                    for hf in range(2):
                        mm = t_.matmul(psA[:, hf * 512:(hf + 1) * 512],
                                       XselT[:, c * 256 + sb_ * 128: c * 256 + sb_ * 128 + 128],
                                       WvT[:, c * D + hf * 512: c * D + hf * 512 + 512],
                                       start=(c == 0), stop=(c == 7))
                inc(mm, ts)

            @block.vector
            def _(v, sb_=sb_):
                v.wait_ge(ts.h, ts.n)
                inc(v.tensor_copy(Vsel[:, sb_ * D:(sb_ + 1) * D], psA[:, 0:1024]), vs)

        @block.tensor
        def _(t_):
            t_.wait_ge(vs.h, vs.n)
            mm = None
            for c in range(8):
                mm = t_.transpose(psB[:, c:c + 1], vrow[0:1, c * 128:(c + 1) * 128],
                                  ones[0:1, 0:1])
            inc(mm, ts)

        @block.vector
        def _(v):
            v.wait_ge(ts.h, ts.n)
            inc(v.tensor_copy(xcolb[:, 0:8], psB[:, 0:8]), vs)

        @block.tensor
        def _(t_):
            t_.wait_ge(vs.h, vs.n)
            mm = None
            for c in range(8):
                for hf in range(2):
                    mm = t_.matmul(psA[0:1, hf * 512:(hf + 1) * 512], xcolb[:, c:c + 1],
                                   WvT[:, c * D + hf * 512: c * D + hf * 512 + 512],
                                   start=(c == 0), stop=(c == 7))
            for sb_ in range(2):
                mm = t_.matmul(psC[0:1, 0:512], onesb[:, 0:1], Vsel[:, sb_ * D: sb_ * D + 512],
                               start=(sb_ == 0), stop=(sb_ == 1))
                mm = t_.matmul(psC[0:1, 512:1024]
                               if False else psB[0:1, 0:512], onesb[:, 0:1],
                               Vsel[:, sb_ * D + 512: sb_ * D + 1024],
                               start=(sb_ == 0), stop=(sb_ == 1))
            inc(mm, ts)

        @block.vector
        def _(v):
            v.wait_ge(ts.h, ts.n)
            inc(v.tensor_copy(vrow[0:1, D:2 * D], psA[0:1, 0:1024]), vs)
            v.wait_ge(vs.h, vs.n)
            v.tensor_tensor(out=vrow[0:1, 2 * D:2 * D + 512], in0=vrow[0:1, D:D + 512],
                            in1=psC[0:1, 0:512], op=ALU.subtract)
            inc(v.tensor_tensor(out=vrow[0:1, 2 * D + 512:3 * D], in0=vrow[0:1, D + 512:2 * D],
                                in1=psB[0:1, 0:512], op=ALU.subtract), vs)
            v.wait_ge(vs.h, vs.n)
            inc(v.tensor_copy(vrowb[0:1, :], vrow[0:1, 2 * D:3 * D]), vs)

        @block.sync
        def _(sync):
            sync.wait_ge(vs.h, vs.n)
            sync.wait_ge(ts.h, ts.n)

        @block.gpsimd
        def _(g_):
            g_.wait_ge(vs.h, vs.n)
            g_.wait_ge(ts.h, ts.n)

        for h in range(16):
            jb = h // 2
            base = 64 * (h % 2)

            @block.tensor
            def _(t_, h=h, jb=jb, base=base):
                t_.wait_ge(vs.h, vs.n)
                mm = None
                for kc in range(2):
                    mm = t_.matmul(psA[:, kc * 256:(kc + 1) * 256],
                                   qkselT[base:base + 64,
                                          jb * 512 + 256 + kc * 128: jb * 512 + 256 + kc * 128 + 128],
                                   qkselT[base:base + 64, jb * 512: jb * 512 + 256],
                                   start=True, stop=True)
                inc(mm, ts)

            @block.vector
            def _(v, h=h):
                v.wait_ge(ts.h, ts.n)
                for kc in range(2):
                    inc(v.tensor_scalar(out=scrap[:, kc * 256:(kc + 1) * 256],
                                        in0=psA[:, kc * 256:(kc + 1) * 256],
                                        scalar1=gsel_col[:, kc:kc + 1], op0=ALU.mult,
                                        scalar2=0.125, op1=ALU.mult), vs)
                v.wait_ge(vs.h, vs.n)
                for kc in range(2):
                    inc(v.tensor_tensor(out=scrap[:, kc * 256:(kc + 1) * 256],
                                        in0=scrap[:, kc * 256:(kc + 1) * 256],
                                        in1=gbc[:, 0:256], op=ALU.mult), vs)

            @block.scalar
            def _(s, h=h):
                s.wait_ge(vs.h, vs.n)
                inc(s.activation(out=pTt[:], in_=scrap[:, 0:512], func=AF.Exp), ss)

            @block.tensor
            def _(t_, h=h):
                t_.wait_ge(ss.h, ss.n)
                t_.wait_ge(vs.h, vs.n)
                for kc in range(2):
                    t_.matmul(psC[0:64, 0:256], Vsel[:, kc * D + h * 64: kc * D + h * 64 + 64],
                              pTt[:, kc * 256:(kc + 1) * 256],
                              start=(kc == 0), stop=False)
                    t_.matmul(psC[64:65, 0:256], onesb[:, 0:1], pTt[:, kc * 256:(kc + 1) * 256],
                              start=False, stop=False)
                mm = t_.matmul(psC[0:64, 0:256], vrowb[0:1, h * 64: h * 64 + 64],
                               onesrowb[0:1, 0:256], start=False, stop=True)
                inc(mm, ts)

            @block.vector
            def _(v, h=h):
                v.wait_ge(ts.h, ts.n)
                inc(v.tensor_scalar(out=rdrow[0:1, 0:256], in0=psC[64:65, 0:256],
                                    scalar1=3840.0, op0=ALU.add, scalar2=None), vs)
                v.wait_ge(vs.h, vs.n)
                inc(v.reciprocal(rdrow[0:1, 256:512], rdrow[0:1, 0:256]), vs)

            @block.tensor
            def _(t_, h=h):
                t_.wait_ge(vs.h, vs.n)
                inc(t_.matmul(psB[0:64, 0:256], ones[0:1, 0:64], rdrow[0:1, 256:512],
                              start=True, stop=True), ts)

            @block.vector
            def _(v, h=h, jb=jb, base=base):
                v.wait_ge(ts.h, ts.n)
                inc(v.tensor_copy(rdbc[0:64, :], psB[0:64, 0:256]), vs)
                v.wait_ge(vs.h, vs.n)
                inc(v.tensor_tensor(out=outT[base:base + 64, jb * 256:(jb + 1) * 256],
                                    in0=psC[0:64, 0:256], in1=rdbc[0:64, :], op=ALU.mult), vs)

        if debug == "c5":
            @block.sync
            def _(sync):
                sync.wait_ge(gs.h, gs.n)
                sync.wait_ge(vs.h, vs.n)
                sync.wait_ge(ts.h, ts.n)
            return st

        @block.sync
        def _(sync):
            sync.wait_ge(vs.h, vs.n)
            sync.wait_ge(ts.h, ts.n)

        @block.gpsimd
        def _(g_):
            g_.wait_ge(vs.h, vs.n)
            g_.wait_ge(ts.h, ts.n)

        for m in range(8):
            @block.tensor
            def _(t_, m=m):
                t_.wait_ge(vs.h, vs.n)
                mm = None
                for c in range(8):
                    mm = t_.matmul(psA[:, 0:256], WoT[:, c * D + m * 128: c * D + m * 128 + 128],
                                   outT[:, c * 256:(c + 1) * 256],
                                   start=(c == 0), stop=(c == 7))
                inc(mm, ts)

            @block.vector
            def _(v, m=m):
                v.wait_ge(ts.h, ts.n)
                inc(v.tensor_scalar(out=Xsel[:, m * 256: m * 256 + 256], in0=psA[:, 0:256],
                                    scalar1=bout_col[:, m:m + 1], op0=ALU.add, scalar2=None), vs)

        for m in range(8):
            for sb_ in range(2):
                @block.tensor
                def _(t_, m=m, sb_=sb_):
                    t_.wait_ge(vs.h, vs.n)
                    inc(t_.transpose(psTb[:, 128:256],
                                     Xsel[:, m * 256 + sb_ * 128: m * 256 + sb_ * 128 + 128],
                                     idTb[:]), ts)

                @block.vector
                def _(v, m=m, sb_=sb_):
                    v.wait_ge(ts.h, ts.n)
                    inc(v.tensor_copy(Vsel[:, sb_ * D + m * 128: sb_ * D + (m + 1) * 128],
                                      psTb[:, 128:256]), vs)

        @block.vector
        def _(v):
            v.wait_ge(vs.h, vs.n)
            inc(v.tensor_scalar(out=vrow[0:1, 2 * D:3 * D], in0=vrow[0:1, D:2 * D],
                                scalar1=1.0 / T, op0=ALU.mult, scalar2=None), vs)

        @block.tensor
        def _(t_):
            t_.wait_ge(vs.h, vs.n)
            mm = None
            for c in range(8):
                mm = t_.transpose(psB[:, c:c + 1], vrow[0:1, 2 * D + c * 128: 2 * D + (c + 1) * 128],
                                  ones[0:1, 0:1])
            inc(mm, ts)

        @block.vector
        def _(v):
            v.wait_ge(ts.h, ts.n)
            inc(v.tensor_copy(xcolb[:, 8:16], psB[:, 0:8]), vs)

        @block.sync
        def _(sync):
            sync.wait_ge(vs.h, vs.n)
            inc(sync.dma_start(out=rowbuf[0:1, 0:D], in_=wts["b_out"][None, :]), ld, 16)

        @block.tensor
        def _(t_):
            t_.wait_ge(vs.h, vs.n)
            mm = None
            for c in range(8):
                for hf in range(2):
                    mm = t_.matmul(psA[0:1, hf * 512:(hf + 1) * 512], xcolb[:, 8 + c:9 + c],
                                   WoT[:, c * D + hf * 512: c * D + hf * 512 + 512],
                                   start=(c == 0), stop=(c == 7))
            inc(mm, ts)

        @block.vector
        def _(v):
            v.wait_ge(ts.h, ts.n)
            v.wait_ge(ld.h, ld.n)
            inc(v.tensor_tensor(out=vrow[0:1, 3 * D:4 * D], in0=psA[0:1, 0:1024],
                                in1=rowbuf[0:1, 0:D], op=ALU.add), vs)

        @block.tensor
        def _(t_):
            t_.wait_ge(vs.h, vs.n)
            t_.matmul(psA[:, 0:512], ones[0:1, 0:128], vrow[0:1, 3 * D:3 * D + 512],
                      start=True, stop=True)
            inc(t_.matmul(psA[:, 512:1024], ones[0:1, 0:128], vrow[0:1, 3 * D + 512:4 * D],
                          start=True, stop=True), ts)

        @block.vector
        def _(v):
            v.wait_ge(ts.h, ts.n)
            inc(v.tensor_copy(yuni_bc[:], psA[:, 0:1024]), vs)

        rsu = emit_ln_stats(block, yuni_bc[:], stats[:, 0:8], scrap, vs, ss)
        emit_ln_apply(block, yuni_bc[:], scrap[:, D:2 * D], stats[:, 0:8],
                      scrap, vs, ss, rsu)

        corr_gs, corr_ap = {}, {}
        for sb_ in range(2):
            @block.gpsimd
            def _(g_, sb_=sb_):
                g_.wait_ge(vs.h, vs.n)
                inc(g_.indirect_dma_start(
                    out=xtp[:, sb_ * D:(sb_ + 1) * D], in_=x_in[:, :],
                    out_offset=None,
                    in_offset=IOA(ap=idx_sb[:, sb_:sb_ + 1], axis=0)), gs, 16)
            corr_gs[sb_] = gs.n
        corr_sp = {}
        for sb_ in range(2):
            rsc = emit_ln_stats(block, Vsel[:, sb_ * D:(sb_ + 1) * D],
                                stats[:, 8:16], scrap, vs, ss)

            @block.vector
            def _(v, sb_=sb_):
                v.wait_ge(gs.h, corr_gs[sb_])
            emit_ln_apply(block, Vsel[:, sb_ * D:(sb_ + 1) * D],
                          xop[:, sb_ * D:(sb_ + 1) * D], stats[:, 8:16],
                          scrap, vs, ss, rsc,
                          add_ap=xtp[:, sb_ * D:(sb_ + 1) * D])
            corr_ap[sb_] = vs.n

            @block.gpsimd
            def _(g_, sb_=sb_):
                g_.wait_ge(vs.h, corr_ap[sb_])
                inc(g_.dma_start(out=xcorr_dram[sb_ * 128:(sb_ + 1) * 128, :],
                                 in_=xop[:, sb_ * D:(sb_ + 1) * D]), gs, 16)
            corr_sp[sb_] = gs.n

        if debug:
            @block.gpsimd
            def _(g_):
                g_.wait_ge(gs.h, gs.n)
                inc(g_.dma_start(out=dbg_ysel.rearrange("(b p) d -> p b d", p=128),
                                 in_=Vsel[:].rearrange("p (b d) -> p b d", d=D)), gs, 16)

    @block.sync
    def _(sync):
        sync.wait_ge(vs.h, vs.n)
        sync.wait_ge(ts.h, ts.n)

    @block.gpsimd
    def _(g_):
        g_.wait_ge(vs.h, vs.n)
        g_.wait_ge(ts.h, ts.n)

    c7_ld, c7_ap, c7_st = {}, {}, {}
    corr_done = corr_ap[1]

    def c7_load(t):
        @block.sync
        def _(sync, t=t):
            if t - 2 in c7_ap:
                sync.wait_ge(vs.h, c7_ap[t - 2])
            else:
                sync.wait_ge(vs.h, corr_done)
            inc(sync.dma_start(out=xtp[:, (t % 2) * D:(t % 2 + 1) * D],
                               in_=x_in[t * 128:(t + 1) * 128, :]), ld, 16)
        c7_ld[t] = ld.n

    def c7_apply(t):
        @block.vector
        def _(v, t=t):
            v.wait_ge(ld.h, c7_ld[t])
            if t - 2 in c7_st:
                v.wait_ge(st.h, c7_st[t - 2])
            else:
                v.wait_ge(gs.h, corr_sp[t % 2])
            inc(v.tensor_tensor(out=xop[:, (t % 2) * D:(t % 2 + 1) * D],
                                in0=xtp[:, (t % 2) * D:(t % 2 + 1) * D],
                                in1=scrap[:, D:2 * D], op=ALU.add), vs)
        c7_ap[t] = vs.n

        @block.sync
        def _(sync, t=t):
            sync.wait_ge(vs.h, c7_ap[t])
            inc(sync.dma_start(out=x_out[t * 128:(t + 1) * 128, :],
                               in_=xop[:, (t % 2) * D:(t % 2 + 1) * D]), st, 16)
        c7_st[t] = st.n

    c7_load(0)
    c7_load(1)
    for t in range(NT):
        c7_apply(t)
        if t + 2 < NT:
            c7_load(t + 2)

    @block.sync
    def _(sync):
        sync.wait_ge(vs.h, c7_ap[NT - 1])
        inc(sync.dma_start(out=xtp[:, 0:D], in_=xcorr_dram[0:128, :]), ld, 16)
        inc(sync.dma_start(out=xtp[:, D:2 * D], in_=xcorr_dram[128:256, :]), ld, 16)

    @block.gpsimd
    def _(g_):
        g_.wait_ge(st.h, c7_st[NT - 1])
        g_.wait_ge(ld.h, ld.n)
        for sb_ in range(2):
            inc(g_.indirect_dma_start(
                out=x_out[:, :], in_=xtp[:, sb_ * D:(sb_ + 1) * D],
                out_offset=IOA(ap=idx_sb[:, sb_:sb_ + 1], axis=0),
                in_offset=None), gs, 16)

    @block.sync
    def _(sync):
        sync.wait_ge(st.h, st.n)
        sync.wait_ge(gs.h, gs.n)

    return st



def build_prog_c(debug=False):
    nc = bass.Bass("TRN2", target_bir_lowering=False)
    x_in = nc.dram_tensor("x_in", [T, D], F32, kind="ExternalInput").ap()
    wts = {
        "w_qkv": nc.dram_tensor("w_qkv", [3 * D, D], F32, kind="ExternalInput").ap(),
        "w_out": nc.dram_tensor("w_out", [D, D], F32, kind="ExternalInput").ap(),
        "b_out": nc.dram_tensor("b_out", [D], F32, kind="ExternalInput").ap(),
        "ln_g": nc.dram_tensor("ln_g", [64], F32, kind="ExternalInput").ap(),
        "ln_b": nc.dram_tensor("ln_b", [64], F32, kind="ExternalInput").ap(),
        "w1": nc.dram_tensor("w1", [16, 64], F32, kind="ExternalInput").ap(),
        "b1": nc.dram_tensor("b1", [16], F32, kind="ExternalInput").ap(),
        "w2": nc.dram_tensor("w2", [1, 16], F32, kind="ExternalInput").ap(),
        "b2": nc.dram_tensor("b2", [1], F32, kind="ExternalInput").ap(),
    }
    x_out = nc.dram_tensor("x_out", [T, D], F32, kind="ExternalOutput").ap()
    with ExitStack() as ctx:
        block = ctx.enter_context(nc.Block())
        emit_stage_c(nc, ctx, block, x_in, wts, x_out, debug=debug)
    return nc


def build_prog_a():
    nc = bass.Bass("TRN2", target_bir_lowering=False)
    x_in = nc.dram_tensor("x_in", [T, D], F32, kind="ExternalInput").ap()
    x_out = nc.dram_tensor("x_out", [T, D], F32, kind="ExternalOutput").ap()
    with ExitStack() as ctx:
        block = ctx.enter_context(nc.Block())
        emit_stage_a(nc, ctx, block, x_in, x_out)
    return nc


def emit_stage_barrier(block, sems):
    for section in [block.sync, block.gpsimd, block.vector, block.scalar, block.tensor]:
        @section
        def _(e):
            for s in sems:
                e.wait_ge(s.h, s.n)


def build_prog_fused():
    nc = bass.Bass("TRN2", target_bir_lowering=False)
    x_in = nc.dram_tensor("x_in", [T, D], F32, kind="ExternalInput").ap()
    w0 = nc.dram_tensor("w_in", [D, D], F32, kind="ExternalInput").ap()
    h_in = nc.dram_tensor("h_in", [D, D], BF16, kind="ExternalInput").ap()
    wc = {
        "w_qkv": nc.dram_tensor("w_qkv", [3 * D, D], F32, kind="ExternalInput").ap(),
        "w_out": nc.dram_tensor("w_out", [D, D], F32, kind="ExternalInput").ap(),
        "b_out": nc.dram_tensor("b_out", [D], F32, kind="ExternalInput").ap(),
        "ln_g": nc.dram_tensor("ln_g", [64], F32, kind="ExternalInput").ap(),
        "ln_b": nc.dram_tensor("ln_b", [64], F32, kind="ExternalInput").ap(),
        "w1": nc.dram_tensor("w1", [16, 64], F32, kind="ExternalInput").ap(),
        "b1": nc.dram_tensor("b1", [16], F32, kind="ExternalInput").ap(),
        "w2": nc.dram_tensor("w2", [1, 16], F32, kind="ExternalInput").ap(),
        "b2": nc.dram_tensor("b2", [1], F32, kind="ExternalInput").ap(),
    }
    wd = {
        "wq": nc.dram_tensor("wq", [D, D], F32, kind="ExternalInput").ap(),
        "wk": nc.dram_tensor("wk", [256, D], F32, kind="ExternalInput").ap(),
        "wv": nc.dram_tensor("wv", [256, D], F32, kind="ExternalInput").ap(),
        "wo": nc.dram_tensor("wo", [D, D], F32, kind="ExternalInput").ap(),
        "gate_w": nc.dram_tensor("gate_w", [8, D], F32, kind="ExternalInput").ap(),
        "expert_w": nc.dram_tensor("expert_w", [8, D, D], F32, kind="ExternalInput").ap(),
        "anw": nc.dram_tensor("anw", [D], F32, kind="ExternalInput").ap(),
        "fnw": nc.dram_tensor("fnw", [D], F32, kind="ExternalInput").ap(),
        "masks": nc.dram_tensor("masks", [4 * 128, 512], BF16, kind="ExternalInput").ap(),
    }
    x_out = nc.dram_tensor("x_out", [T, D], F32, kind="ExternalOutput").ap()
    x1d = nc.dram_tensor("x1d", [T, D], F32, kind="Internal").ap()
    x2d = nc.dram_tensor("x2d", [T, D], F32, kind="Internal").ap()
    x3d = nc.dram_tensor("x3d", [T, D], F32, kind="Internal").ap()

    with ExitStack() as ctx:
        block = ctx.enter_context(nc.Block())
        sems = mk_sems(nc, ctx, ["ld", "vs", "ss", "ts", "gs", "trs", "st"], "f_")
        with ExitStack() as sctx:
            emit_stage_a(nc, sctx, block, x_in, x1d, sems=sems)
        emit_stage_barrier(block, sems)
        with ExitStack() as sctx:
            emit_stage_b(nc, sctx, block, x1d, w0, h_in, x2d, sems=sems)
        emit_stage_barrier(block, sems)
        with ExitStack() as sctx:
            emit_stage_c(nc, sctx, block, x2d, wc, x3d, sems=sems)
        emit_stage_barrier(block, sems)
        with ExitStack() as sctx:
            emit_stage_d(nc, sctx, block, x3d, wd, x_out, sems=sems)
    return nc


def kernel(L0_attn_x, L0_ffn_weight, L1_attn_w_qkv, L1_attn_w_out, L1_attn_b_out,
           L1_attn_ln_g, L1_attn_ln_b, L1_attn_w1, L1_attn_b1, L1_attn_w2,
           L1_attn_b2, L1_ffn_attn_norm_w, L1_ffn_ffn_norm_w, L1_ffn_wq,
           L1_ffn_wk, L1_ffn_wv, L1_ffn_wo, L1_ffn_gate_w, L1_ffn_expert_w):
    f32 = lambda a: np.ascontiguousarray(np.asarray(a, dtype=np.float32))
    x = f32(L0_attn_x)
    nc = build_prog_fused()
    shared = {
        "w_in": f32(L0_ffn_weight), "h_in": hadamard_bf16(),
        "w_qkv": f32(L1_attn_w_qkv), "w_out": f32(L1_attn_w_out),
        "b_out": f32(L1_attn_b_out), "ln_g": f32(L1_attn_ln_g),
        "ln_b": f32(L1_attn_ln_b), "w1": f32(L1_attn_w1), "b1": f32(L1_attn_b1),
        "w2": f32(L1_attn_w2), "b2": f32(L1_attn_b2),
        "wq": f32(L1_ffn_wq), "wk": f32(L1_ffn_wk), "wv": f32(L1_ffn_wv),
        "wo": f32(L1_ffn_wo), "gate_w": f32(L1_ffn_gate_w),
        "expert_w": f32(L1_ffn_expert_w), "anw": f32(L1_ffn_attn_norm_w),
        "fnw": f32(L1_ffn_ffn_norm_w), "masks": causal_masks_bf16(),
    }
    res = run_bass_kernel_spmd(nc, [dict(shared, x_in=x[b]) for b in range(8)],
                               core_ids=list(range(8)))
    return np.stack([np.asarray(res.results[b]["x_out"]) for b in range(8)], axis=0)


# version 09 (final)
# speedup vs baseline: 1.2008x; 1.0034x over previous
import numpy as np
from contextlib import ExitStack

import concourse.bass as bass
import concourse.mybir as mybir
from concourse.bass_utils import run_bass_kernel_spmd

F32 = mybir.dt.float32
BF16 = mybir.dt.bfloat16
U32 = mybir.dt.uint32
AF = mybir.ActivationFunctionType
ALU = mybir.AluOpType
AX = mybir.AxisListType

T = 4096
D = 1024
NT = T // 128
ND = D // 128
LN_EPS = 1e-5


class Sem:

    def __init__(self, handle):
        self.h = handle
        self.n = 0


def inc(instr, sem, k=1):
    sem.n += k
    return instr.then_inc(sem.h, k)


def mk_sems(nc, ctx, names, pfx=""):
    return [Sem(ctx.enter_context(nc.semaphore(pfx + n))) for n in names]


def emit_ln_tile(block, src_ap, dst_ap, stats, scrap, vs, ss,
                 add_ap=None, width=D):
    w = width

    @block.vector
    def _(v):
        v.tensor_scalar(out=scrap[:, 0:w], in0=src_ap, scalar1=0.0, op0=ALU.add,
                        scalar2=0.0, op1=ALU.add, accum_out=stats[:, 0:1])
        inc(v.scalar_tensor_tensor(out=scrap[:, w:2 * w], in0=src_ap, scalar=1.0,
                                   op0=ALU.mult, op1=ALU.mult, in1=src_ap,
                                   accum_out=stats[:, 1:2]), vs)
        v.wait_ge(vs.h, vs.n)
        v.tensor_scalar(out=stats[:, 2:3], in0=stats[:, 0:1], scalar1=1.0 / w,
                        op0=ALU.mult, scalar2=None)
        inc(v.tensor_scalar(out=stats[:, 4:5], in0=stats[:, 1:2], scalar1=1.0 / w,
                            op0=ALU.mult, scalar2=None), vs)
        v.wait_ge(vs.h, vs.n)
        inc(v.tensor_tensor(out=stats[:, 5:6], in0=stats[:, 2:3], in1=stats[:, 2:3],
                            op=ALU.mult), vs)
        v.wait_ge(vs.h, vs.n)
        inc(v.tensor_scalar(out=stats[:, 4:5], in0=stats[:, 4:5], scalar1=stats[:, 5:6],
                            op0=ALU.subtract, scalar2=LN_EPS, op1=ALU.add), vs)

    @block.scalar
    def _(s):
        s.wait_ge(vs.h, vs.n)
        inc(s.activation(out=stats[:, 5:6], in_=stats[:, 4:5], func=AF.Ln), ss)
        s.wait_ge(ss.h, ss.n)
        inc(s.activation(out=stats[:, 3:4], in_=stats[:, 5:6], func=AF.Exp, scale=-0.5), ss)

    @block.vector
    def _(v):
        v.wait_ge(ss.h, ss.n)
        if add_ap is None:
            op = v.tensor_scalar(out=dst_ap, in0=src_ap, scalar1=stats[:, 2:3],
                                 op0=ALU.subtract, scalar2=stats[:, 3:4], op1=ALU.mult)
        else:
            v.tensor_scalar(out=scrap[:, 0:w], in0=src_ap, scalar1=stats[:, 2:3],
                            op0=ALU.subtract, scalar2=stats[:, 3:4], op1=ALU.mult)
            op = v.tensor_tensor(out=dst_ap, in0=scrap[:, 0:w], in1=add_ap, op=ALU.add)
        inc(op, vs)


def emit_ln_stats(block, src_ap, st8, scrap, vs, ss, width=D):
    w = width

    @block.vector
    def _(v):
        v.tensor_scalar(out=scrap[:, 0:w], in0=src_ap, scalar1=0.0, op0=ALU.add,
                        scalar2=0.0, op1=ALU.add, accum_out=st8[:, 0:1])
        inc(v.scalar_tensor_tensor(out=scrap[:, 0:w], in0=src_ap, scalar=1.0,
                                   op0=ALU.mult, op1=ALU.mult, in1=src_ap,
                                   accum_out=st8[:, 1:2]), vs)
        v.wait_ge(vs.h, vs.n)
        v.tensor_scalar(out=st8[:, 2:3], in0=st8[:, 0:1], scalar1=1.0 / w,
                        op0=ALU.mult, scalar2=None)
        inc(v.tensor_scalar(out=st8[:, 4:5], in0=st8[:, 1:2], scalar1=1.0 / w,
                            op0=ALU.mult, scalar2=None), vs)
        v.wait_ge(vs.h, vs.n)
        inc(v.tensor_tensor(out=st8[:, 5:6], in0=st8[:, 2:3], in1=st8[:, 2:3],
                            op=ALU.mult), vs)
        v.wait_ge(vs.h, vs.n)
        inc(v.tensor_scalar(out=st8[:, 4:5], in0=st8[:, 4:5], scalar1=st8[:, 5:6],
                            op0=ALU.subtract, scalar2=LN_EPS, op1=ALU.add), vs)

    @block.scalar
    def _(s):
        s.wait_ge(vs.h, vs.n)
        inc(s.activation(out=st8[:, 5:6], in_=st8[:, 4:5], func=AF.Ln), ss)
        s.wait_ge(ss.h, ss.n)
        inc(s.activation(out=st8[:, 3:4], in_=st8[:, 5:6], func=AF.Exp, scale=-0.5), ss)
    return ss.n


def emit_ln_apply(block, src_ap, dst_ap, st8, scrap, vs, ss, rstd_ready,
                  add_ap=None, width=D):
    w = width

    @block.vector
    def _(v):
        v.wait_ge(ss.h, rstd_ready)
        if add_ap is None:
            op = v.tensor_scalar(out=dst_ap, in0=src_ap, scalar1=st8[:, 2:3],
                                 op0=ALU.subtract, scalar2=st8[:, 3:4], op1=ALU.mult)
        else:
            v.tensor_scalar(out=scrap[:, 0:w], in0=src_ap, scalar1=st8[:, 2:3],
                            op0=ALU.subtract, scalar2=st8[:, 3:4], op1=ALU.mult)
            op = v.tensor_tensor(out=dst_ap, in0=scrap[:, 0:w], in1=add_ap, op=ALU.add)
        inc(op, vs)


def emit_stage_a(nc, ctx, block, x_in, x_out, pfx="a_", sems=None):
    E = ctx.enter_context
    ld, vs, ss, ts, gs, trs, st = sems or mk_sems(
        nc, ctx, ["ld", "vs", "ss", "ts", "gs", "trs", "st"], pfx)

    aseg_dram = nc.dram_tensor(pfx + "segbf", [512, D], BF16, kind="Internal").ap()

    xs = E(nc.sbuf_tensor(pfx + "xs", [128, 4 * D], F32))
    aseg = E(nc.sbuf_tensor(pfx + "aseg", [128, 4 * D], BF16))
    asegT = E(nc.sbuf_tensor(pfx + "asegT", [128, 8 * 512], BF16))
    pT = E(nc.sbuf_tensor(pfx + "pT", [128, 4 * 512], BF16))
    stats = E(nc.sbuf_tensor(pfx + "stats", [128, 16], F32))
    scrap = E(nc.sbuf_tensor(pfx + "scrap", [128, 3 * D], F32))
    ones = E(nc.sbuf_tensor(pfx + "ones", [128, 2], BF16))
    id1 = E(nc.sbuf_tensor(pfx + "id1", [1, 2], F32))
    rrow = E(nc.sbuf_tensor(pfx + "rrow", [1, 512], F32))
    attn = E(nc.sbuf_tensor(pfx + "attn", [128, 4 * D], F32))
    x1seg = E(nc.sbuf_tensor(pfx + "x1seg", [128, 4 * D], F32))

    psS = E(nc.psum_tensor(pfx + "psS", [128, 2 * 512], F32))
    psO = E(nc.psum_tensor(pfx + "psO", [128, 2 * 1024], F32))
    psden = E(nc.psum_tensor(pfx + "psden", [128, 512], F32))
    psdc = E(nc.psum_tensor(pfx + "psdc", [128, 8], F32))

    SEGS = [(g, r) for g in range(2) for r in range(4)]

    def seg_src(ap, g, r):
        base = g * 2048 + r
        return ap[base:base + 2045:4, :].rearrange("(t p) d -> p t d", p=128)

    x3 = xs[:].rearrange("p (t d) -> p t d", d=D)
    o3 = x1seg[:].rearrange("p (t d) -> p t d", d=D)

    @block.gpsimd
    def _(g_):
        g_.memset(ones[:], 1.0)
        g_.memset(id1[:], 0.0)
        inc(g_.memset(id1[0:1, 0:1], 1.0), gs)

    for i, (g, r) in enumerate(SEGS):
        @block.sync
        def _(sync, g=g, r=r):
            if i > 0:
                sync.wait_ge(st.h, st.n)
            inc(sync.dma_start(out=x3, in_=seg_src(x_in, g, r)), ld, 16)

        @block.vector
        def _(v):
            v.wait_ge(ld.h, ld.n)

        @block.scalar
        def _(s):
            s.wait_ge(ld.h, ld.n)

        for t in range(4):
            emit_ln_tile(block, xs[:, t * D:(t + 1) * D], aseg[:, t * D:(t + 1) * D],
                         stats, scrap, vs, ss)

        @block.gpsimd
        def _(g_):
            g_.wait_ge(vs.h, vs.n)
            inc(g_.dma_start(out=aseg_dram.rearrange("(t p) d -> p t d", p=128),
                             in_=aseg[:].rearrange("p (t d) -> p t d", d=D)), gs, 16)

        @block.scalar
        def _(s):
            s.wait_ge(gs.h, gs.n)
            for dc in range(8):
                inc(s.dma_start(out=asegT[:, dc * 512:(dc + 1) * 512],
                                in_=aseg_dram[:, dc * 128:(dc + 1) * 128],
                                transpose=True), trs, 16)

        for kc in range(4):
            sb = psS[:, (kc % 2) * 512:(kc % 2) * 512 + 512]

            @block.tensor
            def _(t_, kc=kc, sb=sb):
                t_.wait_ge(trs.h, trs.n)
                if kc >= 2:
                    t_.wait_ge(ss.h, ss.n)
                for dc in range(8):
                    mm = t_.matmul(sb, asegT[:, dc * 512 + kc * 128: dc * 512 + kc * 128 + 128],
                                   asegT[:, dc * 512:(dc + 1) * 512],
                                   start=(dc == 0), stop=(dc == 7))
                inc(mm, ts)

            @block.scalar
            def _(s, kc=kc, sb=sb):
                s.wait_ge(ts.h, ts.n)
                inc(s.activation(out=pT[:, kc * 512:(kc + 1) * 512], in_=sb,
                                 func=AF.Exp, scale=1.0 / 32.0), ss)

        @block.tensor
        def _(t_):
            t_.wait_ge(ss.h, ss.n)
            for kc in range(4):
                mm = t_.matmul(psden[0:1, :], ones[:, 0:1], pT[:, kc * 512:(kc + 1) * 512],
                               start=(kc == 0), stop=(kc == 3))
            inc(mm, ts)

        @block.vector
        def _(v):
            v.wait_ge(ts.h, ts.n)
            inc(v.reciprocal(rrow[0:1, :], psden[0:1, :]), vs)

        @block.tensor
        def _(t_):
            t_.wait_ge(vs.h, vs.n)
            for qc in range(4):
                mm = t_.transpose(psdc[:, qc:qc + 1], rrow[0:1, qc * 128:(qc + 1) * 128],
                                  id1[0:1, 0:1])
            inc(mm, ts)

        for qc in range(4):
            ob = psO[:, (qc % 2) * 1024:(qc % 2) * 1024 + 1024]

            @block.tensor
            def _(t_, qc=qc, ob=ob):
                if qc >= 2:
                    t_.wait_ge(vs.h, vs.n)
                for kc in range(4):
                    for h in range(2):
                        mm = t_.matmul(ob[:, h * 512:(h + 1) * 512],
                                       pT[:, kc * 512 + qc * 128: kc * 512 + qc * 128 + 128],
                                       aseg[:, kc * D + h * 512: kc * D + h * 512 + 512],
                                       start=(kc == 0), stop=(kc == 3))
                inc(mm, ts)

            @block.vector
            def _(v, qc=qc, ob=ob):
                v.wait_ge(ts.h, ts.n)
                inc(v.tensor_scalar(out=attn[:, qc * D:(qc + 1) * D], in0=ob,
                                    scalar1=psdc[:, qc:qc + 1], op0=ALU.mult,
                                    scalar2=0.25, op1=ALU.mult), vs)

        @block.scalar
        def _(s):
            s.wait_ge(vs.h, vs.n)

        for t in range(4):
            emit_ln_tile(block, attn[:, t * D:(t + 1) * D], x1seg[:, t * D:(t + 1) * D],
                         stats, scrap, vs, ss, add_ap=xs[:, t * D:(t + 1) * D])

        @block.sync
        def _(sync, g=g, r=r):
            sync.wait_ge(vs.h, vs.n)
            inc(sync.dma_start(out=seg_src(x_out, g, r), in_=o3), st, 16)

    @block.sync
    def _(sync):
        sync.wait_ge(st.h, st.n)

    return st


MAGIC = 12582912.0


def emit_stage_b(nc, ctx, block, x_in, w_in, h_in, x_out, pfx="b_", debug=False, sems=None):
    E = ctx.enter_context
    ld, vs, ss, ts, gs, trs, st = sems or mk_sems(
        nc, ctx, ["ld", "vs", "ss", "ts", "gs", "trs", "st"], pfx)

    kind = "ExternalOutput" if debug else "Internal"
    fdram = nc.dram_tensor(pfx + "fdram", [T, D], BF16, kind=kind).ap()
    kdram = nc.dram_tensor(pfx + "kdram", [T, D], BF16, kind=kind).ap()
    wqdram = nc.dram_tensor(pfx + "wqdram", [D, D], BF16, kind=kind).ap()
    if debug:
        dbg_ffn = nc.dram_tensor(pfx + "dbg_ffn", [T, D], F32, kind="ExternalOutput").ap()
        dbg_scl = nc.dram_tensor(pfx + "dbg_scl", [128, NT + 8], F32, kind="ExternalOutput").ap()
        dbg_xabs = nc.dram_tensor(pfx + "dbg_xabs", [128, D], F32, kind="ExternalOutput").ap()
        dbg_xh = nc.dram_tensor(pfx + "dbg_xh", [128, D], F32, kind="ExternalOutput").ap()

    finT = E(nc.sbuf_tensor(pfx + "finT", [128, 8 * T], BF16))
    hsb = E(nc.sbuf_tensor(pfx + "hsb", [128, 8 * D], BF16))
    wsb = E(nc.sbuf_tensor(pfx + "wsb", [128, 8 * D], F32))
    wqv = E(nc.sbuf_tensor(pfx + "wqv", [128, 8 * D], BF16))
    wqvT = E(nc.sbuf_tensor(pfx + "wqvT", [128, 8 * D], BF16))
    kqT = finT
    scales = E(nc.sbuf_tensor(pfx + "scales", [128, NT + 8], F32))
    stats = E(nc.sbuf_tensor(pfx + "stats", [128, 16], F32))
    scrap = E(nc.sbuf_tensor(pfx + "scrap", [128, 3 * D], F32))
    wstat = E(nc.sbuf_tensor(pfx + "wstat", [128, 16], F32))
    wtmp = E(nc.sbuf_tensor(pfx + "wtmp", [128, 2 * D], F32))

    psA = E(nc.psum_tensor(pfx + "psA", [128, 2 * D], F32))
    psB = E(nc.psum_tensor(pfx + "psB", [128, 2 * D], F32))

    @block.sync
    def _(sync):
        inc(sync.dma_start(out=hsb[:].rearrange("p (c d) -> p c d", d=D),
                           in_=h_in.rearrange("(c p) d -> p c d", p=128)), ld, 16)
        inc(sync.dma_start(out=wsb[:].rearrange("p (c d) -> p c d", d=D),
                           in_=w_in.rearrange("(c p) d -> p c d", p=128)), ld, 16)

    @block.scalar
    def _(s):
        s.wait_ge(ld.h, ld.n)
        for c in range(8):
            inc(s.activation(out=wtmp[:, 0:D], in_=wsb[:, c * D:(c + 1) * D],
                             func=AF.Abs, accum_out=wstat[:, c:c + 1]), ss)

    @block.vector
    def _(v):
        v.wait_ge(ss.h, ss.n)
        for c in range(8):
            inc(v.tensor_scalar(out=wstat[:, c:c + 1], in0=wstat[:, c:c + 1],
                                scalar1=1.0 / D, op0=ALU.mult, scalar2=1e-5, op1=ALU.max), vs)
        v.wait_ge(vs.h, vs.n)
        for c in range(8):
            inc(v.reciprocal(wstat[:, 8 + c:9 + c], wstat[:, c:c + 1]), vs)
        v.wait_ge(vs.h, vs.n)
        for c in range(8):
            w_ = wsb[:, c * D:(c + 1) * D]
            v.tensor_scalar(out=wtmp[:, 0:D], in0=w_, scalar1=wstat[:, 8 + c:9 + c],
                            op0=ALU.mult, scalar2=0.5, op1=ALU.is_gt)
            v.tensor_scalar(out=wtmp[:, D:2 * D], in0=w_, scalar1=wstat[:, 8 + c:9 + c],
                            op0=ALU.mult, scalar2=-0.5, op1=ALU.is_lt)
            v.tensor_tensor(out=wtmp[:, 0:D], in0=wtmp[:, 0:D], in1=wtmp[:, D:2 * D],
                            op=ALU.subtract)
            inc(v.tensor_scalar(out=wqv[:, c * D:(c + 1) * D], in0=wtmp[:, 0:D],
                                scalar1=wstat[:, c:c + 1], op0=ALU.mult, scalar2=None), vs)

    @block.gpsimd
    def _(g_):
        g_.wait_ge(vs.h, vs.n)
        inc(g_.dma_start(out=wqdram.rearrange("(c p) d -> p c d", p=128),
                         in_=wqv[:].rearrange("p (c d) -> p c d", d=D)), gs, 16)

    @block.scalar
    def _(s):
        s.wait_ge(gs.h, gs.n)
        for c in range(8):
            inc(s.dma_start(out=wqvT[:, c * D:(c + 1) * D],
                            in_=wqdram[:, c * 128:(c + 1) * 128], transpose=True), trs, 16)

    xtp = E(nc.sbuf_tensor(pfx + "xtp", [128, 2 * D], F32))
    fbp = E(nc.sbuf_tensor(pfx + "fbp", [128, 2 * D], BF16))
    ld_at, sp_at, rstd_at, ap_at = {}, {}, {}, {}

    def b1_load(t):
        @block.sync
        def _(sync, t=t):
            if t - 2 in ap_at:
                sync.wait_ge(vs.h, ap_at[t - 2])
            inc(sync.dma_start(out=xtp[:, (t % 2) * D:(t % 2 + 1) * D],
                               in_=x_in[t * 128:(t + 1) * 128, :]), ld, 16)
        ld_at[t] = ld.n

    def b1_stats(t):
        @block.vector
        def _(v, t=t):
            v.wait_ge(ld.h, ld_at[t])
        rstd_at[t] = emit_ln_stats(block, xtp[:, (t % 2) * D:(t % 2 + 1) * D],
                                   stats[:, (t % 2) * 8:(t % 2) * 8 + 8], scrap, vs, ss)

    def b1_apply(t):
        if t - 2 in sp_at:
            @block.vector
            def _(v, t=t):
                v.wait_ge(gs.h, sp_at[t - 2])
        emit_ln_apply(block, xtp[:, (t % 2) * D:(t % 2 + 1) * D],
                      fbp[:, (t % 2) * D:(t % 2 + 1) * D],
                      stats[:, (t % 2) * 8:(t % 2) * 8 + 8], scrap, vs, ss, rstd_at[t])
        ap_at[t] = vs.n

        @block.gpsimd
        def _(g_, t=t):
            g_.wait_ge(vs.h, ap_at[t])
            inc(g_.dma_start(out=fdram[t * 128:(t + 1) * 128, :],
                             in_=fbp[:, (t % 2) * D:(t % 2 + 1) * D]), gs, 16)
        sp_at[t] = gs.n

    b1_load(0)
    b1_stats(0)
    b1_load(1)
    for t in range(NT):
        if t + 1 < NT:
            b1_stats(t + 1)
        b1_apply(t)
        if t + 2 < NT:
            b1_load(t + 2)

    @block.scalar
    def _(s):
        s.wait_ge(gs.h, gs.n)
        for c in range(8):
            inc(s.dma_start(out=finT[:, c * T:(c + 1) * T],
                            in_=fdram[:, c * 128:(c + 1) * 128], transpose=True), trs, 16)

    mm3_at, abs3_at, red3_at, q3_at, sp3_at = {}, {}, {}, {}, {}
    xabs2 = xtp
    kq2 = fbp
    for t in range(NT):
        pa = psA[:, (t % 2) * D:(t % 2) * D + D]

        @block.tensor
        def _(t_, t=t, pa=pa):
            t_.wait_ge(trs.h, trs.n)
            if t - 2 in q3_at:
                t_.wait_ge(ss.h, abs3_at[t - 2])
                t_.wait_ge(vs.h, q3_at[t - 2])
            mm = None
            for half in range(2):
                for c in range(8):
                    mm = t_.matmul(pa[:, half * 512:(half + 1) * 512],
                                   finT[:, c * T + t * 128: c * T + t * 128 + 128],
                                   hsb[:, c * D + half * 512: c * D + half * 512 + 512],
                                   start=(c == 0), stop=(c == 7))
            inc(mm, ts)
        mm3_at[t] = ts.n

        @block.scalar
        def _(s, t=t, pa=pa):
            s.wait_ge(ts.h, mm3_at[t])
            if t - 2 in red3_at:
                s.wait_ge(vs.h, red3_at[t - 2])
            inc(s.activation(out=xabs2[:, (t % 2) * D:(t % 2 + 1) * D], in_=pa,
                             func=AF.Abs, accum_out=None), ss)
        abs3_at[t] = ss.n

        @block.vector
        def _(v, t=t, pa=pa):
            st8 = stats[:, (t % 2) * 8:(t % 2) * 8 + 8]
            v.wait_ge(ss.h, abs3_at[t])
            inc(v.reduce_max(st8[:, 0:1], xabs2[:, (t % 2) * D:(t % 2 + 1) * D],
                             axis=AX.X), vs)
            red3_at[t] = vs.n
            v.wait_ge(vs.h, vs.n)
            inc(v.tensor_scalar(out=scales[:, t:t + 1], in0=st8[:, 0:1],
                                scalar1=1e-5, op0=ALU.max, scalar2=1.0 / 7.0, op1=ALU.mult), vs)
            v.wait_ge(vs.h, vs.n)
            inc(v.reciprocal(st8[:, 1:2], scales[:, t:t + 1]), vs)
            v.wait_ge(vs.h, vs.n)
            if t - 2 in sp3_at:
                v.wait_ge(gs.h, sp3_at[t - 2])
            v.tensor_scalar(out=scrap[:, 0:D], in0=pa, scalar1=st8[:, 1:2],
                            op0=ALU.mult, scalar2=MAGIC, op1=ALU.add)
            inc(v.tensor_scalar(out=kq2[:, (t % 2) * D:(t % 2 + 1) * D], in0=scrap[:, 0:D],
                                scalar1=MAGIC, op0=ALU.subtract, scalar2=None), vs)
        q3_at[t] = vs.n

        @block.gpsimd
        def _(g_, t=t):
            g_.wait_ge(vs.h, q3_at[t])
            inc(g_.dma_start(out=kdram[t * 128:(t + 1) * 128, :],
                             in_=kq2[:, (t % 2) * D:(t % 2 + 1) * D]), gs, 16)
        sp3_at[t] = gs.n


    @block.scalar
    def _(s):
        s.wait_ge(gs.h, gs.n)
        for c in range(8):
            inc(s.dma_start(out=kqT[:, c * T:(c + 1) * T],
                            in_=kdram[:, c * 128:(c + 1) * 128], transpose=True), trs, 16)

    mm5_at, ev5_at, rs5_at, ap5_at, st5_at, ld5_at = {}, {}, {}, {}, {}, {}
    ffp = xtp
    x1p = E(nc.sbuf_tensor(pfx + "x1p", [128, 2 * D], F32))
    xo5 = E(nc.sbuf_tensor(pfx + "xo5", [128, 2 * D], F32))

    def b5_load(t):
        @block.sync
        def _(sync, t=t):
            if t - 2 in ap5_at:
                sync.wait_ge(vs.h, ap5_at[t - 2])
            inc(sync.dma_start(out=x1p[:, (t % 2) * D:(t % 2 + 1) * D],
                               in_=x_in[t * 128:(t + 1) * 128, :]), ld, 16)
        ld5_at[t] = ld.n

    def b5_mm(t):
        pb = psB[:, (t % 2) * D:(t % 2) * D + D]

        @block.tensor
        def _(t_, t=t, pb=pb):
            t_.wait_ge(trs.h, trs.n)
            if t - 2 in ev5_at:
                t_.wait_ge(vs.h, ev5_at[t - 2])
            mm = None
            for half in range(2):
                for c in range(8):
                    mm = t_.matmul(pb[:, half * 512:(half + 1) * 512],
                                   kqT[:, c * T + t * 128: c * T + t * 128 + 128],
                                   wqvT[:, c * D + half * 512: c * D + half * 512 + 512],
                                   start=(c == 0), stop=(c == 7))
            inc(mm, ts)
        mm5_at[t] = ts.n

    def b5_evac_stats(t):
        pb = psB[:, (t % 2) * D:(t % 2) * D + D]

        @block.vector
        def _(v, t=t, pb=pb):
            v.wait_ge(ts.h, mm5_at[t])
            inc(v.tensor_scalar(out=ffp[:, (t % 2) * D:(t % 2 + 1) * D], in0=pb,
                                scalar1=scales[:, t:t + 1], op0=ALU.mult, scalar2=None), vs)
        ev5_at[t] = vs.n
        rs5_at[t] = emit_ln_stats(block, ffp[:, (t % 2) * D:(t % 2 + 1) * D],
                                  stats[:, (t % 2) * 8:(t % 2) * 8 + 8], scrap, vs, ss)

    def b5_apply(t):
        if t - 2 in st5_at:
            @block.vector
            def _(v, t=t):
                v.wait_ge(st.h, st5_at[t - 2])

        @block.vector
        def _(v, t=t):
            v.wait_ge(ld.h, ld5_at[t])
        emit_ln_apply(block, ffp[:, (t % 2) * D:(t % 2 + 1) * D],
                      xo5[:, (t % 2) * D:(t % 2 + 1) * D],
                      stats[:, (t % 2) * 8:(t % 2) * 8 + 8], scrap, vs, ss, rs5_at[t],
                      add_ap=x1p[:, (t % 2) * D:(t % 2 + 1) * D])
        ap5_at[t] = vs.n

        @block.sync
        def _(sync, t=t):
            sync.wait_ge(vs.h, ap5_at[t])
            inc(sync.dma_start(out=x_out[t * 128:(t + 1) * 128, :],
                               in_=xo5[:, (t % 2) * D:(t % 2 + 1) * D]), st, 16)
        st5_at[t] = st.n

    b5_load(0)
    b5_mm(0)
    b5_evac_stats(0)
    b5_load(1)
    b5_mm(1)
    for t in range(NT):
        if t + 1 < NT:
            b5_evac_stats(t + 1)
        b5_apply(t)
        if t + 2 < NT:
            b5_load(t + 2)
            b5_mm(t + 2)



    if debug:
        @block.gpsimd
        def _(g_):
            g_.wait_ge(vs.h, vs.n)
            inc(g_.dma_start(out=dbg_scl[:, :], in_=scales[:]), gs, 16)

    @block.sync
    def _(sync):
        sync.wait_ge(st.h, st.n)
        sync.wait_ge(gs.h, gs.n)

    return st


def hadamard_bf16():
    import ml_dtypes
    h = np.array([[1.0]], dtype=np.float64)
    while h.shape[0] < D:
        h = np.block([[h, h], [h, -h]])
    return (h / 32.0).astype(np.float32).astype(ml_dtypes.bfloat16)


def build_prog_b(debug=False):
    nc = bass.Bass("TRN2", target_bir_lowering=False)
    x_in = nc.dram_tensor("x_in", [T, D], F32, kind="ExternalInput").ap()
    w_in = nc.dram_tensor("w_in", [D, D], F32, kind="ExternalInput").ap()
    h_in = nc.dram_tensor("h_in", [D, D], BF16, kind="ExternalInput").ap()
    x_out = nc.dram_tensor("x_out", [T, D], F32, kind="ExternalOutput").ap()
    with ExitStack() as ctx:
        block = ctx.enter_context(nc.Block())
        emit_stage_b(nc, ctx, block, x_in, w_in, h_in, x_out, debug=debug)
    return nc


def qhead_of(tau, half):
    return (tau + 4 * half) if tau < 4 else (tau + 4 + 4 * half)


def emit_stage_d(nc, ctx, block, x_in, wts, x_out, pfx="d_", debug=False, glevel=5,
                 sems=None):
    E = ctx.enter_context
    ld, vs, ss, ts, gs, trs, st = sems or mk_sems(
        nc, ctx, ["ld", "vs", "ss", "ts", "gs", "trs", "st"], pfx)
    HD = 64
    kind = "ExternalOutput" if debug else "Internal"

    din_dram = nc.dram_tensor(pfx + "din", [T, D], F32, kind=kind).ap()
    rms1_dram = nc.dram_tensor(pfx + "rms1", [T, D], BF16, kind=kind).ap()
    wqbf = nc.dram_tensor(pfx + "wqbf", [D, D], BF16, kind="Internal").ap()
    wkbf = nc.dram_tensor(pfx + "wkbf", [256, D], BF16, kind="Internal").ap()
    wvbf = nc.dram_tensor(pfx + "wvbf", [256, D], BF16, kind="Internal").ap()
    wobf = nc.dram_tensor(pfx + "wobf", [D, D], BF16, kind="Internal").ap()
    expbf = nc.dram_tensor(pfx + "expbf", [8 * D, D], BF16, kind="Internal").ap()
    gqa_dram = nc.dram_tensor(pfx + "gqa", [D, T], BF16, kind=kind).ap()
    h_dram = nc.dram_tensor(pfx + "h", [T, D], F32, kind=kind).ap()
    rms2_dram = nc.dram_tensor(pfx + "rms2", [T, D], BF16, kind=kind).ap()

    stats = E(nc.sbuf_tensor(pfx + "stats", [128, 16], F32))
    scrap = E(nc.sbuf_tensor(pfx + "scrap", [128, D], F32))
    xdp = E(nc.sbuf_tensor(pfx + "xdp", [128, 2 * D], F32))
    dinp = E(nc.sbuf_tensor(pfx + "dinp", [128, 2 * D], F32))
    rmsp = E(nc.sbuf_tensor(pfx + "rmsp", [128, 2 * D], BF16))
    anw_sb = E(nc.sbuf_tensor(pfx + "anw", [128, 8], F32))
    fnw_sb = E(nc.sbuf_tensor(pfx + "fnw", [128, 8], F32))
    ones = E(nc.sbuf_tensor(pfx + "ones", [128, 128], F32))
    masks = E(nc.sbuf_tensor(pfx + "masks", [128, 4 * 512], BF16))
    aph = ExitStack()
    qT = aph.enter_context(nc.sbuf_tensor(pfx + "qT", [128, 8 * T], BF16))
    kT = aph.enter_context(nc.sbuf_tensor(pfx + "kT", [128, 2 * T], BF16))
    v_tm = aph.enter_context(nc.sbuf_tensor(pfx + "vtm", [128, NT * 260], BF16))

    nrm_row = aph.enter_context(nc.sbuf_tensor(pfx + "nrmrow", [1, 2 * D], F32))
    psW = aph.enter_context(nc.psum_tensor(pfx + "psW", [128, 16], F32))

    @block.sync
    def _(sync):
        inc(sync.dma_start(out=nrm_row[0:1, 0:D], in_=wts["anw"][None, :]), ld, 16)
        inc(sync.dma_start(out=nrm_row[0:1, D:2 * D], in_=wts["fnw"][None, :]), ld, 16)
        inc(sync.dma_start(out=masks[:].rearrange("p (j n) -> p j n", n=512),
                           in_=wts["masks"].rearrange("(j p) n -> p j n", p=128)), ld, 16)

    @block.tensor
    def _(t_):
        t_.wait_ge(ld.h, ld.n)
        t_.wait_ge(gs.h, 1)
        for c in range(8):
            t_.transpose(psW[:, c:c + 1], nrm_row[0:1, c * 128:(c + 1) * 128], ones[0:1, 0:1])
        for c in range(8):
            mm = t_.transpose(psW[:, 8 + c:9 + c], nrm_row[0:1, D + c * 128:D + (c + 1) * 128],
                              ones[0:1, 0:1])
        inc(mm, ts)

    @block.vector
    def _(v):
        v.wait_ge(ts.h, ts.n)
        inc(v.tensor_copy(anw_sb[:], psW[:, 0:8]), vs)
        inc(v.tensor_copy(fnw_sb[:], psW[:, 8:16]), vs)

    @block.gpsimd
    def _(g_):
        inc(g_.memset(ones[:], 1.0), gs)
        inc(g_.dma_start(out=wqbf[:, :], in_=wts["wq"][:, :]), gs, 16)
        inc(g_.dma_start(out=wkbf[:, :], in_=wts["wk"][:, :]), gs, 16)
        inc(g_.dma_start(out=wvbf[:, :], in_=wts["wv"][:, :]), gs, 16)
        inc(g_.dma_start(out=wobf[:, :], in_=wts["wo"][:, :]), gs, 16)
        inc(g_.dma_start(out=expbf.rearrange("(e r) d -> e r d", e=8),
                         in_=wts["expert_w"][:, :, :]), gs, 16)

    d1_ld, d1_rs, d1_ap, d1_sp = {}, {}, {}, {}

    def d1_load(t):
        @block.sync
        def _(sync, t=t):
            if t - 2 in d1_ap:
                sync.wait_ge(vs.h, d1_ap[t - 2])
            inc(sync.dma_start(out=xdp[:, (t % 2) * D:(t % 2 + 1) * D],
                               in_=x_in[t * 128:(t + 1) * 128, :]), ld, 16)
        d1_ld[t] = ld.n

    def d1_stats(t):
        @block.vector
        def _(v, t=t):
            v.wait_ge(ld.h, d1_ld[t])
        st8 = stats[:, (t % 2) * 8:(t % 2) * 8 + 8]
        emit_ln_stats(block, xdp[:, (t % 2) * D:(t % 2 + 1) * D], st8, scrap, vs, ss)

        @block.vector
        def _(v, t=t):
            v.wait_ge(ss.h, ss.n)
            inc(v.tensor_tensor(out=st8[:, 6:7], in0=st8[:, 3:4], in1=st8[:, 3:4],
                                op=ALU.mult), vs)
            v.wait_ge(vs.h, vs.n)
            inc(v.tensor_scalar(out=st8[:, 6:7], in0=st8[:, 6:7], scalar1=-LN_EPS,
                                op0=ALU.mult, scalar2=1.0 + 1e-6, op1=ALU.add), vs)

        @block.scalar
        def _(s, t=t):
            s.wait_ge(vs.h, vs.n)
            inc(s.activation(out=st8[:, 7:8], in_=st8[:, 6:7], func=AF.Ln), ss)
            s.wait_ge(ss.h, ss.n)
            inc(s.activation(out=st8[:, 6:7], in_=st8[:, 7:8], func=AF.Exp, scale=-0.5), ss)
        d1_rs[t] = ss.n

    def d1_apply(t):
        st8 = stats[:, (t % 2) * 8:(t % 2) * 8 + 8]
        if t - 2 in d1_sp:
            @block.vector
            def _(v, t=t):
                v.wait_ge(gs.h, d1_sp[t - 2])

        @block.vector
        def _(v, t=t):
            v.wait_ge(ss.h, d1_rs[t])
            v.tensor_scalar(out=dinp[:, (t % 2) * D:(t % 2 + 1) * D],
                            in0=xdp[:, (t % 2) * D:(t % 2 + 1) * D],
                            scalar1=st8[:, 2:3], op0=ALU.subtract,
                            scalar2=st8[:, 3:4], op1=ALU.mult)
            inc(v.tensor_scalar(out=rmsp[:, (t % 2) * D:(t % 2 + 1) * D],
                                in0=dinp[:, (t % 2) * D:(t % 2 + 1) * D],
                                scalar1=st8[:, 6:7], op0=ALU.mult, scalar2=None), vs)
        d1_ap[t] = vs.n

        @block.gpsimd
        def _(g_, t=t):
            g_.wait_ge(vs.h, d1_ap[t])
            inc(g_.dma_start(out=din_dram[t * 128:(t + 1) * 128, :],
                             in_=dinp[:, (t % 2) * D:(t % 2 + 1) * D]), gs, 16)
            inc(g_.dma_start(out=rms1_dram[t * 128:(t + 1) * 128, :],
                             in_=rmsp[:, (t % 2) * D:(t % 2 + 1) * D]), gs, 16)
        d1_sp[t] = gs.n

    d1_load(0)
    d1_stats(0)
    d1_load(1)
    for t in range(NT):
        if t + 1 < NT:
            d1_stats(t + 1)
        d1_apply(t)
        if t + 2 < NT:
            d1_load(t + 2)


    TH = T // 2
    with ExitStack() as ph:
        rms1T = ph.enter_context(nc.sbuf_tensor(pfx + "rms1T", [128, 8 * TH], BF16))
        wqT = ph.enter_context(nc.sbuf_tensor(pfx + "wqT", [128, 8 * D], BF16))
        wkvT = ph.enter_context(nc.sbuf_tensor(pfx + "wkvT", [128, 8 * 512], BF16))
        psq = ph.enter_context(nc.psum_tensor(pfx + "psq", [128, 4 * 512], F32))
        psv = ph.enter_context(nc.psum_tensor(pfx + "psv", [128, 2 * 256], F32))

        @block.scalar
        def _(s):
            s.wait_ge(gs.h, gs.n)
            for c in range(8):
                inc(s.dma_start(out=wqT[:, c * D:(c + 1) * D],
                                in_=wqbf[:, c * 128:(c + 1) * 128], transpose=True), trs, 16)
                inc(s.dma_start(out=wkvT[:, c * 512:c * 512 + 256],
                                in_=wkbf[:, c * 128:(c + 1) * 128], transpose=True), trs, 16)
                inc(s.dma_start(out=wkvT[:, c * 512 + 256:(c + 1) * 512],
                                in_=wvbf[:, c * 128:(c + 1) * 128], transpose=True), trs, 16)

        @block.vector
        def _(v):
            v.wait_ge(trs.h, trs.n)
            for c in range(8):
                inc(v.tensor_scalar(out=wqT[:, c * D:(c + 1) * D], in0=wqT[:, c * D:(c + 1) * D],
                                    scalar1=anw_sb[:, c:c + 1], op0=ALU.mult, scalar2=None), vs)
                inc(v.tensor_scalar(out=wkvT[:, c * 512:(c + 1) * 512],
                                    in0=wkvT[:, c * 512:(c + 1) * 512],
                                    scalar1=anw_sb[:, c:c + 1], op0=ALU.mult, scalar2=None), vs)

        nbank = 0
        for hf in range(2):
            tok0 = hf * TH

            @block.scalar
            def _(s, hf=hf, tok0=tok0):
                if hf > 0:
                    s.wait_ge(ts.h, ts.n)
                for c in range(8):
                    inc(s.dma_start(out=rms1T[:, c * TH:(c + 1) * TH],
                                    in_=rms1_dram[tok0:tok0 + TH, c * 128:(c + 1) * 128],
                                    transpose=True), trs, 16)

            for tau in range(8):
                hA, hB = qhead_of(tau, 0), qhead_of(tau, 1)
                for n in range(4):
                    pq = psq[:, (nbank % 4) * 512:(nbank % 4) * 512 + 512]
                    nbank += 1

                    @block.tensor
                    def _(t_, tau=tau, hA=hA, hB=hB, n=n, pq=pq):
                        t_.wait_ge(vs.h, vs.n)
                        t_.wait_ge(trs.h, trs.n)
                        mm = None
                        for c in range(8):
                            rhs = rms1T[:, c * TH + n * 512: c * TH + (n + 1) * 512]
                            t_.matmul(pq[0:64, :], wqT[:, c * D + 64 * hA: c * D + 64 * hA + 64],
                                      rhs, start=(c == 0), stop=(c == 7))
                            mm = t_.matmul(pq[64:128, :], wqT[:, c * D + 64 * hB: c * D + 64 * hB + 64],
                                           rhs, start=(c == 0), stop=(c == 7))
                        inc(mm, ts)

                    @block.vector
                    def _(v, tau=tau, n=n, pq=pq, tok0=tok0):
                        v.wait_ge(ts.h, ts.n)
                        inc(v.tensor_copy(
                            qT[:, tau * T + tok0 + n * 512: tau * T + tok0 + (n + 1) * 512], pq), vs)

            for tau in range(2):
                for n in range(4):
                    pq = psq[:, (nbank % 4) * 512:(nbank % 4) * 512 + 512]
                    nbank += 1

                    @block.tensor
                    def _(t_, tau=tau, n=n, pq=pq):
                        t_.wait_ge(vs.h, vs.n)
                        mm = None
                        for c in range(8):
                            mm = t_.matmul(pq, wkvT[:, c * 512 + tau * 128: c * 512 + tau * 128 + 128],
                                           rms1T[:, c * TH + n * 512: c * TH + (n + 1) * 512],
                                           start=(c == 0), stop=(c == 7))
                        inc(mm, ts)

                    @block.vector
                    def _(v, tau=tau, n=n, pq=pq, tok0=tok0):
                        v.wait_ge(ts.h, ts.n)
                        inc(v.tensor_copy(
                            kT[:, tau * T + tok0 + n * 512: tau * T + tok0 + (n + 1) * 512], pq), vs)

            for tt in range(16):
                pv = psv[:, (tt % 2) * 256:(tt % 2) * 256 + 256]
                gt = hf * 16 + tt

                @block.tensor
                def _(t_, tt=tt, pv=pv):
                    t_.wait_ge(vs.h, vs.n)
                    mm = None
                    for c in range(8):
                        mm = t_.matmul(pv, rms1T[:, c * TH + tt * 128: c * TH + tt * 128 + 128],
                                       wkvT[:, c * 512 + 256:(c + 1) * 512],
                                       start=(c == 0), stop=(c == 7))
                    inc(mm, ts)

                @block.vector
                def _(v, gt=gt, pv=pv):
                    v.wait_ge(ts.h, ts.n)
                    dst = v_tm[:, gt * 260: gt * 260 + 260].rearrange("p (g e) -> p g e", e=65)[:, :, 0:64]
                    inc(v.tensor_copy(dst, pv[:].rearrange("p (g e) -> p g e", e=64)), vs)

        @block.gpsimd
        def _(g_):
            g_.wait_ge(vs.h, vs.n)
            ocol = v_tm[:].rearrange("p (t g e) -> p t g e", g=4, e=65)[:, :, :, 64:65]
            inc(g_.memset(ocol, 1.0), gs)

        if debug:
            dbg_qT = nc.dram_tensor(pfx + "dbg_qT", [128, 8 * T], BF16, kind="ExternalOutput").ap()
            dbg_kT = nc.dram_tensor(pfx + "dbg_kT", [128, 2 * T], BF16, kind="ExternalOutput").ap()
            dbg_v = nc.dram_tensor(pfx + "dbg_v", [128, NT * 260], BF16, kind="ExternalOutput").ap()

            @block.gpsimd
            def _(g_):
                g_.wait_ge(gs.h, gs.n)
                inc(g_.dma_start(out=dbg_qT[:, :], in_=qT[:]), gs, 16)
                inc(g_.dma_start(out=dbg_kT[:, :], in_=kT[:]), gs, 16)
                inc(g_.dma_start(out=dbg_v[:, :], in_=v_tm[:]), gs, 16)

        @block.tensor
        def _(t_):
            t_.wait_ge(gs.h, gs.n)

    if debug == "d3":
        @block.sync
        def _(sync):
            sync.wait_ge(gs.h, gs.n)
            sync.wait_ge(vs.h, vs.n)
        aph.close()
        return st

    if True:
        ph = ph_attn = ExitStack()
        attnT = ph.enter_context(nc.sbuf_tensor(pfx + "attnT", [128, 8 * T], BF16))
        ph_flash = ExitStack()
        pTbuf = ph_flash.enter_context(nc.sbuf_tensor(pfx + "pTbuf", [128, 4 * 512], BF16))
        rrow = ph_flash.enter_context(nc.sbuf_tensor(pfx + "rrow", [1, 512], F32))
        rbc = ph_flash.enter_context(nc.sbuf_tensor(pfx + "rbc", [128, 512], F32))
        psS = ph_flash.enter_context(nc.psum_tensor(pfx + "psS", [128, 4 * 512], F32))
        psO = ph_flash.enter_context(nc.psum_tensor(pfx + "psO", [128, 2 * 512], F32))
        psR = ph_flash.enter_context(nc.psum_tensor(pfx + "psR", [128, 512], F32))
        for tau in range(8):
            ktile = 0 if tau < 4 else 1
            for qb in range(8):
                nkb = 4 * (qb + 1)
                n = nkb
                tsb, ssb, vsb = ts.n, ss.n, vs.n

                def idxS(x, k):
                    return (2 * k if k < 2 else 4 * k - 2) + x

                def idxP(x, k):
                    if k <= n - 3:
                        return 4 + 4 * k + x
                    return (4 * n - 4 + x) if k == n - 2 else (4 * n - 2 + x)

                def mcnt(x, k):
                    kk = min(k, n - 1) - (n - 4)
                    return max(0, 2 * kk + 1 + x)

                def sbank(x, k):
                    return (k % 2) * 2 + x

                def emit_S(t_, x, k):
                    if k >= 2:
                        t_.wait_ge(ss.h, ssb + (k - 2) + 1)
                    else:
                        t_.wait_ge(ss.h, ssb)
                    base = 64 * x
                    mm = t_.matmul(psS[:, sbank(x, k) * 512:(sbank(x, k) + 1) * 512],
                                   kT[base:base + 64,
                                      ktile * T + k * 128: ktile * T + k * 128 + 128],
                                   qT[base:base + 64,
                                      tau * T + qb * 512: tau * T + (qb + 1) * 512],
                                   start=True, stop=True)
                    inc(mm, ts)

                def emit_P(t_, x, k):
                    if k == 0:
                        t_.wait_ge(vs.h, vsb)
                    t_.wait_ge(ss.h, ssb + k + 1)
                    if k >= n - 4:
                        t_.wait_ge(vs.h, vsb + mcnt(x, k))
                    g = qhead_of(tau, x) // 4
                    mm = t_.matmul(psO[x * 512:x * 512 + 65, 0:512]
                                   if False else psO[0:65, x * 512:(x + 1) * 512],
                                   v_tm[:, k * 260 + g * 65: k * 260 + g * 65 + 65],
                                   pTbuf[:, sbank(x, k) * 512:(sbank(x, k) + 1) * 512],
                                   start=(k == 0), stop=(k == n - 1))
                    inc(mm, ts)

                @block.tensor
                def _(t_):
                    for k in (0, 1):
                        emit_S(t_, 0, k)
                        emit_S(t_, 1, k)
                    for k in range(n - 2):
                        emit_P(t_, 0, k)
                        emit_P(t_, 1, k)
                        emit_S(t_, 0, k + 2)
                        emit_S(t_, 1, k + 2)
                    for k in (n - 2, n - 1):
                        emit_P(t_, 0, k)
                        emit_P(t_, 1, k)

                @block.scalar
                def _(s):
                    for k in range(n):
                        s.wait_ge(ts.h, tsb + idxS(1, k) + 1)
                        if k >= 2:
                            s.wait_ge(ts.h, tsb + idxP(1, k - 2) + 1)
                        w0 = (k % 2) * 1024
                        inc(s.activation(out=pTbuf[:, w0:w0 + 1024],
                                         in_=psS[:, w0:w0 + 1024],
                                         func=AF.Exp, scale=0.125), ss)

                @block.vector
                def _(v):
                    for k in range(max(0, n - 4), n):
                        for x in (0, 1):
                            v.wait_ge(ss.h, ssb + k + 1)
                            inc(v.tensor_tensor(
                                out=pTbuf[:, sbank(x, k) * 512:(sbank(x, k) + 1) * 512],
                                in0=pTbuf[:, sbank(x, k) * 512:(sbank(x, k) + 1) * 512],
                                in1=masks[:, (k - 4 * qb) * 512:(k - 4 * qb + 1) * 512],
                                op=ALU.mult), vs)

                assert ts.n == tsb + 4 * n and ss.n == ssb + n and vs.n == vsb + 8

                for half in range(2):
                    hh = qhead_of(tau, half)
                    pO = psO[:, half * 512:(half + 1) * 512]

                    @block.vector
                    def _(v, pO=pO):
                        v.wait_ge(ts.h, ts.n)
                        inc(v.reciprocal(rrow[0:1, :], pO[64:65, :]), vs)

                    @block.tensor
                    def _(t_):
                        t_.wait_ge(vs.h, vs.n)
                        inc(t_.matmul(psR[0:64, :], ones[0:1, 0:64], rrow[0:1, :],
                                      start=True, stop=True), ts)

                    @block.vector
                    def _(v, pO=pO, hh=hh, qb=qb):
                        v.wait_ge(ts.h, ts.n)
                        inc(v.tensor_copy(rbc[0:64, :], psR[0:64, :]), vs)
                        v.wait_ge(vs.h, vs.n)
                        dst = attnT[64 * (hh % 2):64 * (hh % 2) + 64,
                                    (hh // 2) * T + qb * 512:(hh // 2) * T + (qb + 1) * 512]
                        inc(v.tensor_tensor(out=dst, in0=pO[0:64, :], in1=rbc[0:64, :],
                                            op=ALU.mult), vs)

        ph_flash.close()
        woT = ph.enter_context(nc.sbuf_tensor(pfx + "woT", [128, 4 * D], BF16))
        gout = ph.enter_context(nc.sbuf_tensor(pfx + "gout", [128, 2 * 512], BF16))
        psG = ph.enter_context(nc.psum_tensor(pfx + "psG", [128, 2 * 512], F32))

        d5_mm, d5_ev, d5_sp = {}, {}, {}
        for mhalf in range(2):
            @block.scalar
            def _(s, mhalf=mhalf):
                if mhalf == 0:
                    s.wait_ge(vs.h, vs.n)
                else:
                    s.wait_ge(ts.h, ts.n)
                for c in range(8):
                    inc(s.dma_start(out=woT[:, c * 512:(c + 1) * 512],
                                    in_=wobf[mhalf * 512:(mhalf + 1) * 512,
                                             c * 128:(c + 1) * 128], transpose=True), trs, 16)

            for mo in range(mhalf * 4, mhalf * 4 + 4):
                for n in range(8):
                    i = mo * 8 + n
                    par = i % 2
                    pg = psG[:, par * 512:(par + 1) * 512]

                    @block.tensor
                    def _(t_, mo=mo, n=n, pg=pg, i=i,
                          first=(mo == mhalf * 4 and n == 0)):
                        if first:
                            t_.wait_ge(trs.h, trs.n)
                        if i - 2 in d5_ev:
                            t_.wait_ge(vs.h, d5_ev[i - 2])
                        else:
                            t_.wait_ge(vs.h, vs.n)
                        mm = None
                        for c in range(8):
                            mm = t_.matmul(pg, woT[:, c * 512 + (mo % 4) * 128: c * 512 + (mo % 4) * 128 + 128],
                                           attnT[:, c * T + n * 512:c * T + (n + 1) * 512],
                                           start=(c == 0), stop=(c == 7))
                        inc(mm, ts)
                    d5_mm[i] = ts.n

                    @block.vector
                    def _(v, pg=pg, par=par, i=i):
                        v.wait_ge(ts.h, d5_mm[i])
                        if i - 2 in d5_sp:
                            v.wait_ge(gs.h, d5_sp[i - 2])
                        inc(v.tensor_copy(gout[:, par * 512:(par + 1) * 512], pg), vs)
                    d5_ev[i] = vs.n

                    @block.gpsimd
                    def _(g_, mo=mo, n=n, par=par, i=i):
                        g_.wait_ge(vs.h, d5_ev[i])
                        inc(g_.dma_start(out=gqa_dram[mo * 128:(mo + 1) * 128, n * 512:(n + 1) * 512],
                                         in_=gout[:, par * 512:(par + 1) * 512]), gs, 16)
                    d5_sp[i] = gs.n

    ph_attn.close()
    aph.close()

    if debug == "d5":
        @block.sync
        def _(sync):
            sync.wait_ge(gs.h, gs.n)
            sync.wait_ge(vs.h, vs.n)
        return st

    ph6 = ExitStack()
    comb = ph6.enter_context(nc.sbuf_tensor(pfx + "comb", [128, NT * 8], F32))
    glog = ph6.enter_context(nc.sbuf_tensor(pfx + "glog", [128, 64], F32))
    idT = ph6.enter_context(nc.sbuf_tensor(pfx + "idT", [128, 2 * 128], F32))
    gateT = ph6.enter_context(nc.sbuf_tensor(pfx + "gateT", [128, 8 * 8], F32))
    gsb = ph6.enter_context(nc.sbuf_tensor(pfx + "gsb", [128, D], F32))
    hT_st = ph6.enter_context(nc.sbuf_tensor(pfx + "hTst", [128, 2 * D], F32))
    ph6p = ExitStack()
    psHT = ph6p.enter_context(nc.psum_tensor(pfx + "psHT", [128, 2 * 1024], F32))
    psg = ph6p.enter_context(nc.psum_tensor(pfx + "psg", [128, 1024], F32))

    if debug:
        dbg_lg_d = nc.dram_tensor(pfx + "dbg_lg", [128, NT * 8], F32, kind="ExternalOutput").ap()
        dbg_lg = ph6.enter_context(nc.sbuf_tensor(pfx + "dbglg", [128, NT * 8], F32))

    @block.sync
    def _(sync):
        sync.wait_ge(vs.h, vs.n)
        sync.wait_ge(ts.h, ts.n)

    @block.gpsimd
    def _(g_):
        g_.wait_ge(vs.h, vs.n)
        g_.wait_ge(ts.h, ts.n)

    if glevel < 1:
        @block.gpsimd
        def _(g_):
            g_.memset(comb[:], 0.25)

    if glevel >= 1:
        @block.sync
        def _(sync):
            inc(sync.dma_start(out=gsb[0:8, :], in_=wts["gate_w"][:, :]), ld, 16)

    if glevel >= 1:
        @block.gpsimd
        def _(g_):
            inc(g_.iota(out=idT[:, 0:128], base=0, pattern=[[1, 128]], channel_multiplier=0,
                        allow_small_or_imprecise_dtypes=True), gs)
            inc(g_.iota(out=idT[:, 128:256], base=0, pattern=[[0, 128]], channel_multiplier=1,
                        allow_small_or_imprecise_dtypes=True), gs)

        @block.vector
        def _(v):
            v.wait_ge(gs.h, gs.n)
            inc(v.tensor_tensor(out=idT[:, 0:128], in0=idT[:, 0:128], in1=idT[:, 128:256],
                                op=ALU.is_equal), vs)

        for c in range(8):
            @block.tensor
            def _(t_, c=c):
                t_.wait_ge(vs.h, vs.n)
                t_.wait_ge(ld.h, ld.n)
                inc(t_.transpose(psg[:, 8:16], gsb[0:8, c * 128:(c + 1) * 128],
                                 idT[0:8, 0:8]), ts)

            @block.vector
            def _(v, c=c):
                v.wait_ge(ts.h, ts.n)
                inc(v.tensor_scalar(out=gateT[:, c * 8:(c + 1) * 8], in0=psg[:, 8:16],
                                    scalar1=fnw_sb[:, c:c + 1], op0=ALU.mult, scalar2=None), vs)

        if debug:
            dbg_gT0 = nc.dram_tensor(pfx + "dbg_gT0", [128, 64], F32, kind="ExternalOutput").ap()
            dbg_gsb = nc.dram_tensor(pfx + "dbg_gsb", [8, D], F32, kind="ExternalOutput").ap()
            dbg_idT = nc.dram_tensor(pfx + "dbg_idT", [128, 128], F32, kind="ExternalOutput").ap()
            dbg_fnw = nc.dram_tensor(pfx + "dbg_fnw", [128, 8], F32, kind="ExternalOutput").ap()

            @block.gpsimd
            def _(g_):
                g_.wait_ge(vs.h, vs.n)
                inc(g_.dma_start(out=dbg_gT0[:, :], in_=gateT[:]), gs, 16)
                inc(g_.dma_start(out=dbg_gsb[:, :], in_=gsb[0:8, :]), gs, 16)
                inc(g_.dma_start(out=dbg_idT[:, :], in_=idT[:, 0:128]), gs, 16)
                inc(g_.dma_start(out=dbg_fnw[:, :], in_=fnw_sb[:]), gs, 16)

    d6_ld, d6_add, d6_rs, d6_rms, d6_sp = {}, {}, {}, {}, {}
    d6_tr, d6_hT, d6_mm, d6_lg, d6_d, d6_sg = {}, {}, {}, {}, {}, {}

    def d6_load(t):
        p = t % 2

        @block.sync
        def _(sync, t=t, p=p):
            if t - 2 in d6_add:
                sync.wait_ge(vs.h, d6_add[t - 2])
                sync.wait_ge(gs.h, d6_sp[t - 2])
            else:
                sync.wait_ge(gs.h, gs.n)
            inc(sync.dma_start(out=xdp[:, p * D:(p + 1) * D],
                               in_=din_dram[t * 128:(t + 1) * 128, :]), ld, 16)
            inc(sync.dma_start(out=rmsp[:, p * D:(p + 1) * D],
                               in_=gqa_dram[:, t * 128:(t + 1) * 128],
                               transpose=True), ld, 16)
        d6_ld[t] = ld.n

    def d6_stats(t):
        p = t % 2
        b = p * 8

        @block.vector
        def _(v, t=t, p=p):
            v.wait_ge(ld.h, d6_ld[t])
            inc(v.tensor_tensor(out=dinp[:, p * D:(p + 1) * D],
                                in0=xdp[:, p * D:(p + 1) * D],
                                in1=rmsp[:, p * D:(p + 1) * D], op=ALU.add), vs)
        d6_add[t] = vs.n

        @block.vector
        def _(v, p=p, b=b):
            inc(v.scalar_tensor_tensor(out=scrap[:, 0:D],
                                       in0=dinp[:, p * D:(p + 1) * D], scalar=1.0,
                                       op0=ALU.mult, op1=ALU.mult,
                                       in1=dinp[:, p * D:(p + 1) * D],
                                       accum_out=stats[:, b:b + 1]), vs)
            v.wait_ge(vs.h, vs.n)
            inc(v.tensor_scalar(out=stats[:, b + 1:b + 2], in0=stats[:, b:b + 1],
                                scalar1=1.0 / D, op0=ALU.mult, scalar2=1e-6,
                                op1=ALU.add), vs)

        @block.scalar
        def _(s, b=b):
            s.wait_ge(vs.h, vs.n)
            inc(s.activation(out=stats[:, b + 3:b + 4], in_=stats[:, b + 1:b + 2],
                             func=AF.Ln), ss)
            s.wait_ge(ss.h, ss.n)
            inc(s.activation(out=stats[:, b + 2:b + 3], in_=stats[:, b + 3:b + 4],
                             func=AF.Exp, scale=-0.5), ss)
        d6_rs[t] = ss.n

    def d6_rms_spill(t):
        p = t % 2
        b = p * 8

        @block.vector
        def _(v, t=t, p=p, b=b):
            v.wait_ge(ss.h, d6_rs[t])
            inc(v.tensor_scalar(out=rmsp[:, p * D:(p + 1) * D],
                                in0=dinp[:, p * D:(p + 1) * D],
                                scalar1=stats[:, b + 2:b + 3], op0=ALU.mult,
                                scalar2=None), vs)
        d6_rms[t] = vs.n

        @block.gpsimd
        def _(g_, t=t, p=p):
            g_.wait_ge(vs.h, d6_rms[t])
            inc(g_.dma_start(out=h_dram[t * 128:(t + 1) * 128, :],
                             in_=dinp[:, p * D:(p + 1) * D]), gs, 16)
            inc(g_.dma_start(out=rms2_dram[t * 128:(t + 1) * 128, :],
                             in_=rmsp[:, p * D:(p + 1) * D]), gs, 16)
        d6_sp[t] = gs.n

    def d6_gate_pe(t):
        p = t % 2

        @block.tensor
        def _(t_, t=t, p=p):
            t_.wait_ge(vs.h, d6_add[t])
            if t - 2 in d6_hT:
                t_.wait_ge(vs.h, d6_hT[t - 2])
            mm = None
            for c in range(8):
                mm = t_.transpose(psHT[:, p * 1024 + c * 128:p * 1024 + (c + 1) * 128],
                                  dinp[:, p * D + c * 128:p * D + (c + 1) * 128],
                                  idT[:, 0:128])
            inc(mm, ts)
        d6_tr[t] = ts.n

        @block.vector
        def _(v, t=t, p=p):
            v.wait_ge(ts.h, d6_tr[t])
            inc(v.tensor_copy(hT_st[:, p * D:(p + 1) * D],
                              psHT[:, p * 1024:(p + 1) * 1024]), vs)
        d6_hT[t] = vs.n

        @block.tensor
        def _(t_, t=t, p=p):
            t_.wait_ge(vs.h, d6_hT[t])
            if t - 2 in d6_lg:
                t_.wait_ge(vs.h, d6_lg[t - 2])
            mm = None
            for c in range(8):
                mm = t_.matmul(psg[:, p * 512:p * 512 + 8],
                               hT_st[:, p * D + c * 128:p * D + (c + 1) * 128],
                               gateT[:, c * 8:(c + 1) * 8], start=(c == 0), stop=(c == 7))
            inc(mm, ts)
        d6_mm[t] = ts.n

    def d6_gate_dve(t):
        p = t % 2
        b = p * 8
        gl = glog[:, p * 32:(p + 1) * 32]

        @block.vector
        def _(v, t=t, p=p, b=b, gl=gl):
            v.wait_ge(ts.h, d6_mm[t])
            inc(v.tensor_scalar(out=gl[:, 0:8], in0=psg[:, p * 512:p * 512 + 8],
                                scalar1=stats[:, b + 2:b + 3], op0=ALU.mult,
                                scalar2=None), vs)
        d6_lg[t] = vs.n

        @block.vector
        def _(v, b=b, gl=gl):
            v.wait_ge(vs.h, vs.n)
            inc(v.reduce_max(gl[:, 8:9], gl[:, 0:8], axis=AX.X), vs)
            v.wait_ge(vs.h, vs.n)
            inc(v.tensor_scalar(out=gl[:, 16:24], in0=gl[:, 0:8], scalar1=gl[:, 8:9],
                                op0=ALU.is_equal, scalar2=None), vs)
            v.wait_ge(vs.h, vs.n)
            inc(v.scalar_tensor_tensor(out=gl[:, 24:32], in0=gl[:, 16:24], scalar=-1e30,
                                       op0=ALU.mult, op1=ALU.add, in1=gl[:, 0:8]), vs)
            v.wait_ge(vs.h, vs.n)
            inc(v.reduce_max(gl[:, 9:10], gl[:, 24:32], axis=AX.X), vs)
            v.wait_ge(vs.h, vs.n)
            inc(v.tensor_tensor(out=stats[:, b + 4:b + 5], in0=gl[:, 8:9],
                                in1=gl[:, 9:10], op=ALU.subtract), vs)
        d6_d[t] = vs.n

        @block.scalar
        def _(s, t=t, b=b):
            s.wait_ge(vs.h, d6_d[t])
            inc(s.activation(out=stats[:, b + 5:b + 6], in_=stats[:, b + 4:b + 5],
                             func=AF.Exp, scale=-1.0), ss)
        d6_sg[t] = ss.n

        @block.vector
        def _(v, t=t, b=b, gl=gl):
            v.wait_ge(ss.h, d6_sg[t])
            inc(v.tensor_scalar(out=stats[:, b + 7:b + 8], in0=stats[:, b + 5:b + 6],
                                scalar1=1.0, op0=ALU.add, scalar2=None), vs)
            v.wait_ge(vs.h, vs.n)
            inc(v.reciprocal(stats[:, b + 4:b + 5], stats[:, b + 7:b + 8]), vs)
            v.wait_ge(vs.h, vs.n)
            inc(v.tensor_copy(stats[:, b + 5:b + 6], stats[:, b + 4:b + 5]), vs)
            v.wait_ge(vs.h, vs.n)
            inc(v.tensor_scalar(out=stats[:, b + 6:b + 7], in0=stats[:, b + 5:b + 6],
                                scalar1=-1.0, op0=ALU.mult, scalar2=1.0, op1=ALU.add), vs)
            inc(v.tensor_scalar(out=gl[:, 24:32], in0=gl[:, 0:8], scalar1=gl[:, 9:10],
                                op0=ALU.is_equal, scalar2=None), vs)
            v.wait_ge(vs.h, vs.n)
            inc(v.tensor_scalar(out=gl[:, 16:24], in0=gl[:, 16:24],
                                scalar1=stats[:, b + 5:b + 6], op0=ALU.mult,
                                scalar2=None), vs)
            inc(v.tensor_scalar(out=gl[:, 24:32], in0=gl[:, 24:32],
                                scalar1=stats[:, b + 6:b + 7], op0=ALU.mult,
                                scalar2=None), vs)
            v.wait_ge(vs.h, vs.n)
            inc(v.tensor_tensor(out=comb[:, t * 8:(t + 1) * 8], in0=gl[:, 16:24],
                                in1=gl[:, 24:32], op=ALU.add), vs)

    d6_load(0)
    d6_stats(0)
    d6_load(1)
    for t in range(NT):
        if t + 1 < NT:
            d6_stats(t + 1)
        d6_rms_spill(t)
        d6_gate_pe(t)
        d6_gate_dve(t)
        if t + 2 < NT:
            d6_load(t + 2)

    if debug:
        dbg_gT = nc.dram_tensor(pfx + "dbg_gT", [128, 64], F32, kind="ExternalOutput").ap()
        dbg_hT = nc.dram_tensor(pfx + "dbg_hT", [128, D], F32, kind="ExternalOutput").ap()

        @block.gpsimd
        def _(g_):
            g_.wait_ge(vs.h, vs.n)
            inc(g_.dma_start(out=dbg_lg_d[:, :], in_=dbg_lg[:]), gs, 16)
            inc(g_.dma_start(out=dbg_gT[:, :], in_=gateT[:]), gs, 16)
            inc(g_.dma_start(out=dbg_hT[:, :], in_=hT_st[:]), gs, 16)

    if debug == "d6":
        @block.sync
        def _(sync):
            sync.wait_ge(gs.h, gs.n)
            sync.wait_ge(vs.h, vs.n)
        ph6.close()
        return st

    ph6p.close()

    with ExitStack() as ph:
        rms2T = ph.enter_context(nc.sbuf_tensor(pfx + "rms2T", [128, 8 * T], BF16))
        expT = ph.enter_context(nc.sbuf_tensor(pfx + "expT", [128, 2 * 8 * D], BF16))
        moe_out = ph.enter_context(nc.sbuf_tensor(pfx + "moeout", [128, 16 * D], F32))
        psE = ph.enter_context(nc.psum_tensor(pfx + "psE", [128, 4 * 512], F32))

        @block.scalar
        def _(s):
            s.wait_ge(gs.h, gs.n)
            for c in range(8):
                inc(s.dma_start(out=rms2T[:, c * T:(c + 1) * T],
                                in_=rms2_dram[:, c * 128:(c + 1) * 128], transpose=True), trs, 16)

        if debug:
            dbg_comb = nc.dram_tensor(pfx + "dbg_comb", [128, NT * 8], F32, kind="ExternalOutput").ap()

            @block.gpsimd
            def _(g_):
                g_.wait_ge(vs.h, vs.n)
                inc(g_.dma_start(out=dbg_comb[:, :], in_=comb[:]), gs, 16)

        scale_done = {}
        evac_hist = []
        mm_hist = []
        d7_ld, d7_rs, d7_ap, d7_st = {}, {}, {}, {}

        def d7_load(t):
            p = t % 2

            @block.sync
            def _(sync, t=t, p=p):
                if t - 2 in d7_st:
                    sync.wait_ge(st.h, d7_st[t - 2])
                else:
                    sync.wait_ge(vs.h, vs.n)
                inc(sync.dma_start(out=xdp[:, p * D:(p + 1) * D],
                                   in_=h_dram[t * 128:(t + 1) * 128, :]), ld, 16)
                inc(sync.dma_start(out=dinp[:, p * D:(p + 1) * D],
                                   in_=x_in[t * 128:(t + 1) * 128, :]), ld, 16)
            d7_ld[t] = ld.n

        def d7_stats(t, ti):
            p = t % 2

            @block.vector
            def _(v, t=t, ti=ti, p=p):
                v.wait_ge(ld.h, d7_ld[t])
                inc(v.tensor_tensor(out=moe_out[:, ti * D:(ti + 1) * D],
                                    in0=moe_out[:, ti * D:(ti + 1) * D],
                                    in1=xdp[:, p * D:(p + 1) * D], op=ALU.add), vs)
            d7_rs[t] = emit_ln_stats(block, moe_out[:, ti * D:(ti + 1) * D],
                                     stats[:, p * 8:p * 8 + 8], scrap, vs, ss)

        def d7_apply(t, ti):
            p = t % 2
            emit_ln_apply(block, moe_out[:, ti * D:(ti + 1) * D],
                          xdp[:, p * D:(p + 1) * D], stats[:, p * 8:p * 8 + 8],
                          scrap, vs, ss, d7_rs[t], add_ap=dinp[:, p * D:(p + 1) * D])
            d7_ap[t] = vs.n

            @block.sync
            def _(sync, t=t, p=p):
                sync.wait_ge(vs.h, d7_ap[t])
                inc(sync.dma_start(out=x_out[t * 128:(t + 1) * 128, :],
                                   in_=xdp[:, p * D:(p + 1) * D]), st, 16)
            d7_st[t] = st.n


        def load_expert(e, rnd):
            buf = e % 2

            @block.scalar
            def _(s, e=e, buf=buf):
                s.wait_ge(ts.h, ts.n)
                for c in range(8):
                    inc(s.dma_start(out=expT[:, buf * 8 * D + c * D:buf * 8 * D + (c + 1) * D],
                                    in_=expbf[e * D:(e + 1) * D, c * 128:(c + 1) * 128],
                                    transpose=True), trs, 16)

            @block.vector
            def _(v, e=e, buf=buf):
                v.wait_ge(trs.h, trs.n)
                for c in range(8):
                    inc(v.tensor_scalar(out=expT[:, buf * 8 * D + c * D:buf * 8 * D + (c + 1) * D],
                                        in0=expT[:, buf * 8 * D + c * D:buf * 8 * D + (c + 1) * D],
                                        scalar1=fnw_sb[:, c:c + 1], op0=ALU.mult, scalar2=None), vs)
            scale_done[(e, rnd)] = vs.n

        ebank = 0
        for grp in range(2):
            tiles = range(grp * 16, grp * 16 + 16)
            for e in range(8):
                if grp == 0 and e < 2:
                    load_expert(e, 0)
                for ti, t in enumerate(tiles):
                    pe_ = psE[:, (ebank % 2) * 1024:(ebank % 2) * 1024 + 1024]
                    ebank += 1

                    @block.tensor
                    def _(t_, t=t, e=e, grp=grp, buf=e % 2, pe_=pe_):
                        key = (e, grp) if (e, grp) in scale_done else (e, 0)
                        t_.wait_ge(vs.h, scale_done[key])
                        if len(evac_hist) >= 2:
                            t_.wait_ge(vs.h, evac_hist[-2])
                        mm = None
                        for half in range(2):
                            for c in range(8):
                                mm = t_.matmul(pe_[:, half * 512:(half + 1) * 512],
                                               rms2T[:, c * T + t * 128: c * T + t * 128 + 128],
                                               expT[:, buf * 8 * D + c * D + half * 512:
                                                    buf * 8 * D + c * D + half * 512 + 512],
                                               start=(c == 0), stop=(c == 7))
                        inc(mm, ts)
                        mm_hist.append(ts.n)

                    @block.vector
                    def _(v, t=t, e=e, ti=ti, pe_=pe_):
                        v.wait_ge(ts.h, mm_hist[-1])
                        cc = comb[:, t * 8 + e: t * 8 + e + 1]
                        dst = moe_out[:, ti * D:(ti + 1) * D]
                        if e == 0:
                            op = v.tensor_scalar(out=dst, in0=pe_, scalar1=cc,
                                                 op0=ALU.mult, scalar2=None)
                        else:
                            op = v.scalar_tensor_tensor(out=dst, in0=pe_, scalar=cc,
                                                        op0=ALU.mult, op1=ALU.add, in1=dst)
                        inc(op, vs)
                        evac_hist.append(vs.n)

                if e + 2 < 8:
                    load_expert(e + 2, grp)
                elif grp == 0 and e >= 6:
                    load_expert(e - 6, 1)

            d7_load(tiles[0])
            d7_stats(tiles[0], 0)
            d7_load(tiles[1])
            for ti, t in enumerate(tiles):
                if ti + 1 < 16:
                    d7_stats(t + 1, ti + 1)
                d7_apply(t, ti)
                if ti + 2 < 16:
                    d7_load(t + 2)

    ph6.close()

    @block.sync
    def _(sync):
        sync.wait_ge(st.h, st.n)

    return st


def causal_masks_bf16():
    import ml_dtypes
    m = np.zeros((4, 128, 512), np.float32)
    for j in range(4):
        for kk in range(128):
            q0 = j * 128 + kk
            m[j, kk, q0:] = 1.0
    return m.reshape(4 * 128, 512).astype(ml_dtypes.bfloat16)


def build_prog_d(debug=False, glevel=5):
    nc = bass.Bass("TRN2", target_bir_lowering=False)
    x_in = nc.dram_tensor("x_in", [T, D], F32, kind="ExternalInput").ap()
    wts = {
        "wq": nc.dram_tensor("wq", [D, D], F32, kind="ExternalInput").ap(),
        "wk": nc.dram_tensor("wk", [256, D], F32, kind="ExternalInput").ap(),
        "wv": nc.dram_tensor("wv", [256, D], F32, kind="ExternalInput").ap(),
        "wo": nc.dram_tensor("wo", [D, D], F32, kind="ExternalInput").ap(),
        "gate_w": nc.dram_tensor("gate_w", [8, D], F32, kind="ExternalInput").ap(),
        "expert_w": nc.dram_tensor("expert_w", [8, D, D], F32, kind="ExternalInput").ap(),
        "anw": nc.dram_tensor("anw", [D], F32, kind="ExternalInput").ap(),
        "fnw": nc.dram_tensor("fnw", [D], F32, kind="ExternalInput").ap(),
        "masks": nc.dram_tensor("masks", [4 * 128, 512], BF16, kind="ExternalInput").ap(),
    }
    x_out = nc.dram_tensor("x_out", [T, D], F32, kind="ExternalOutput").ap()
    with ExitStack() as ctx:
        block = ctx.enter_context(nc.Block())
        emit_stage_d(nc, ctx, block, x_in, wts, x_out, debug=debug, glevel=glevel)
    return nc


def emit_stage_c(nc, ctx, block, x_in, wts, x_out, pfx="c_", debug=False, sems=None):
    E = ctx.enter_context
    ld, vs, ss, ts, gs, trs, st = sems or mk_sems(
        nc, ctx, ["ld", "vs", "ss", "ts", "gs", "trs", "st"], pfx)
    kind = "ExternalOutput" if debug else "Internal"

    cin_dram = nc.dram_tensor(pfx + "cin", [T, D], BF16, kind=kind).ap()
    qkvbf = nc.dram_tensor(pfx + "qkvbf", [3 * D, D], BF16, kind="Internal").ap()
    wobf = nc.dram_tensor(pfx + "wobf", [D, D], BF16, kind="Internal").ap()
    attnfull = nc.dram_tensor(pfx + "attnfull", [T, D], BF16, kind=kind).ap()
    if debug:
        dbg_scores = nc.dram_tensor(pfx + "dbg_scores", [128, 48], F32, kind="ExternalOutput").ap()
        dbg_idx = nc.dram_tensor(pfx + "dbg_idx", [2, 256], F32, kind="ExternalOutput").ap()
        dbg_ysel = nc.dram_tensor(pfx + "dbg_ysel", [256, D], BF16, kind="ExternalOutput").ap()

    stats = E(nc.sbuf_tensor(pfx + "stats", [128, 16], F32))
    scrap = E(nc.sbuf_tensor(pfx + "scrap", [128, 2 * D], F32))
    ones = E(nc.sbuf_tensor(pfx + "ones", [128, 128], F32))
    onesb = E(nc.sbuf_tensor(pfx + "onesb", [128, 2], BF16))
    onesrowb = E(nc.sbuf_tensor(pfx + "onesrowb", [1, 256], BF16))
    idT = E(nc.sbuf_tensor(pfx + "idT", [128, 2 * 128], F32))
    ltri = E(nc.sbuf_tensor(pfx + "ltri", [128, 128], F32))
    iotaS = E(nc.sbuf_tensor(pfx + "iotaS", [128, 256], F32))
    tokid = E(nc.sbuf_tensor(pfx + "tokid", [128, 32], F32))
    emat = E(nc.sbuf_tensor(pfx + "emat", [128, 64], F32))
    scores_tm = E(nc.sbuf_tensor(pfx + "scorestm", [128, 48], F32))
    gate_tm = E(nc.sbuf_tensor(pfx + "gatetm", [128, 48], F32))
    bis = E(nc.sbuf_tensor(pfx + "bis", [1, 16], F32))
    rowbuf = E(nc.sbuf_tensor(pfx + "rowbuf", [1, D], F32))
    vrow = E(nc.sbuf_tensor(pfx + "vrow", [1, 4 * D], F32))
    idx_sb = E(nc.sbuf_tensor(pfx + "idx", [128, 4], U32))
    gsel_col = E(nc.sbuf_tensor(pfx + "gselcol", [128, 2], F32))
    gbc = E(nc.sbuf_tensor(pfx + "gbc", [128, 256], F32))
    bout_col = E(nc.sbuf_tensor(pfx + "boutcol", [128, 8], F32))
    yuni_bc = E(nc.sbuf_tensor(pfx + "yunibc", [128, D], BF16))

    psA = E(nc.psum_tensor(pfx + "psA", [128, 2 * 512], F32))
    psB = E(nc.psum_tensor(pfx + "psB", [128, 512], F32))
    psC = E(nc.psum_tensor(pfx + "psC", [128, 512], F32))

    @block.gpsimd
    def _(g_):
        inc(g_.memset(ones[:], 1.0), gs)
        inc(g_.memset(onesb[:], 1.0), gs)
        inc(g_.memset(onesrowb[:], 1.0), gs)
        inc(g_.iota(out=idT[:, 0:128], base=0, pattern=[[1, 128]], channel_multiplier=0,
                    allow_small_or_imprecise_dtypes=True), gs)
        inc(g_.iota(out=idT[:, 128:256], base=0, pattern=[[0, 128]], channel_multiplier=1,
                    allow_small_or_imprecise_dtypes=True), gs)
        inc(g_.iota(out=iotaS[:], base=1, pattern=[[1, 256]], channel_multiplier=0,
                    allow_small_or_imprecise_dtypes=True), gs)
        inc(g_.iota(out=tokid[:], base=0, pattern=[[128, 32]], channel_multiplier=1,
                    allow_small_or_imprecise_dtypes=True), gs)
        inc(g_.iota(out=emat[:], base=0, pattern=[[-1, 64]], channel_multiplier=1,
                    allow_small_or_imprecise_dtypes=True), gs)
        inc(g_.dma_start(out=qkvbf[:, :], in_=wts["w_qkv"][:, :]), gs, 16)
        inc(g_.dma_start(out=wobf[:, :], in_=wts["w_out"][:, :]), gs, 16)

    @block.vector
    def _(v):
        v.wait_ge(gs.h, gs.n)
        inc(v.tensor_tensor(out=ltri[:], in0=idT[:, 128:256], in1=idT[:, 0:128], op=ALU.is_le), vs)
        inc(v.tensor_scalar(out=scrap[:, 0:64], in0=emat[:], scalar1=0.0, op0=ALU.is_equal,
                            scalar2=None), vs)
        inc(v.tensor_scalar(out=scrap[:, 64:128], in0=emat[:], scalar1=64.0, op0=ALU.is_equal,
                            scalar2=None), vs)
        v.wait_ge(vs.h, vs.n)
        inc(v.tensor_tensor(out=emat[:], in0=scrap[:, 0:64], in1=scrap[:, 64:128], op=ALU.max), vs)
        inc(v.tensor_tensor(out=idT[:, 0:128], in0=idT[:, 128:256], in1=idT[:, 0:128],
                            op=ALU.is_equal), vs)
        v.wait_ge(vs.h, vs.n)
        inc(v.tensor_scalar(out=emat[:], in0=emat[:], scalar1=1.0 / 16.0, op0=ALU.mult,
                            scalar2=None), vs)

    @block.sync
    def _(sync):
        inc(sync.dma_start(out=vrow[0:1, 0:D], in_=wts["b_out"][None, :]), ld, 16)

    @block.tensor
    def _(t_):
        t_.wait_ge(ld.h, ld.n)
        t_.wait_ge(gs.h, gs.n)
        mm = None
        for c in range(8):
            mm = t_.transpose(psB[:, c:c + 1], vrow[0:1, c * 128:(c + 1) * 128], ones[0:1, 0:1])
        inc(mm, ts)

    @block.vector
    def _(v):
        v.wait_ge(ts.h, ts.n)
        inc(v.tensor_copy(bout_col[:], psB[:, 0:8]), vs)

    xtp = E(nc.sbuf_tensor(pfx + "xtp", [128, 2 * D], F32))
    fbp = E(nc.sbuf_tensor(pfx + "fbp", [128, 2 * D], BF16))
    xop = E(nc.sbuf_tensor(pfx + "xop", [128, 2 * D], F32))
    xcorr_dram = nc.dram_tensor(pfx + "xcorrd", [256, D], F32, kind="Internal").ap()
    ld_at, sp_at, rstd_at, ap_at, xs_at = {}, {}, {}, {}, {}

    def c1_load(t):
        @block.sync
        def _(sync, t=t):
            if t - 2 in ap_at:
                sync.wait_ge(vs.h, ap_at[t - 2])
            inc(sync.dma_start(out=xtp[:, (t % 2) * D:(t % 2 + 1) * D],
                               in_=x_in[t * 128:(t + 1) * 128, :]), ld, 16)
        ld_at[t] = ld.n

    def c1_stats(t):
        @block.vector
        def _(v, t=t):
            v.wait_ge(ld.h, ld_at[t])
        rstd_at[t] = emit_ln_stats(block, xtp[:, (t % 2) * D:(t % 2 + 1) * D],
                                   stats[:, (t % 2) * 8:(t % 2) * 8 + 8], scrap, vs, ss)

    def c1_apply(t):
        if t - 2 in sp_at:
            @block.vector
            def _(v, t=t):
                v.wait_ge(gs.h, sp_at[t - 2])
                v.wait_ge(ts.h, xs_at[t - 2])
        emit_ln_apply(block, xtp[:, (t % 2) * D:(t % 2 + 1) * D],
                      fbp[:, (t % 2) * D:(t % 2 + 1) * D],
                      stats[:, (t % 2) * 8:(t % 2) * 8 + 8], scrap, vs, ss, rstd_at[t])
        ap_at[t] = vs.n

        @block.tensor
        def _(t_, t=t):
            t_.wait_ge(vs.h, ap_at[t])
            t_.matmul(psA[0:1, 0:512], onesb[:, 0:1], fbp[:, (t % 2) * D:(t % 2) * D + 512],
                      start=(t == 0), stop=(t == NT - 1))
            mm = t_.matmul(psA[0:1, 512:1024], onesb[:, 0:1],
                           fbp[:, (t % 2) * D + 512:(t % 2 + 1) * D],
                           start=(t == 0), stop=(t == NT - 1))
            inc(mm, ts)
        xs_at[t] = ts.n

        @block.gpsimd
        def _(g_, t=t):
            g_.wait_ge(vs.h, ap_at[t])
            inc(g_.dma_start(out=cin_dram[t * 128:(t + 1) * 128, :],
                             in_=fbp[:, (t % 2) * D:(t % 2 + 1) * D]), gs, 16)
        sp_at[t] = gs.n

    c1_load(0)
    c1_stats(0)
    c1_load(1)
    for t in range(NT):
        if t + 1 < NT:
            c1_stats(t + 1)
        c1_apply(t)
        if t + 2 < NT:
            c1_load(t + 2)

    @block.vector
    def _(v):
        v.wait_ge(ts.h, ts.n)
        inc(v.tensor_copy(vrow[0:1, 0:D], psA[0:1, 0:1024]), vs)

    if debug == "c1":
        @block.sync
        def _(sync):
            sync.wait_ge(gs.h, gs.n)
            sync.wait_ge(vs.h, vs.n)
        return st

    with ExitStack() as ph:
        WqkT = ph.enter_context(nc.sbuf_tensor(pfx + "WqkT", [128, 8 * 2048], BF16))
        WvT = ph.enter_context(nc.sbuf_tensor(pfx + "WvT", [128, 8 * D], BF16))
        WoT = ph.enter_context(nc.sbuf_tensor(pfx + "WoT", [128, 8 * D], BF16))
        wqmb = ph.enter_context(nc.sbuf_tensor(pfx + "wqmb", [128, 8 * 64], BF16))
        qmT = ph.enter_context(nc.sbuf_tensor(pfx + "qmT", [128, T], F32))
        h1T = ph.enter_context(nc.sbuf_tensor(pfx + "h1T", [128, T], F32))
        w1T = ph.enter_context(nc.sbuf_tensor(pfx + "w1T", [128, 32], F32))
        ph_mrow = ExitStack()
        mrow = ph_mrow.enter_context(nc.sbuf_tensor(pfx + "mrow", [1, 2 * T], F32))
        ph_cin = ExitStack()
        cinT = ph_cin.enter_context(nc.sbuf_tensor(pfx + "cinT", [128, 8 * (T // 4)], BF16))
        wrows = ph_cin.enter_context(nc.sbuf_tensor(pfx + "wrows", [128, D], F32))

        @block.sync
        def _(sync):
            sync.wait_ge(vs.h, vs.n)
            sync.wait_ge(ts.h, ts.n)

        @block.gpsimd
        def _(g_):
            g_.wait_ge(vs.h, vs.n)
            g_.wait_ge(ts.h, ts.n)

        for c in range(8):
            @block.sync
            def _(sync, c=c):
                if c > 0:
                    sync.wait_ge(ts.h, ts.n)
                inc(sync.dma_start(out=wrows[:], in_=wts["w_qkv"][c * 128:(c + 1) * 128, :]), ld, 16)

            @block.tensor
            def _(t_, c=c):
                t_.wait_ge(ld.h, ld.n)
                t_.wait_ge(vs.h, vs.n)
                t_.matmul(psA[0:64, 0:512], emat[:], wrows[:, 0:512],
                          start=(c == 0), stop=(c == 7))
                mm = t_.matmul(psA[0:64, 512:1024], emat[:], wrows[:, 512:1024],
                               start=(c == 0), stop=(c == 7))
                inc(mm, ts)

        @block.vector
        def _(v):
            v.wait_ge(ts.h, ts.n)
            inc(v.tensor_copy(scrap[0:64, 0:D], psA[0:64, 0:1024]), vs)

        for c in range(8):
            @block.tensor
            def _(t_, c=c):
                t_.wait_ge(vs.h, vs.n)
                inc(t_.transpose(psB[:, 0:64], scrap[0:64, c * 128:(c + 1) * 128],
                                 idT[0:64, 0:64]), ts)

            @block.vector
            def _(v, c=c):
                v.wait_ge(ts.h, ts.n)
                inc(v.tensor_copy(wqmb[:, c * 64:(c + 1) * 64], psB[:, 0:64]), vs)

        @block.scalar
        def _(s):
            s.wait_ge(gs.h, gs.n)
            for c in range(8):
                inc(s.dma_start(out=WqkT[:, c * 2048:c * 2048 + 1024],
                                in_=qkvbf[0:1024, c * 128:(c + 1) * 128], transpose=True), trs, 16)
                inc(s.dma_start(out=WqkT[:, c * 2048 + 1024:(c + 1) * 2048],
                                in_=qkvbf[1024:2048, c * 128:(c + 1) * 128], transpose=True), trs, 16)
                inc(s.dma_start(out=WvT[:, c * D:(c + 1) * D],
                                in_=qkvbf[2048:3072, c * 128:(c + 1) * 128], transpose=True), trs, 16)
                inc(s.dma_start(out=WoT[:, c * D:(c + 1) * D],
                                in_=wobf[0:1024, c * 128:(c + 1) * 128], transpose=True), trs, 16)

        TH2 = T // 4
        for half in range(4):
            @block.scalar
            def _(s, half=half):
                if half > 0:
                    s.wait_ge(ts.h, ts.n)
                for c in range(8):
                    inc(s.dma_start(out=cinT[:, c * TH2:(c + 1) * TH2],
                                    in_=cin_dram[half * TH2:(half + 1) * TH2,
                                                 c * 128:(c + 1) * 128], transpose=True), trs, 16)

            for n in range(2):
                @block.tensor
                def _(t_, half=half, n=n):
                    t_.wait_ge(trs.h, trs.n)
                    t_.wait_ge(vs.h, vs.n)
                    mm = None
                    for c in range(8):
                        mm = t_.matmul(psA[0:64, 0:512], wqmb[:, c * 64:(c + 1) * 64],
                                       cinT[:, c * TH2 + n * 512: c * TH2 + (n + 1) * 512],
                                       start=(c == 0), stop=(c == 7))
                    inc(mm, ts)

                @block.vector
                def _(v, half=half, n=n):
                    v.wait_ge(ts.h, ts.n)
                    inc(v.tensor_copy(
                        qmT[0:64, half * TH2 + n * 512: half * TH2 + (n + 1) * 512],
                        psA[0:64, 0:512]), vs)

        ph_cin.close()

        if debug:
            dbg_qm = nc.dram_tensor(pfx + "dbg_qm", [64, T], F32, kind="ExternalOutput").ap()
            dbg_wqm = nc.dram_tensor(pfx + "dbg_wqm", [128, 8 * 64], BF16, kind="ExternalOutput").ap()

            @block.gpsimd
            def _(g_):
                g_.wait_ge(vs.h, vs.n)
                inc(g_.dma_start(out=dbg_wqm[:, :], in_=wqmb[:]), gs, 16)

            @block.gpsimd
            def _(g_):
                g_.wait_ge(vs.h, vs.n)
                inc(g_.dma_start(out=dbg_qm[:, :], in_=qmT[0:64, :]), gs, 16)

        if debug == "c2":
            @block.sync
            def _(sync):
                sync.wait_ge(gs.h, gs.n)
                sync.wait_ge(vs.h, vs.n)
                sync.wait_ge(ts.h, ts.n)
            ph_mrow.close()
            return st

        @block.sync
        def _(sync):
            sync.wait_ge(vs.h, vs.n)
            sync.wait_ge(ts.h, ts.n)

        @block.gpsimd
        def _(g_):
            g_.wait_ge(vs.h, vs.n)
            g_.wait_ge(ts.h, ts.n)

        @block.scalar
        def _(s):
            s.wait_ge(vs.h, vs.n)
            inc(s.activation(out=h1T[0:64, :], in_=qmT[0:64, :], func=AF.Square), ss)

        for n in range(8):
            @block.tensor
            def _(t_, n=n):
                t_.wait_ge(ss.h, ss.n)
                t_.wait_ge(vs.h, vs.n)
                t_.matmul(psA[0:1, 0:512], ones[0:64, 0:1],
                          qmT[0:64, n * 512:(n + 1) * 512], start=True, stop=True)
                mm = t_.matmul(psA[0:1, 512:1024], ones[0:64, 0:1],
                               h1T[0:64, n * 512:(n + 1) * 512], start=True, stop=True)
                inc(mm, ts)

            @block.vector
            def _(v, n=n):
                v.wait_ge(ts.h, ts.n)
                inc(v.tensor_scalar(out=mrow[0:1, n * 512:(n + 1) * 512], in0=psA[0:1, 0:512],
                                    scalar1=1.0 / 64, op0=ALU.mult, scalar2=None), vs)
                inc(v.tensor_scalar(out=mrow[0:1, T + n * 512:T + (n + 1) * 512],
                                    in0=psA[0:1, 512:1024],
                                    scalar1=1.0 / 64, op0=ALU.mult, scalar2=None), vs)


        if debug == "c2s":
            @block.sync
            def _(sync):
                sync.wait_ge(gs.h, gs.n)
                sync.wait_ge(vs.h, vs.n)
                sync.wait_ge(ts.h, ts.n)
            ph_mrow.close()
            return st

        @block.vector
        def _(v):
            v.wait_ge(vs.h, vs.n)
            v.tensor_tensor(out=h1T[0:1, 0:T], in0=mrow[0:1, 0:T], in1=mrow[0:1, 0:T],
                            op=ALU.mult)
            inc(v.tensor_tensor(out=mrow[0:1, T:2 * T], in0=mrow[0:1, T:2 * T],
                                in1=h1T[0:1, 0:T], op=ALU.subtract), vs)
            v.wait_ge(vs.h, vs.n)
            inc(v.tensor_scalar(out=mrow[0:1, T:2 * T], in0=mrow[0:1, T:2 * T],
                                scalar1=LN_EPS, op0=ALU.add, scalar2=None), vs)

        @block.scalar
        def _(s):
            s.wait_ge(vs.h, vs.n)
            inc(s.activation(out=h1T[0:1, 0:T], in_=mrow[0:1, T:2 * T], func=AF.Ln), ss)
            s.wait_ge(ss.h, ss.n)
            inc(s.activation(out=mrow[0:1, T:2 * T], in_=h1T[0:1, 0:T], func=AF.Exp,
                             scale=-0.5), ss)


        if debug == "c2m":
            @block.sync
            def _(sync):
                sync.wait_ge(gs.h, gs.n)
                sync.wait_ge(vs.h, vs.n)
                sync.wait_ge(ts.h, ts.n)
            ph_mrow.close()
            return st

        @block.sync
        def _(sync):
            inc(sync.dma_start(out=rowbuf[0:1, 0:64], in_=wts["ln_g"][None, :]), ld, 16)
            inc(sync.dma_start(out=rowbuf[0:1, 64:128], in_=wts["ln_b"][None, :]), ld, 16)
            inc(sync.dma_start(out=rowbuf[0:1, 128:144], in_=wts["b1"][None, :]), ld, 16)
            inc(sync.dma_start(out=rowbuf[0:1, 144:160], in_=wts["w2"][0:1, :]), ld, 16)
            inc(sync.dma_start(out=rowbuf[0:1, 160:161], in_=wts["b2"][None, :]), ld, 16)
            inc(sync.dma_start(out=gate_tm[0:64, 0:16], in_=wts["w1"].rearrange("a b -> b a"),
                               allow_slow_non_contiguous=True), ld, 16)

        @block.tensor
        def _(t_):
            t_.wait_ge(ld.h, ld.n)
            t_.wait_ge(vs.h, vs.n)
            t_.transpose(psB[0:64, 0:1], rowbuf[0:1, 0:64], ones[0:1, 0:1])
            t_.transpose(psB[0:64, 1:2], rowbuf[0:1, 64:128], ones[0:1, 0:1])
            t_.transpose(psB[0:16, 2:3], rowbuf[0:1, 128:144], ones[0:1, 0:1])
            t_.transpose(psB[0:16, 3:4], rowbuf[0:1, 144:160], ones[0:1, 0:1])
            inc(t_.transpose(psB[0:16, 4:5], rowbuf[0:1, 144:160], ones[0:1, 0:1]), ts)

        @block.vector
        def _(v):
            v.wait_ge(ts.h, ts.n)
            inc(v.tensor_copy(w1T[:, 0:8], psB[:, 0:8]), vs)
            v.wait_ge(ld.h, ld.n)
            inc(v.tensor_copy(w1T[0:64, 8:24], gate_tm[0:64, 0:16]), vs)

        for n in range(8):
            @block.tensor
            def _(t_, n=n):
                t_.wait_ge(vs.h, vs.n)
                t_.matmul(psA[0:64, 0:512], ones[0:1, 0:64], mrow[0:1, n * 512:(n + 1) * 512],
                          start=True, stop=True)
                inc(t_.matmul(psA[0:64, 512:1024], ones[0:1, 0:64],
                              mrow[0:1, T + n * 512:T + (n + 1) * 512], start=True, stop=True), ts)

            @block.vector
            def _(v, n=n):
                v.wait_ge(ts.h, ts.n)
                sl = slice(n * 512, (n + 1) * 512)
                v.tensor_tensor(out=scrap[0:64, 0:512], in0=qmT[0:64, sl], in1=psA[0:64, 0:512],
                                op=ALU.subtract)
                inc(v.engine_nop() if False else v.tensor_tensor(
                    out=scrap[0:64, 512:1024], in0=scrap[0:64, 0:512], in1=psA[0:64, 512:1024],
                    op=ALU.mult), vs)
                v.wait_ge(vs.h, vs.n)
                inc(v.tensor_scalar(out=scrap[0:64, 0:512], in0=scrap[0:64, 512:1024],
                                    scalar1=w1T[0:64, 0:1], op0=ALU.mult,
                                    scalar2=w1T[0:64, 1:2], op1=ALU.add), vs)
                v.wait_ge(vs.h, vs.n)
                inc(v.tensor_copy(qmT[0:64, sl], scrap[0:64, 0:512]), vs)


        if debug:
            dbg_hn = nc.dram_tensor(pfx + "dbg_hn", [64, T], F32, kind="ExternalOutput").ap()

            @block.gpsimd
            def _(g_):
                g_.wait_ge(vs.h, vs.n)
                inc(g_.dma_start(out=dbg_hn[:, :], in_=qmT[0:64, :]), gs, 16)

        if debug == "c2h":
            @block.sync
            def _(sync):
                sync.wait_ge(gs.h, gs.n)
                sync.wait_ge(vs.h, vs.n)
                sync.wait_ge(ts.h, ts.n)
            ph_mrow.close()
            return st

        ph_mrow.close()

        for n in range(8):
            @block.tensor
            def _(t_, n=n):
                t_.wait_ge(vs.h, vs.n)
                if n > 0:
                    t_.wait_ge(ss.h, ss.n)
                inc(t_.matmul(psB[0:16, 0:512], w1T[0:64, 8:24],
                              qmT[0:64, n * 512:(n + 1) * 512], start=True, stop=True), ts)

            @block.scalar
            def _(s, n=n):
                s.wait_ge(ts.h, ts.n)
                inc(s.activation(out=h1T[0:16, n * 512:(n + 1) * 512], in_=psB[0:16, 0:512],
                                 func=AF.Gelu, bias=w1T[0:16, 2:3], scale=1.0), ss)

        if debug:
            dbg_h1 = nc.dram_tensor(pfx + "dbg_h1", [16, T], F32, kind="ExternalOutput").ap()
            dbg_w1T = nc.dram_tensor(pfx + "dbg_w1T", [128, 32], F32, kind="ExternalOutput").ap()

            @block.gpsimd
            def _(g_):
                g_.wait_ge(ss.h, ss.n)
                inc(g_.dma_start(out=dbg_w1T[:, :], in_=w1T[:]), gs, 16)

            @block.gpsimd
            def _(g_):
                g_.wait_ge(ss.h, ss.n)
                inc(g_.dma_start(out=dbg_h1[:, :], in_=h1T[0:16, :]), gs, 16)

        @block.tensor
        def _(t_):
            t_.wait_ge(ss.h, ss.n)
            mm = None
            for t in range(NT):
                mm = t_.matmul(psC[:, t:t + 1], h1T[0:16, t * 128:(t + 1) * 128],
                               w1T[0:16, 3:4], start=True, stop=True)
            mm2 = t_.matmul(psC[:, 40:41], ones[0:1, 0:128], rowbuf[0:1, 160:161],
                            start=True, stop=True)
            inc(mm2, ts)

        @block.vector
        def _(v):
            v.wait_ge(ts.h, ts.n)
            inc(v.tensor_scalar(out=scores_tm[:, 0:32], in0=psC[:, 0:32],
                                scalar1=psC[:, 40:41], op0=ALU.add, scalar2=None), vs)

        if debug:
            @block.gpsimd
            def _(g_):
                g_.wait_ge(vs.h, vs.n)
                inc(g_.dma_start(out=dbg_scores[:, :], in_=scores_tm[:]), gs, 16)

        if debug == "c2b":
            @block.sync
            def _(sync):
                sync.wait_ge(gs.h, gs.n)
                sync.wait_ge(vs.h, vs.n)
                sync.wait_ge(ts.h, ts.n)
            return st

        @block.sync
        def _(sync):
            sync.wait_ge(vs.h, vs.n)
            sync.wait_ge(ts.h, ts.n)

        @block.gpsimd
        def _(g_):
            g_.wait_ge(vs.h, vs.n)
            g_.wait_ge(ts.h, ts.n)

        @block.vector
        def _(v):
            v.wait_ge(vs.h, vs.n)
            inc(v.reduce_max(scores_tm[:, 32:33], scores_tm[:, 0:32], axis=AX.X), vs)
            inc(v.tensor_scalar(out=scrap[:, 0:32], in0=scores_tm[:, 0:32], scalar1=-1.0,
                                op0=ALU.mult, scalar2=None), vs)
            v.wait_ge(vs.h, vs.n)
            inc(v.reduce_max(scores_tm[:, 33:34], scrap[:, 0:32], axis=AX.X), vs)

        @block.tensor
        def _(t_):
            t_.wait_ge(vs.h, vs.n)
            t_.transpose(psB[0:1, 0:128], scores_tm[:, 32:33], idT[:, 0:128])
            inc(t_.transpose(psB[0:1, 128:256], scores_tm[:, 33:34], idT[:, 0:128]), ts)

        @block.vector
        def _(v):
            v.wait_ge(ts.h, ts.n)
            inc(v.reduce_max(bis[0:1, 1:2], psB[0:1, 0:128], axis=AX.X), vs)
            inc(v.reduce_max(bis[0:1, 3:4], psB[0:1, 128:256], axis=AX.X), vs)
            v.wait_ge(vs.h, vs.n)
            inc(v.tensor_scalar(out=bis[0:1, 1:2], in0=bis[0:1, 1:2], scalar1=1.0,
                                op0=ALU.add, scalar2=None), vs)
            inc(v.tensor_scalar(out=bis[0:1, 0:1], in0=bis[0:1, 3:4], scalar1=-1.0,
                                op0=ALU.mult, scalar2=-1.0, op1=ALU.add), vs)

        for it in range(45):
            @block.vector
            def _(v):
                v.wait_ge(vs.h, vs.n)
                if it > 0:
                    v.wait_ge(ts.h, ts.n)
                inc(v.tensor_tensor(out=bis[0:1, 2:3], in0=bis[0:1, 0:1], in1=bis[0:1, 1:2],
                                    op=ALU.add), vs)
                v.wait_ge(vs.h, vs.n)
                inc(v.tensor_scalar(out=bis[0:1, 2:3], in0=bis[0:1, 2:3], scalar1=0.5,
                                    op0=ALU.mult, scalar2=None), vs)

            @block.tensor
            def _(t_):
                t_.wait_ge(vs.h, vs.n)
                inc(t_.matmul(psB[:, 300:301], ones[0:1, 0:128], bis[0:1, 2:3],
                              start=True, stop=True), ts)

            @block.vector
            def _(v):
                v.wait_ge(ts.h, ts.n)
                inc(v.tensor_scalar(out=scrap[:, 0:32], in0=scores_tm[:, 0:32],
                                    scalar1=psB[:, 300:301], op0=ALU.is_ge,
                                    scalar2=0.0, op1=ALU.add, accum_out=scores_tm[:, 34:35]), vs)

            @block.tensor
            def _(t_):
                t_.wait_ge(vs.h, vs.n)
                inc(t_.matmul(psB[0:1, 310:311], scores_tm[:, 34:35], ones[:, 0:1],
                              start=True, stop=True), ts)

            @block.vector
            def _(v):
                v.wait_ge(ts.h, ts.n)
                inc(v.tensor_scalar(out=bis[0:1, 4:5], in0=psB[0:1, 310:311], scalar1=255.5,
                                    op0=ALU.is_ge, scalar2=None), vs)
                v.wait_ge(vs.h, vs.n)
                inc(v.tensor_tensor(out=bis[0:1, 5:6], in0=bis[0:1, 2:3], in1=bis[0:1, 0:1],
                                    op=ALU.subtract), vs)
                inc(v.tensor_tensor(out=bis[0:1, 6:7], in0=bis[0:1, 1:2], in1=bis[0:1, 2:3],
                                    op=ALU.subtract), vs)
                v.wait_ge(vs.h, vs.n)
                inc(v.tensor_tensor(out=bis[0:1, 5:6], in0=bis[0:1, 5:6], in1=bis[0:1, 4:5],
                                    op=ALU.mult), vs)
                inc(v.tensor_tensor(out=bis[0:1, 6:7], in0=bis[0:1, 6:7], in1=bis[0:1, 4:5],
                                    op=ALU.mult), vs)
                v.wait_ge(vs.h, vs.n)
                inc(v.tensor_tensor(out=bis[0:1, 0:1], in0=bis[0:1, 0:1], in1=bis[0:1, 5:6],
                                    op=ALU.add), vs)
                inc(v.tensor_tensor(out=bis[0:1, 1:2], in0=bis[0:1, 2:3], in1=bis[0:1, 6:7],
                                    op=ALU.add), vs)

        if debug:
            dbg_bis = nc.dram_tensor(pfx + "dbg_bis", [1, 16], F32, kind="ExternalOutput").ap()

            @block.gpsimd
            def _(g_):
                g_.wait_ge(vs.h, vs.n)
                inc(g_.dma_start(out=dbg_bis[:, :], in_=bis[:]), gs, 16)

        @block.tensor
        def _(t_):
            t_.wait_ge(vs.h, vs.n)
            inc(t_.matmul(psB[:, 300:301], ones[0:1, 0:128], bis[0:1, 0:1],
                          start=True, stop=True), ts)

        @block.scalar
        def _(s):
            s.wait_ge(vs.h, vs.n)
            inc(s.activation(out=gate_tm[:, 0:32], in_=scores_tm[:, 0:32], func=AF.Exp,
                             scale=-1.0), ss)

        @block.vector
        def _(v):
            v.wait_ge(ts.h, ts.n)
            v.wait_ge(ss.h, ss.n)
            inc(v.tensor_scalar(out=gate_tm[:, 0:32], in0=gate_tm[:, 0:32],
                                scalar1=1.0, op0=ALU.add, scalar2=None), vs)
            v.wait_ge(vs.h, vs.n)
            inc(v.reciprocal(gate_tm[:, 0:32], gate_tm[:, 0:32]), vs)
            v.wait_ge(vs.h, vs.n)
            inc(v.tensor_scalar(out=scores_tm[:, 0:32], in0=scores_tm[:, 0:32],
                                scalar1=psB[:, 300:301], op0=ALU.is_ge, scalar2=None), vs)
            v.wait_ge(vs.h, vs.n)
            inc(v.tensor_tensor(out=gate_tm[:, 0:32], in0=gate_tm[:, 0:32],
                                in1=scores_tm[:, 0:32], op=ALU.mult), vs)

        @block.tensor
        def _(t_):
            t_.wait_ge(vs.h, vs.n)
            t_.matmul(psA[:, 0:32], ltri[:], scores_tm[:, 0:32], start=True, stop=False)
            inc(t_.matmul(psC[0:1, 0:32], ones[:, 0:1], scores_tm[:, 0:32],
                          start=True, stop=True), ts)

        @block.vector
        def _(v):
            v.wait_ge(ts.h, ts.n)
            inc(v.tensor_copy(rowbuf[0:1, 232:264], psC[0:1, 0:32]), vs)
            v.wait_ge(vs.h, vs.n)
            inc(v.tensor_tensor_scan(out=rowbuf[0:1, 200:232], data0=rowbuf[0:1, 232:264],
                                     data1=rowbuf[0:1, 232:264], initial=0.0,
                                     op0=ALU.add, op1=ALU.bypass), vs)

        @block.tensor
        def _(t_):
            t_.wait_ge(vs.h, vs.n)
            inc(t_.matmul(psA[:, 1:32], ones[0:1, 0:128], rowbuf[0:1, 200:231],
                          start=False, stop=True), ts)

        @block.vector
        def _(v):
            v.wait_ge(ts.h, ts.n)
            inc(v.tensor_tensor(out=gate_tm[:, 32:64] if False else scrap[:, 0:32],
                                in0=psA[:, 0:32], in1=scores_tm[:, 0:32], op=ALU.mult), vs)

        for j in range(NT):
            @block.vector
            def _(v, j=j):
                v.wait_ge(vs.h, vs.n)
                if j > 0:
                    v.wait_ge(ts.h, ts.n)
                inc(v.tensor_scalar(out=scrap[:, 64:320], in0=iotaS[:],
                                    scalar1=scrap[:, j:j + 1], op0=ALU.is_equal, scalar2=None), vs)
                v.wait_ge(vs.h, vs.n)
                v.tensor_copy(scrap[:, 320:321], tokid[:, j:j + 1])
                inc(v.tensor_copy(scrap[:, 321:322], gate_tm[:, j:j + 1]), vs)

            @block.tensor
            def _(t_, j=j):
                t_.wait_ge(vs.h, vs.n)
                t_.matmul(psC[0:1, 0:256], scrap[:, 320:321], scrap[:, 64:320],
                          start=(j == 0), stop=(j == NT - 1))
                inc(t_.matmul(psC[0:1, 256:512], scrap[:, 321:322], scrap[:, 64:320],
                              start=(j == 0), stop=(j == NT - 1)), ts)

        @block.vector
        def _(v):
            v.wait_ge(ts.h, ts.n)
            inc(v.tensor_copy(rowbuf[0:1, 0:256], psC[0:1, 0:256]), vs)
            inc(v.tensor_copy(rowbuf[0:1, 256:512], psC[0:1, 256:512]), vs)

        @block.tensor
        def _(t_):
            t_.wait_ge(vs.h, vs.n)
            t_.transpose(psB[:, 0:1], rowbuf[0:1, 0:128], ones[0:1, 0:1])
            t_.transpose(psB[:, 1:2], rowbuf[0:1, 128:256], ones[0:1, 0:1])
            t_.transpose(psB[:, 2:3], rowbuf[0:1, 256:384], ones[0:1, 0:1])
            t_.transpose(psB[:, 3:4], rowbuf[0:1, 384:512], ones[0:1, 0:1])
            inc(t_.matmul(psA[:, 0:256], ones[0:1, 0:128], rowbuf[0:1, 256:512],
                          start=True, stop=True), ts)

        @block.vector
        def _(v):
            v.wait_ge(ts.h, ts.n)
            inc(v.tensor_copy(idx_sb[:, 0:2], psB[:, 0:2]), vs)
            inc(v.tensor_copy(gsel_col[:, 0:2], psB[:, 2:4]), vs)
            inc(v.tensor_copy(gbc[:], psA[:, 0:256]), vs)

        if debug:
            @block.gpsimd
            def _(g_):
                g_.wait_ge(vs.h, vs.n)
                inc(g_.dma_start(out=dbg_idx.rearrange("b s -> (b s)")[None, :],
                                 in_=rowbuf[0:1, 0:512]), gs, 16)

        if debug == "c3":
            @block.sync
            def _(sync):
                sync.wait_ge(gs.h, gs.n)
                sync.wait_ge(vs.h, vs.n)
                sync.wait_ge(ts.h, ts.n)
            return st

        @block.sync
        def _(sync):
            sync.wait_ge(vs.h, vs.n)
            sync.wait_ge(ts.h, ts.n)

        @block.gpsimd
        def _(g_):
            g_.wait_ge(vs.h, vs.n)
            g_.wait_ge(ts.h, ts.n)

        Xsel = ph.enter_context(nc.sbuf_tensor(pfx + "Xsel", [128, 2 * D], BF16))
        XselT = ph.enter_context(nc.sbuf_tensor(pfx + "XselT", [128, 8 * 256], BF16))
        qkselT = ph.enter_context(nc.sbuf_tensor(pfx + "qkselT", [128, 8 * 512], BF16))
        Vsel = ph.enter_context(nc.sbuf_tensor(pfx + "Vsel", [128, 2 * D], BF16))
        outT = ph.enter_context(nc.sbuf_tensor(pfx + "outT", [128, 8 * 256], BF16))
        pTt = ph.enter_context(nc.sbuf_tensor(pfx + "pTt", [128, 512], BF16))
        rdrow = ph.enter_context(nc.sbuf_tensor(pfx + "rdrow", [1, 512], F32))
        rdbc = ph.enter_context(nc.sbuf_tensor(pfx + "rdbc", [128, 256], F32))
        idTb = ph.enter_context(nc.sbuf_tensor(pfx + "idTb", [128, 128], BF16))
        vrowb = ph.enter_context(nc.sbuf_tensor(pfx + "vrowb", [1, D], BF16))
        xcolb = ph.enter_context(nc.sbuf_tensor(pfx + "xcolb", [128, 16], BF16))
        psTb = ph.enter_context(nc.psum_tensor(pfx + "psTb", [128, 256], BF16))

        from concourse.bass import IndirectOffsetOnAxis as IOA

        @block.vector
        def _(v):
            v.wait_ge(vs.h, vs.n)
            inc(v.tensor_copy(idTb[:], idT[:, 0:128]), vs)

        @block.gpsimd
        def _(g_):
            g_.wait_ge(vs.h, vs.n)
            for sb_ in range(2):
                inc(g_.indirect_dma_start(
                    out=Xsel[:, sb_ * D:(sb_ + 1) * D], in_=cin_dram[:, :],
                    out_offset=None,
                    in_offset=IOA(ap=idx_sb[:, sb_:sb_ + 1], axis=0)), gs, 16)

        for sb_ in range(2):
            for c in range(8):
                @block.tensor
                def _(t_, sb_=sb_, c=c):
                    t_.wait_ge(gs.h, gs.n)
                    t_.wait_ge(vs.h, vs.n)
                    inc(t_.transpose(psTb[:, 0:128],
                                     Xsel[:, sb_ * D + c * 128: sb_ * D + (c + 1) * 128],
                                     idTb[:]), ts)

                @block.vector
                def _(v, sb_=sb_, c=c):
                    v.wait_ge(ts.h, ts.n)
                    inc(v.tensor_copy(
                        XselT[:, c * 256 + sb_ * 128: c * 256 + sb_ * 128 + 128],
                        psTb[:, 0:128]), vs)

        for part in range(2):
            for m in range(8):
                @block.tensor
                def _(t_, part=part, m=m):
                    t_.wait_ge(vs.h, vs.n)
                    t_.wait_ge(trs.h, trs.n)
                    mm = None
                    for c in range(8):
                        mm = t_.matmul(psA[:, 0:256],
                                       WqkT[:, c * 2048 + part * 1024 + m * 128:
                                            c * 2048 + part * 1024 + m * 128 + 128],
                                       XselT[:, c * 256:(c + 1) * 256],
                                       start=(c == 0), stop=(c == 7))
                    inc(mm, ts)

                @block.vector
                def _(v, part=part, m=m):
                    v.wait_ge(ts.h, ts.n)
                    inc(v.tensor_copy(qkselT[:, m * 512 + part * 256: m * 512 + part * 256 + 256],
                                      psA[:, 0:256]), vs)

        for sb_ in range(2):
            @block.tensor
            def _(t_, sb_=sb_):
                t_.wait_ge(vs.h, vs.n)
                mm = None
                for c in range(8):
                    for hf in range(2):
                        mm = t_.matmul(psA[:, hf * 512:(hf + 1) * 512],
                                       XselT[:, c * 256 + sb_ * 128: c * 256 + sb_ * 128 + 128],
                                       WvT[:, c * D + hf * 512: c * D + hf * 512 + 512],
                                       start=(c == 0), stop=(c == 7))
                inc(mm, ts)

            @block.vector
            def _(v, sb_=sb_):
                v.wait_ge(ts.h, ts.n)
                inc(v.tensor_copy(Vsel[:, sb_ * D:(sb_ + 1) * D], psA[:, 0:1024]), vs)

        @block.tensor
        def _(t_):
            t_.wait_ge(vs.h, vs.n)
            mm = None
            for c in range(8):
                mm = t_.transpose(psB[:, c:c + 1], vrow[0:1, c * 128:(c + 1) * 128],
                                  ones[0:1, 0:1])
            inc(mm, ts)

        @block.vector
        def _(v):
            v.wait_ge(ts.h, ts.n)
            inc(v.tensor_copy(xcolb[:, 0:8], psB[:, 0:8]), vs)

        @block.tensor
        def _(t_):
            t_.wait_ge(vs.h, vs.n)
            mm = None
            for c in range(8):
                for hf in range(2):
                    mm = t_.matmul(psA[0:1, hf * 512:(hf + 1) * 512], xcolb[:, c:c + 1],
                                   WvT[:, c * D + hf * 512: c * D + hf * 512 + 512],
                                   start=(c == 0), stop=(c == 7))
            for sb_ in range(2):
                mm = t_.matmul(psC[0:1, 0:512], onesb[:, 0:1], Vsel[:, sb_ * D: sb_ * D + 512],
                               start=(sb_ == 0), stop=(sb_ == 1))
                mm = t_.matmul(psC[0:1, 512:1024]
                               if False else psB[0:1, 0:512], onesb[:, 0:1],
                               Vsel[:, sb_ * D + 512: sb_ * D + 1024],
                               start=(sb_ == 0), stop=(sb_ == 1))
            inc(mm, ts)

        @block.vector
        def _(v):
            v.wait_ge(ts.h, ts.n)
            inc(v.tensor_copy(vrow[0:1, D:2 * D], psA[0:1, 0:1024]), vs)
            v.wait_ge(vs.h, vs.n)
            v.tensor_tensor(out=vrow[0:1, 2 * D:2 * D + 512], in0=vrow[0:1, D:D + 512],
                            in1=psC[0:1, 0:512], op=ALU.subtract)
            inc(v.tensor_tensor(out=vrow[0:1, 2 * D + 512:3 * D], in0=vrow[0:1, D + 512:2 * D],
                                in1=psB[0:1, 0:512], op=ALU.subtract), vs)
            v.wait_ge(vs.h, vs.n)
            inc(v.tensor_copy(vrowb[0:1, :], vrow[0:1, 2 * D:3 * D]), vs)

        @block.sync
        def _(sync):
            sync.wait_ge(vs.h, vs.n)
            sync.wait_ge(ts.h, ts.n)

        @block.gpsimd
        def _(g_):
            g_.wait_ge(vs.h, vs.n)
            g_.wait_ge(ts.h, ts.n)

        for h in range(16):
            jb = h // 2
            base = 64 * (h % 2)

            @block.tensor
            def _(t_, h=h, jb=jb, base=base):
                t_.wait_ge(vs.h, vs.n)
                mm = None
                for kc in range(2):
                    mm = t_.matmul(psA[:, kc * 256:(kc + 1) * 256],
                                   qkselT[base:base + 64,
                                          jb * 512 + 256 + kc * 128: jb * 512 + 256 + kc * 128 + 128],
                                   qkselT[base:base + 64, jb * 512: jb * 512 + 256],
                                   start=True, stop=True)
                inc(mm, ts)

            @block.vector
            def _(v, h=h):
                v.wait_ge(ts.h, ts.n)
                for kc in range(2):
                    inc(v.tensor_scalar(out=scrap[:, kc * 256:(kc + 1) * 256],
                                        in0=psA[:, kc * 256:(kc + 1) * 256],
                                        scalar1=gsel_col[:, kc:kc + 1], op0=ALU.mult,
                                        scalar2=0.125, op1=ALU.mult), vs)
                v.wait_ge(vs.h, vs.n)
                for kc in range(2):
                    inc(v.tensor_tensor(out=scrap[:, kc * 256:(kc + 1) * 256],
                                        in0=scrap[:, kc * 256:(kc + 1) * 256],
                                        in1=gbc[:, 0:256], op=ALU.mult), vs)

            @block.scalar
            def _(s, h=h):
                s.wait_ge(vs.h, vs.n)
                inc(s.activation(out=pTt[:], in_=scrap[:, 0:512], func=AF.Exp), ss)

            @block.tensor
            def _(t_, h=h):
                t_.wait_ge(ss.h, ss.n)
                t_.wait_ge(vs.h, vs.n)
                for kc in range(2):
                    t_.matmul(psC[0:64, 0:256], Vsel[:, kc * D + h * 64: kc * D + h * 64 + 64],
                              pTt[:, kc * 256:(kc + 1) * 256],
                              start=(kc == 0), stop=False)
                    t_.matmul(psC[64:65, 0:256], onesb[:, 0:1], pTt[:, kc * 256:(kc + 1) * 256],
                              start=False, stop=False)
                mm = t_.matmul(psC[0:64, 0:256], vrowb[0:1, h * 64: h * 64 + 64],
                               onesrowb[0:1, 0:256], start=False, stop=True)
                inc(mm, ts)

            @block.vector
            def _(v, h=h):
                v.wait_ge(ts.h, ts.n)
                inc(v.tensor_scalar(out=rdrow[0:1, 0:256], in0=psC[64:65, 0:256],
                                    scalar1=3840.0, op0=ALU.add, scalar2=None), vs)
                v.wait_ge(vs.h, vs.n)
                inc(v.reciprocal(rdrow[0:1, 256:512], rdrow[0:1, 0:256]), vs)

            @block.tensor
            def _(t_, h=h):
                t_.wait_ge(vs.h, vs.n)
                inc(t_.matmul(psB[0:64, 0:256], ones[0:1, 0:64], rdrow[0:1, 256:512],
                              start=True, stop=True), ts)

            @block.vector
            def _(v, h=h, jb=jb, base=base):
                v.wait_ge(ts.h, ts.n)
                inc(v.tensor_copy(rdbc[0:64, :], psB[0:64, 0:256]), vs)
                v.wait_ge(vs.h, vs.n)
                inc(v.tensor_tensor(out=outT[base:base + 64, jb * 256:(jb + 1) * 256],
                                    in0=psC[0:64, 0:256], in1=rdbc[0:64, :], op=ALU.mult), vs)

        if debug == "c5":
            @block.sync
            def _(sync):
                sync.wait_ge(gs.h, gs.n)
                sync.wait_ge(vs.h, vs.n)
                sync.wait_ge(ts.h, ts.n)
            return st

        @block.sync
        def _(sync):
            sync.wait_ge(vs.h, vs.n)
            sync.wait_ge(ts.h, ts.n)

        @block.gpsimd
        def _(g_):
            g_.wait_ge(vs.h, vs.n)
            g_.wait_ge(ts.h, ts.n)

        for m in range(8):
            @block.tensor
            def _(t_, m=m):
                t_.wait_ge(vs.h, vs.n)
                mm = None
                for c in range(8):
                    mm = t_.matmul(psA[:, 0:256], WoT[:, c * D + m * 128: c * D + m * 128 + 128],
                                   outT[:, c * 256:(c + 1) * 256],
                                   start=(c == 0), stop=(c == 7))
                inc(mm, ts)

            @block.vector
            def _(v, m=m):
                v.wait_ge(ts.h, ts.n)
                inc(v.tensor_scalar(out=Xsel[:, m * 256: m * 256 + 256], in0=psA[:, 0:256],
                                    scalar1=bout_col[:, m:m + 1], op0=ALU.add, scalar2=None), vs)

        for m in range(8):
            for sb_ in range(2):
                @block.tensor
                def _(t_, m=m, sb_=sb_):
                    t_.wait_ge(vs.h, vs.n)
                    inc(t_.transpose(psTb[:, 128:256],
                                     Xsel[:, m * 256 + sb_ * 128: m * 256 + sb_ * 128 + 128],
                                     idTb[:]), ts)

                @block.vector
                def _(v, m=m, sb_=sb_):
                    v.wait_ge(ts.h, ts.n)
                    inc(v.tensor_copy(Vsel[:, sb_ * D + m * 128: sb_ * D + (m + 1) * 128],
                                      psTb[:, 128:256]), vs)

        @block.vector
        def _(v):
            v.wait_ge(vs.h, vs.n)
            inc(v.tensor_scalar(out=vrow[0:1, 2 * D:3 * D], in0=vrow[0:1, D:2 * D],
                                scalar1=1.0 / T, op0=ALU.mult, scalar2=None), vs)

        @block.tensor
        def _(t_):
            t_.wait_ge(vs.h, vs.n)
            mm = None
            for c in range(8):
                mm = t_.transpose(psB[:, c:c + 1], vrow[0:1, 2 * D + c * 128: 2 * D + (c + 1) * 128],
                                  ones[0:1, 0:1])
            inc(mm, ts)

        @block.vector
        def _(v):
            v.wait_ge(ts.h, ts.n)
            inc(v.tensor_copy(xcolb[:, 8:16], psB[:, 0:8]), vs)

        @block.sync
        def _(sync):
            sync.wait_ge(vs.h, vs.n)
            inc(sync.dma_start(out=rowbuf[0:1, 0:D], in_=wts["b_out"][None, :]), ld, 16)

        @block.tensor
        def _(t_):
            t_.wait_ge(vs.h, vs.n)
            mm = None
            for c in range(8):
                for hf in range(2):
                    mm = t_.matmul(psA[0:1, hf * 512:(hf + 1) * 512], xcolb[:, 8 + c:9 + c],
                                   WoT[:, c * D + hf * 512: c * D + hf * 512 + 512],
                                   start=(c == 0), stop=(c == 7))
            inc(mm, ts)

        @block.vector
        def _(v):
            v.wait_ge(ts.h, ts.n)
            v.wait_ge(ld.h, ld.n)
            inc(v.tensor_tensor(out=vrow[0:1, 3 * D:4 * D], in0=psA[0:1, 0:1024],
                                in1=rowbuf[0:1, 0:D], op=ALU.add), vs)

        @block.tensor
        def _(t_):
            t_.wait_ge(vs.h, vs.n)
            t_.matmul(psA[:, 0:512], ones[0:1, 0:128], vrow[0:1, 3 * D:3 * D + 512],
                      start=True, stop=True)
            inc(t_.matmul(psA[:, 512:1024], ones[0:1, 0:128], vrow[0:1, 3 * D + 512:4 * D],
                          start=True, stop=True), ts)

        @block.vector
        def _(v):
            v.wait_ge(ts.h, ts.n)
            inc(v.tensor_copy(yuni_bc[:], psA[:, 0:1024]), vs)

        rsu = emit_ln_stats(block, yuni_bc[:], stats[:, 0:8], scrap, vs, ss)
        emit_ln_apply(block, yuni_bc[:], scrap[:, D:2 * D], stats[:, 0:8],
                      scrap, vs, ss, rsu)

        corr_gs, corr_ap = {}, {}
        for sb_ in range(2):
            @block.gpsimd
            def _(g_, sb_=sb_):
                g_.wait_ge(vs.h, vs.n)
                inc(g_.indirect_dma_start(
                    out=xtp[:, sb_ * D:(sb_ + 1) * D], in_=x_in[:, :],
                    out_offset=None,
                    in_offset=IOA(ap=idx_sb[:, sb_:sb_ + 1], axis=0)), gs, 16)
            corr_gs[sb_] = gs.n
        corr_sp = {}
        for sb_ in range(2):
            rsc = emit_ln_stats(block, Vsel[:, sb_ * D:(sb_ + 1) * D],
                                stats[:, 8:16], scrap, vs, ss)

            @block.vector
            def _(v, sb_=sb_):
                v.wait_ge(gs.h, corr_gs[sb_])
            emit_ln_apply(block, Vsel[:, sb_ * D:(sb_ + 1) * D],
                          xop[:, sb_ * D:(sb_ + 1) * D], stats[:, 8:16],
                          scrap, vs, ss, rsc,
                          add_ap=xtp[:, sb_ * D:(sb_ + 1) * D])
            corr_ap[sb_] = vs.n

            @block.gpsimd
            def _(g_, sb_=sb_):
                g_.wait_ge(vs.h, corr_ap[sb_])
                inc(g_.dma_start(out=xcorr_dram[sb_ * 128:(sb_ + 1) * 128, :],
                                 in_=xop[:, sb_ * D:(sb_ + 1) * D]), gs, 16)
            corr_sp[sb_] = gs.n

        if debug:
            @block.gpsimd
            def _(g_):
                g_.wait_ge(gs.h, gs.n)
                inc(g_.dma_start(out=dbg_ysel.rearrange("(b p) d -> p b d", p=128),
                                 in_=Vsel[:].rearrange("p (b d) -> p b d", d=D)), gs, 16)

    @block.sync
    def _(sync):
        sync.wait_ge(vs.h, vs.n)
        sync.wait_ge(ts.h, ts.n)

    @block.gpsimd
    def _(g_):
        g_.wait_ge(vs.h, vs.n)
        g_.wait_ge(ts.h, ts.n)

    c7_ld, c7_ap, c7_st = {}, {}, {}
    corr_done = corr_ap[1]

    def c7_load(t):
        @block.sync
        def _(sync, t=t):
            if t - 2 in c7_ap:
                sync.wait_ge(vs.h, c7_ap[t - 2])
            else:
                sync.wait_ge(vs.h, corr_done)
            inc(sync.dma_start(out=xtp[:, (t % 2) * D:(t % 2 + 1) * D],
                               in_=x_in[t * 128:(t + 1) * 128, :]), ld, 16)
        c7_ld[t] = ld.n

    def c7_apply(t):
        @block.vector
        def _(v, t=t):
            v.wait_ge(ld.h, c7_ld[t])
            if t - 2 in c7_st:
                v.wait_ge(st.h, c7_st[t - 2])
            else:
                v.wait_ge(gs.h, corr_sp[t % 2])
            inc(v.tensor_tensor(out=xop[:, (t % 2) * D:(t % 2 + 1) * D],
                                in0=xtp[:, (t % 2) * D:(t % 2 + 1) * D],
                                in1=scrap[:, D:2 * D], op=ALU.add), vs)
        c7_ap[t] = vs.n

        @block.sync
        def _(sync, t=t):
            sync.wait_ge(vs.h, c7_ap[t])
            inc(sync.dma_start(out=x_out[t * 128:(t + 1) * 128, :],
                               in_=xop[:, (t % 2) * D:(t % 2 + 1) * D]), st, 16)
        c7_st[t] = st.n

    c7_load(0)
    c7_load(1)
    for t in range(NT):
        c7_apply(t)
        if t + 2 < NT:
            c7_load(t + 2)

    @block.sync
    def _(sync):
        sync.wait_ge(vs.h, c7_ap[NT - 1])
        inc(sync.dma_start(out=xtp[:, 0:D], in_=xcorr_dram[0:128, :]), ld, 16)
        inc(sync.dma_start(out=xtp[:, D:2 * D], in_=xcorr_dram[128:256, :]), ld, 16)

    @block.gpsimd
    def _(g_):
        g_.wait_ge(st.h, c7_st[NT - 1])
        g_.wait_ge(ld.h, ld.n)
        for sb_ in range(2):
            inc(g_.indirect_dma_start(
                out=x_out[:, :], in_=xtp[:, sb_ * D:(sb_ + 1) * D],
                out_offset=IOA(ap=idx_sb[:, sb_:sb_ + 1], axis=0),
                in_offset=None), gs, 16)

    @block.sync
    def _(sync):
        sync.wait_ge(st.h, st.n)
        sync.wait_ge(gs.h, gs.n)

    return st



def build_prog_c(debug=False):
    nc = bass.Bass("TRN2", target_bir_lowering=False)
    x_in = nc.dram_tensor("x_in", [T, D], F32, kind="ExternalInput").ap()
    wts = {
        "w_qkv": nc.dram_tensor("w_qkv", [3 * D, D], F32, kind="ExternalInput").ap(),
        "w_out": nc.dram_tensor("w_out", [D, D], F32, kind="ExternalInput").ap(),
        "b_out": nc.dram_tensor("b_out", [D], F32, kind="ExternalInput").ap(),
        "ln_g": nc.dram_tensor("ln_g", [64], F32, kind="ExternalInput").ap(),
        "ln_b": nc.dram_tensor("ln_b", [64], F32, kind="ExternalInput").ap(),
        "w1": nc.dram_tensor("w1", [16, 64], F32, kind="ExternalInput").ap(),
        "b1": nc.dram_tensor("b1", [16], F32, kind="ExternalInput").ap(),
        "w2": nc.dram_tensor("w2", [1, 16], F32, kind="ExternalInput").ap(),
        "b2": nc.dram_tensor("b2", [1], F32, kind="ExternalInput").ap(),
    }
    x_out = nc.dram_tensor("x_out", [T, D], F32, kind="ExternalOutput").ap()
    with ExitStack() as ctx:
        block = ctx.enter_context(nc.Block())
        emit_stage_c(nc, ctx, block, x_in, wts, x_out, debug=debug)
    return nc


def build_prog_a():
    nc = bass.Bass("TRN2", target_bir_lowering=False)
    x_in = nc.dram_tensor("x_in", [T, D], F32, kind="ExternalInput").ap()
    x_out = nc.dram_tensor("x_out", [T, D], F32, kind="ExternalOutput").ap()
    with ExitStack() as ctx:
        block = ctx.enter_context(nc.Block())
        emit_stage_a(nc, ctx, block, x_in, x_out)
    return nc


def emit_stage_barrier(block, sems):
    for section in [block.sync, block.gpsimd, block.vector, block.scalar, block.tensor]:
        @section
        def _(e):
            for s in sems:
                e.wait_ge(s.h, s.n)


def build_prog_fused():
    nc = bass.Bass("TRN2", target_bir_lowering=False)
    x_in = nc.dram_tensor("x_in", [T, D], F32, kind="ExternalInput").ap()
    w0 = nc.dram_tensor("w_in", [D, D], F32, kind="ExternalInput").ap()
    h_in = nc.dram_tensor("h_in", [D, D], BF16, kind="ExternalInput").ap()
    wc = {
        "w_qkv": nc.dram_tensor("w_qkv", [3 * D, D], F32, kind="ExternalInput").ap(),
        "w_out": nc.dram_tensor("w_out", [D, D], F32, kind="ExternalInput").ap(),
        "b_out": nc.dram_tensor("b_out", [D], F32, kind="ExternalInput").ap(),
        "ln_g": nc.dram_tensor("ln_g", [64], F32, kind="ExternalInput").ap(),
        "ln_b": nc.dram_tensor("ln_b", [64], F32, kind="ExternalInput").ap(),
        "w1": nc.dram_tensor("w1", [16, 64], F32, kind="ExternalInput").ap(),
        "b1": nc.dram_tensor("b1", [16], F32, kind="ExternalInput").ap(),
        "w2": nc.dram_tensor("w2", [1, 16], F32, kind="ExternalInput").ap(),
        "b2": nc.dram_tensor("b2", [1], F32, kind="ExternalInput").ap(),
    }
    wd = {
        "wq": nc.dram_tensor("wq", [D, D], F32, kind="ExternalInput").ap(),
        "wk": nc.dram_tensor("wk", [256, D], F32, kind="ExternalInput").ap(),
        "wv": nc.dram_tensor("wv", [256, D], F32, kind="ExternalInput").ap(),
        "wo": nc.dram_tensor("wo", [D, D], F32, kind="ExternalInput").ap(),
        "gate_w": nc.dram_tensor("gate_w", [8, D], F32, kind="ExternalInput").ap(),
        "expert_w": nc.dram_tensor("expert_w", [8, D, D], F32, kind="ExternalInput").ap(),
        "anw": nc.dram_tensor("anw", [D], F32, kind="ExternalInput").ap(),
        "fnw": nc.dram_tensor("fnw", [D], F32, kind="ExternalInput").ap(),
        "masks": nc.dram_tensor("masks", [4 * 128, 512], BF16, kind="ExternalInput").ap(),
    }
    x_out = nc.dram_tensor("x_out", [T, D], F32, kind="ExternalOutput").ap()
    x1d = nc.dram_tensor("x1d", [T, D], F32, kind="Internal").ap()
    x2d = nc.dram_tensor("x2d", [T, D], F32, kind="Internal").ap()
    x3d = nc.dram_tensor("x3d", [T, D], F32, kind="Internal").ap()

    with ExitStack() as ctx:
        block = ctx.enter_context(nc.Block())
        sems = mk_sems(nc, ctx, ["ld", "vs", "ss", "ts", "gs", "trs", "st"], "f_")
        with ExitStack() as sctx:
            emit_stage_a(nc, sctx, block, x_in, x1d, sems=sems)
        emit_stage_barrier(block, sems)
        with ExitStack() as sctx:
            emit_stage_b(nc, sctx, block, x1d, w0, h_in, x2d, sems=sems)
        emit_stage_barrier(block, sems)
        with ExitStack() as sctx:
            emit_stage_c(nc, sctx, block, x2d, wc, x3d, sems=sems)
        emit_stage_barrier(block, sems)
        with ExitStack() as sctx:
            emit_stage_d(nc, sctx, block, x3d, wd, x_out, sems=sems)
    return nc


def kernel(L0_attn_x, L0_ffn_weight, L1_attn_w_qkv, L1_attn_w_out, L1_attn_b_out,
           L1_attn_ln_g, L1_attn_ln_b, L1_attn_w1, L1_attn_b1, L1_attn_w2,
           L1_attn_b2, L1_ffn_attn_norm_w, L1_ffn_ffn_norm_w, L1_ffn_wq,
           L1_ffn_wk, L1_ffn_wv, L1_ffn_wo, L1_ffn_gate_w, L1_ffn_expert_w):
    f32 = lambda a: np.ascontiguousarray(np.asarray(a, dtype=np.float32))
    x = f32(L0_attn_x)
    nc = build_prog_fused()
    shared = {
        "w_in": f32(L0_ffn_weight), "h_in": hadamard_bf16(),
        "w_qkv": f32(L1_attn_w_qkv), "w_out": f32(L1_attn_w_out),
        "b_out": f32(L1_attn_b_out), "ln_g": f32(L1_attn_ln_g),
        "ln_b": f32(L1_attn_ln_b), "w1": f32(L1_attn_w1), "b1": f32(L1_attn_b1),
        "w2": f32(L1_attn_w2), "b2": f32(L1_attn_b2),
        "wq": f32(L1_ffn_wq), "wk": f32(L1_ffn_wk), "wv": f32(L1_ffn_wv),
        "wo": f32(L1_ffn_wo), "gate_w": f32(L1_ffn_gate_w),
        "expert_w": f32(L1_ffn_expert_w), "anw": f32(L1_ffn_attn_norm_w),
        "fnw": f32(L1_ffn_ffn_norm_w), "masks": causal_masks_bf16(),
    }
    res = run_bass_kernel_spmd(nc, [dict(shared, x_in=x[b]) for b in range(8)],
                               core_ids=list(range(8)))
    return np.stack([np.asarray(res.results[b]["x_out"]) for b in range(8)], axis=0)
